# version 11
# baseline (speedup 1.0000x reference)
import numpy as np
import ml_dtypes
from contextlib import ExitStack

import concourse.bass as bass
import concourse.mybir as mybir
from concourse.bass import IndirectOffsetOnAxis
from concourse.bass_utils import run_bass_kernel_spmd

F32 = mybir.dt.float32
BF16 = mybir.dt.bfloat16
I32 = mybir.dt.int32
U32 = mybir.dt.uint32
AF = mybir.ActivationFunctionType
ALU = mybir.AluOpType
AX = mybir.AxisListType

B, T, E = 4, 4096, 2048
TOK = 2048
NCORE = 8
NF = 16
QCH = 512
NCH = TOK // QCH
HD = 64
NH = 32
KVD = 512
FFN_H = 5632
LAM_INIT = 0.8 - 0.6 * np.exp(-0.3 * 12)
MOE_E, MOE_K, MOE_I, MOE_CAP = 16, 4, 1408, 1024
MOE_IS = 2816

bf = ml_dtypes.bfloat16


class Ctx:
    ENGS = ("sync", "scalar", "vector", "tensor", "gpsimd")

    def __init__(self, nc, st):
        self.nc = nc
        self.st = st
        self.sem_st = st
        self.q = {e: [] for e in self.ENGS}
        self.entry = {}
        self.cnt = {}
        self.phase_names = []
        self.phys_free = []
        self._nsem = 0

    def _entry(self, name):
        if name not in self.entry:
            if self.phys_free:
                h, base = self.phys_free.pop()
            else:
                self._nsem += 1
                h = self.sem_st.enter_context(self.nc.semaphore(f"S{self._nsem}"))
                base = 0
            self.entry[name] = (h, base)
            self.cnt[name] = 0
            self.phase_names.append(name)
        return self.entry[name]

    def get_sem(self, name):
        return self._entry(name)[0]

    def plan_inc(self, name, by):
        h, base = self._entry(name)
        self.cnt[name] += by
        return h, self.cnt[name]

    def retire_phase(self):
        for n in self.phase_names:
            if n.endswith("_nowait"):
                continue
            h, base = self.entry.pop(n)
            self.phys_free.append((h, base + self.cnt.pop(n)))
        self.phase_names = []

    def emit(self, eng, fn):
        self.q[eng].append(fn)

    def emit_inc(self, eng, name, by, fn):
        h, _ = self.plan_inc(name, by)
        self.q[eng].append(lambda e, fn=fn, s=h, by=by: fn(e).then_inc(s, by))

    def emit_wait(self, eng, name, val=None):
        h, base = self._entry(name)
        v = base + (self.cnt[name] if val is None else val)
        self.q[eng].append(lambda e, s=h, v=v: e.wait_ge(s, v))

    def sbuf(self, name, shape, dtype):
        return self.st.enter_context(self.nc.sbuf_tensor(name, shape, dtype))

    def psum(self, name, shape, dtype=F32):
        return self.st.enter_context(self.nc.psum_tensor(name, shape, dtype))

    def finalize(self):
        nc = self.nc
        with nc.Block() as block:
            @block.sync
            def _(e):
                for f in self.q["sync"]:
                    f(e)

            @block.scalar
            def _(e):
                for f in self.q["scalar"]:
                    f(e)

            @block.vector
            def _(e):
                for f in self.q["vector"]:
                    f(e)

            @block.tensor
            def _(e):
                for f in self.q["tensor"]:
                    f(e)

            @block.gpsimd
            def _(e):
                for f in self.q["gpsimd"]:
                    f(e)


def emit_ln(c, xsrc, xn_dst, pref, ncols=TOK, src_f32=True, onesf=None):
    nc = c.nc
    nch = ncols // QCH
    xt = c.sbuf(f"{pref}_x", [128, 2 * NF * QCH], F32)
    sq = c.sbuf(f"{pref}_sq", [128, QCH], F32)
    at = c.sbuf(f"{pref}_at", [128, QCH], F32)
    mtile = c.sbuf(f"{pref}_m", [128, QCH], F32)
    rtile = c.sbuf(f"{pref}_r", [128, QCH], F32)
    vtile = c.sbuf(f"{pref}_v", [128, QCH], F32)
    ttile = c.sbuf(f"{pref}_t", [128, QCH], F32)
    xnout = c.sbuf(f"{pref}_o", [128, NF * QCH], BF16)
    ps_s = c.psum(f"{pref}_ps_s", [128, QCH], F32)
    ps_q = c.psum(f"{pref}_ps_q", [128, QCH], F32)

    ld = f"{pref}_ld"
    sqd = f"{pref}_sqd"
    mmS = f"{pref}_mmS"
    vch = f"{pref}_vch"
    sch = f"{pref}_sch"
    odone = f"{pref}_od"
    ostored = f"{pref}_os"

    for ch in range(nch):
        buf = ch % 2
        xv = lambda f, b=buf: xt[:, (b * NF + f) * QCH:(b * NF + f + 1) * QCH]
        if ch >= 2:
            c.emit_wait("sync", f"{ostored}{ch % 2}", (ch // 2) * 16)
        src = xsrc[:, ch * QCH:(ch + 1) * QCH].rearrange("(a p) c -> p a c", p=128)
        c.emit_inc("sync", ld, 16,
                   lambda e, s=src, b=buf: e.dma_start(
                       out=xt[:, b * NF * QCH:(b + 1) * NF * QCH].rearrange(
                           "p (a c) -> p a c", a=NF), in_=s))

        qcons = f"{pref}_qc"
        for f in range(NF):
            c.emit_wait("scalar", ld, (ch + 1) * 16)
            c.emit_wait("scalar", qcons)
            c.emit_inc("scalar", sqd, 1,
                       lambda e, f=f, xv=xv: e.activation(sq[:, :], xv(f), AF.Square))
            c.emit_wait("tensor", sqd)
            c.emit_wait("tensor", ld, (ch + 1) * 16)
            if f == 0 and ch >= 1:
                c.emit_wait("tensor", vch)
            fn_s = lambda e, f=f, xv=xv, first=(f == 0): e.matmul(
                ps_s[:, :], onesf[:, :], xv(f), start=first, stop=(f == NF - 1))
            fn_q = lambda e, f=f, first=(f == 0): e.matmul(
                ps_q[:, :], onesf[:, :], sq[:, :], start=first, stop=(f == NF - 1))
            if f == NF - 1:
                c.emit_inc("tensor", mmS, 1, fn_s)
            else:
                c.emit("tensor", fn_s)
            c.emit_inc("tensor", qcons, 1, fn_q)

        c.emit_wait("vector", mmS, ch + 1)
        c.emit_wait("vector", qcons, 16 * (ch + 1))
        c.emit_inc("vector", vch, 1,
                   lambda e: e.tensor_scalar(mtile[:, :], ps_s[:, :], 1.0 / E, None, ALU.mult))
        c.emit_inc("vector", vch, 1,
                   lambda e: e.tensor_scalar(vtile[:, :], ps_q[:, :], 1.0 / E, None, ALU.mult))
        c.emit_wait("vector", vch)
        c.emit_inc("vector", vch, 1,
                   lambda e: e.tensor_tensor(ttile[:, :], mtile[:, :], mtile[:, :], ALU.mult))
        c.emit_wait("vector", vch)
        c.emit_inc("vector", vch, 1,
                   lambda e: e.scalar_tensor_tensor(
                       vtile[:, :], vtile[:, :], 1e-5, ttile[:, :], ALU.add, ALU.subtract))
        c.emit_wait("scalar", vch)
        c.emit_inc("scalar", sch, 1,
                   lambda e: e.activation(ttile[:, :], vtile[:, :], AF.Sqrt))
        c.emit_wait("vector", sch)
        c.emit_inc("vector", vch, 1,
                   lambda e: e.reciprocal(rtile[:, :], ttile[:, :]))

        for f in range(NF):
            c.emit_wait("vector", vch)
            if f == 0 and ch >= 1:
                c.emit_wait("vector", f"{ostored}{(ch - 1) % 2}", ((ch - 1) // 2 + 1) * 16)
            c.emit_inc("vector", vch, 1,
                       lambda e, f=f, xv=xv: e.tensor_tensor(at[:, :], xv(f), mtile[:, :], ALU.subtract))
            c.emit_wait("vector", vch)
            last = (f == NF - 1)
            c.emit_inc("vector", odone if last else vch, 1,
                       lambda e, f=f: e.tensor_tensor(
                           xnout[:, f * QCH:(f + 1) * QCH], at[:, :], rtile[:, :], ALU.mult))

        c.emit_wait("sync", odone, ch + 1)
        dst = xn_dst[:, ch * QCH:(ch + 1) * QCH].rearrange("(a p) c -> p a c", p=128)
        c.emit_inc("sync", f"{ostored}{ch % 2}", 16,
                   lambda e, d=dst: e.dma_start(
                       out=d, in_=xnout[:, :].rearrange("p (a c) -> p a c", a=NF)))
    c.emit_wait("sync", f"{ostored}0")
    c.emit_wait("sync", f"{ostored}1")


def emit_qkv(c, xn_src, wq_src, kvw_src, onesf, qt_dst, kv_own, kv_all_in,
             kv_all_out, qmax_out, pref="B"):
    nc = c.nc
    xt = c.sbuf(f"{pref}_xn", [128, 2 * NF * QCH], BF16)
    wq = c.sbuf(f"{pref}_wq", [128, NF * E], BF16)
    kvw = c.sbuf(f"{pref}_kvw", [128, NF * 1024], BF16)
    qevac = c.sbuf(f"{pref}_qe", [128, 4 * QCH], BF16)
    kvevac = c.sbuf(f"{pref}_kve", [128, 2 * 1024], BF16)
    sq = c.sbuf(f"{pref}_sq", [128, QCH], F32)
    sq2 = c.sbuf(f"{pref}_sq2", [128, TOK], F32)
    qm = c.sbuf(f"{pref}_qm", [1, NH], F32)
    tmp1 = c.sbuf(f"{pref}_t1", [1, 1], F32)
    ps_q = [c.psum(f"{pref}_psq{i}", [128, QCH], F32) for i in range(4)]
    ps_k = c.psum(f"{pref}_psk", [128, 512], F32)
    ps_v = c.psum(f"{pref}_psv", [128, 512], F32)
    ps_n = ps_q[0][0:1, 0:QCH]

    wld = f"{pref}_wld"
    xld = f"{pref}_xld"
    qmm = f"{pref}_qmm"
    qev = f"{pref}_qev"
    qst = f"{pref}_qst"
    kvmm = f"{pref}_kvmm"
    kvev = f"{pref}_kvev"
    kvst = f"{pref}_kvst"
    sqd = f"{pref}_sqd"
    nmm = f"{pref}_nmm"
    vch = f"{pref}_vch"
    agd = f"{pref}_agd"

    for e_i in range(NF):
        c.emit_inc("gpsimd", wld, 16,
                   lambda e, i=e_i: e.dma_start(
                       out=wq[:, i * E:(i + 1) * E], in_=wq_src[i * 128:(i + 1) * 128, :]))
        c.emit_inc("gpsimd", wld, 16,
                   lambda e, i=e_i: e.dma_start(
                       out=kvw[:, i * 1024:(i + 1) * 1024], in_=kvw_src[i * 128:(i + 1) * 128, :]))
    c.emit_inc("vector", vch, 1, lambda e: e.memset(qm[:, :], 0.0))

    for ch in range(NCH):
        buf = ch % 2
        xv = lambda e_i, t0, n, b=buf: xt[:, (b * NF + e_i) * QCH + t0:(b * NF + e_i) * QCH + t0 + n]
        if ch >= 2:
            for par in range(4):
                c.emit_wait("sync", f"{qst}{par}", (ch - 1) * (NF // 4) * 16)
            for par in range(2):
                c.emit_wait("sync", f"{kvst}{par}", (ch - 1) * (QCH // 256) * 16)
        src = xn_src[:, ch * QCH:(ch + 1) * QCH].rearrange("(a p) c -> p a c", p=128)
        c.emit_inc("sync", xld, 16,
                   lambda e, s=src, b=buf: e.dma_start(
                       out=xt[:, b * NF * QCH:(b + 1) * NF * QCH].rearrange(
                           "p (a c) -> p a c", a=NF), in_=s))
        c.emit_wait("tensor", xld, (ch + 1) * 16)
        if ch == 0:
            c.emit_wait("tensor", wld, 32 * 16)

        for qd in range(NF):
            ti = ch * NF + qd
            pq = ps_q[ti % 4]
            if ti >= 4:
                c.emit_wait("tensor", qev, ti - 3)
            for e_i in range(NF):
                fn = lambda e, qd=qd, e_i=e_i, pq=pq, xv=xv: e.matmul(
                    pq[:, :], wq[:, e_i * E + qd * 128:e_i * E + qd * 128 + 128],
                    xv(e_i, 0, QCH), start=(e_i == 0), stop=(e_i == NF - 1))
                if e_i == NF - 1:
                    c.emit_inc("tensor", qmm, 1, fn)
                else:
                    c.emit("tensor", fn)
            c.emit_wait("scalar", qmm, ti + 1)
            qeb = qevac[:, (ti % 4) * QCH:(ti % 4 + 1) * QCH]
            if ti >= 4:
                c.emit_wait("scalar", f"{qst}{ti % 4}", (ti // 4) * 16)
            c.emit_inc("scalar", qev, 1,
                       lambda e, pq=pq, qeb=qeb: e.activation(qeb, pq[:, :], AF.Copy))
            c.emit_wait("sync", qev)
            dst = qt_dst[qd * 128:(qd + 1) * 128, ch * QCH:(ch + 1) * QCH]
            c.emit_inc("sync", f"{qst}{ti % 4}", 16,
                       lambda e, d=dst, qeb=qeb: e.dma_start(out=d, in_=qeb))

        for tt in range(QCH // 128):
            gt = ch * (QCH // 128) + tt
            if gt >= 1:
                c.emit_wait("tensor", kvev, gt)
            for e_i in range(NF):
                c.emit("tensor",
                       lambda e, e_i=e_i, tt=tt, xv=xv: e.matmul(
                           ps_k[:, :], xv(e_i, tt * 128, 128),
                           kvw[:, e_i * 1024:e_i * 1024 + 512],
                           start=(e_i == 0), stop=(e_i == NF - 1)))
                fnv = lambda e, e_i=e_i, tt=tt, xv=xv: e.matmul(
                    ps_v[:, :], xv(e_i, tt * 128, 128),
                    kvw[:, e_i * 1024 + 512:e_i * 1024 + 1024],
                    start=(e_i == 0), stop=(e_i == NF - 1))
                if e_i == NF - 1:
                    c.emit_inc("tensor", kvmm, 1, fnv)
                else:
                    c.emit("tensor", fnv)
            c.emit_wait("vector", kvmm)
            kvb = kvevac[:, (gt % 2) * 1024:(gt % 2 + 1) * 1024]
            if gt >= 2:
                c.emit_wait("vector", f"{kvst}{gt % 2}", (gt // 2) * 16)
            c.emit_inc("vector", vch, 1,
                       lambda e, kvb=kvb: e.tensor_copy(kvb[:, 0:512], ps_k[:, :]))
            c.emit_inc("vector", kvev, 1,
                       lambda e, kvb=kvb: e.tensor_copy(kvb[:, 512:1024], ps_v[:, :]))
            c.emit_wait("sync", kvev)
            dst = kv_own[gt * 128:(gt + 1) * 128, :]
            c.emit_inc("sync", f"{kvst}{gt % 2}", 16,
                       lambda e, d=dst, kvb=kvb: e.dma_start(out=d, in_=kvb))

    qnb = c.sbuf(f"{pref}_qnb", [128, 2 * TOK], BF16)
    c.emit_wait("tensor", qev)
    for qd in range(NF):
        par = qd % 2
        c.emit_wait("sync", f"{qst}{qd % 4}")
        if qd >= 2:
            c.emit_wait("sync", sqd, (qd - 1))
        c.emit_inc("sync", f"{pref}_qnl{par}", 16,
                   lambda e, qd=qd, par=par: e.dma_start(
                       out=qnb[:, par * TOK:(par + 1) * TOK],
                       in_=qt_dst[qd * 128:(qd + 1) * 128, :]))
        c.emit_wait("scalar", f"{pref}_qnl{par}", (qd // 2 + 1) * 16)
        c.emit_wait("scalar", nmm)
        c.emit_inc("scalar", sqd, 1,
                   lambda e, par=par: e.activation(sq2[:, :], qnb[:, par * TOK:(par + 1) * TOK],
                                                   AF.Square))
        c.emit_wait("tensor", sqd)
        for half in range(2):
            for cc in range(TOK // QCH):
                c.emit_wait("tensor", vch)
                c.emit_inc("tensor", nmm, 1,
                           lambda e, half=half, cc=cc: e.matmul(
                               ps_n, onesf[half * 64:half * 64 + 64, 0:1],
                               sq2[half * 64:half * 64 + 64, cc * QCH:(cc + 1) * QCH],
                               start=True, stop=True))
                h_idx = qd * 2 + half
                c.emit_wait("vector", nmm)
                c.emit_inc("vector", vch, 1,
                           lambda e: e.reduce_max(tmp1[:, :], ps_n, axis=AX.X))
                c.emit_wait("vector", vch)
                c.emit_inc("vector", vch, 1,
                           lambda e, h=h_idx: e.tensor_tensor(
                               qm[:, h:h + 1], qm[:, h:h + 1], tmp1[:, :], ALU.max))

    c.emit_wait("sync", vch)
    c.emit_inc("sync", "B_qmst", 16, lambda e: e.dma_start(out=qmax_out, in_=qm[:, :]))
    for par in range(2):
        c.emit_wait("gpsimd", f"{kvst}{par}", NCH * (QCH // 256) * 16)
    rg = [[0, 1], [2, 3], [4, 5], [6, 7]]
    c.emit_inc("gpsimd", agd, 1,
               lambda e: e.collective_compute(
                   "AllGather", ALU.bypass, replica_groups=rg,
                   ins=[kv_all_in.opt()], outs=[kv_all_out.opt()]))
    c.emit_wait("gpsimd", agd, 1)
    c.emit_wait("gpsimd", "B_qmst", 16)


def build_phaseB():
    nc = bass.Bass("TRN2", target_bir_lowering=False, debug=False)
    xn0 = nc.dram_tensor("XN0", [E, TOK], BF16, kind="ExternalInput").ap()
    wq = nc.dram_tensor("WqT", [E, E], F32, kind="ExternalInput").ap()
    kvw = nc.dram_tensor("KVW", [E, 1024], F32, kind="ExternalInput").ap()
    ones_in = nc.dram_tensor("ones128", [128, 128], F32, kind="ExternalInput").ap()
    qt = nc.dram_tensor("QT", [E, TOK], BF16, kind="ExternalOutput").ap()
    kv_own = nc.dram_tensor("KV_OWN", [TOK, 1024], BF16, kind="Internal").ap()
    kv_all_i = nc.dram_tensor("KV_ALL_I", [2 * TOK, 1024], BF16, kind="Internal").ap()
    kv_all = nc.dram_tensor("KV_ALL", [2 * TOK, 1024], BF16, kind="ExternalOutput").ap()
    qmax = nc.dram_tensor("QMAX", [1, NH], F32, kind="ExternalOutput").ap()
    with ExitStack() as st:
        c = Ctx(nc, st)
        onesf = c.sbuf("onesf", [128, 128], F32)
        c.emit_inc("sync", "init", 16, lambda e: e.dma_start(out=onesf[:, :], in_=ones_in))
        c.emit_wait("tensor", "init", 16)
        emit_qkv(c, xn0, wq, kvw, onesf, qt, kv_own, kv_own, kv_all_i, qmax)
        c.emit_inc("gpsimd", "B_cp", 16,
                   lambda e: e.dma_start(out=kv_all, in_=kv_all_i))
        c.emit_wait("gpsimd", "B_cp", 16)
        c.finalize()
    return nc


def _ag_row(t):
    g = t // 4
    return (g % 2) * TOK + (g // 2) * QCH + (t % 4) * 128


def emit_attn(c, qt_src, kv_src, qmax_src, lams_src, lnw_src, mask_src,
              ident_src, onesb_src, onesf, attn_dst, pref="C"):
    nc = c.nc
    NT = 2 * TOK // 128
    kt = c.sbuf(f"{pref}_kt", [128, 9 * 2 * TOK], BF16)
    vt = c.sbuf(f"{pref}_vt", [128, NT * 512], BF16)
    qch = c.sbuf(f"{pref}_q", [128, 2 * NF * QCH], BF16)
    att = c.sbuf(f"{pref}_a", [128, NF * QCH], BF16)
    ksq = c.sbuf(f"{pref}_ksq", [128, 2 * TOK], F32)
    masks = c.sbuf(f"{pref}_mk", [128, 8 * QCH], BF16)
    ident = c.sbuf(f"{pref}_id", [128, 128], BF16)
    onesb = c.sbuf(f"{pref}_ob", [128, 128], BF16)
    qm = c.sbuf(f"{pref}_qm", [1, NH], F32)
    km = c.sbuf(f"{pref}_km", [1, 8], F32)
    bprod = c.sbuf(f"{pref}_bp", [1, NH], F32)
    bh = c.sbuf(f"{pref}_bh", [128, NH], F32)
    lams = c.sbuf(f"{pref}_lam", [1, 256], F32)
    lscr = c.sbuf(f"{pref}_ls", [1, 64], F32)
    lam1 = c.sbuf(f"{pref}_l1", [1, 1], F32)
    lam2 = c.sbuf(f"{pref}_l2", [1, 1], F32)
    nlamb = c.sbuf(f"{pref}_nlb", [128, 1], F32)
    lnw2 = c.sbuf(f"{pref}_lnw", [128, 1], F32)
    epsb = c.sbuf(f"{pref}_eps", [128, 1], F32)
    tmp1 = c.sbuf(f"{pref}_t1", [1, 1], F32)
    expe = c.sbuf(f"{pref}_ee", [128, 2 * QCH], BF16)
    expo = c.sbuf(f"{pref}_eo", [128, 2 * QCH], BF16)
    expem = c.sbuf(f"{pref}_eem", [128, 2 * QCH], BF16)
    expom = c.sbuf(f"{pref}_eom", [128, 2 * QCH], BF16)
    av_e = c.sbuf(f"{pref}_ave", [128, QCH], F32)
    av_o = c.sbuf(f"{pref}_avo", [128, QCH], F32)
    r_e = c.sbuf(f"{pref}_re", [128, QCH], F32)
    r_o = c.sbuf(f"{pref}_ro", [128, QCH], F32)
    o_e = c.sbuf(f"{pref}_oe", [128, QCH], F32)
    o_p = c.sbuf(f"{pref}_op", [128, QCH], F32)
    sqs = c.sbuf(f"{pref}_sqs", [128, QCH], F32)
    stdt = c.sbuf(f"{pref}_std", [128, QCH], F32)
    rr = c.sbuf(f"{pref}_rr", [128, QCH], F32)

    ps_se = [c.psum(f"{pref}_se{i}", [128, QCH], F32) for i in range(2)]
    ps_so = [c.psum(f"{pref}_so{i}", [128, QCH], F32) for i in range(2)]
    ps_n = ps_se[0][0:1, 0:QCH]
    ps_b = ps_se[0][:, 0:1]
    ps_ave = c.psum(f"{pref}_pave", [128, QCH], F32)
    ps_avo = c.psum(f"{pref}_pavo", [128, QCH], F32)
    ps_sme = c.psum(f"{pref}_psme", [128, QCH], F32)
    ps_smo = c.psum(f"{pref}_psmo", [128, QCH], F32)
    ps_ms = ps_se[0]

    ld = f"{pref}_ld"
    kld = f"{pref}_kld"
    tpd = f"{pref}_tpd"
    tpe = f"{pref}_tpe"
    vch = f"{pref}_vch"
    sch = f"{pref}_sch"
    nmm = f"{pref}_nmm"
    qld = f"{pref}_qld"
    ast = f"{pref}_ast"
    smm = f"{pref}_smm"
    exd = f"{pref}_exd"
    mkd = f"{pref}_mkd"
    acc = f"{pref}_acc"
    evd = f"{pref}_evd"
    cmb = f"{pref}_cmb"

    c.emit_inc("sync", ld, 16, lambda e: e.dma_start(out=ident[:, :], in_=ident_src))
    c.emit_inc("sync", ld, 16, lambda e: e.dma_start(out=onesb[:, :], in_=onesb_src))
    c.emit_inc("sync", ld, 16, lambda e: e.dma_start(
        out=masks[:, :].rearrange("p (a c) -> p a c", a=8),
        in_=mask_src.rearrange("(a p) c -> p a c", p=128)))
    c.emit_inc("sync", ld, 16, lambda e: e.dma_start(out=qm[:, :], in_=qmax_src))
    c.emit_inc("sync", ld, 16, lambda e: e.dma_start(out=lams[:, :], in_=lams_src))
    c.emit_inc("sync", ld, 16, lambda e: e.dma_start(out=lnw2[:, :], in_=lnw_src))

    for cb in range(8):
        row = cb * QCH
        g = 2 * (cb % 4) + (cb // 4)
        for c64 in range(8):
            c.emit_inc("sync", tpd, 16,
                       lambda e, row=row, c64=c64, g=g: e.dma_start(
                           out=kt[:, c64 * 2 * TOK + g * 512:c64 * 2 * TOK + (g + 1) * 512],
                           in_=kv_src[row:row + 512, c64 * 64:c64 * 64 + 128],
                           transpose=True))
    c.emit_wait("gpsimd", tpd, 8 * 8 * 16)
    c.emit_inc("gpsimd", f"{pref}_sh", 16,
               lambda e: e.dma_start(
                   out=kt[64:128, 8 * 2 * TOK:9 * 2 * TOK],
                   in_=kt[0:64, 0:2 * TOK]))
    for t in range(NT):
        row = _ag_row(t)
        c.emit_inc("sync", f"{pref}_vld", 16,
                   lambda e, row=row, t=t: e.dma_start(
                       out=vt[:, t * 512:(t + 1) * 512],
                       in_=kv_src[row:row + 128, 512:1024]))

    c.emit_inc("vector", vch, 1, lambda e: e.memset(km[:, :], 0.0))
    c.emit_wait("scalar", tpd, 8 * 8 * 16)
    for kh in range(8):
        c.emit_wait("scalar", nmm)
        c.emit_inc("scalar", sch, 1,
                   lambda e, kh=kh: e.activation(
                       ksq[0:64, :], kt[0:64, kh * 2 * TOK:(kh + 1) * 2 * TOK], AF.Square))
        for cc in range(2 * TOK // QCH):
            c.emit_wait("tensor", sch)
            c.emit_wait("tensor", vch)
            c.emit_inc("tensor", nmm, 1,
                       lambda e, cc=cc: e.matmul(
                           ps_n, onesf[0:64, 0:1],
                           ksq[0:64, cc * QCH:(cc + 1) * QCH],
                           start=True, stop=True))
            c.emit_wait("vector", nmm)
            c.emit_inc("vector", vch, 1,
                       lambda e: e.reduce_max(tmp1[:, :], ps_n, axis=AX.X))
            c.emit_wait("vector", vch)
            c.emit_inc("vector", vch, 1,
                       lambda e, kh=kh: e.tensor_tensor(
                           km[:, kh:kh + 1], km[:, kh:kh + 1], tmp1[:, :], ALU.max))
    c.emit_wait("vector", ld, 6 * 16)
    c.emit_wait("vector", vch)
    for h in range(NH):
        c.emit_inc("vector", vch, 1,
                   lambda e, h=h: e.tensor_tensor(
                       bprod[:, h:h + 1], qm[:, h:h + 1], km[:, h // 4:h // 4 + 1], ALU.mult))
    c.emit_wait("scalar", vch)
    c.emit_inc("scalar", sch, 1, lambda e: e.activation(bprod[:, :], bprod[:, :], AF.Sqrt))
    c.emit_wait("vector", sch)
    c.emit_inc("vector", vch, 1,
               lambda e: e.tensor_scalar(bprod[:, :], bprod[:, :], -0.125, None, ALU.mult))
    for h in range(NH):
        c.emit_wait("tensor", vch)
        c.emit_wait("tensor", f"{pref}_bhe")
        c.emit_inc("tensor", f"{pref}_bhm", 1,
                   lambda e, h=h: e.matmul(
                       ps_b, onesf[0:1, :], bprod[:, h:h + 1], start=True, stop=True))
        c.emit_wait("vector", f"{pref}_bhm")
        c.emit_inc("vector", f"{pref}_bhe", 1,
                   lambda e, h=h: e.tensor_copy(bh[:, h:h + 1], ps_b))
    c.emit_inc("vector", vch, 1,
               lambda e: e.tensor_tensor(lscr[:, :], lams[:, 0:64], lams[:, 64:128], ALU.mult))
    c.emit_wait("vector", vch)
    c.emit_inc("vector", vch, 1, lambda e: e.reduce_sum(lam1[:, :], lscr[:, :], axis=AX.X))
    c.emit_wait("vector", vch)
    c.emit_inc("vector", vch, 1,
               lambda e: e.tensor_tensor(lscr[:, :], lams[:, 128:192], lams[:, 192:256], ALU.mult))
    c.emit_wait("vector", vch)
    c.emit_inc("vector", vch, 1, lambda e: e.reduce_sum(lam2[:, :], lscr[:, :], axis=AX.X))
    c.emit_wait("scalar", vch)
    c.emit_inc("scalar", sch, 1, lambda e: e.activation(lam1[:, :], lam1[:, :], AF.Exp))
    c.emit_inc("scalar", sch, 1, lambda e: e.activation(lam2[:, :], lam2[:, :], AF.Exp))
    c.emit_wait("vector", sch)
    c.emit_inc("vector", vch, 1,
               lambda e: e.scalar_tensor_tensor(
                   tmp1[:, :], lam2[:, :], -1.0, lam1[:, :], ALU.mult, ALU.add))
    c.emit_wait("vector", vch)
    c.emit_inc("vector", vch, 1,
               lambda e: e.tensor_scalar(
                   tmp1[:, :], tmp1[:, :], float(LAM_INIT), -1.0, ALU.add, ALU.mult))
    c.emit_wait("tensor", vch)
    c.emit_inc("tensor", f"{pref}_bhm", 1,
               lambda e: e.matmul(ps_b, onesf[0:1, :], tmp1[:, :], start=True, stop=True))
    c.emit_wait("vector", f"{pref}_bhm")
    c.emit_inc("vector", f"{pref}_bhe", 1, lambda e: e.tensor_copy(nlamb[:, :], ps_b))
    c.emit_inc("vector", vch, 1,
               lambda e: e.tensor_scalar(lnw2[:, :], lnw2[:, :], float(1.0 - LAM_INIT), None, ALU.mult))
    c.emit_inc("vector", vch, 1, lambda e: e.memset(epsb[:, :], 1e-5))

    npair_done = 0
    for j in range(NCH):
        par = j % 2
        span = 8 * j + 8
        if j >= 2:
            c.emit_wait("sync", cmb, (j - 1) * 16)
        src = qt_src[:, j * QCH:(j + 1) * QCH].rearrange("(a p) c -> p a c", p=128)
        c.emit_inc("sync", f"{qld}{par}", 16,
                   lambda e, s=src, par=par: e.dma_start(
                       out=qch[:, par * NF * QCH:(par + 1) * NF * QCH].rearrange(
                           "p (a c) -> p a c", a=NF), in_=s))
        for p in range(NF):
            he, kh, vh = 2 * p, p // 2, p // 4
            q_e = qch[0:64, (par * NF + p) * QCH:(par * NF + p + 1) * QCH]
            q_o = qch[64:128, (par * NF + p) * QCH:(par * NF + p + 1) * QCH]
            kte = lambda t, kh=kh: kt[0:64, kh * 2 * TOK + t * 128:kh * 2 * TOK + (t + 1) * 128]
            koff = (kh - 1) if kh >= 1 else 8
            kto = lambda t, koff=koff: kt[64:128, koff * 2 * TOK + t * 128:koff * 2 * TOK + (t + 1) * 128]
            vtile = lambda t, vh=vh: vt[:, t * 512 + vh * 128:t * 512 + (vh + 1) * 128]
            c.emit_wait("tensor", f"{qld}{par}", (j // 2 + 1) * 16)
            if j == 0 and p == 0:
                c.emit_wait("tensor", tpd, 8 * 8 * 16)
                c.emit_wait("tensor", f"{pref}_sh", 16)
                c.emit_wait("tensor", f"{pref}_vld", NT * 16)
                c.emit_wait("tensor", f"{pref}_bhe")
                c.emit_wait("scalar", f"{pref}_bhe")
                c.emit_wait("scalar", vch)

            masked = {}
            def ee_eo(t):
                tp = t % 2
                if t in masked:
                    return (expem[:, tp * QCH:(tp + 1) * QCH],
                            expom[:, tp * QCH:(tp + 1) * QCH])
                return (expe[:, tp * QCH:(tp + 1) * QCH],
                        expo[:, tp * QCH:(tp + 1) * QCH])

            def emit_av(t, pp=p):
                first, last = (t == 0), (t == span - 1)
                eet, eot = ee_eo(t)
                if t in masked:
                    c.emit_wait("tensor", mkd, masked[t])
                else:
                    c.emit_wait("tensor", exd, 2 * (t + 1) + exd_base)
                if first:
                    c.emit_wait("tensor", evd)
                fn1 = lambda e, t=t, eet=eet, vtile=vtile, first=first, last=last: e.matmul(
                    ps_ave[:, :], vtile(t), eet, start=first, stop=last)
                fn2 = lambda e, t=t, eot=eot, vtile=vtile, first=first, last=last: e.matmul(
                    ps_avo[:, :], vtile(t), eot, start=first, stop=last)
                fn3 = lambda e, eet=eet, first=first, last=last: e.matmul(
                    ps_sme[:, :], onesb[:, :], eet, start=first, stop=last)
                fn4 = lambda e, eot=eot, first=first, last=last: e.matmul(
                    ps_smo[:, :], onesb[:, :], eot, start=first, stop=last)
                c.emit("tensor", fn1)
                c.emit("tensor", fn2)
                c.emit("tensor", fn3)
                c.emit_inc("tensor", acc, 1, fn4)

            exd_base = c.cnt.get(exd, 0)
            for t in range(span):
                tp = t % 2
                s_loc = t - (span - 8)
                c.emit_wait("tensor", exd, max(0, c.cnt.get(exd, 0) - 2))
                if t == 0:
                    c.emit_wait("tensor", f"{pref}_msev")
                c.emit_inc("tensor", smm, 1,
                           lambda e, t=t, q_e=q_e, kte=kte, tp=tp: e.matmul(
                               ps_se[tp][:, :], kte(t), q_e, start=True, stop=True))
                c.emit_inc("tensor", smm, 1,
                           lambda e, t=t, q_o=q_o, kto=kto, tp=tp: e.matmul(
                               ps_so[tp][:, :], kto(t), q_o, start=True, stop=True))
                c.emit_wait("scalar", smm)
                c.emit_wait("scalar", acc)
                c.emit_wait("scalar", mkd)
                c.emit_inc("scalar", exd, 1,
                           lambda e, he=he, tp=tp: e.activation(
                               expe[:, tp * QCH:(tp + 1) * QCH], ps_se[tp][:, :],
                               AF.Exp, bias=bh[:, he:he + 1], scale=0.125))
                c.emit_inc("scalar", exd, 1,
                           lambda e, he=he, tp=tp: e.activation(
                               expo[:, tp * QCH:(tp + 1) * QCH], ps_so[tp][:, :],
                               AF.Exp, bias=bh[:, he + 1:he + 2], scale=0.125))
                if s_loc >= 0:
                    c.emit_wait("vector", exd)
                    c.emit_wait("vector", acc)
                    c.emit_inc("vector", mkd, 1,
                               lambda e, s=s_loc, tp=tp: e.tensor_tensor(
                                   expem[:, tp * QCH:(tp + 1) * QCH],
                                   expe[:, tp * QCH:(tp + 1) * QCH],
                                   masks[:, s * QCH:(s + 1) * QCH], ALU.mult))
                    c.emit_inc("vector", mkd, 1,
                               lambda e, s=s_loc, tp=tp: e.tensor_tensor(
                                   expom[:, tp * QCH:(tp + 1) * QCH],
                                   expo[:, tp * QCH:(tp + 1) * QCH],
                                   masks[:, s * QCH:(s + 1) * QCH], ALU.mult))
                    masked[t] = c.cnt[mkd]
                if t >= 1:
                    emit_av(t - 1)
            emit_av(span - 1)
            npair_done += 1
            c.emit_wait("vector", acc)
            c.emit_inc("vector", evd, 1, lambda e: e.reciprocal(r_e[:, :], ps_sme[:, :]))
            c.emit_inc("vector", evd, 1, lambda e: e.reciprocal(r_o[:, :], ps_smo[:, :]))
            c.emit_wait("scalar", acc)
            c.emit_inc("scalar", evd, 1, lambda e: e.activation(av_e[:, :], ps_ave[:, :], AF.Copy))
            c.emit_inc("scalar", evd, 1, lambda e: e.activation(av_o[:, :], ps_avo[:, :], AF.Copy))
            c.emit_wait("vector", evd)
            c.emit_inc("vector", vch, 1,
                       lambda e: e.tensor_tensor(o_e[:, :], av_e[:, :], r_e[:, :], ALU.mult))
            c.emit_inc("vector", vch, 1,
                       lambda e: e.tensor_tensor(o_p[:, :], av_o[:, :], r_o[:, :], ALU.mult))
            c.emit_wait("vector", vch)
            c.emit_inc("vector", vch, 1,
                       lambda e: e.scalar_tensor_tensor(
                           o_p[:, :], o_p[:, :], nlamb[:, 0:1], o_e[:, :], ALU.mult, ALU.add))
            c.emit_wait("scalar", vch)
            c.emit_inc("scalar", sch, 1, lambda e: e.activation(sqs[:, :], o_p[:, :], AF.Square))
            c.emit_wait("tensor", sch)
            c.emit_wait("tensor", exd)
            c.emit_wait("tensor", f"{pref}_msev")
            c.emit_inc("tensor", f"{pref}_msmm", 1,
                       lambda e: e.matmul(ps_ms[:, :], onesf[:, :], sqs[:, :], start=True, stop=True))
            c.emit_wait("scalar", f"{pref}_msmm")
            c.emit_inc("scalar", f"{pref}_msev", 1,
                       lambda e: e.activation(stdt[:, :], ps_ms[:, :], AF.Sqrt,
                                              bias=epsb[:, 0:1], scale=1.0 / 128.0))
            c.emit_wait("vector", f"{pref}_msev")
            c.emit_inc("vector", vch, 1, lambda e: e.reciprocal(rr[:, :], stdt[:, :]))
            c.emit_wait("vector", vch)
            c.emit_inc("vector", vch, 1,
                       lambda e: e.tensor_tensor(o_p[:, :], o_p[:, :], rr[:, :], ALU.mult))
            c.emit_wait("vector", vch)
            if j >= 1 and p == 0:
                c.emit_wait("vector", ast, j * 16)
            c.emit_inc("vector", cmb, 1,
                       lambda e, p=p: e.tensor_scalar(
                           att[:, p * QCH:(p + 1) * QCH], o_p[:, :], lnw2[:, 0:1], None, ALU.mult))
        c.emit_wait("sync", cmb, (j + 1) * 16)
        dst = attn_dst[:, j * QCH:(j + 1) * QCH].rearrange("(a p) c -> p a c", p=128)
        c.emit_inc("sync", ast, 16,
                   lambda e, d=dst: e.dma_start(
                       out=d, in_=att[:, :].rearrange("p (a c) -> p a c", a=NF)))
    c.emit_wait("sync", ast)


def build_phaseC():
    nc = bass.Bass("TRN2", target_bir_lowering=False, debug=False)
    qt = nc.dram_tensor("QT", [E, TOK], BF16, kind="ExternalInput").ap()
    kv = nc.dram_tensor("KV_ALL", [2 * TOK, 1024], BF16, kind="ExternalInput").ap()
    qmax = nc.dram_tensor("QMAX", [1, NH], F32, kind="ExternalInput").ap()
    lams = nc.dram_tensor("LAMS", [1, 256], F32, kind="ExternalInput").ap()
    lnw = nc.dram_tensor("LNW", [128, 1], F32, kind="ExternalInput").ap()
    mask = nc.dram_tensor("MASK8", [8 * 128, QCH], BF16, kind="ExternalInput").ap()
    ident = nc.dram_tensor("IDENT", [128, 128], BF16, kind="ExternalInput").ap()
    onesb = nc.dram_tensor("ONESB", [128, 128], BF16, kind="ExternalInput").ap()
    ones_in = nc.dram_tensor("ones128", [128, 128], F32, kind="ExternalInput").ap()
    attn = nc.dram_tensor("ATTN", [E, TOK], BF16, kind="ExternalOutput").ap()
    with ExitStack() as st:
        c = Ctx(nc, st)
        onesf = c.sbuf("onesf", [128, 128], F32)
        c.emit_inc("sync", "init", 16, lambda e: e.dma_start(out=onesf[:, :], in_=ones_in))
        c.emit_wait("tensor", "init", 16)
        emit_attn(c, qt, kv, qmax, lams, lnw, mask, ident, onesb, onesf, attn)
        c.finalize()
    return nc


def host_masks(role):
    kk = np.arange(128)[:, None]
    qq = np.arange(QCH)[None, :]
    out = np.zeros((8 * 128, QCH), np.float32)
    for s in range(8):
        out[s * 128:(s + 1) * 128] = (128 * s + kk <= QCH * role + qq)
    return out.astype(bf)


def emit_wo(c, attn_src, wo_src, x_src, x1_dst, pref="D"):
    wo = c.sbuf(f"{pref}_wo", [128, NF * E], BF16)
    an = c.sbuf(f"{pref}_a", [128, 2 * NF * QCH], BF16)
    xr = c.sbuf(f"{pref}_x", [128, 2 * NF * QCH], F32)
    ot = c.sbuf(f"{pref}_o", [128, NF * QCH], F32)
    ps = [c.psum(f"{pref}_ps{i}", [128, QCH], F32) for i in range(4)]

    wld, ald, xld = f"{pref}_wld", f"{pref}_ald", f"{pref}_xld"
    mm, ev, st = f"{pref}_mm", f"{pref}_ev", f"{pref}_st"

    for e_i in range(NF):
        c.emit_inc("gpsimd", wld, 16,
                   lambda e, i=e_i: e.dma_start(
                       out=wo[:, i * E:(i + 1) * E], in_=wo_src[i * 128:(i + 1) * 128, :]))
    for ch in range(NCH):
        par = ch % 2
        if ch >= 2:
            c.emit_wait("sync", st, (ch - 1) * 16)
        asrc = attn_src[:, ch * QCH:(ch + 1) * QCH].rearrange("(a p) c -> p a c", p=128)
        c.emit_inc("sync", f"{ald}{par}", 16,
                   lambda e, s=asrc, par=par: e.dma_start(
                       out=an[:, par * NF * QCH:(par + 1) * NF * QCH].rearrange(
                           "p (a c) -> p a c", a=NF), in_=s))
        xsrc = x_src[:, ch * QCH:(ch + 1) * QCH].rearrange("(a p) c -> p a c", p=128)
        c.emit_inc("sync", f"{xld}{par}", 16,
                   lambda e, s=xsrc, par=par: e.dma_start(
                       out=xr[:, par * NF * QCH:(par + 1) * NF * QCH].rearrange(
                           "p (a c) -> p a c", a=NF), in_=s))
        c.emit_wait("tensor", f"{ald}{par}", (ch // 2 + 1) * 16)
        if ch == 0:
            c.emit_wait("tensor", wld, NF * 16)
        for od in range(NF):
            ti = ch * NF + od
            pq = ps[ti % 4]
            if ti >= 4:
                c.emit_wait("tensor", ev, ti - 3)
            for e_i in range(NF):
                fn = lambda e, od=od, e_i=e_i, pq=pq, par=par: e.matmul(
                    pq[:, :], wo[:, e_i * E + od * 128:e_i * E + od * 128 + 128],
                    an[:, (par * NF + e_i) * QCH:(par * NF + e_i + 1) * QCH],
                    start=(e_i == 0), stop=(e_i == NF - 1))
                if e_i == NF - 1:
                    c.emit_inc("tensor", mm, 1, fn)
                else:
                    c.emit("tensor", fn)
            c.emit_wait("vector", mm)
            c.emit_wait("vector", f"{xld}{par}", (ch // 2 + 1) * 16)
            if od == 0 and ch >= 1:
                c.emit_wait("vector", st, ch * 16)
            c.emit_inc("vector", ev, 1,
                       lambda e, od=od, pq=pq, par=par: e.tensor_tensor(
                           ot[:, od * QCH:(od + 1) * QCH], pq[:, :],
                           xr[:, (par * NF + od) * QCH:(par * NF + od + 1) * QCH], ALU.add))
        c.emit_wait("sync", ev)
        dst = x1_dst[:, ch * QCH:(ch + 1) * QCH].rearrange("(a p) c -> p a c", p=128)
        c.emit_inc("sync", st, 16,
                   lambda e, d=dst: e.dma_start(
                       out=d, in_=ot[:, :].rearrange("p (a c) -> p a c", a=NF)))
    c.emit_wait("sync", st)


def quant_bufs(c, CC):
    return (c.sbuf("Q_w", [128, 2 * CC], F32),
            c.sbuf("Q_ac", [128, 1], F32),
            c.sbuf("Q_rd", [128, 1], F32),
            c.sbuf("Q_cp", [128, 1], F32),
            c.sbuf("Q_cn", [128, 1], F32),
            c.sbuf("Q_s1", [128, CC], BF16),
            c.sbuf("Q_s2", [128, CC], BF16),
            c.sbuf("Q_q", [128, 2 * CC], BF16),
            c.psum("Q_ps", [128, 1], F32))


def emit_quant(c, wt_src, g_dst, R, CC, onesf, pref, shared=None):
    nb = R // 128
    if shared is None:
        shared = quant_bufs(c, CC)
    wt, acc, red, cpos, cneg, s1, s2, qt, ps = shared
    wt0 = wt
    wt = lambda par: wt0[:, par * CC:(par + 1) * CC]
    qt0 = qt
    qt = lambda par: qt0[:, par * CC:(par + 1) * CC]
    s1 = s1[:, 0:CC]
    s2 = s2[:, 0:CC]

    ld, vch, gmm, sch, qd, st = (f"{pref}_ld", f"{pref}_vch", f"{pref}_gmm",
                                 f"{pref}_sch", f"{pref}_qd", f"{pref}_st")
    c.emit_inc("vector", vch, 1, lambda e: e.memset(acc[:, :], 0.0))
    for b_i in range(nb):
        par = b_i % 2
        if b_i >= 2:
            c.emit_wait("sync", vch, 2 * b_i - 1)
        c.emit_inc("sync", f"{ld}{par}", 16,
                   lambda e, b_i=b_i, par=par: e.dma_start(
                       out=wt(par), in_=wt_src[b_i * 128:(b_i + 1) * 128, :]))
        c.emit_wait("vector", f"{ld}{par}", (b_i // 2 + 1) * 16)
        c.emit_wait("vector", vch)
        c.emit_inc("vector", vch, 1,
                   lambda e, par=par: e.tensor_reduce(
                       red[:, :], wt(par), AX.X, ALU.add,
                       apply_absolute_value=True))
        c.emit_wait("vector", vch)
        c.emit_inc("vector", vch, 1,
                   lambda e: e.tensor_tensor(acc[:, :], acc[:, :], red[:, :], ALU.add))
    c.emit_wait("tensor", vch)
    c.emit_inc("tensor", gmm, 1,
               lambda e: e.matmul(ps[:, :], onesf[:, :], acc[:, :], start=True, stop=True))
    c.emit_wait("vector", gmm)
    nelem = float(R * CC)
    c.emit_inc("vector", vch, 1,
               lambda e: e.tensor_scalar(cpos[:, :], ps[:, :], 0.5 / nelem, None, ALU.mult))
    c.emit_inc("vector", vch, 1,
               lambda e: e.tensor_scalar(cneg[:, :], ps[:, :], -0.5 / nelem, None, ALU.mult))
    c.emit_wait("scalar", vch)
    for b_i in range(nb):
        par = b_i % 2
        if b_i >= 2:
            c.emit_wait("sync", sch, 2 * (b_i - 1))
        else:
            c.emit_wait("sync", vch)
        c.emit_inc("sync", f"{ld}2{par}", 16,
                   lambda e, b_i=b_i, par=par: e.dma_start(
                       out=wt(par), in_=wt_src[b_i * 128:(b_i + 1) * 128, :]))
        c.emit_wait("scalar", f"{ld}2{par}", (b_i // 2 + 1) * 16)
        c.emit_wait("scalar", qd)
        c.emit_inc("scalar", sch, 1,
                   lambda e, par=par: e.activation(
                       s1[:, :], wt(par), AF.Sign, bias=cneg[:, 0:1]))
        c.emit_inc("scalar", sch, 1,
                   lambda e, par=par: e.activation(
                       s2[:, :], wt(par), AF.Sign, bias=cpos[:, 0:1]))
        c.emit_wait("vector", sch)
        c.emit_wait("vector", f"{st}{par}", (b_i // 2) * 16)
        c.emit_inc("vector", qd, 3,
                   lambda e, par=par: e.tensor_tensor(
                       qt(par), s1[:, :], s2[:, :], ALU.add))
        c.emit_wait("sync", qd)
        c.emit_inc("sync", f"{st}{par}", 16,
                   lambda e, b_i=b_i, par=par: e.dma_start(
                       out=g_dst[b_i * 128:(b_i + 1) * 128, :],
                       in_=qt(par)))
    c.emit_wait("sync", f"{st}0")
    c.emit_wait("sync", f"{st}1")


def emit_ffn(c, xn_src, g1_src, g2_src, g3_src, x1_src, x1f_dst, pref="F"):
    NHT = FFN_H // 128
    TC = 1024
    ncch = TOK // TC
    xn = c.sbuf(f"{pref}_xn", [128, 2 * NF * TC], BF16)
    g12 = c.sbuf(f"{pref}_g12", [128, 4 * NF * 128], BF16)
    g3t = c.sbuf(f"{pref}_g3", [128, 4 * 512], BF16)
    hid = c.sbuf(f"{pref}_h", [128, NHT * TC], BF16)
    sg = c.sbuf(f"{pref}_sg", [128, 1024], BF16)
    x1t = c.sbuf(f"{pref}_x1", [128, 4 * TC], F32)
    ot = c.sbuf(f"{pref}_o", [128, 4 * TC], F32)
    ps = [c.psum(f"{pref}_ps{i}", [128, 512], F32) for i in range(8)]

    xld, gld, mmh, sgd, vsg = (f"{pref}_xld", f"{pref}_gld", f"{pref}_mmh",
                               f"{pref}_sgd", f"{pref}_vsg")
    g3ld, mmo, x1ld, ev, st = (f"{pref}_g3ld", f"{pref}_mmo", f"{pref}_x1ld",
                               f"{pref}_ev", f"{pref}_st")

    for tc in range(ncch):
        tp = tc % 2
        src = xn_src[:, tc * TC:(tc + 1) * TC].rearrange("(a p) c -> p a c", p=128)
        c.emit_inc("sync", f"{xld}{tp}", 16,
                   lambda e, s=src, tp=tp: e.dma_start(
                       out=xn[:, tp * NF * TC:(tp + 1) * NF * TC].rearrange(
                           "p (a c) -> p a c", a=NF), in_=s))
        for hd in range(NHT):
            par = hd % 2
            it = tc * NHT + hd
            if it >= 2:
                c.emit_wait("sync", mmh, (it - 1) * 2)
            for gsrc, off in ((g1_src, 0), (g2_src, NF * 128)):
                gs = gsrc[:, hd * 128:(hd + 1) * 128].rearrange("(a p) c -> p a c", p=128)
                c.emit_inc("sync", f"{gld}{par}", 16,
                           lambda e, gs=gs, par=par, off=off: e.dma_start(
                               out=g12[:, par * 2 * NF * 128 + off:
                                       par * 2 * NF * 128 + off + NF * 128]
                               .rearrange("p (a c) -> p a c", a=NF), in_=gs))
            c.emit_wait("tensor", f"{gld}{par}", (it // 2 + 1) * 32)
            c.emit_wait("tensor", f"{xld}{tp}", (tc // 2 + 1) * 16)
            if tc > 0 and hd == 0:
                c.emit_wait("tensor", ev)
            for n in range(2):
                bb = 2 * n
                c.emit_wait("tensor", sgd, max(0, c.cnt.get(sgd, 0) - 1))
                c.emit_wait("tensor", vsg, max(0, c.cnt.get(vsg, 0) - 1))
                for gi in range(2):
                    for e_i in range(NF):
                        fn = lambda e, e_i=e_i, par=par, n=n, gi=gi, tp=tp, bb=bb: e.matmul(
                            ps[bb + gi][:, :],
                            g12[:, par * 2 * NF * 128 + gi * NF * 128 + e_i * 128:
                                par * 2 * NF * 128 + gi * NF * 128 + (e_i + 1) * 128],
                            xn[:, (tp * NF + e_i) * TC + n * 512:
                               (tp * NF + e_i) * TC + (n + 1) * 512],
                            start=(e_i == 0), stop=(e_i == NF - 1))
                        if gi == 1 and e_i == NF - 1:
                            c.emit_inc("tensor", mmh, 1, fn)
                        else:
                            c.emit("tensor", fn)
                c.emit_wait("scalar", mmh)
                c.emit_wait("scalar", vsg, max(0, c.cnt.get(vsg, 0) - 1))
                c.emit_inc("scalar", sgd, 1,
                           lambda e, bb=bb, n=n: e.activation(
                               sg[:, n * 512:(n + 1) * 512], ps[bb][:, :],
                               AF.Sigmoid, scale=0.5))
                c.emit_wait("vector", sgd)
                c.emit_inc("vector", vsg, 1,
                           lambda e, hd=hd, n=n, bb=bb: e.scalar_tensor_tensor(
                               hid[:, hd * TC + n * 512:hd * TC + (n + 1) * 512],
                               ps[bb + 1][:, :], 0.5, sg[:, n * 512:(n + 1) * 512],
                               ALU.mult, ALU.mult))

        c.emit_wait("tensor", vsg)
        for grp in range(4):
            gi_glob = tc * 4 + grp
            x1s = x1_src[grp * 512:(grp + 1) * 512, tc * TC:(tc + 1) * TC]
            c.emit_wait("sync", ev)
            c.emit_inc("sync", x1ld, 16,
                       lambda e, s=x1s: e.dma_start(
                           out=x1t[:, :].rearrange("p (a c) -> p a c", a=4),
                           in_=s.rearrange("(a p) c -> p a c", p=128)))
            for hd in range(NHT):
                par = hd % 4
                g3s = g3_src[hd * 128:(hd + 1) * 128, grp * 512:(grp + 1) * 512]
                if hd >= 4:
                    c.emit_wait("sync", mmo, gi_glob * NHT * 8 + (hd - 3) * 8)
                c.emit_inc("sync", f"{g3ld}{par}", 16,
                           lambda e, s=g3s, par=par: e.dma_start(
                               out=g3t[:, par * 512:(par + 1) * 512], in_=s))
                c.emit_wait("tensor", f"{g3ld}{par}",
                            (gi_glob * NHT // 4 + hd // 4 + 1) * 16)
                if hd == 0:
                    c.emit_wait("tensor", ev)
                for o in range(4):
                    for n in range(2):
                        fn = lambda e, o=o, n=n, par=par, hd=hd: e.matmul(
                            ps[o * 2 + n][:, :],
                            g3t[:, par * 512 + o * 128:par * 512 + (o + 1) * 128],
                            hid[:, hd * TC + n * 512:hd * TC + (n + 1) * 512],
                            start=(hd == 0), stop=(hd == NHT - 1))
                        c.emit_inc("tensor", mmo, 1, fn)
            c.emit_wait("vector", mmo)
            c.emit_wait("vector", x1ld)
            c.emit_wait("vector", st)
            for o in range(4):
                for n in range(2):
                    c.emit_inc("vector", ev, 1,
                               lambda e, o=o, n=n: e.scalar_tensor_tensor(
                                   ot[:, o * TC + n * 512:o * TC + (n + 1) * 512],
                                   ps[o * 2 + n][:, :], 0.5,
                                   x1t[:, o * TC + n * 512:o * TC + (n + 1) * 512],
                                   ALU.mult, ALU.add))
            c.emit_wait("sync", ev)
            dstv = x1f_dst[grp * 512:(grp + 1) * 512, tc * TC:(tc + 1) * TC]
            c.emit_inc("sync", st, 16,
                       lambda e, d=dstv: e.dma_start(
                           out=d.rearrange("(a p) c -> p a c", p=128),
                           in_=ot[:, :].rearrange("p (a c) -> p a c", a=4)))
    c.emit_wait("sync", st)
def emit_ln1_tm(c, x1f_src, xn1_dst, xn1tm_dst, onesf, pref="G"):
    emit_ln(c, x1f_src, xn1_dst, pref, ncols=TOK, onesf=onesf)
    tmt = c.sbuf(f"{pref}_tm", [128, 2 * E], BF16)
    tld, tst = f"{pref}_tld", f"{pref}_tst"
    for tt in range(TOK // 128):
        par = tt % 2
        if tt >= 2:
            c.emit_wait("sync", f"{tst}{par}", (tt // 2) * 16)
        c.emit_inc("sync", f"{tld}{par}", 16,
                   lambda e, tt=tt, par=par: e.dma_start(
                       out=tmt[:, par * E:(par + 1) * E],
                       in_=xn1_dst[:, tt * 128:(tt + 1) * 128], transpose=True))
        c.emit_wait("sync", f"{tld}{par}", (tt // 2 + 1) * 16)
        c.emit_inc("sync", f"{tst}{par}", 16,
                   lambda e, tt=tt, par=par: e.dma_start(
                       out=xn1tm_dst[tt * 128:(tt + 1) * 128, :],
                       in_=tmt[:, par * E:(par + 1) * E]))
    c.emit_wait("sync", f"{tst}0")
    c.emit_wait("sync", f"{tst}1")


def emit_l1attn(c, xn1_src, wq_src, bq_src, wkv_src, bkv_src, x1f_src, x2a_dst,
                onesf, pref="H"):
    wq = c.sbuf(f"{pref}_wq", [128, NF * E], BF16)
    wkv = c.sbuf(f"{pref}_wkv", [128, NF * 256], BF16)
    bq = c.sbuf(f"{pref}_bq", [128, NF], F32)
    bkv = c.sbuf(f"{pref}_bkv", [128, 2], F32)
    xn = c.sbuf(f"{pref}_xn", [128, 2 * NF * QCH], BF16)
    x1f = c.sbuf(f"{pref}_x1", [128, NF * QCH], F32)
    qtl = c.sbuf(f"{pref}_qt", [128, NF * QCH], BF16)
    ktl = c.sbuf(f"{pref}_kt", [128, QCH], BF16)
    vtl = c.sbuf(f"{pref}_vt", [128, QCH], F32)
    prod = c.sbuf(f"{pref}_pr", [128, 2 * QCH], F32)
    ssb = c.sbuf(f"{pref}_ss", [128, 2 * QCH], F32)
    esb = c.sbuf(f"{pref}_es", [128, 2 * QCH], F32)
    psb = c.sbuf(f"{pref}_ps", [128, 2 * QCH], F32)
    ot = c.sbuf(f"{pref}_o", [128, NF * QCH], F32)
    ps_q = [c.psum(f"{pref}_pq{i}", [128, QCH], F32) for i in range(4)]
    ps_s = [c.psum(f"{pref}_pss{i}", [128, QCH], F32) for i in range(2)]

    wld, xld, x1ld = f"{pref}_wld", f"{pref}_xld", f"{pref}_x1ld"
    qmm, qev, smm, vch, sch, st = (f"{pref}_qmm", f"{pref}_qev", f"{pref}_smm",
                                   f"{pref}_vch", f"{pref}_sch", f"{pref}_st")

    for e_i in range(NF):
        c.emit_inc("gpsimd", wld, 16,
                   lambda e, i=e_i: e.dma_start(
                       out=wq[:, i * E:(i + 1) * E], in_=wq_src[i * 128:(i + 1) * 128, :]))
        c.emit_inc("gpsimd", wld, 16,
                   lambda e, i=e_i: e.dma_start(
                       out=wkv[:, i * 256:(i + 1) * 256], in_=wkv_src[i * 128:(i + 1) * 128, :]))
    c.emit_inc("gpsimd", wld, 16,
               lambda e: e.dma_start(
                   out=bq[:, :].rearrange("p (a o) -> p a o", o=1),
                   in_=bq_src.rearrange("(a p) o -> p a o", p=128),
                   allow_slow_non_contiguous=True))
    c.emit_inc("gpsimd", wld, 16,
               lambda e: e.dma_start(
                   out=bkv[:, :].rearrange("p (a o) -> p a o", o=1),
                   in_=bkv_src.rearrange("(a p) o -> p a o", p=128),
                   allow_slow_non_contiguous=True))

    for ch in range(NCH):
        par = ch % 2
        if ch >= 2:
            c.emit_wait("sync", st, (ch - 1) * 16)
        src = xn1_src[:, ch * QCH:(ch + 1) * QCH].rearrange("(a p) c -> p a c", p=128)
        c.emit_inc("sync", f"{xld}{par}", 16,
                   lambda e, s=src, par=par: e.dma_start(
                       out=xn[:, par * NF * QCH:(par + 1) * NF * QCH].rearrange(
                           "p (a c) -> p a c", a=NF), in_=s))
        src1 = x1f_src[:, ch * QCH:(ch + 1) * QCH].rearrange("(a p) c -> p a c", p=128)
        c.emit_wait("sync", st, ch * 16)
        c.emit_inc("sync", x1ld, 16,
                   lambda e, s=src1: e.dma_start(
                       out=x1f[:, :].rearrange("p (a c) -> p a c", a=NF), in_=s))
        c.emit_wait("tensor", f"{xld}{par}", (ch // 2 + 1) * 16)
        if ch == 0:
            c.emit_wait("tensor", wld, (2 * NF + 2) * 16)
        for qd in range(NF):
            ti = ch * (NF + 2) + qd
            pq = ps_q[ti % 4]
            if ti >= 4:
                c.emit_wait("tensor", qev, ti - 3)
            for e_i in range(NF):
                fn = lambda e, qd=qd, e_i=e_i, pq=pq, par=par: e.matmul(
                    pq[:, :], wq[:, e_i * E + qd * 128:e_i * E + qd * 128 + 128],
                    xn[:, (par * NF + e_i) * QCH:(par * NF + e_i + 1) * QCH],
                    start=(e_i == 0), stop=(e_i == NF - 1))
                if e_i == NF - 1:
                    c.emit_inc("tensor", qmm, 1, fn)
                else:
                    c.emit("tensor", fn)
            c.emit_wait("vector", qmm)
            c.emit_inc("vector", qev, 1,
                       lambda e, qd=qd, pq=pq: e.tensor_scalar(
                           qtl[:, qd * QCH:(qd + 1) * QCH], pq[:, :],
                           bq[:, qd:qd + 1], None, ALU.add))
        for kv in range(2):
            ti = ch * (NF + 2) + NF + kv
            pq = ps_q[ti % 4]
            c.emit_wait("tensor", qev, ti - 3)
            for e_i in range(NF):
                fn = lambda e, kv=kv, e_i=e_i, pq=pq, par=par: e.matmul(
                    pq[:, :], wkv[:, e_i * 256 + kv * 128:e_i * 256 + kv * 128 + 128],
                    xn[:, (par * NF + e_i) * QCH:(par * NF + e_i + 1) * QCH],
                    start=(e_i == 0), stop=(e_i == NF - 1))
                if e_i == NF - 1:
                    c.emit_inc("tensor", qmm, 1, fn)
                else:
                    c.emit("tensor", fn)
            c.emit_wait("vector", qmm)
            if kv == 0:
                c.emit_inc("vector", qev, 1,
                           lambda e, pq=pq: e.tensor_scalar(
                               ktl[:, :], pq[:, :], bkv[:, 0:1], None, ALU.add))
            else:
                c.emit_inc("vector", qev, 1,
                           lambda e, pq=pq: e.tensor_scalar(
                               vtl[:, :], pq[:, :], bkv[:, 1:2], None, ALU.add))
        for h in range(NF):
            sl = h % 2
            pr = prod[:, sl * QCH:(sl + 1) * QCH]
            ssl = ssb[:, sl * QCH:(sl + 1) * QCH]
            esl = esb[:, sl * QCH:(sl + 1) * QCH]
            pl = psb[:, sl * QCH:(sl + 1) * QCH]
            pss = ps_s[h % 2]
            if h == 0:
                c.emit_wait("vector", qev, (ch + 1) * (NF + 2))
            c.emit_wait("vector", f"{pref}_od", max(0, c.cnt.get(f"{pref}_od", 0) - 1))
            c.emit_inc("vector", vch, 1,
                       lambda e, h=h, pr=pr: e.tensor_tensor(
                           pr, qtl[:, h * QCH:(h + 1) * QCH], ktl[:, :], ALU.mult))
            c.emit_wait("tensor", vch)
            c.emit_wait("tensor", f"{pref}_psev", max(0, c.cnt.get(f"{pref}_psev", 0) - 1))
            c.emit_inc("tensor", smm, 1,
                       lambda e, pr=pr, pss=pss: e.matmul(pss[:, :], onesf[:, :], pr,
                                                          start=True, stop=True))
            c.emit_wait("scalar", smm)
            c.emit_inc("scalar", f"{pref}_psev", 1,
                       lambda e, ssl=ssl, pss=pss: e.activation(
                           ssl, pss[:, :], AF.Copy, scale=float(1.0 / np.sqrt(128.0))))
            c.emit_wait("vector", f"{pref}_psev")
            c.emit_inc("vector", vch, 1,
                       lambda e, ssl=ssl, pl=pl: e.tensor_tensor(pl, ssl, ssl, ALU.subtract))
            c.emit_wait("scalar", vch)
            c.emit_inc("scalar", sch, 1,
                       lambda e, esl=esl, pl=pl: e.activation(esl, pl, AF.Exp))
            c.emit_wait("vector", sch)
            c.emit_inc("vector", vch, 1,
                       lambda e, esl=esl, ssl=ssl: e.reciprocal(ssl, esl))
            c.emit_wait("vector", vch)
            c.emit_inc("vector", vch, 1,
                       lambda e, esl=esl, ssl=ssl, pl=pl: e.tensor_tensor(pl, esl, ssl, ALU.mult))
            c.emit_wait("vector", vch)
            if h == 0:
                c.emit_wait("vector", x1ld, (ch + 1) * 16)
            c.emit_inc("vector", vch, 1,
                       lambda e, pr=pr, pl=pl: e.tensor_tensor(pr, pl, vtl[:, :], ALU.mult))
            c.emit_wait("vector", vch)
            if h == 0 and ch >= 1:
                c.emit_wait("vector", st, ch * 16)
            c.emit_inc("vector", f"{pref}_od", 1,
                       lambda e, h=h, pr=pr: e.tensor_tensor(
                           ot[:, h * QCH:(h + 1) * QCH], pr,
                           x1f[:, h * QCH:(h + 1) * QCH], ALU.add))
        c.emit_wait("sync", f"{pref}_od", (ch + 1) * NF)
        dst = x2a_dst[:, ch * QCH:(ch + 1) * QCH].rearrange("(a p) c -> p a c", p=128)
        c.emit_inc("sync", st, 16,
                   lambda e, d=dst: e.dma_start(
                       out=d, in_=ot[:, :].rearrange("p (a c) -> p a c", a=NF)))
    c.emit_wait("sync", st)


def build_phaseG():
    nc = bass.Bass("TRN2", target_bir_lowering=False, debug=False)
    x1f = nc.dram_tensor("X1F", [E, TOK], F32, kind="ExternalInput").ap()
    ones_in = nc.dram_tensor("ones128", [128, 128], F32, kind="ExternalInput").ap()
    xn1 = nc.dram_tensor("XN1", [E, TOK], BF16, kind="ExternalOutput").ap()
    xn1tm = nc.dram_tensor("XN1TM", [TOK, E], BF16, kind="ExternalOutput").ap()
    with ExitStack() as st:
        c = Ctx(nc, st)
        onesf = c.sbuf("onesf", [128, 128], F32)
        c.emit_inc("sync", "init", 16, lambda e: e.dma_start(out=onesf[:, :], in_=ones_in))
        c.emit_wait("tensor", "init", 16)
        emit_ln1_tm(c, x1f, xn1, xn1tm, onesf)
        c.finalize()
    return nc


def build_phaseH():
    nc = bass.Bass("TRN2", target_bir_lowering=False, debug=False)
    xn1 = nc.dram_tensor("XN1", [E, TOK], BF16, kind="ExternalInput").ap()
    wq = nc.dram_tensor("Wq1T", [E, E], F32, kind="ExternalInput").ap()
    bq = nc.dram_tensor("BQ1", [E, 1], F32, kind="ExternalInput").ap()
    wkv = nc.dram_tensor("WkvT", [E, 256], F32, kind="ExternalInput").ap()
    bkv = nc.dram_tensor("BKV", [256, 1], F32, kind="ExternalInput").ap()
    x1f = nc.dram_tensor("X1F", [E, TOK], F32, kind="ExternalInput").ap()
    ones_in = nc.dram_tensor("ones128", [128, 128], F32, kind="ExternalInput").ap()
    x2a = nc.dram_tensor("X2A", [E, TOK], F32, kind="ExternalOutput").ap()
    with ExitStack() as st:
        c = Ctx(nc, st)
        onesf = c.sbuf("onesf", [128, 128], F32)
        c.emit_inc("sync", "init", 16, lambda e: e.dma_start(out=onesf[:, :], in_=ones_in))
        c.emit_wait("tensor", "init", 16)
        emit_l1attn(c, xn1, wq, bq, wkv, bkv, x1f, x2a, onesf)
        c.finalize()
    return nc



def emit_route(c, xn1_src, gwt_src, ltc_src, agrow_src, iota_src, identf_src,
               onesf, wts_dst, cmbi_dst, bufl, bufm, hist_in, hist_all,
               pref="I"):
    nc = c.nc
    NTT = TOK // 128
    xn = c.sbuf(f"{pref}_xn", [128, 2 * NF * QCH], BF16)
    gwt = c.sbuf(f"{pref}_gw", [128, NF * 16], BF16)
    iota = c.sbuf(f"{pref}_io", [128, 16], F32)
    identf = c.sbuf(f"{pref}_idf", [128, 128], F32)
    agrow = c.sbuf(f"{pref}_ag", [128, NTT], I32)
    sm = c.sbuf(f"{pref}_sm", [128, 16], F32)
    mx = c.sbuf(f"{pref}_mx", [128, 1], F32)
    nm = c.sbuf(f"{pref}_nm", [128, 1], F32)
    topv = c.sbuf(f"{pref}_tv", [128, 8], F32)
    topi = c.sbuf(f"{pref}_ti", [128, 8], U32)
    wsum = c.sbuf(f"{pref}_ws", [128, 1], F32)
    idxall = c.sbuf(f"{pref}_ix", [128, NTT * 4], F32)
    wtsall = c.sbuf(f"{pref}_wt", [128, NTT * 4], F32)
    ohall = c.sbuf(f"{pref}_oh", [128, NTT * 4 * 16], F32)
    hall = c.sbuf(f"{pref}_h", [128, NTT * 16], F32)
    hes = c.sbuf(f"{pref}_he", [128, NTT * 16], F32)
    histr = c.sbuf(f"{pref}_hr", [1, 64], F32)
    hsb = c.sbuf(f"{pref}_hs", [32, 16], F32)
    ltc = c.sbuf(f"{pref}_ltc", [32, 4], F32)
    offr = c.sbuf(f"{pref}_of", [1, 64], F32)
    init = c.sbuf(f"{pref}_in", [16, 4], F32)
    htb = c.sbuf(f"{pref}_ht", [16, QCH], F32)
    zt = c.sbuf(f"{pref}_z", [16, QCH], F32)
    scn = c.sbuf(f"{pref}_sc", [16, QCH + 1], F32)
    t16 = c.sbuf(f"{pref}_t16", [128, 16], F32)
    sl = c.sbuf(f"{pref}_sl", [128, 1], F32)
    dstf = c.sbuf(f"{pref}_df", [128, 1], F32)
    msk = c.sbuf(f"{pref}_mk", [128, 1], F32)
    aux = c.sbuf(f"{pref}_ax", [128, 1], F32)
    dsti = c.sbuf(f"{pref}_di", [128, NTT * 4], I32)
    cmbi = c.sbuf(f"{pref}_ci", [128, NTT * 4], I32)
    ztile = c.sbuf(f"{pref}_zt", [128, 128], I32)
    ps_g = c.psum(f"{pref}_pg", [128, 16], F32)
    ps_x = c.psum(f"{pref}_px", [128, 512], F32)

    ld, xld, gmm, vch, sch = (f"{pref}_ld", f"{pref}_xld", f"{pref}_gmm",
                              f"{pref}_vch", f"{pref}_sch")
    hmm_, agd, tmm, tev = f"{pref}_hmm", f"{pref}_agd", f"{pref}_tmm", f"{pref}_tev"
    scd, std_, zd = f"{pref}_scd", f"{pref}_std", f"{pref}_zd"

    ldg = f"{pref}_ldg"
    for e_i in range(NF):
        c.emit_inc("gpsimd", ldg, 16,
                   lambda e, i=e_i: e.dma_start(
                       out=gwt[:, i * 16:(i + 1) * 16], in_=gwt_src[i * 128:(i + 1) * 128, :]))
    c.emit_inc("sync", ld, 16, lambda e: e.dma_start(out=iota[:, :], in_=iota_src))
    c.emit_inc("sync", ld, 16, lambda e: e.dma_start(out=identf[:, :], in_=identf_src))
    c.emit_inc("sync", ld, 16, lambda e: e.dma_start(out=ltc[:, :], in_=ltc_src))
    c.emit_inc("sync", ld, 16,
               lambda e: e.dma_start(
                   out=agrow[:, :].rearrange("p (a o) -> p a o", o=1),
                   in_=agrow_src.rearrange("(a p) o -> p a o", p=128),
                   allow_slow_non_contiguous=True))
    c.emit_inc("vector", vch, 1, lambda e: e.memset(zt[:, :], 0.0))
    c.emit_inc("vector", vch, 1, lambda e: e.memset(ztile[:, :], 0))

    c.emit_wait("gpsimd", vch, 2)
    c.emit_inc("gpsimd", zd, 16,
               lambda e: e.dma_start(
                   out=bufl[0:TOK * NCORE, :].rearrange("(p a) o -> p a o", p=128),
                   in_=ztile[:, :].rearrange("p (a o) -> p a o", o=1)))
    c.emit_inc("gpsimd", zd, 16,
               lambda e: e.dma_start(
                   out=bufl[TOK * NCORE:TOK * NCORE + 64, :].rearrange(
                       "(o a) b -> o a b", o=1),
                   in_=ztile[0:1, 0:64].rearrange("p (a b) -> p a b", b=1)))

    for tt in range(NTT):
        ch, par = tt // 4, (tt // 4) % 2
        if tt % 4 == 0:
            src = xn1_src[:, ch * QCH:(ch + 1) * QCH].rearrange("(a p) c -> p a c", p=128)
            if ch >= 2:
                c.emit_wait("sync", gmm, (ch - 1) * 4)
            c.emit_inc("sync", f"{xld}{par}", 16,
                       lambda e, s=src, par=par: e.dma_start(
                           out=xn[:, par * NF * QCH:(par + 1) * NF * QCH].rearrange(
                               "p (a c) -> p a c", a=NF), in_=s))
        c.emit_wait("tensor", f"{xld}{par}", (ch // 2 + 1) * 16)
        if tt == 0:
            c.emit_wait("tensor", ld, 4 * 16)
            c.emit_wait("tensor", ldg, NF * 16)
        c.emit_wait("tensor", vch)
        for e_i in range(NF):
            fn = lambda e, e_i=e_i, tt=tt, par=par: e.matmul(
                ps_g[:, :],
                xn[:, (par * NF + e_i) * QCH + (tt % 4) * 128:
                   (par * NF + e_i) * QCH + (tt % 4 + 1) * 128],
                gwt[:, e_i * 16:(e_i + 1) * 16],
                start=(e_i == 0), stop=(e_i == NF - 1))
            if e_i == NF - 1:
                c.emit_inc("tensor", gmm, 1, fn)
            else:
                c.emit("tensor", fn)
        c.emit_wait("vector", gmm)
        c.emit_inc("vector", vch, 1, lambda e: e.reduce_max(mx[:, :], ps_g[:, :], axis=AX.X))
        c.emit_wait("vector", vch)
        c.emit_inc("vector", vch, 1, lambda e: e.tensor_scalar(nm[:, :], mx[:, :], -1.0, None, ALU.mult))
        c.emit_wait("scalar", vch)
        c.emit_inc("scalar", sch, 1,
                   lambda e: e.activation(sm[:, :], ps_g[:, :], AF.Exp, bias=nm[:, 0:1]))
        c.emit_wait("vector", sch)
        c.emit_inc("vector", vch, 1, lambda e: e.reduce_sum(mx[:, :], sm[:, :], axis=AX.X))
        c.emit_wait("vector", vch)
        c.emit_inc("vector", vch, 1, lambda e: e.reciprocal(nm[:, :], mx[:, :]))
        c.emit_wait("vector", vch)
        c.emit_inc("vector", vch, 1,
                   lambda e: e.tensor_scalar(sm[:, :], sm[:, :], nm[:, 0:1], None, ALU.mult))
        c.emit_wait("vector", vch)
        c.emit_inc("vector", vch, 1, lambda e: e.max(topv[:, :], sm[:, :]))
        c.emit_wait("vector", vch)
        c.emit_inc("vector", vch, 1, lambda e: e.max_index(topi[:, :], topv[:, :], sm[:, :]))
        c.emit_wait("vector", vch)
        c.emit_inc("vector", vch, 1,
                   lambda e: e.reduce_sum(wsum[:, :], topv[:, 0:4], axis=AX.X))
        c.emit_wait("vector", vch)
        c.emit_inc("vector", vch, 1,
                   lambda e: e.tensor_scalar(wsum[:, :], wsum[:, :], 1e-20, None, ALU.add))
        c.emit_wait("vector", vch)
        c.emit_inc("vector", vch, 1, lambda e: e.reciprocal(nm[:, :], wsum[:, :]))
        c.emit_wait("vector", vch)
        c.emit_inc("vector", vch, 1,
                   lambda e, tt=tt: e.tensor_scalar(
                       wtsall[:, tt * 4:(tt + 1) * 4], topv[:, 0:4], nm[:, 0:1], None, ALU.mult))
        c.emit_inc("vector", vch, 1,
                   lambda e, tt=tt: e.tensor_copy(idxall[:, tt * 4:(tt + 1) * 4], topi[:, 0:4]))
        c.emit_wait("vector", vch)
        for k in range(4):
            c.emit_inc("vector", vch, 1,
                       lambda e, tt=tt, k=k: e.tensor_scalar(
                           ohall[:, (tt * 4 + k) * 16:(tt * 4 + k + 1) * 16],
                           iota[:, :], idxall[:, tt * 4 + k:tt * 4 + k + 1], None, ALU.is_equal))
        c.emit_wait("vector", vch)
        c.emit_inc("vector", vch, 1,
                   lambda e, tt=tt: e.tensor_tensor(
                       hall[:, tt * 16:(tt + 1) * 16],
                       ohall[:, (tt * 4) * 16:(tt * 4 + 1) * 16],
                       ohall[:, (tt * 4 + 1) * 16:(tt * 4 + 2) * 16], ALU.add))
        c.emit_wait("vector", vch)
        c.emit_inc("vector", vch, 1,
                   lambda e, tt=tt: e.tensor_tensor(
                       hall[:, tt * 16:(tt + 1) * 16], hall[:, tt * 16:(tt + 1) * 16],
                       ohall[:, (tt * 4 + 2) * 16:(tt * 4 + 3) * 16], ALU.add))
        c.emit_wait("vector", vch)
        c.emit_inc("vector", vch, 1,
                   lambda e, tt=tt: e.tensor_tensor(
                       hall[:, tt * 16:(tt + 1) * 16], hall[:, tt * 16:(tt + 1) * 16],
                       ohall[:, (tt * 4 + 3) * 16:(tt * 4 + 4) * 16], ALU.add))
        c.emit_wait("tensor", vch)
        if tt % 4 == 0:
            c.emit_wait("tensor", tev)
        fnh = lambda e, tt=tt: e.matmul(
            ps_x[0:1, 0:16], onesf[:, 0:1], hall[:, tt * 16:(tt + 1) * 16],
            start=(tt % 4 == 0), stop=(tt % 4 == 3))
        c.emit_inc("tensor", hmm_, 1, fnh)
        if tt % 4 == 3:
            c.emit_wait("vector", hmm_)
            c.emit_inc("vector", tev, 1,
                       lambda e, ch=ch: e.tensor_copy(
                           histr[0:1, ch * 16:(ch + 1) * 16], ps_x[0:1, 0:16]))

    c.emit_wait("sync", tev)
    c.emit_inc("sync", f"{pref}_hst", 16,
               lambda e: e.dma_start(
                   out=hist_in.rearrange("(o a) b -> o a b", o=1),
                   in_=histr[0:1, :].rearrange("p (a b) -> p a b", a=4)))
    c.emit_wait("gpsimd", f"{pref}_hst", 16)
    c.emit_inc("gpsimd", agd, 1,
               lambda e: e.collective_compute(
                   "AllGather", ALU.bypass, replica_groups=[list(range(8))],
                   ins=[hist_in.opt()], outs=[hist_all.opt()]))
    c.emit_wait("sync", agd, 1)
    c.emit_inc("sync", f"{pref}_hld", 16, lambda e: e.dma_start(out=hsb[:, :], in_=hist_all))

    for mc in range(NCH):
        c.emit_wait("tensor", f"{pref}_hld", 16)
        c.emit_wait("tensor", tev)
        c.emit_inc("tensor", tmm, 1,
                   lambda e, mc=mc: e.matmul(
                       ps_x[0:1, 0:16], ltc[:, mc:mc + 1], hsb[:, :], start=True, stop=True))
        c.emit_wait("vector", tmm)
        c.emit_inc("vector", tev, 1,
                   lambda e, mc=mc: e.tensor_copy(offr[0:1, mc * 16:(mc + 1) * 16],
                                                  ps_x[0:1, 0:16]))
    for mc in range(NCH):
        c.emit_wait("tensor", tev)
        c.emit_inc("tensor", tmm, 1,
                   lambda e, mc=mc: e.matmul(
                       ps_x[0:16, 0:1], offr[0:1, mc * 16:(mc + 1) * 16],
                       onesf[0:1, 0:1], start=True, stop=True))
        c.emit_wait("vector", tmm)
        c.emit_inc("vector", tev, 1,
                   lambda e, mc=mc: e.tensor_copy(init[:, mc:mc + 1], ps_x[0:16, 0:1]))

    for mc in range(NCH):
        for q4 in range(4):
            tt = mc * 4 + q4
            c.emit_wait("tensor", tev)
            c.emit_inc("tensor", tmm, 1,
                       lambda e, tt=tt: e.transpose(
                           ps_x[0:16, 0:128], hall[:, tt * 16:(tt + 1) * 16], identf[:, :]))
            c.emit_wait("vector", tmm)
            c.emit_inc("vector", tev, 1,
                       lambda e, q4=q4: e.tensor_copy(htb[:, q4 * 128:(q4 + 1) * 128],
                                                      ps_x[0:16, 0:128]))
        c.emit_wait("vector", tev)
        c.emit_inc("vector", vch, 1,
                   lambda e, mc=mc: e.tensor_copy(scn[:, 0:1], init[:, mc:mc + 1]))
        c.emit_wait("vector", vch)
        c.emit_inc("vector", scd, 1,
                   lambda e, mc=mc: e.tensor_tensor_scan(
                       scn[:, 1:QCH + 1], htb[:, :], zt[:, :], init[:, mc:mc + 1],
                       ALU.add, ALU.add))
        for q4 in range(4):
            tt = mc * 4 + q4
            c.emit_wait("tensor", scd)
            c.emit_wait("tensor", tev)
            c.emit_inc("tensor", tmm, 1,
                       lambda e, q4=q4: e.transpose(
                           ps_x[0:128, 0:16], scn[0:16, q4 * 128:(q4 + 1) * 128],
                           identf[0:16, 0:16]))
            c.emit_wait("vector", tmm)
            c.emit_inc("vector", tev, 1,
                       lambda e, tt=tt: e.tensor_copy(hes[:, tt * 16:(tt + 1) * 16],
                                                      ps_x[0:128, 0:16]))

    for tt in range(NTT):
        for k in range(4):
            kk = tt * 4 + k
            c.emit_wait("vector", tev)
            c.emit_wait("vector", vch)
            c.emit_inc("vector", vch, 1,
                       lambda e, tt=tt, kk=kk: e.tensor_tensor(
                           t16[:, :], ohall[:, kk * 16:(kk + 1) * 16],
                           hes[:, tt * 16:(tt + 1) * 16], ALU.mult))
            c.emit_wait("vector", vch)
            c.emit_inc("vector", vch, 1, lambda e: e.reduce_sum(sl[:, :], t16[:, :], axis=AX.X))
            c.emit_wait("vector", vch)
            c.emit_inc("vector", vch, 1,
                       lambda e, kk=kk: e.scalar_tensor_tensor(
                           dstf[:, :], idxall[:, kk:kk + 1], 1024.0, sl[:, :],
                           ALU.mult, ALU.add))
            c.emit_inc("vector", vch, 1,
                       lambda e: e.tensor_scalar(msk[:, :], sl[:, :], 1024.0, None, ALU.is_lt))
            c.emit_wait("vector", vch)
            trash = float(TOK * NCORE + kk)
            c.emit_inc("vector", vch, 1,
                       lambda e, trash=trash: e.tensor_scalar(
                           aux[:, :], dstf[:, :], -trash, None, ALU.add))
            c.emit_wait("vector", vch)
            c.emit_inc("vector", vch, 1,
                       lambda e: e.tensor_tensor(aux[:, :], aux[:, :], msk[:, :], ALU.mult))
            c.emit_wait("vector", vch)
            c.emit_inc("vector", vch, 1,
                       lambda e, trash=trash: e.tensor_scalar(
                           aux[:, :], aux[:, :], trash, None, ALU.add))
            c.emit_wait("vector", vch)
            c.emit_inc("vector", vch, 1,
                       lambda e, kk=kk: e.tensor_copy(dsti[:, kk:kk + 1], aux[:, :]))
            c.emit_wait("vector", vch)
            c.emit_inc("vector", vch, 1,
                       lambda e: e.tensor_tensor(aux[:, :], dstf[:, :], msk[:, :], ALU.mult))
            c.emit_wait("vector", vch)
            c.emit_inc("vector", vch, 1,
                       lambda e, kk=kk: e.tensor_copy(cmbi[:, kk:kk + 1], aux[:, :]))
            c.emit_inc("vector", vch, 1,
                       lambda e, kk=kk: e.tensor_tensor(
                           wtsall[:, kk:kk + 1], wtsall[:, kk:kk + 1], msk[:, :], ALU.mult))

    c.emit_wait("sync", vch)
    c.emit_inc("sync", f"{pref}_wst", 16,
               lambda e: e.dma_start(
                   out=wts_dst.rearrange("(a p) k -> p a k", p=128),
                   in_=wtsall[:, :].rearrange("p (a k) -> p a k", a=NTT)))
    c.emit_inc("sync", f"{pref}_wst", 16,
               lambda e: e.dma_start(
                   out=cmbi_dst.rearrange("(a p) k -> p a k", p=128),
                   in_=cmbi[:, :].rearrange("p (a k) -> p a k", a=NTT)))
    c.emit_wait("gpsimd", vch)
    c.emit_wait("gpsimd", zd, 32)
    c.emit_wait("gpsimd", ld)
    for tt in range(NTT):
        for k in range(4):
            kk = tt * 4 + k
            c.emit_inc("gpsimd", f"{pref}_sct", 16,
                       lambda e, tt=tt, kk=kk: e.indirect_dma_start(
                           bufl, IndirectOffsetOnAxis(dsti[:, kk:kk + 1], 0),
                           agrow[:, tt:tt + 1], None))
    c.emit_wait("gpsimd", f"{pref}_sct", NTT * 4 * 16)
    c.emit_inc("gpsimd", f"{pref}_rs_nowait", 1,
               lambda e: e.collective_compute(
                   "ReduceScatter", ALU.add, replica_groups=[list(range(8))],
                   ins=[bufl[0:TOK * NCORE, :].opt()], outs=[bufm.opt()]))
    c.emit_wait("gpsimd", f"{pref}_wst", 32)


def build_phaseI():
    nc = bass.Bass("TRN2", target_bir_lowering=False, debug=False)
    xn1 = nc.dram_tensor("XN1", [E, TOK], BF16, kind="ExternalInput").ap()
    gwt = nc.dram_tensor("GWT", [E, 16], F32, kind="ExternalInput").ap()
    ltc = nc.dram_tensor("LTC", [32, 4], F32, kind="ExternalInput").ap()
    agrow = nc.dram_tensor("AGROW", [TOK, 1], I32, kind="ExternalInput").ap()
    iota = nc.dram_tensor("IOTA16", [128, 16], F32, kind="ExternalInput").ap()
    identf = nc.dram_tensor("IDENTF", [128, 128], F32, kind="ExternalInput").ap()
    ones_in = nc.dram_tensor("ones128", [128, 128], F32, kind="ExternalInput").ap()
    wts = nc.dram_tensor("WTS", [TOK, 4], F32, kind="ExternalOutput").ap()
    cmbi = nc.dram_tensor("CMBI", [TOK, 4], I32, kind="ExternalOutput").ap()
    bufl = nc.dram_tensor("BUFL", [NCORE * TOK + 64, 1], I32, kind="Internal").ap()
    bufm_i = nc.dram_tensor("BUFM_I", [TOK, 1], I32, kind="Internal").ap()
    bufm = nc.dram_tensor("BUFM", [TOK, 1], I32, kind="ExternalOutput").ap()
    hist_in = nc.dram_tensor("HIST_IN", [4, 16], F32, kind="Internal").ap()
    hist_all = nc.dram_tensor("HIST_ALL", [32, 16], F32, kind="Internal").ap()
    with ExitStack() as st:
        c = Ctx(nc, st)
        onesf = c.sbuf("onesf", [128, 128], F32)
        c.emit_inc("sync", "init", 16, lambda e: e.dma_start(out=onesf[:, :], in_=ones_in))
        c.emit_wait("tensor", "init", 16)
        c.emit_wait("vector", "init", 16)
        emit_route(c, xn1, gwt, ltc, agrow, iota, identf, onesf, wts, cmbi,
                   bufl, bufm_i, hist_in, hist_all)
        c.emit_wait("gpsimd", "I_rs_nowait", 1)
        c.emit_inc("gpsimd", "fcp", 16, lambda e: e.dma_start(out=bufm, in_=bufm_i))
        c.emit_wait("gpsimd", "fcp", 16)
        c.finalize()
    return nc


def host_route_consts(cid):
    def gorder(cc, j):
        return (cc // 2) * 8 + 2 * j + (cc % 2)
    gall = np.array([gorder(cc, j) for cc in range(8) for j in range(4)])
    ltc = np.zeros((32, 4), np.float32)
    for myj in range(4):
        mine = gorder(cid, myj)
        ltc[:, myj] = (gall < mine).astype(np.float32)
    agrow = (cid * TOK + np.arange(TOK)).astype(np.int32).reshape(TOK, 1)
    iota16 = np.tile(np.arange(16, dtype=np.float32), (128, 1))
    identf = np.eye(128, dtype=np.float32)
    return ltc, agrow, iota16, identf



def emit_ag_table(c, src, ag, pad, pref):
    c.emit_inc("gpsimd", f"{pref}_agd", 1,
               lambda e: e.collective_compute(
                   "AllGather", ALU.bypass, replica_groups=[list(range(8))],
                   ins=[src.opt()], outs=[ag.opt()]))


def emit_moe_experts(c, xn1tm_src, bufm_src, wg_src, wu_src, wd_src,
                     xn1ag, xpad, xg, eoutf, eout_dst, pref="J", agpref="JA"):
    NI = MOE_I // 128
    bufm = c.sbuf(f"{pref}_bm", [128, 16], I32)
    gtile = c.sbuf(f"{pref}_gt", [128, E], BF16)
    xgt = c.sbuf(f"{pref}_xg", [128, NF * E], BF16)
    wg = c.sbuf(f"{pref}_wg", [128, 2 * NF * 128], BF16)
    wu = c.sbuf(f"{pref}_wu", [128, 2 * NF * 128], BF16)
    wd = c.sbuf(f"{pref}_wd", [128, 4 * 512], BF16)
    sg = c.sbuf(f"{pref}_sg", [128, 1024], BF16)
    mid = c.sbuf(f"{pref}_md", [128, NI * E], BF16)
    ev = c.sbuf(f"{pref}_ev", [128, 8 * 512], BF16)
    ps = [c.psum(f"{pref}_ps{i}", [128, 512], F32) for i in range(8)]

    agd, bld, gth, xst, tld = (f"{pref}_agd", f"{pref}_bld", f"{pref}_gth",
                               f"{pref}_xst", f"{pref}_tld")
    wld, mmh, sgd, vmd, dld = (f"{pref}_wld", f"{pref}_mmh", f"{pref}_sgd",
                               f"{pref}_vmd", f"{pref}_dld")
    mmo, oev, ost, tst2, est = (f"{pref}_mmo", f"{pref}_oev", f"{pref}_ost",
                                f"{pref}_tst", f"{pref}_est")

    c.emit_wait("sync", "I_rs_nowait", 1)
    c.emit_inc("sync", bld, 16,
               lambda e: e.dma_start(
                   out=bufm[:, :].rearrange("p (a o) -> p a o", o=1),
                   in_=bufm_src.rearrange("(a p) o -> p a o", p=128),
                   allow_slow_non_contiguous=True))
    c.emit_wait("gpsimd", bld, 16)
    c.emit_wait("gpsimd", f"{agpref}_agd", 1)
    for st_i in range(16):
        if st_i >= 1:
            c.emit_wait("gpsimd", xst, st_i * 16)
        c.emit_inc("gpsimd", gth, 16,
                   lambda e, st_i=st_i: e.indirect_dma_start(
                       gtile[:, :], None, xn1ag,
                       IndirectOffsetOnAxis(bufm[:, st_i:st_i + 1], 0)))
        c.emit_wait("gpsimd", gth, (st_i + 1) * 16)
        c.emit_inc("gpsimd", xst, 16,
                   lambda e, st_i=st_i: e.dma_start(
                       out=xg[st_i * 128:(st_i + 1) * 128, :], in_=gtile[:, :]))
    for ft in range(NF):
        c.emit_wait("sync", xst, 16 * 16)
        c.emit_inc("sync", tld, 16,
                   lambda e, ft=ft: e.dma_start(
                       out=xgt[:, ft * E:(ft + 1) * E],
                       in_=xg[:, ft * 128:(ft + 1) * 128], transpose=True))
    for ee in range(2):
        for i_t in range(NI):
            it = ee * NI + i_t
            par = it % 2
            if it >= 2:
                c.emit_wait("gpsimd", mmh, (it - 1) * 2)
            for src, dstb in ((wg_src, wg), (wu_src, wu)):
                ss = src[ee * E:(ee + 1) * E, i_t * 128:(i_t + 1) * 128].rearrange(
                    "(a p) c -> p a c", p=128)
                c.emit_inc("gpsimd", f"{wld}{par}", 16,
                           lambda e, ss=ss, dstb=dstb, par=par: e.dma_start(
                               out=dstb[:, par * NF * 128:(par + 1) * NF * 128]
                               .rearrange("p (a c) -> p a c", a=NF), in_=ss))
            c.emit_wait("tensor", f"{wld}{par}", (it // 2 + 1) * 32)
            if it == 0:
                c.emit_wait("tensor", tld, NF * 16)
            if i_t == 0 and ee == 1:
                c.emit_wait("tensor", oev)
            for n in range(2):
                bb = 2 * n
                c.emit_wait("tensor", sgd, max(0, c.cnt.get(sgd, 0) - 1))
                c.emit_wait("tensor", vmd, max(0, c.cnt.get(vmd, 0) - 1))
                col0 = ee * 1024 + n * 512
                for gi, wsb in ((0, wg), (1, wu)):
                    for e_i in range(NF):
                        fn = lambda e, e_i=e_i, gi=gi, wsb=wsb, par=par, col0=col0, bb=bb: e.matmul(
                            ps[bb + gi][:, :],
                            wsb[:, par * NF * 128 + e_i * 128:par * NF * 128 + (e_i + 1) * 128],
                            xgt[:, e_i * E + col0:e_i * E + col0 + 512],
                            start=(e_i == 0), stop=(e_i == NF - 1))
                        if gi == 1 and e_i == NF - 1:
                            c.emit_inc("tensor", mmh, 1, fn)
                        else:
                            c.emit("tensor", fn)
                c.emit_wait("scalar", mmh)
                c.emit_wait("scalar", vmd, max(0, c.cnt.get(vmd, 0) - 1))
                c.emit_inc("scalar", sgd, 1,
                           lambda e, bb=bb, n=n: e.activation(
                               sg[:, n * 512:(n + 1) * 512], ps[bb][:, :], AF.Silu))
                c.emit_wait("vector", sgd)
                c.emit_inc("vector", vmd, 1,
                           lambda e, i_t=i_t, col0=col0, bb=bb, n=n: e.tensor_tensor(
                               mid[:, i_t * E + col0:i_t * E + col0 + 512],
                               sg[:, n * 512:(n + 1) * 512], ps[bb + 1][:, :], ALU.mult))
        c.emit_wait("tensor", vmd)
        for og in range(4):
            for i_t in range(NI):
                it2 = (ee * 4 + og) * NI + i_t
                par = it2 % 4
                ss = wd_src[ee * MOE_I + i_t * 128:ee * MOE_I + (i_t + 1) * 128,
                            og * 512:(og + 1) * 512]
                if it2 >= 4:
                    c.emit_wait("gpsimd", mmo, (it2 - 3) * 8)
                c.emit_inc("gpsimd", f"{dld}{par}", 16,
                           lambda e, ss=ss, par=par: e.dma_start(
                               out=wd[:, par * 512:(par + 1) * 512], in_=ss))
                c.emit_wait("tensor", f"{dld}{par}", (it2 // 4 + 1) * 16)
                if i_t == 0:
                    c.emit_wait("tensor", oev)
                for o in range(4):
                    for n in range(2):
                        fn = lambda e, o=o, n=n, par=par, i_t=i_t, ee=ee: e.matmul(
                            ps[o * 2 + n][:, :],
                            wd[:, par * 512 + o * 128:par * 512 + (o + 1) * 128],
                            mid[:, i_t * E + ee * 1024 + n * 512:
                                i_t * E + ee * 1024 + (n + 1) * 512],
                            start=(i_t == 0), stop=(i_t == NI - 1))
                        c.emit_inc("tensor", mmo, 1, fn)
            c.emit_wait("vector", mmo)
            c.emit_wait("vector", ost)
            for o in range(4):
                for n in range(2):
                    c.emit_inc("vector", oev, 1,
                               lambda e, o=o, n=n: e.tensor_copy(
                                   ev[:, (o * 2 + n) * 512:(o * 2 + n + 1) * 512],
                                   ps[o * 2 + n][:, :]))
            c.emit_wait("sync", oev)
            for o in range(4):
                dstv = eoutf[(og * 4 + o) * 128:(og * 4 + o + 1) * 128,
                             ee * 1024:(ee + 1) * 1024]
                c.emit_inc("sync", ost, 16,
                           lambda e, d=dstv, o=o: e.dma_start(
                               out=d, in_=ev[:, o * 1024:(o + 1) * 1024]))
    c.emit_wait("sync", ost)
    for st_i in range(16):
        par = st_i % 2
        if st_i >= 2:
            c.emit_wait("sync", f"{est}{par}", (st_i // 2) * 16)
        c.emit_inc("sync", f"{tst2}{par}", 16,
                   lambda e, st_i=st_i, par=par: e.dma_start(
                       out=xgt[:, par * E:(par + 1) * E],
                       in_=eoutf[:, st_i * 128:(st_i + 1) * 128], transpose=True))
        c.emit_wait("sync", f"{tst2}{par}", (st_i // 2 + 1) * 16)
        c.emit_inc("sync", f"{est}{par}", 16,
                   lambda e, st_i=st_i, par=par: e.dma_start(
                       out=eout_dst[st_i * 128:(st_i + 1) * 128, :],
                       in_=xgt[:, par * E:(par + 1) * E]))
    c.emit_wait("sync", f"{est}0")
    c.emit_wait("sync", f"{est}1")


def emit_moe_combine(c, eout_src, cmbi_src, wts_src, eoutag, epad, ytm_dst,
                     pref="K", agpref="KA"):
    cmbi = c.sbuf(f"{pref}_ci", [128, 2 * 4], I32)
    wts = c.sbuf(f"{pref}_wt", [128, 2 * 4], F32)
    rows = [c.sbuf(f"{pref}_r{k}", [128, 2 * E], BF16) for k in range(4)]
    acc = c.sbuf(f"{pref}_ac", [128, E], F32)
    accb = c.sbuf(f"{pref}_ab", [128, E], BF16)

    agd, ild, gth, vch, yst = (f"{pref}_agd", f"{pref}_ild", f"{pref}_gth",
                               f"{pref}_vch", f"{pref}_yst")

    c.emit_wait("gpsimd", f"{agpref}_agd", 1)
    for tt in range(16):
        tp = tt % 2
        if tt >= 2:
            c.emit_wait("sync", vch, (tt - 1) * 4)
        c.emit_inc("sync", f"{ild}{tp}", 16,
                   lambda e, tt=tt, tp=tp: e.dma_start(
                       out=cmbi[:, tp * 4:(tp + 1) * 4],
                       in_=cmbi_src[tt * 128:(tt + 1) * 128, :]))
        c.emit_inc("sync", f"{ild}{tp}", 16,
                   lambda e, tt=tt, tp=tp: e.dma_start(
                       out=wts[:, tp * 4:(tp + 1) * 4],
                       in_=wts_src[tt * 128:(tt + 1) * 128, :]))
        c.emit_wait("gpsimd", f"{ild}{tp}", (tt // 2 + 1) * 32)
        if tt >= 2:
            c.emit_wait("gpsimd", vch, (tt - 1) * 4)
        for k in range(4):
            c.emit_inc("gpsimd", gth, 16,
                       lambda e, k=k, tp=tp: e.indirect_dma_start(
                           rows[k][:, tp * E:(tp + 1) * E], None, eoutag,
                           IndirectOffsetOnAxis(cmbi[:, tp * 4 + k:tp * 4 + k + 1], 0)))
        c.emit_wait("vector", gth, (tt + 1) * 64)
        c.emit_wait("vector", yst)
        c.emit_inc("vector", vch, 1,
                   lambda e, tp=tp: e.tensor_scalar(
                       acc[:, :], rows[0][:, tp * E:(tp + 1) * E],
                       wts[:, tp * 4:tp * 4 + 1], None, ALU.mult))
        for k in range(1, 4):
            c.emit_wait("vector", vch)
            last = (k == 3)
            c.emit_inc("vector", vch, 1,
                       lambda e, k=k, last=last, tp=tp: e.scalar_tensor_tensor(
                           accb[:, :] if last else acc[:, :],
                           rows[k][:, tp * E:(tp + 1) * E],
                           wts[:, tp * 4 + k:tp * 4 + k + 1], acc[:, :],
                           ALU.mult, ALU.add))
        c.emit_wait("sync", vch)
        c.emit_inc("sync", yst, 16,
                   lambda e, tt=tt: e.dma_start(
                       out=ytm_dst[tt * 128:(tt + 1) * 128, :], in_=accb[:, :]))
    c.emit_wait("sync", yst)


def emit_shared_final(c, xn1_src, swg_src, swu_src, swd_src, x2a_src, ytm_src,
                      out_dst, pref="L", xsh_mode=False):
    NI = MOE_IS // 128
    TC = 1024
    xn = c.sbuf(f"{pref}_xn", [128, NF * TC], BF16)
    wg = c.sbuf(f"{pref}_wg", [128, 2 * NF * 128], BF16)
    wu = c.sbuf(f"{pref}_wu", [128, 2 * NF * 128], BF16)
    wd = c.sbuf(f"{pref}_wd", [128, 4 * 512], BF16)
    sg = c.sbuf(f"{pref}_sg", [128, 1024], BF16)
    mid = c.sbuf(f"{pref}_md", [128, NI * TC], BF16)
    x2a = c.sbuf(f"{pref}_x2", [128, 4 * TC], F32)
    ytt = c.sbuf(f"{pref}_yt", [128, 4 * TC], BF16)
    ot = c.sbuf(f"{pref}_o", [128, 4 * TC], F32)
    ps = [c.psum(f"{pref}_ps{i}", [128, 512], F32) for i in range(8)]

    xld, wld, mmh, sgd, vmd = (f"{pref}_xld", f"{pref}_wld", f"{pref}_mmh",
                               f"{pref}_sgd", f"{pref}_vmd")
    dld, mmo, x2ld, ytld, ev, st = (f"{pref}_dld", f"{pref}_mmo", f"{pref}_x2ld",
                                    f"{pref}_ytld", f"{pref}_ev", f"{pref}_st")

    for tc in range(2):
        c.emit_wait("sync", vmd)
        src = xn1_src[:, tc * TC:(tc + 1) * TC].rearrange("(a p) c -> p a c", p=128)
        c.emit_inc("sync", xld, 16,
                   lambda e, s=src: e.dma_start(
                       out=xn[:, :].rearrange("p (a c) -> p a c", a=NF), in_=s))
        for i_t in range(NI):
            par = i_t % 2
            it = tc * NI + i_t
            if it >= 2:
                c.emit_wait("gpsimd", mmh, (it - 1) * 2)
            for src_w, dstb in ((swg_src, wg), (swu_src, wu)):
                ss = src_w[:, i_t * 128:(i_t + 1) * 128].rearrange("(a p) c -> p a c", p=128)
                c.emit_inc("gpsimd", f"{wld}{par}", 16,
                           lambda e, ss=ss, dstb=dstb, par=par: e.dma_start(
                               out=dstb[:, par * NF * 128:(par + 1) * NF * 128]
                               .rearrange("p (a c) -> p a c", a=NF), in_=ss))
            c.emit_wait("tensor", f"{wld}{par}", (it // 2 + 1) * 32)
            c.emit_wait("tensor", xld, (tc + 1) * 16)
            if tc > 0 and i_t == 0:
                c.emit_wait("tensor", ev)
            for n in range(2):
                bb = 2 * n
                c.emit_wait("tensor", sgd, max(0, c.cnt.get(sgd, 0) - 1))
                c.emit_wait("tensor", vmd, max(0, c.cnt.get(vmd, 0) - 1))
                for gi, wsb in ((0, wg), (1, wu)):
                    for e_i in range(NF):
                        fn = lambda e, e_i=e_i, gi=gi, wsb=wsb, par=par, n=n, bb=bb: e.matmul(
                            ps[bb + gi][:, :],
                            wsb[:, par * NF * 128 + e_i * 128:par * NF * 128 + (e_i + 1) * 128],
                            xn[:, e_i * TC + n * 512:e_i * TC + (n + 1) * 512],
                            start=(e_i == 0), stop=(e_i == NF - 1))
                        if gi == 1 and e_i == NF - 1:
                            c.emit_inc("tensor", mmh, 1, fn)
                        else:
                            c.emit("tensor", fn)
                c.emit_wait("scalar", mmh)
                c.emit_wait("scalar", vmd, max(0, c.cnt.get(vmd, 0) - 1))
                c.emit_inc("scalar", sgd, 1,
                           lambda e, bb=bb, n=n: e.activation(
                               sg[:, n * 512:(n + 1) * 512], ps[bb][:, :], AF.Silu))
                c.emit_wait("vector", sgd)
                c.emit_inc("vector", vmd, 1,
                           lambda e, i_t=i_t, n=n, bb=bb: e.tensor_tensor(
                               mid[:, i_t * TC + n * 512:i_t * TC + (n + 1) * 512],
                               sg[:, n * 512:(n + 1) * 512], ps[bb + 1][:, :], ALU.mult))
        c.emit_wait("tensor", vmd)
        for grp in range(4):
            gg = tc * 4 + grp
            c.emit_wait("sync", ev)
            if not xsh_mode:
                x2s = x2a_src[grp * 512:(grp + 1) * 512, tc * TC:(tc + 1) * TC]
                c.emit_inc("sync", x2ld, 16,
                           lambda e, s=x2s: e.dma_start(
                               out=x2a[:, :].rearrange("p (a c) -> p a c", a=4),
                               in_=s.rearrange("(a p) c -> p a c", p=128)))
                for o in range(4):
                    c.emit_inc("sync", ytld, 16,
                               lambda e, o=o, grp=grp, tc=tc: e.dma_start(
                                   out=ytt[:, o * TC:(o + 1) * TC],
                                   in_=ytm_src[tc * TC:(tc + 1) * TC,
                                               grp * 512 + o * 128:grp * 512 + (o + 1) * 128],
                                   transpose=True))
            for i_t in range(NI):
                it2 = gg * NI + i_t
                par = it2 % 4
                ss = swd_src[i_t * 128:(i_t + 1) * 128, grp * 512:(grp + 1) * 512]
                if it2 >= 4:
                    c.emit_wait("gpsimd", mmo, (it2 - 3) * 8)
                c.emit_inc("gpsimd", f"{dld}{par}", 16,
                           lambda e, ss=ss, par=par: e.dma_start(
                               out=wd[:, par * 512:(par + 1) * 512], in_=ss))
                c.emit_wait("tensor", f"{dld}{par}", (it2 // 4 + 1) * 16)
                if i_t == 0:
                    c.emit_wait("tensor", ev)
                for o in range(4):
                    for n in range(2):
                        fn = lambda e, o=o, n=n, par=par, i_t=i_t: e.matmul(
                            ps[o * 2 + n][:, :],
                            wd[:, par * 512 + o * 128:par * 512 + (o + 1) * 128],
                            mid[:, i_t * TC + n * 512:i_t * TC + (n + 1) * 512],
                            start=(i_t == 0), stop=(i_t == NI - 1))
                        c.emit_inc("tensor", mmo, 1, fn)
            c.emit_wait("vector", mmo)
            if not xsh_mode:
                c.emit_wait("vector", x2ld, (gg + 1) * 16)
                c.emit_wait("vector", ytld, (gg + 1) * 64)
            c.emit_wait("vector", st)
            for o in range(4):
                for n in range(2):
                    c.emit_wait("vector", ev)
                    if xsh_mode:
                        c.emit_inc("vector", ev, 1,
                                   lambda e, o=o, n=n: e.tensor_copy(
                                       ot[:, o * TC + n * 512:o * TC + (n + 1) * 512],
                                       ps[o * 2 + n][:, :]))
                        continue
                    c.emit_inc("vector", ev, 1,
                               lambda e, o=o, n=n: e.tensor_tensor(
                                   ot[:, o * TC + n * 512:o * TC + (n + 1) * 512],
                                   ps[o * 2 + n][:, :],
                                   x2a[:, o * TC + n * 512:o * TC + (n + 1) * 512],
                                   ALU.add))
                    c.emit_wait("vector", ev)
                    c.emit_inc("vector", ev, 1,
                               lambda e, o=o, n=n: e.tensor_tensor(
                                   ot[:, o * TC + n * 512:o * TC + (n + 1) * 512],
                                   ot[:, o * TC + n * 512:o * TC + (n + 1) * 512],
                                   ytt[:, o * TC + n * 512:o * TC + (n + 1) * 512],
                                   ALU.add))
            c.emit_wait("sync", ev)
            dstv = out_dst[grp * 512:(grp + 1) * 512, tc * TC:(tc + 1) * TC]
            c.emit_inc("sync", st, 16,
                       lambda e, d=dstv: e.dma_start(
                           out=d.rearrange("(a p) c -> p a c", p=128),
                           in_=ot[:, :].rearrange("p (a c) -> p a c", a=4)))
    c.emit_wait("sync", st)






def _core_cols(b_idx, role):
    cols = []
    for j in range(NCH):
        g = 2 * j + role
        s = b_idx * T + g * QCH
        cols.append(np.arange(s, s + QCH))
    return np.concatenate(cols)


def host_prep(inputs):
    x = np.asarray(inputs["L0_attn_x"], dtype=np.float32).reshape(B * T, E)
    per_core = []
    for cid in range(NCORE):
        b_idx, role = cid // 2, cid % 2
        cols = _core_cols(b_idx, role)
        m = {}
        m["xT_own"] = np.ascontiguousarray(x[cols].T)
        m["ones128"] = np.ones((128, 128), np.float32)
        per_core.append(m)
    return per_core


def build_phaseA():
    nc = bass.Bass("TRN2", target_bir_lowering=False, debug=False)
    xT = nc.dram_tensor("xT_own", [E, TOK], F32, kind="ExternalInput").ap()
    ones_in = nc.dram_tensor("ones128", [128, 128], F32, kind="ExternalInput").ap()
    xn0 = nc.dram_tensor("XN0", [E, TOK], BF16, kind="ExternalOutput").ap()
    with ExitStack() as st:
        c = Ctx(nc, st)
        onesf = c.sbuf("onesf", [128, 128], F32)
        c.emit_inc("sync", "init", 16, lambda e: e.dma_start(out=onesf[:, :], in_=ones_in))
        c.emit_wait("tensor", "init", 16)
        emit_ln(c, xT, xn0, "A", ncols=TOK, onesf=onesf)
        c.finalize()
    return nc


_PROGS = {}
FUSED = True


def _prog(name):
    if name not in _PROGS:
        _PROGS[name] = globals()[f"build_phase{name}"]()
    return _PROGS[name]


def kernel(**inputs):
    f32 = np.float32
    x = np.asarray(inputs["L0_attn_x"], dtype=f32).reshape(B * T, E)
    WqT = np.ascontiguousarray(np.asarray(inputs["L0_attn_Wq"], f32).T)
    KVW = np.ascontiguousarray(np.concatenate(
        [np.asarray(inputs["L0_attn_Wk"], f32).T,
         np.asarray(inputs["L0_attn_Wv"], f32).T], axis=1))
    WoT = np.ascontiguousarray(np.asarray(inputs["L0_attn_Wo"], f32).T)
    LAMS = np.concatenate([np.asarray(inputs[k], f32).ravel() for k in
                           ("L0_attn_lambda_q1", "L0_attn_lambda_k1",
                            "L0_attn_lambda_q2", "L0_attn_lambda_k2")]).reshape(1, 256)
    LNW = np.asarray(inputs["L0_attn_ln_w"], f32).reshape(128, 1)
    W1T = np.ascontiguousarray(np.asarray(inputs["L0_ffn_w1"], f32).T)
    W2T = np.ascontiguousarray(np.asarray(inputs["L0_ffn_w2"], f32).T)
    W3T = np.ascontiguousarray(np.asarray(inputs["L0_ffn_w3"], f32).T)
    Wq1T = np.ascontiguousarray(np.asarray(inputs["L1_attn_Wq"], f32).T)
    BQ1 = np.asarray(inputs["L1_attn_bq"], f32).reshape(E, 1)
    WkvT = np.ascontiguousarray(np.asarray(inputs["L1_attn_Wkv"], f32).T)
    BKV = np.asarray(inputs["L1_attn_bkv"], f32).reshape(256, 1)
    GWT = np.ascontiguousarray(np.asarray(inputs["L1_ffn_gate_w"], f32).T)
    wgate = np.asarray(inputs["L1_ffn_w_gate"], f32)
    wup = np.asarray(inputs["L1_ffn_w_up"], f32)
    wdn = np.asarray(inputs["L1_ffn_w_down"], f32)
    SWG = np.ascontiguousarray(np.asarray(inputs["L1_ffn_sw_gate"], f32))
    SWU = np.ascontiguousarray(np.asarray(inputs["L1_ffn_sw_up"], f32))
    SWD = np.ascontiguousarray(np.asarray(inputs["L1_ffn_sw_down"], f32))
    ones = np.ones((128, 128), f32)
    identb = np.eye(128, dtype=bf)
    onesb = np.ones((128, 128), bf)
    cores = list(range(NCORE))
    allcols = [_core_cols(cid // 2, cid % 2) for cid in cores]
    consts = [host_route_consts(cid) for cid in cores]

    def run(name, maps):
        return run_bass_kernel_spmd(_prog(name), maps, core_ids=cores).results

    xT = [np.ascontiguousarray(x[allcols[cid]].T) for cid in cores]
    if FUSED:
        if "FUSED" not in _PROGS:
            _PROGS["FUSED"] = build_fused()
        maps = []
        for c in cores:
            maps.append({
                "xT_own": xT[c], "ones128": ones, "WqT": WqT, "KVW": KVW,
                "LAMS": LAMS, "LNW": LNW, "MASK8": host_masks(c % 2),
                "IDENT": identb, "ONESB": onesb, "WoT": WoT,
                "W1T": W1T, "W2T": W2T, "W3T": W3T,
                "Wq1T": Wq1T, "BQ1": BQ1, "WkvT": WkvT, "BKV": BKV,
                "GWT": GWT, "LTC": consts[c][0], "AGROW": consts[c][1],
                "IOTA16": consts[c][2], "IDENTF": consts[c][3],
                "WG_MY": wgate[2 * c:2 * c + 2].reshape(2 * E, MOE_I),
                "WU_MY": wup[2 * c:2 * c + 2].reshape(2 * E, MOE_I),
                "WD_MY": wdn[2 * c:2 * c + 2].reshape(2 * MOE_I, E),
                "SWG": SWG, "SWU": SWU, "SWD": SWD,
            })
        res = run_bass_kernel_spmd(_PROGS["FUSED"], maps, core_ids=cores).results
        out = np.zeros((B * T, E), np.float32)
        for cid in cores:
            out[allcols[cid]] = np.asarray(res[cid]["OUT"]).T
        return out.reshape(B, T, E)

    rA = run("A", [{"xT_own": xT[c], "ones128": ones} for c in cores])
    rB = run("B", [{"XN0": rA[c]["XN0"], "WqT": WqT, "KVW": KVW, "ones128": ones}
                   for c in cores])
    rC = run("C", [{"QT": rB[c]["QT"], "KV_ALL": rB[c]["KV_ALL"],
                    "QMAX": rB[c]["QMAX"], "LAMS": LAMS, "LNW": LNW,
                    "MASK8": host_masks(c % 2), "IDENT": identb, "ONESB": onesb,
                    "ones128": ones} for c in cores])
    rD = run("D", [{"ATTN": rC[c]["ATTN"], "WoT": WoT, "xT_own": xT[c]}
                   for c in cores])
    rE = run("E", [{"W1T": W1T, "W2T": W2T, "W3T": W3T, "ones128": ones}
                   for c in cores])
    rF = run("F", [{"XN0": rA[c]["XN0"], "G1": rE[c]["G1"], "G2": rE[c]["G2"],
                    "G3": rE[c]["G3"], "X1": rD[c]["X1"]} for c in cores])
    rG = run("G", [{"X1F": rF[c]["X1F"], "ones128": ones} for c in cores])
    rH = run("H", [{"XN1": rG[c]["XN1"], "Wq1T": Wq1T, "BQ1": BQ1, "WkvT": WkvT,
                    "BKV": BKV, "X1F": rF[c]["X1F"], "ones128": ones}
                   for c in cores])
    rI = run("I", [{"XN1": rG[c]["XN1"], "GWT": GWT, "LTC": consts[c][0],
                    "AGROW": consts[c][1], "IOTA16": consts[c][2],
                    "IDENTF": consts[c][3], "ones128": ones} for c in cores])
    rJ = run("J", [{"XN1TM": rG[c]["XN1TM"], "BUFM": rI[c]["BUFM"],
                    "WG_MY": wgate[2 * c:2 * c + 2].reshape(2 * E, MOE_I),
                    "WU_MY": wup[2 * c:2 * c + 2].reshape(2 * E, MOE_I),
                    "WD_MY": wdn[2 * c:2 * c + 2].reshape(2 * MOE_I, E)}
                   for c in cores])
    rK = run("K", [{"EOUT": rJ[c]["EOUT"], "CMBI": rI[c]["CMBI"],
                    "WTS": rI[c]["WTS"]} for c in cores])
    rL = run("L", [{"XN1": rG[c]["XN1"], "SWG": SWG, "SWU": SWU, "SWD": SWD,
                    "X2A": rH[c]["X2A"], "YTM": rK[c]["YTM"]} for c in cores])
    out = np.zeros((B * T, E), np.float32)
    for cid in cores:
        out[allcols[cid]] = np.asarray(rL[cid]["OUT"]).T
    return out.reshape(B, T, E)


if __name__ == "__main__":
    pass


def build_phaseD():
    nc = bass.Bass("TRN2", target_bir_lowering=False, debug=False)
    attn = nc.dram_tensor("ATTN", [E, TOK], BF16, kind="ExternalInput").ap()
    wo = nc.dram_tensor("WoT", [E, E], F32, kind="ExternalInput").ap()
    xt = nc.dram_tensor("xT_own", [E, TOK], F32, kind="ExternalInput").ap()
    x1 = nc.dram_tensor("X1", [E, TOK], F32, kind="ExternalOutput").ap()
    with ExitStack() as st:
        c = Ctx(nc, st)
        emit_wo(c, attn, wo, xt, x1)
        c.finalize()
    return nc


def build_phaseE():
    nc = bass.Bass("TRN2", target_bir_lowering=False, debug=False)
    w1 = nc.dram_tensor("W1T", [E, FFN_H], F32, kind="ExternalInput").ap()
    w2 = nc.dram_tensor("W2T", [E, FFN_H], F32, kind="ExternalInput").ap()
    w3 = nc.dram_tensor("W3T", [FFN_H, E], F32, kind="ExternalInput").ap()
    ones_in = nc.dram_tensor("ones128", [128, 128], F32, kind="ExternalInput").ap()
    g1 = nc.dram_tensor("G1", [E, FFN_H], BF16, kind="ExternalOutput").ap()
    g2 = nc.dram_tensor("G2", [E, FFN_H], BF16, kind="ExternalOutput").ap()
    g3 = nc.dram_tensor("G3", [FFN_H, E], BF16, kind="ExternalOutput").ap()
    with ExitStack() as st:
        c = Ctx(nc, st)
        onesf = c.sbuf("onesf", [128, 128], F32)
        c.emit_inc("sync", "init", 16, lambda e: e.dma_start(out=onesf[:, :], in_=ones_in))
        c.emit_wait("tensor", "init", 16)
        sh = quant_bufs(c, FFN_H)
        for i, (w, g, R, CC) in enumerate((
                (w1, g1, E, FFN_H), (w2, g2, E, FFN_H), (w3, g3, FFN_H, E))):
            if i > 0:
                for eng in ("sync", "scalar", "vector", "tensor"):
                    c.emit_wait(eng, f"E{i}_st0")
                    c.emit_wait(eng, f"E{i}_st1")
            emit_quant(c, w, g, R, CC, onesf, f"E{i + 1}", sh)
        c.finalize()
    return nc


def build_phaseF():
    nc = bass.Bass("TRN2", target_bir_lowering=False, debug=False)
    xn = nc.dram_tensor("XN0", [E, TOK], BF16, kind="ExternalInput").ap()
    g1 = nc.dram_tensor("G1", [E, FFN_H], BF16, kind="ExternalInput").ap()
    g2 = nc.dram_tensor("G2", [E, FFN_H], BF16, kind="ExternalInput").ap()
    g3 = nc.dram_tensor("G3", [FFN_H, E], BF16, kind="ExternalInput").ap()
    x1 = nc.dram_tensor("X1", [E, TOK], F32, kind="ExternalInput").ap()
    x1f = nc.dram_tensor("X1F", [E, TOK], F32, kind="ExternalOutput").ap()
    with ExitStack() as st:
        c = Ctx(nc, st)
        emit_ffn(c, xn, g1, g2, g3, x1, x1f)
        c.finalize()
    return nc


def build_phaseJ():
    nc = bass.Bass("TRN2", target_bir_lowering=False, debug=False)
    xn1tm = nc.dram_tensor("XN1TM", [TOK, E], BF16, kind="ExternalInput").ap()
    bufm = nc.dram_tensor("BUFM", [TOK, 1], I32, kind="ExternalInput").ap()
    wg = nc.dram_tensor("WG_MY", [2 * E, MOE_I], F32, kind="ExternalInput").ap()
    wu = nc.dram_tensor("WU_MY", [2 * E, MOE_I], F32, kind="ExternalInput").ap()
    wd = nc.dram_tensor("WD_MY", [2 * MOE_I, E], F32, kind="ExternalInput").ap()
    xn1ag = nc.dram_tensor("XN1AG", [NCORE * TOK, E], BF16, kind="Internal").ap()
    xpad = nc.dram_tensor("XPAD", [NCORE * TOK + 1, E], BF16, kind="Internal").ap()
    xg = nc.dram_tensor("XG", [TOK, E], BF16, kind="Internal").ap()
    eoutf = nc.dram_tensor("EOUTF", [E, TOK], BF16, kind="Internal").ap()
    eout = nc.dram_tensor("EOUT", [TOK, E], BF16, kind="ExternalOutput").ap()
    xn1tm_b = nc.dram_tensor("XN1TM_B", [TOK, E], BF16, kind="Internal").ap()
    with ExitStack() as st:
        c = Ctx(nc, st)
        c.emit_inc("gpsimd", "J_bnc", 16, lambda e: e.dma_start(out=xn1tm_b, in_=xn1tm))
        c.emit_wait("gpsimd", "J_bnc", 16)
        emit_ag_table(c, xn1tm_b, xn1ag, xpad, "JA")
        emit_moe_experts(c, xn1tm_b, bufm, wg, wu, wd, xn1ag, xpad, xg, eoutf, eout)
        c.finalize()
    return nc


def build_phaseK():
    nc = bass.Bass("TRN2", target_bir_lowering=False, debug=False)
    eout = nc.dram_tensor("EOUT", [TOK, E], BF16, kind="ExternalInput").ap()
    cmbi = nc.dram_tensor("CMBI", [TOK, 4], I32, kind="ExternalInput").ap()
    wts = nc.dram_tensor("WTS", [TOK, 4], F32, kind="ExternalInput").ap()
    eoutag = nc.dram_tensor("EOUTAG", [NCORE * TOK, E], BF16, kind="Internal").ap()
    epad = nc.dram_tensor("EPAD", [NCORE * TOK + 1, E], BF16, kind="Internal").ap()
    ytm = nc.dram_tensor("YTM", [TOK, E], BF16, kind="ExternalOutput").ap()
    eout_b = nc.dram_tensor("EOUT_B", [TOK, E], BF16, kind="Internal").ap()
    with ExitStack() as st:
        c = Ctx(nc, st)
        c.emit_inc("gpsimd", "K_bnc", 16, lambda e: e.dma_start(out=eout_b, in_=eout))
        c.emit_wait("gpsimd", "K_bnc", 16)
        emit_ag_table(c, eout_b, eoutag, epad, "KA")
        emit_moe_combine(c, eout_b, cmbi, wts, eoutag, epad, ytm)
        c.finalize()
    return nc


def build_phaseL():
    nc = bass.Bass("TRN2", target_bir_lowering=False, debug=False)
    xn1 = nc.dram_tensor("XN1", [E, TOK], BF16, kind="ExternalInput").ap()
    swg = nc.dram_tensor("SWG", [E, MOE_IS], F32, kind="ExternalInput").ap()
    swu = nc.dram_tensor("SWU", [E, MOE_IS], F32, kind="ExternalInput").ap()
    swd = nc.dram_tensor("SWD", [MOE_IS, E], F32, kind="ExternalInput").ap()
    x2a = nc.dram_tensor("X2A", [E, TOK], F32, kind="ExternalInput").ap()
    ytm = nc.dram_tensor("YTM", [TOK, E], BF16, kind="ExternalInput").ap()
    out = nc.dram_tensor("OUT", [E, TOK], F32, kind="ExternalOutput").ap()
    with ExitStack() as st:
        c = Ctx(nc, st)
        emit_shared_final(c, xn1, swg, swu, swd, x2a, ytm, out)
        c.finalize()
    return nc


def build_fused():
    nc = bass.Bass("TRN2", target_bir_lowering=False, debug=False)
    EI, INT = "ExternalInput", "Internal"
    d = lambda n, shp, dt, kind: nc.dram_tensor(n, shp, dt, kind=kind).ap()
    xT = d("xT_own", [E, TOK], F32, EI)
    ones_in = d("ones128", [128, 128], F32, EI)
    wq = d("WqT", [E, E], F32, EI)
    kvw = d("KVW", [E, 1024], F32, EI)
    lams = d("LAMS", [1, 256], F32, EI)
    lnw = d("LNW", [128, 1], F32, EI)
    mask = d("MASK8", [8 * 128, QCH], BF16, EI)
    identb = d("IDENT", [128, 128], BF16, EI)
    onesb_in = d("ONESB", [128, 128], BF16, EI)
    wo = d("WoT", [E, E], F32, EI)
    w1 = d("W1T", [E, FFN_H], F32, EI)
    w2 = d("W2T", [E, FFN_H], F32, EI)
    w3 = d("W3T", [FFN_H, E], F32, EI)
    wq1 = d("Wq1T", [E, E], F32, EI)
    bq1 = d("BQ1", [E, 1], F32, EI)
    wkv1 = d("WkvT", [E, 256], F32, EI)
    bkv1 = d("BKV", [256, 1], F32, EI)
    gwt = d("GWT", [E, 16], F32, EI)
    ltc = d("LTC", [32, 4], F32, EI)
    agrow = d("AGROW", [TOK, 1], I32, EI)
    iota = d("IOTA16", [128, 16], F32, EI)
    identf = d("IDENTF", [128, 128], F32, EI)
    wgm = d("WG_MY", [2 * E, MOE_I], F32, EI)
    wum = d("WU_MY", [2 * E, MOE_I], F32, EI)
    wdm = d("WD_MY", [2 * MOE_I, E], F32, EI)
    swg = d("SWG", [E, MOE_IS], F32, EI)
    swu = d("SWU", [E, MOE_IS], F32, EI)
    swd = d("SWD", [MOE_IS, E], F32, EI)

    xn0 = d("XN0", [E, TOK], BF16, INT)
    qt = d("QT", [E, TOK], BF16, INT)
    kv_own = d("KV_OWN", [TOK, 1024], BF16, INT)
    kv_all = d("KV_ALL", [2 * TOK, 1024], BF16, INT)
    qmax = d("QMAX", [1, NH], F32, INT)
    attn = d("ATTN", [E, TOK], BF16, INT)
    x1 = d("X1", [E, TOK], F32, INT)
    g1 = d("G1", [E, FFN_H], BF16, INT)
    g2 = d("G2", [E, FFN_H], BF16, INT)
    g3 = d("G3", [FFN_H, E], BF16, INT)
    x1f = d("X1F", [E, TOK], F32, INT)
    xn1 = d("XN1", [E, TOK], BF16, INT)
    xn1tm = d("XN1TM", [TOK, E], BF16, INT)
    x2a = d("X2A", [E, TOK], F32, INT)
    wts = d("WTS", [TOK, 4], F32, INT)
    cmbi = d("CMBI", [TOK, 4], I32, INT)
    bufl = d("BUFL", [NCORE * TOK + 64, 1], I32, INT)
    bufm = d("BUFM", [TOK, 1], I32, INT)
    hist_in = d("HIST_IN", [4, 16], F32, INT)
    hist_all = d("HIST_ALL", [32, 16], F32, INT)
    xn1ag = nc.dram_tensor("XN1AG", [NCORE * TOK, E], BF16, kind=INT, addr_space="Shared").ap()
    xpad = d("XPAD", [NCORE * TOK + 1, E], BF16, INT)
    xg = d("XG", [TOK, E], BF16, INT)
    eoutf = d("EOUTF", [E, TOK], BF16, INT)
    eout = d("EOUT", [TOK, E], BF16, INT)
    eoutag = nc.dram_tensor("EOUTAG", [NCORE * TOK, E], BF16, kind=INT, addr_space="Shared").ap()
    epad = d("EPAD", [NCORE * TOK + 1, E], BF16, INT)
    ytm = d("YTM", [TOK, E], BF16, INT)
    xsh = d("XSH", [E, TOK], F32, INT)
    out = d("OUT", [E, TOK], F32, "ExternalOutput")

    with ExitStack() as gst:
        c = Ctx(nc, gst)
        c.sem_st = gst
        bar = c.get_sem("BAR")
        onesf = c.sbuf("onesf", [128, 128], F32)
        c.emit_inc("sync", "init", 16, lambda e: e.dma_start(out=onesf[:, :], in_=ones_in))
        for eng in Ctx.ENGS:
            c.emit_wait(eng, "init", 16)
        c.phase_names = []

        phases = [
            (lambda: emit_ln(c, xT, xn0, "A", ncols=TOK, onesf=onesf), 1, 1, 0),
            (lambda: emit_qkv(c, xn0, wq, kvw, onesf, qt, kv_own, kv_own,
                              kv_all, qmax), 1, 1, 0),
            (lambda: emit_attn(c, qt, kv_all, qmax, lams, lnw, mask, identb,
                               onesb_in, onesf, attn), 1, 1, 0),
            (lambda: emit_wo(c, attn, wo, xT, x1), 1, 1, 0),
            (lambda: _emit_quant3(c, w1, w2, w3, g1, g2, g3, onesf), 1, 1, 0),
            (lambda: emit_ffn(c, xn0, g1, g2, g3, x1, x1f), 1, 1, 0),
            (lambda: emit_ln1_tm(c, x1f, xn1, xn1tm, onesf), 1, 1, 0),
            (lambda: emit_ag_table(c, xn1tm, xn1ag, xpad, "JA"), 0, 0, 1),
            (lambda: emit_route(c, xn1, gwt, ltc, agrow, iota, identf, onesf,
                                wts, cmbi, bufl, bufm, hist_in, hist_all), 1, 1, 0),
            (lambda: emit_l1attn(c, xn1, wq1, bq1, wkv1, bkv1, x1f, x2a, onesf),
             1, 1, 0),
            (lambda: emit_moe_experts(c, xn1tm, bufm, wgm, wum, wdm, xn1ag,
                                      xpad, xg, eoutf, eout), 1, 1, 0),
            (lambda: emit_ag_table(c, eout, eoutag, epad, "KA"), 0, 0, 1),
            (lambda: emit_shared_final(c, xn1, swg, swu, swd, None, None, xsh,
                                       xsh_mode=True), 1, 1, 0),
            (lambda: emit_moe_combine(c, eout, cmbi, wts, eoutag, epad, ytm),
             1, 1, 0),
            (lambda: emit_final(c, x2a, ytm, xsh, out), 1, 1, 0),
        ]
        nbar = 0
        for ph, dobar, doretire, gsb in phases:
            with ExitStack() as pst:
                c.st = gst if gsb else pst
                ph()
                if dobar:
                    for name in list(c.phase_names):
                        if name.endswith("_nowait"):
                            continue
                        c.emit_wait("sync", name)
                    nbar += 1
                    for eng in Ctx.ENGS:
                        c.plan_inc("BAR", 1)
                        c.q[eng].append(lambda e, s=bar: e.nop().then_inc(s, 1))
                    for eng in Ctx.ENGS:
                        c.emit_wait(eng, "BAR", nbar * 5)
            if doretire:
                c.retire_phase()
            else:
                c.phase_names = []
        c.finalize()
    return nc


def _emit_quant3(c, w1, w2, w3, g1, g2, g3, onesf):
    sh = quant_bufs(c, FFN_H)
    for i, (w, g, R, CC) in enumerate(((w1, g1, E, FFN_H), (w2, g2, E, FFN_H),
                                       (w3, g3, FFN_H, E))):
        if i > 0:
            for eng in ("sync", "scalar", "vector", "tensor"):
                c.emit_wait(eng, f"E{i}_st0")
                c.emit_wait(eng, f"E{i}_st1")
        emit_quant(c, w, g, R, CC, onesf, f"E{i + 1}", sh)


def emit_final(c, x2a_src, ytm_src, xsh_src, out_dst, pref="M"):
    TC = 1024
    x2a = c.sbuf(f"{pref}_x2", [128, 4 * TC], F32)
    xsh = c.sbuf(f"{pref}_xs", [128, 4 * TC], F32)
    ytt = c.sbuf(f"{pref}_yt", [128, 4 * TC], BF16)
    ot = c.sbuf(f"{pref}_o", [128, 4 * TC], F32)
    ld, ytld, ev, st = f"{pref}_ld", f"{pref}_ytld", f"{pref}_ev", f"{pref}_st"
    for tc in range(2):
        for grp in range(4):
            gg = tc * 4 + grp
            c.emit_wait("sync", ev)
            for src_t, dst_t in ((x2a_src, x2a), (xsh_src, xsh)):
                ss = src_t[grp * 512:(grp + 1) * 512, tc * TC:(tc + 1) * TC]
                c.emit_inc("sync", ld, 16,
                           lambda e, ss=ss, dst_t=dst_t: e.dma_start(
                               out=dst_t[:, :].rearrange("p (a c) -> p a c", a=4),
                               in_=ss.rearrange("(a p) c -> p a c", p=128)))
            for o in range(4):
                c.emit_inc("sync", ytld, 16,
                           lambda e, o=o, grp=grp, tc=tc: e.dma_start(
                               out=ytt[:, o * TC:(o + 1) * TC],
                               in_=ytm_src[tc * TC:(tc + 1) * TC,
                                           grp * 512 + o * 128:grp * 512 + (o + 1) * 128],
                               transpose=True))
            c.emit_wait("vector", ld, (gg + 1) * 32)
            c.emit_wait("vector", ytld, (gg + 1) * 64)
            c.emit_wait("vector", st)
            for q in range(8):
                c.emit_wait("vector", ev)
                c.emit_inc("vector", ev, 1,
                           lambda e, q=q: e.tensor_tensor(
                               ot[:, q * 512:(q + 1) * 512], x2a[:, q * 512:(q + 1) * 512],
                               xsh[:, q * 512:(q + 1) * 512], ALU.add))
                c.emit_wait("vector", ev)
                c.emit_inc("vector", ev, 1,
                           lambda e, q=q: e.tensor_tensor(
                               ot[:, q * 512:(q + 1) * 512], ot[:, q * 512:(q + 1) * 512],
                               ytt[:, q * 512:(q + 1) * 512], ALU.add))
            c.emit_wait("sync", ev)
            dstv = out_dst[grp * 512:(grp + 1) * 512, tc * TC:(tc + 1) * TC]
            c.emit_inc("sync", st, 16,
                       lambda e, d=dstv: e.dma_start(
                           out=d.rearrange("(a p) c -> p a c", p=128),
                           in_=ot[:, :].rearrange("p (a c) -> p a c", a=4)))
    c.emit_wait("sync", st)


# version 12
# speedup vs baseline: 1.0086x; 1.0086x over previous
import numpy as np
import ml_dtypes
from contextlib import ExitStack

import concourse.bass as bass
import concourse.mybir as mybir
from concourse.bass import IndirectOffsetOnAxis
from concourse.bass_utils import run_bass_kernel_spmd

F32 = mybir.dt.float32
BF16 = mybir.dt.bfloat16
I32 = mybir.dt.int32
U32 = mybir.dt.uint32
AF = mybir.ActivationFunctionType
ALU = mybir.AluOpType
AX = mybir.AxisListType

B, T, E = 4, 4096, 2048
TOK = 2048
NCORE = 8
NF = 16
QCH = 512
NCH = TOK // QCH
HD = 64
NH = 32
KVD = 512
FFN_H = 5632
LAM_INIT = 0.8 - 0.6 * np.exp(-0.3 * 12)
MOE_E, MOE_K, MOE_I, MOE_CAP = 16, 4, 1408, 1024
MOE_IS = 2816

bf = ml_dtypes.bfloat16


class Ctx:
    ENGS = ("sync", "scalar", "vector", "tensor", "gpsimd")

    def __init__(self, nc, st):
        self.nc = nc
        self.st = st
        self.sem_st = st
        self.q = {e: [] for e in self.ENGS}
        self.entry = {}
        self.cnt = {}
        self.phase_names = []
        self.phys_free = []
        self._nsem = 0

    def _entry(self, name):
        if name not in self.entry:
            if self.phys_free:
                h, base = self.phys_free.pop()
            else:
                self._nsem += 1
                h = self.sem_st.enter_context(self.nc.semaphore(f"S{self._nsem}"))
                base = 0
            self.entry[name] = (h, base)
            self.cnt[name] = 0
            self.phase_names.append(name)
        return self.entry[name]

    def get_sem(self, name):
        return self._entry(name)[0]

    def plan_inc(self, name, by):
        h, base = self._entry(name)
        self.cnt[name] += by
        return h, self.cnt[name]

    def retire_phase(self):
        for n in self.phase_names:
            if n.endswith("_nowait"):
                continue
            h, base = self.entry.pop(n)
            self.phys_free.append((h, base + self.cnt.pop(n)))
        self.phase_names = []

    def emit(self, eng, fn):
        self.q[eng].append(fn)

    def emit_inc(self, eng, name, by, fn):
        h, _ = self.plan_inc(name, by)
        self.q[eng].append(lambda e, fn=fn, s=h, by=by: fn(e).then_inc(s, by))

    def emit_wait(self, eng, name, val=None):
        h, base = self._entry(name)
        v = base + (self.cnt[name] if val is None else val)
        self.q[eng].append(lambda e, s=h, v=v: e.wait_ge(s, v))

    def sbuf(self, name, shape, dtype):
        return self.st.enter_context(self.nc.sbuf_tensor(name, shape, dtype))

    def psum(self, name, shape, dtype=F32):
        return self.st.enter_context(self.nc.psum_tensor(name, shape, dtype))

    def finalize(self):
        nc = self.nc
        with nc.Block() as block:
            @block.sync
            def _(e):
                for f in self.q["sync"]:
                    f(e)

            @block.scalar
            def _(e):
                for f in self.q["scalar"]:
                    f(e)

            @block.vector
            def _(e):
                for f in self.q["vector"]:
                    f(e)

            @block.tensor
            def _(e):
                for f in self.q["tensor"]:
                    f(e)

            @block.gpsimd
            def _(e):
                for f in self.q["gpsimd"]:
                    f(e)


def emit_ln(c, xsrc, xn_dst, pref, ncols=TOK, src_f32=True, onesf=None):
    nc = c.nc
    nch = ncols // QCH
    xt = c.sbuf(f"{pref}_x", [128, 2 * NF * QCH], F32)
    sq = c.sbuf(f"{pref}_sq", [128, QCH], F32)
    at = c.sbuf(f"{pref}_at", [128, QCH], F32)
    mtile = c.sbuf(f"{pref}_m", [128, QCH], F32)
    rtile = c.sbuf(f"{pref}_r", [128, QCH], F32)
    vtile = c.sbuf(f"{pref}_v", [128, QCH], F32)
    ttile = c.sbuf(f"{pref}_t", [128, QCH], F32)
    xnout = c.sbuf(f"{pref}_o", [128, NF * QCH], BF16)
    ps_s = c.psum(f"{pref}_ps_s", [128, QCH], F32)
    ps_q = c.psum(f"{pref}_ps_q", [128, QCH], F32)

    ld = f"{pref}_ld"
    sqd = f"{pref}_sqd"
    mmS = f"{pref}_mmS"
    vch = f"{pref}_vch"
    sch = f"{pref}_sch"
    odone = f"{pref}_od"
    ostored = f"{pref}_os"

    for ch in range(nch):
        buf = ch % 2
        xv = lambda f, b=buf: xt[:, (b * NF + f) * QCH:(b * NF + f + 1) * QCH]
        if ch >= 2:
            c.emit_wait("sync", f"{ostored}{ch % 2}", (ch // 2) * 16)
        src = xsrc[:, ch * QCH:(ch + 1) * QCH].rearrange("(a p) c -> p a c", p=128)
        c.emit_inc("sync", ld, 16,
                   lambda e, s=src, b=buf: e.dma_start(
                       out=xt[:, b * NF * QCH:(b + 1) * NF * QCH].rearrange(
                           "p (a c) -> p a c", a=NF), in_=s))

        qcons = f"{pref}_qc"
        for f in range(NF):
            c.emit_wait("scalar", ld, (ch + 1) * 16)
            c.emit_wait("scalar", qcons)
            c.emit_inc("scalar", sqd, 1,
                       lambda e, f=f, xv=xv: e.activation(sq[:, :], xv(f), AF.Square))
            c.emit_wait("tensor", sqd)
            c.emit_wait("tensor", ld, (ch + 1) * 16)
            if f == 0 and ch >= 1:
                c.emit_wait("tensor", vch)
            fn_s = lambda e, f=f, xv=xv, first=(f == 0): e.matmul(
                ps_s[:, :], onesf[:, :], xv(f), start=first, stop=(f == NF - 1))
            fn_q = lambda e, f=f, first=(f == 0): e.matmul(
                ps_q[:, :], onesf[:, :], sq[:, :], start=first, stop=(f == NF - 1))
            if f == NF - 1:
                c.emit_inc("tensor", mmS, 1, fn_s)
            else:
                c.emit("tensor", fn_s)
            c.emit_inc("tensor", qcons, 1, fn_q)

        c.emit_wait("vector", mmS, ch + 1)
        c.emit_wait("vector", qcons, 16 * (ch + 1))
        c.emit_inc("vector", vch, 1,
                   lambda e: e.tensor_scalar(mtile[:, :], ps_s[:, :], 1.0 / E, None, ALU.mult))
        c.emit_inc("vector", vch, 1,
                   lambda e: e.tensor_scalar(vtile[:, :], ps_q[:, :], 1.0 / E, None, ALU.mult))
        c.emit_wait("vector", vch)
        c.emit_inc("vector", vch, 1,
                   lambda e: e.tensor_tensor(ttile[:, :], mtile[:, :], mtile[:, :], ALU.mult))
        c.emit_wait("vector", vch)
        c.emit_inc("vector", vch, 1,
                   lambda e: e.scalar_tensor_tensor(
                       vtile[:, :], vtile[:, :], 1e-5, ttile[:, :], ALU.add, ALU.subtract))
        c.emit_wait("scalar", vch)
        c.emit_inc("scalar", sch, 1,
                   lambda e: e.activation(ttile[:, :], vtile[:, :], AF.Sqrt))
        c.emit_wait("vector", sch)
        c.emit_inc("vector", vch, 1,
                   lambda e: e.reciprocal(rtile[:, :], ttile[:, :]))

        for f in range(NF):
            c.emit_wait("vector", vch)
            if f == 0 and ch >= 1:
                c.emit_wait("vector", f"{ostored}{(ch - 1) % 2}", ((ch - 1) // 2 + 1) * 16)
            c.emit_inc("vector", vch, 1,
                       lambda e, f=f, xv=xv: e.tensor_tensor(at[:, :], xv(f), mtile[:, :], ALU.subtract))
            c.emit_wait("vector", vch)
            last = (f == NF - 1)
            c.emit_inc("vector", odone if last else vch, 1,
                       lambda e, f=f: e.tensor_tensor(
                           xnout[:, f * QCH:(f + 1) * QCH], at[:, :], rtile[:, :], ALU.mult))

        c.emit_wait("sync", odone, ch + 1)
        dst = xn_dst[:, ch * QCH:(ch + 1) * QCH].rearrange("(a p) c -> p a c", p=128)
        c.emit_inc("sync", f"{ostored}{ch % 2}", 16,
                   lambda e, d=dst: e.dma_start(
                       out=d, in_=xnout[:, :].rearrange("p (a c) -> p a c", a=NF)))
    c.emit_wait("sync", f"{ostored}0")
    c.emit_wait("sync", f"{ostored}1")


def emit_qkv(c, xn_src, wq_src, kvw_src, onesf, qt_dst, kv_own, kv_all_in,
             kv_all_out, qmax_out, pref="B"):
    nc = c.nc
    xt = c.sbuf(f"{pref}_xn", [128, 2 * NF * QCH], BF16)
    wq = c.sbuf(f"{pref}_wq", [128, NF * E], BF16)
    kvw = c.sbuf(f"{pref}_kvw", [128, NF * 1024], BF16)
    qevac = c.sbuf(f"{pref}_qe", [128, 4 * QCH], BF16)
    kvevac = c.sbuf(f"{pref}_kve", [128, 2 * 1024], BF16)
    sq = c.sbuf(f"{pref}_sq", [128, QCH], F32)
    sq2 = c.sbuf(f"{pref}_sq2", [128, TOK], F32)
    qm = c.sbuf(f"{pref}_qm", [1, NH], F32)
    tmp1 = c.sbuf(f"{pref}_t1", [1, 1], F32)
    ps_q = [c.psum(f"{pref}_psq{i}", [128, QCH], F32) for i in range(4)]
    ps_k = c.psum(f"{pref}_psk", [128, 512], F32)
    ps_v = c.psum(f"{pref}_psv", [128, 512], F32)
    ps_n = ps_q[0][0:1, 0:QCH]

    wld = f"{pref}_wld"
    xld = f"{pref}_xld"
    qmm = f"{pref}_qmm"
    qev = f"{pref}_qev"
    qst = f"{pref}_qst"
    kvmm = f"{pref}_kvmm"
    kvev = f"{pref}_kvev"
    kvst = f"{pref}_kvst"
    sqd = f"{pref}_sqd"
    nmm = f"{pref}_nmm"
    vch = f"{pref}_vch"
    agd = f"{pref}_agd"

    for e_i in range(NF):
        c.emit_inc("gpsimd", wld, 16,
                   lambda e, i=e_i: e.dma_start(
                       out=wq[:, i * E:(i + 1) * E], in_=wq_src[i * 128:(i + 1) * 128, :]))
        c.emit_inc("gpsimd", wld, 16,
                   lambda e, i=e_i: e.dma_start(
                       out=kvw[:, i * 1024:(i + 1) * 1024], in_=kvw_src[i * 128:(i + 1) * 128, :]))
    c.emit_inc("vector", vch, 1, lambda e: e.memset(qm[:, :], 0.0))

    for ch in range(NCH):
        buf = ch % 2
        xv = lambda e_i, t0, n, b=buf: xt[:, (b * NF + e_i) * QCH + t0:(b * NF + e_i) * QCH + t0 + n]
        if ch >= 2:
            for par in range(4):
                c.emit_wait("sync", f"{qst}{par}", (ch - 1) * (NF // 4) * 16)
            for par in range(2):
                c.emit_wait("sync", f"{kvst}{par}", (ch - 1) * (QCH // 256) * 16)
        src = xn_src[:, ch * QCH:(ch + 1) * QCH].rearrange("(a p) c -> p a c", p=128)
        c.emit_inc("sync", xld, 16,
                   lambda e, s=src, b=buf: e.dma_start(
                       out=xt[:, b * NF * QCH:(b + 1) * NF * QCH].rearrange(
                           "p (a c) -> p a c", a=NF), in_=s))
        c.emit_wait("tensor", xld, (ch + 1) * 16)
        if ch == 0:
            c.emit_wait("tensor", wld, 32 * 16)

        for qd in range(NF):
            ti = ch * NF + qd
            pq = ps_q[ti % 4]
            if ti >= 4:
                c.emit_wait("tensor", qev, ti - 3)
            for e_i in range(NF):
                fn = lambda e, qd=qd, e_i=e_i, pq=pq, xv=xv: e.matmul(
                    pq[:, :], wq[:, e_i * E + qd * 128:e_i * E + qd * 128 + 128],
                    xv(e_i, 0, QCH), start=(e_i == 0), stop=(e_i == NF - 1))
                if e_i == NF - 1:
                    c.emit_inc("tensor", qmm, 1, fn)
                else:
                    c.emit("tensor", fn)
            c.emit_wait("scalar", qmm, ti + 1)
            qeb = qevac[:, (ti % 4) * QCH:(ti % 4 + 1) * QCH]
            if ti >= 4:
                c.emit_wait("scalar", f"{qst}{ti % 4}", (ti // 4) * 16)
            c.emit_inc("scalar", qev, 1,
                       lambda e, pq=pq, qeb=qeb: e.activation(qeb, pq[:, :], AF.Copy))
            c.emit_wait("sync", qev)
            dst = qt_dst[qd * 128:(qd + 1) * 128, ch * QCH:(ch + 1) * QCH]
            c.emit_inc("sync", f"{qst}{ti % 4}", 16,
                       lambda e, d=dst, qeb=qeb: e.dma_start(out=d, in_=qeb))

        for tt in range(QCH // 128):
            gt = ch * (QCH // 128) + tt
            if gt >= 1:
                c.emit_wait("tensor", kvev, gt)
            for e_i in range(NF):
                c.emit("tensor",
                       lambda e, e_i=e_i, tt=tt, xv=xv: e.matmul(
                           ps_k[:, :], xv(e_i, tt * 128, 128),
                           kvw[:, e_i * 1024:e_i * 1024 + 512],
                           start=(e_i == 0), stop=(e_i == NF - 1)))
                fnv = lambda e, e_i=e_i, tt=tt, xv=xv: e.matmul(
                    ps_v[:, :], xv(e_i, tt * 128, 128),
                    kvw[:, e_i * 1024 + 512:e_i * 1024 + 1024],
                    start=(e_i == 0), stop=(e_i == NF - 1))
                if e_i == NF - 1:
                    c.emit_inc("tensor", kvmm, 1, fnv)
                else:
                    c.emit("tensor", fnv)
            c.emit_wait("vector", kvmm)
            kvb = kvevac[:, (gt % 2) * 1024:(gt % 2 + 1) * 1024]
            if gt >= 2:
                c.emit_wait("vector", f"{kvst}{gt % 2}", (gt // 2) * 16)
            c.emit_inc("vector", vch, 1,
                       lambda e, kvb=kvb: e.tensor_copy(kvb[:, 0:512], ps_k[:, :]))
            c.emit_inc("vector", kvev, 1,
                       lambda e, kvb=kvb: e.tensor_copy(kvb[:, 512:1024], ps_v[:, :]))
            c.emit_wait("sync", kvev)
            dst = kv_own[gt * 128:(gt + 1) * 128, :]
            c.emit_inc("sync", f"{kvst}{gt % 2}", 16,
                       lambda e, d=dst, kvb=kvb: e.dma_start(out=d, in_=kvb))

    qnb = c.sbuf(f"{pref}_qnb", [128, 2 * TOK], BF16)
    c.emit_wait("tensor", qev)
    for qd in range(NF):
        par = qd % 2
        c.emit_wait("sync", f"{qst}{qd % 4}")
        if qd >= 2:
            c.emit_wait("sync", sqd, (qd - 1))
        c.emit_inc("sync", f"{pref}_qnl{par}", 16,
                   lambda e, qd=qd, par=par: e.dma_start(
                       out=qnb[:, par * TOK:(par + 1) * TOK],
                       in_=qt_dst[qd * 128:(qd + 1) * 128, :]))
        c.emit_wait("scalar", f"{pref}_qnl{par}", (qd // 2 + 1) * 16)
        c.emit_wait("scalar", nmm)
        c.emit_inc("scalar", sqd, 1,
                   lambda e, par=par: e.activation(sq2[:, :], qnb[:, par * TOK:(par + 1) * TOK],
                                                   AF.Square))
        c.emit_wait("tensor", sqd)
        for half in range(2):
            for cc in range(TOK // QCH):
                c.emit_wait("tensor", vch)
                c.emit_inc("tensor", nmm, 1,
                           lambda e, half=half, cc=cc: e.matmul(
                               ps_n, onesf[half * 64:half * 64 + 64, 0:1],
                               sq2[half * 64:half * 64 + 64, cc * QCH:(cc + 1) * QCH],
                               start=True, stop=True))
                h_idx = qd * 2 + half
                c.emit_wait("vector", nmm)
                c.emit_inc("vector", vch, 1,
                           lambda e: e.reduce_max(tmp1[:, :], ps_n, axis=AX.X))
                c.emit_wait("vector", vch)
                c.emit_inc("vector", vch, 1,
                           lambda e, h=h_idx: e.tensor_tensor(
                               qm[:, h:h + 1], qm[:, h:h + 1], tmp1[:, :], ALU.max))

    c.emit_wait("sync", vch)
    c.emit_inc("sync", "B_qmst", 16, lambda e: e.dma_start(out=qmax_out, in_=qm[:, :]))
    for par in range(2):
        c.emit_wait("gpsimd", f"{kvst}{par}", NCH * (QCH // 256) * 16)
    rg = [[0, 1], [2, 3], [4, 5], [6, 7]]
    c.emit_inc("gpsimd", agd, 1,
               lambda e: e.collective_compute(
                   "AllGather", ALU.bypass, replica_groups=rg,
                   ins=[kv_all_in.opt()], outs=[kv_all_out.opt()]))
    c.emit_wait("gpsimd", agd, 1)
    c.emit_wait("gpsimd", "B_qmst", 16)


def build_phaseB():
    nc = bass.Bass("TRN2", target_bir_lowering=False, debug=False)
    xn0 = nc.dram_tensor("XN0", [E, TOK], BF16, kind="ExternalInput").ap()
    wq = nc.dram_tensor("WqT", [E, E], F32, kind="ExternalInput").ap()
    kvw = nc.dram_tensor("KVW", [E, 1024], F32, kind="ExternalInput").ap()
    ones_in = nc.dram_tensor("ones128", [128, 128], F32, kind="ExternalInput").ap()
    qt = nc.dram_tensor("QT", [E, TOK], BF16, kind="ExternalOutput").ap()
    kv_own = nc.dram_tensor("KV_OWN", [TOK, 1024], BF16, kind="Internal").ap()
    kv_all_i = nc.dram_tensor("KV_ALL_I", [2 * TOK, 1024], BF16, kind="Internal").ap()
    kv_all = nc.dram_tensor("KV_ALL", [2 * TOK, 1024], BF16, kind="ExternalOutput").ap()
    qmax = nc.dram_tensor("QMAX", [1, NH], F32, kind="ExternalOutput").ap()
    with ExitStack() as st:
        c = Ctx(nc, st)
        onesf = c.sbuf("onesf", [128, 128], F32)
        c.emit_inc("sync", "init", 16, lambda e: e.dma_start(out=onesf[:, :], in_=ones_in))
        c.emit_wait("tensor", "init", 16)
        emit_qkv(c, xn0, wq, kvw, onesf, qt, kv_own, kv_own, kv_all_i, qmax)
        c.emit_inc("gpsimd", "B_cp", 16,
                   lambda e: e.dma_start(out=kv_all, in_=kv_all_i))
        c.emit_wait("gpsimd", "B_cp", 16)
        c.finalize()
    return nc


def _ag_row(t):
    g = t // 4
    return (g % 2) * TOK + (g // 2) * QCH + (t % 4) * 128


def emit_attn(c, qt_src, kv_src, qmax_src, lams_src, lnw_src, mask_src,
              ident_src, onesb_src, onesf, attn_dst, pref="C"):
    nc = c.nc
    NT = 2 * TOK // 128
    kt = c.sbuf(f"{pref}_kt", [128, 9 * 2 * TOK], BF16)
    vt = c.sbuf(f"{pref}_vt", [128, NT * 512], BF16)
    qch = c.sbuf(f"{pref}_q", [128, 2 * NF * QCH], BF16)
    att = c.sbuf(f"{pref}_a", [128, NF * QCH], BF16)
    ksq = c.sbuf(f"{pref}_ksq", [128, 2 * TOK], F32)
    masks = c.sbuf(f"{pref}_mk", [128, 8 * QCH], BF16)
    ident = c.sbuf(f"{pref}_id", [128, 128], BF16)
    onesb = c.sbuf(f"{pref}_ob", [128, 128], BF16)
    qm = c.sbuf(f"{pref}_qm", [1, NH], F32)
    km = c.sbuf(f"{pref}_km", [1, 8], F32)
    bprod = c.sbuf(f"{pref}_bp", [1, NH], F32)
    bh = c.sbuf(f"{pref}_bh", [128, NH], F32)
    lams = c.sbuf(f"{pref}_lam", [1, 256], F32)
    lscr = c.sbuf(f"{pref}_ls", [1, 64], F32)
    lam1 = c.sbuf(f"{pref}_l1", [1, 1], F32)
    lam2 = c.sbuf(f"{pref}_l2", [1, 1], F32)
    nlamb = c.sbuf(f"{pref}_nlb", [128, 1], F32)
    lnw2 = c.sbuf(f"{pref}_lnw", [128, 1], F32)
    epsb = c.sbuf(f"{pref}_eps", [128, 1], F32)
    tmp1 = c.sbuf(f"{pref}_t1", [1, 1], F32)
    expe = c.sbuf(f"{pref}_ee", [128, 2 * QCH], BF16)
    expo = c.sbuf(f"{pref}_eo", [128, 2 * QCH], BF16)
    expem = c.sbuf(f"{pref}_eem", [128, 2 * QCH], BF16)
    expom = c.sbuf(f"{pref}_eom", [128, 2 * QCH], BF16)
    av_e = c.sbuf(f"{pref}_ave", [128, QCH], F32)
    av_o = c.sbuf(f"{pref}_avo", [128, QCH], F32)
    r_e = c.sbuf(f"{pref}_re", [128, QCH], F32)
    r_o = c.sbuf(f"{pref}_ro", [128, QCH], F32)
    o_e = c.sbuf(f"{pref}_oe", [128, QCH], F32)
    o_p = c.sbuf(f"{pref}_op", [128, QCH], F32)
    sqs = c.sbuf(f"{pref}_sqs", [128, QCH], F32)
    stdt = c.sbuf(f"{pref}_std", [128, QCH], F32)
    rr = c.sbuf(f"{pref}_rr", [128, QCH], F32)

    ps_se = [c.psum(f"{pref}_se{i}", [128, QCH], F32) for i in range(2)]
    ps_so = [c.psum(f"{pref}_so{i}", [128, QCH], F32) for i in range(2)]
    ps_n = ps_se[0][0:1, 0:QCH]
    ps_b = ps_se[0][:, 0:1]
    ps_ave = c.psum(f"{pref}_pave", [128, QCH], F32)
    ps_avo = c.psum(f"{pref}_pavo", [128, QCH], F32)
    ps_sme = c.psum(f"{pref}_psme", [128, QCH], F32)
    ps_smo = c.psum(f"{pref}_psmo", [128, QCH], F32)
    ps_ms = ps_se[0]

    ld = f"{pref}_ld"
    kld = f"{pref}_kld"
    tpd = f"{pref}_tpd"
    tpe = f"{pref}_tpe"
    vch = f"{pref}_vch"
    sch = f"{pref}_sch"
    nmm = f"{pref}_nmm"
    qld = f"{pref}_qld"
    ast = f"{pref}_ast"
    smm = f"{pref}_smm"
    exd = f"{pref}_exd"
    mkd = f"{pref}_mkd"
    acc = f"{pref}_acc"
    evd = f"{pref}_evd"
    cmb = f"{pref}_cmb"

    c.emit_inc("sync", ld, 16, lambda e: e.dma_start(out=ident[:, :], in_=ident_src))
    c.emit_inc("sync", ld, 16, lambda e: e.dma_start(out=onesb[:, :], in_=onesb_src))
    c.emit_inc("sync", ld, 16, lambda e: e.dma_start(
        out=masks[:, :].rearrange("p (a c) -> p a c", a=8),
        in_=mask_src.rearrange("(a p) c -> p a c", p=128)))
    c.emit_inc("sync", ld, 16, lambda e: e.dma_start(out=qm[:, :], in_=qmax_src))
    c.emit_inc("sync", ld, 16, lambda e: e.dma_start(out=lams[:, :], in_=lams_src))
    c.emit_inc("sync", ld, 16, lambda e: e.dma_start(out=lnw2[:, :], in_=lnw_src))

    for cb in range(8):
        row = cb * QCH
        g = 2 * (cb % 4) + (cb // 4)
        for c64 in range(8):
            c.emit_inc("sync", tpd, 16,
                       lambda e, row=row, c64=c64, g=g: e.dma_start(
                           out=kt[:, c64 * 2 * TOK + g * 512:c64 * 2 * TOK + (g + 1) * 512],
                           in_=kv_src[row:row + 512, c64 * 64:c64 * 64 + 128],
                           transpose=True))
    c.emit_wait("gpsimd", tpd, 8 * 8 * 16)
    c.emit_inc("gpsimd", f"{pref}_sh", 16,
               lambda e: e.dma_start(
                   out=kt[64:128, 8 * 2 * TOK:9 * 2 * TOK],
                   in_=kt[0:64, 0:2 * TOK]))
    for t in range(NT):
        row = _ag_row(t)
        c.emit_inc("sync", f"{pref}_vld", 16,
                   lambda e, row=row, t=t: e.dma_start(
                       out=vt[:, t * 512:(t + 1) * 512],
                       in_=kv_src[row:row + 128, 512:1024]))

    c.emit_inc("vector", vch, 1, lambda e: e.memset(km[:, :], 0.0))
    c.emit_wait("scalar", tpd, 8 * 8 * 16)
    for kh in range(8):
        c.emit_wait("scalar", nmm)
        c.emit_inc("scalar", sch, 1,
                   lambda e, kh=kh: e.activation(
                       ksq[0:64, :], kt[0:64, kh * 2 * TOK:(kh + 1) * 2 * TOK], AF.Square))
        for cc in range(2 * TOK // QCH):
            c.emit_wait("tensor", sch)
            c.emit_wait("tensor", vch)
            c.emit_inc("tensor", nmm, 1,
                       lambda e, cc=cc: e.matmul(
                           ps_n, onesf[0:64, 0:1],
                           ksq[0:64, cc * QCH:(cc + 1) * QCH],
                           start=True, stop=True))
            c.emit_wait("vector", nmm)
            c.emit_inc("vector", vch, 1,
                       lambda e: e.reduce_max(tmp1[:, :], ps_n, axis=AX.X))
            c.emit_wait("vector", vch)
            c.emit_inc("vector", vch, 1,
                       lambda e, kh=kh: e.tensor_tensor(
                           km[:, kh:kh + 1], km[:, kh:kh + 1], tmp1[:, :], ALU.max))
    c.emit_wait("vector", ld, 6 * 16)
    c.emit_wait("vector", vch)
    for h in range(NH):
        c.emit_inc("vector", vch, 1,
                   lambda e, h=h: e.tensor_tensor(
                       bprod[:, h:h + 1], qm[:, h:h + 1], km[:, h // 4:h // 4 + 1], ALU.mult))
    c.emit_wait("scalar", vch)
    c.emit_inc("scalar", sch, 1, lambda e: e.activation(bprod[:, :], bprod[:, :], AF.Sqrt))
    c.emit_wait("vector", sch)
    c.emit_inc("vector", vch, 1,
               lambda e: e.tensor_scalar(bprod[:, :], bprod[:, :], -0.125, None, ALU.mult))
    for h in range(NH):
        c.emit_wait("tensor", vch)
        c.emit_wait("tensor", f"{pref}_bhe")
        c.emit_inc("tensor", f"{pref}_bhm", 1,
                   lambda e, h=h: e.matmul(
                       ps_b, onesf[0:1, :], bprod[:, h:h + 1], start=True, stop=True))
        c.emit_wait("vector", f"{pref}_bhm")
        c.emit_inc("vector", f"{pref}_bhe", 1,
                   lambda e, h=h: e.tensor_copy(bh[:, h:h + 1], ps_b))
    c.emit_inc("vector", vch, 1,
               lambda e: e.tensor_tensor(lscr[:, :], lams[:, 0:64], lams[:, 64:128], ALU.mult))
    c.emit_wait("vector", vch)
    c.emit_inc("vector", vch, 1, lambda e: e.reduce_sum(lam1[:, :], lscr[:, :], axis=AX.X))
    c.emit_wait("vector", vch)
    c.emit_inc("vector", vch, 1,
               lambda e: e.tensor_tensor(lscr[:, :], lams[:, 128:192], lams[:, 192:256], ALU.mult))
    c.emit_wait("vector", vch)
    c.emit_inc("vector", vch, 1, lambda e: e.reduce_sum(lam2[:, :], lscr[:, :], axis=AX.X))
    c.emit_wait("scalar", vch)
    c.emit_inc("scalar", sch, 1, lambda e: e.activation(lam1[:, :], lam1[:, :], AF.Exp))
    c.emit_inc("scalar", sch, 1, lambda e: e.activation(lam2[:, :], lam2[:, :], AF.Exp))
    c.emit_wait("vector", sch)
    c.emit_inc("vector", vch, 1,
               lambda e: e.scalar_tensor_tensor(
                   tmp1[:, :], lam2[:, :], -1.0, lam1[:, :], ALU.mult, ALU.add))
    c.emit_wait("vector", vch)
    c.emit_inc("vector", vch, 1,
               lambda e: e.tensor_scalar(
                   tmp1[:, :], tmp1[:, :], float(LAM_INIT), -1.0, ALU.add, ALU.mult))
    c.emit_wait("tensor", vch)
    c.emit_inc("tensor", f"{pref}_bhm", 1,
               lambda e: e.matmul(ps_b, onesf[0:1, :], tmp1[:, :], start=True, stop=True))
    c.emit_wait("vector", f"{pref}_bhm")
    c.emit_inc("vector", f"{pref}_bhe", 1, lambda e: e.tensor_copy(nlamb[:, :], ps_b))
    c.emit_inc("vector", vch, 1,
               lambda e: e.tensor_scalar(lnw2[:, :], lnw2[:, :], float(1.0 - LAM_INIT), None, ALU.mult))
    c.emit_inc("vector", vch, 1, lambda e: e.memset(epsb[:, :], 1e-5))

    npair_done = 0
    pending = [None]
    for j in range(NCH):
        par = j % 2
        span = 8 * j + 8
        if j >= 2:
            c.emit_wait("sync", cmb, (j - 1) * 16)
        src = qt_src[:, j * QCH:(j + 1) * QCH].rearrange("(a p) c -> p a c", p=128)
        c.emit_inc("sync", f"{qld}{par}", 16,
                   lambda e, s=src, par=par: e.dma_start(
                       out=qch[:, par * NF * QCH:(par + 1) * NF * QCH].rearrange(
                           "p (a c) -> p a c", a=NF), in_=s))
        for p in range(NF):
            he, kh, vh = 2 * p, p // 2, p // 4
            q_e = qch[0:64, (par * NF + p) * QCH:(par * NF + p + 1) * QCH]
            q_o = qch[64:128, (par * NF + p) * QCH:(par * NF + p + 1) * QCH]
            kte = lambda t, kh=kh: kt[0:64, kh * 2 * TOK + t * 128:kh * 2 * TOK + (t + 1) * 128]
            koff = (kh - 1) if kh >= 1 else 8
            kto = lambda t, koff=koff: kt[64:128, koff * 2 * TOK + t * 128:koff * 2 * TOK + (t + 1) * 128]
            vtile = lambda t, vh=vh: vt[:, t * 512 + vh * 128:t * 512 + (vh + 1) * 128]
            c.emit_wait("tensor", f"{qld}{par}", (j // 2 + 1) * 16)
            if j == 0 and p == 0:
                c.emit_wait("tensor", tpd, 8 * 8 * 16)
                c.emit_wait("tensor", f"{pref}_sh", 16)
                c.emit_wait("tensor", f"{pref}_vld", NT * 16)
                c.emit_wait("tensor", f"{pref}_bhe")
                c.emit_wait("scalar", f"{pref}_bhe")
                c.emit_wait("scalar", vch)

            masked = {}
            def ee_eo(t):
                tp = t % 2
                if t in masked:
                    return (expem[:, tp * QCH:(tp + 1) * QCH],
                            expom[:, tp * QCH:(tp + 1) * QCH])
                return (expe[:, tp * QCH:(tp + 1) * QCH],
                        expo[:, tp * QCH:(tp + 1) * QCH])

            def emit_av(t, pp=p):
                first, last = (t == 0), (t == span - 1)
                eet, eot = ee_eo(t)
                if t in masked:
                    c.emit_wait("tensor", mkd, masked[t])
                else:
                    c.emit_wait("tensor", exd, 2 * (t + 1) + exd_base)
                if first:
                    c.emit_wait("tensor", evd)
                fn1 = lambda e, t=t, eet=eet, vtile=vtile, first=first, last=last: e.matmul(
                    ps_ave[:, :], vtile(t), eet, start=first, stop=last)
                fn2 = lambda e, t=t, eot=eot, vtile=vtile, first=first, last=last: e.matmul(
                    ps_avo[:, :], vtile(t), eot, start=first, stop=last)
                fn3 = lambda e, eet=eet, first=first, last=last: e.matmul(
                    ps_sme[:, :], onesb[:, :], eet, start=first, stop=last)
                fn4 = lambda e, eot=eot, first=first, last=last: e.matmul(
                    ps_smo[:, :], onesb[:, :], eot, start=first, stop=last)
                c.emit("tensor", fn1)
                c.emit("tensor", fn2)
                c.emit("tensor", fn3)
                c.emit_inc("tensor", acc, 1, fn4)

            exd_base = c.cnt.get(exd, 0)
            for t in range(span):
                tp = t % 2
                s_loc = t - (span - 8)
                if t == 6 and pending[0] is not None:
                    pending[0]()
                    pending[0] = None
                c.emit_wait("tensor", exd, max(0, c.cnt.get(exd, 0) - 2))
                if t in (0, 6):
                    c.emit_wait("tensor", f"{pref}_msev")
                c.emit_inc("tensor", smm, 1,
                           lambda e, t=t, q_e=q_e, kte=kte, tp=tp: e.matmul(
                               ps_se[tp][:, :], kte(t), q_e, start=True, stop=True))
                c.emit_inc("tensor", smm, 1,
                           lambda e, t=t, q_o=q_o, kto=kto, tp=tp: e.matmul(
                               ps_so[tp][:, :], kto(t), q_o, start=True, stop=True))
                c.emit_wait("scalar", smm)
                c.emit_wait("scalar", acc)
                c.emit_wait("scalar", mkd)
                c.emit_inc("scalar", exd, 1,
                           lambda e, he=he, tp=tp: e.activation(
                               expe[:, tp * QCH:(tp + 1) * QCH], ps_se[tp][:, :],
                               AF.Exp, bias=bh[:, he:he + 1], scale=0.125))
                c.emit_inc("scalar", exd, 1,
                           lambda e, he=he, tp=tp: e.activation(
                               expo[:, tp * QCH:(tp + 1) * QCH], ps_so[tp][:, :],
                               AF.Exp, bias=bh[:, he + 1:he + 2], scale=0.125))
                if s_loc >= 0:
                    c.emit_wait("vector", exd)
                    c.emit_wait("vector", acc)
                    c.emit_inc("vector", mkd, 1,
                               lambda e, s=s_loc, tp=tp: e.tensor_tensor(
                                   expem[:, tp * QCH:(tp + 1) * QCH],
                                   expe[:, tp * QCH:(tp + 1) * QCH],
                                   masks[:, s * QCH:(s + 1) * QCH], ALU.mult))
                    c.emit_inc("vector", mkd, 1,
                               lambda e, s=s_loc, tp=tp: e.tensor_tensor(
                                   expom[:, tp * QCH:(tp + 1) * QCH],
                                   expo[:, tp * QCH:(tp + 1) * QCH],
                                   masks[:, s * QCH:(s + 1) * QCH], ALU.mult))
                    masked[t] = c.cnt[mkd]
                if t >= 1:
                    emit_av(t - 1)
            emit_av(span - 1)
            npair_done += 1
            c.emit_wait("vector", acc)
            c.emit_inc("vector", evd, 1, lambda e: e.reciprocal(r_e[:, :], ps_sme[:, :]))
            c.emit_inc("vector", evd, 1, lambda e: e.reciprocal(r_o[:, :], ps_smo[:, :]))
            c.emit_wait("scalar", acc)
            c.emit_inc("scalar", evd, 1, lambda e: e.activation(av_e[:, :], ps_ave[:, :], AF.Copy))
            c.emit_inc("scalar", evd, 1, lambda e: e.activation(av_o[:, :], ps_avo[:, :], AF.Copy))
            c.emit_wait("vector", evd)
            c.emit_inc("vector", vch, 1,
                       lambda e: e.tensor_tensor(o_e[:, :], av_e[:, :], r_e[:, :], ALU.mult))
            c.emit_inc("vector", vch, 1,
                       lambda e: e.tensor_tensor(o_p[:, :], av_o[:, :], r_o[:, :], ALU.mult))
            c.emit_wait("vector", vch)
            c.emit_inc("vector", vch, 1,
                       lambda e: e.scalar_tensor_tensor(
                           o_p[:, :], o_p[:, :], nlamb[:, 0:1], o_e[:, :], ALU.mult, ALU.add))
            c.emit_wait("scalar", vch)
            c.emit_inc("scalar", sch, 1, lambda e: e.activation(sqs[:, :], o_p[:, :], AF.Square))

            def part2(p=p, j=j):
                c.emit_wait("tensor", sch)
                c.emit_wait("tensor", exd, max(0, c.cnt.get(exd, 0) - 2))
                c.emit_inc("tensor", f"{pref}_msmm", 1,
                           lambda e: e.matmul(ps_ms[:, :], onesf[:, :], sqs[:, :],
                                              start=True, stop=True))
                c.emit_wait("scalar", f"{pref}_msmm")
                c.emit_inc("scalar", f"{pref}_msev", 1,
                           lambda e: e.activation(stdt[:, :], ps_ms[:, :], AF.Sqrt,
                                                  bias=epsb[:, 0:1], scale=1.0 / 128.0))
                c.emit_wait("vector", f"{pref}_msev")
                c.emit_inc("vector", vch, 1, lambda e: e.reciprocal(rr[:, :], stdt[:, :]))
                c.emit_wait("vector", vch)
                c.emit_inc("vector", vch, 1,
                           lambda e: e.tensor_tensor(o_p[:, :], o_p[:, :], rr[:, :], ALU.mult))
                c.emit_wait("vector", vch)
                if j >= 1 and p == 0:
                    c.emit_wait("vector", ast, j * 16)
                c.emit_inc("vector", cmb, 1,
                           lambda e, p=p: e.tensor_scalar(
                               att[:, p * QCH:(p + 1) * QCH], o_p[:, :],
                               lnw2[:, 0:1], None, ALU.mult))
            pending[0] = part2
        if pending[0] is not None:
            pending[0]()
            pending[0] = None
        c.emit_wait("sync", cmb, (j + 1) * 16)
        dst = attn_dst[:, j * QCH:(j + 1) * QCH].rearrange("(a p) c -> p a c", p=128)
        c.emit_inc("sync", ast, 16,
                   lambda e, d=dst: e.dma_start(
                       out=d, in_=att[:, :].rearrange("p (a c) -> p a c", a=NF)))
    c.emit_wait("sync", ast)


def build_phaseC():
    nc = bass.Bass("TRN2", target_bir_lowering=False, debug=False)
    qt = nc.dram_tensor("QT", [E, TOK], BF16, kind="ExternalInput").ap()
    kv = nc.dram_tensor("KV_ALL", [2 * TOK, 1024], BF16, kind="ExternalInput").ap()
    qmax = nc.dram_tensor("QMAX", [1, NH], F32, kind="ExternalInput").ap()
    lams = nc.dram_tensor("LAMS", [1, 256], F32, kind="ExternalInput").ap()
    lnw = nc.dram_tensor("LNW", [128, 1], F32, kind="ExternalInput").ap()
    mask = nc.dram_tensor("MASK8", [8 * 128, QCH], BF16, kind="ExternalInput").ap()
    ident = nc.dram_tensor("IDENT", [128, 128], BF16, kind="ExternalInput").ap()
    onesb = nc.dram_tensor("ONESB", [128, 128], BF16, kind="ExternalInput").ap()
    ones_in = nc.dram_tensor("ones128", [128, 128], F32, kind="ExternalInput").ap()
    attn = nc.dram_tensor("ATTN", [E, TOK], BF16, kind="ExternalOutput").ap()
    with ExitStack() as st:
        c = Ctx(nc, st)
        onesf = c.sbuf("onesf", [128, 128], F32)
        c.emit_inc("sync", "init", 16, lambda e: e.dma_start(out=onesf[:, :], in_=ones_in))
        c.emit_wait("tensor", "init", 16)
        emit_attn(c, qt, kv, qmax, lams, lnw, mask, ident, onesb, onesf, attn)
        c.finalize()
    return nc


def host_masks(role):
    kk = np.arange(128)[:, None]
    qq = np.arange(QCH)[None, :]
    out = np.zeros((8 * 128, QCH), np.float32)
    for s in range(8):
        out[s * 128:(s + 1) * 128] = (128 * s + kk <= QCH * role + qq)
    return out.astype(bf)


def emit_wo(c, attn_src, wo_src, x_src, x1_dst, pref="D"):
    wo = c.sbuf(f"{pref}_wo", [128, NF * E], BF16)
    an = c.sbuf(f"{pref}_a", [128, 2 * NF * QCH], BF16)
    xr = c.sbuf(f"{pref}_x", [128, 2 * NF * QCH], F32)
    ot = c.sbuf(f"{pref}_o", [128, NF * QCH], F32)
    ps = [c.psum(f"{pref}_ps{i}", [128, QCH], F32) for i in range(4)]

    wld, ald, xld = f"{pref}_wld", f"{pref}_ald", f"{pref}_xld"
    mm, ev, st = f"{pref}_mm", f"{pref}_ev", f"{pref}_st"

    for e_i in range(NF):
        c.emit_inc("gpsimd", wld, 16,
                   lambda e, i=e_i: e.dma_start(
                       out=wo[:, i * E:(i + 1) * E], in_=wo_src[i * 128:(i + 1) * 128, :]))
    for ch in range(NCH):
        par = ch % 2
        if ch >= 2:
            c.emit_wait("sync", st, (ch - 1) * 16)
        asrc = attn_src[:, ch * QCH:(ch + 1) * QCH].rearrange("(a p) c -> p a c", p=128)
        c.emit_inc("sync", f"{ald}{par}", 16,
                   lambda e, s=asrc, par=par: e.dma_start(
                       out=an[:, par * NF * QCH:(par + 1) * NF * QCH].rearrange(
                           "p (a c) -> p a c", a=NF), in_=s))
        xsrc = x_src[:, ch * QCH:(ch + 1) * QCH].rearrange("(a p) c -> p a c", p=128)
        c.emit_inc("sync", f"{xld}{par}", 16,
                   lambda e, s=xsrc, par=par: e.dma_start(
                       out=xr[:, par * NF * QCH:(par + 1) * NF * QCH].rearrange(
                           "p (a c) -> p a c", a=NF), in_=s))
        c.emit_wait("tensor", f"{ald}{par}", (ch // 2 + 1) * 16)
        if ch == 0:
            c.emit_wait("tensor", wld, NF * 16)
        for od in range(NF):
            ti = ch * NF + od
            pq = ps[ti % 4]
            if ti >= 4:
                c.emit_wait("tensor", ev, ti - 3)
            for e_i in range(NF):
                fn = lambda e, od=od, e_i=e_i, pq=pq, par=par: e.matmul(
                    pq[:, :], wo[:, e_i * E + od * 128:e_i * E + od * 128 + 128],
                    an[:, (par * NF + e_i) * QCH:(par * NF + e_i + 1) * QCH],
                    start=(e_i == 0), stop=(e_i == NF - 1))
                if e_i == NF - 1:
                    c.emit_inc("tensor", mm, 1, fn)
                else:
                    c.emit("tensor", fn)
            c.emit_wait("vector", mm)
            c.emit_wait("vector", f"{xld}{par}", (ch // 2 + 1) * 16)
            if od == 0 and ch >= 1:
                c.emit_wait("vector", st, ch * 16)
            c.emit_inc("vector", ev, 1,
                       lambda e, od=od, pq=pq, par=par: e.tensor_tensor(
                           ot[:, od * QCH:(od + 1) * QCH], pq[:, :],
                           xr[:, (par * NF + od) * QCH:(par * NF + od + 1) * QCH], ALU.add))
        c.emit_wait("sync", ev)
        dst = x1_dst[:, ch * QCH:(ch + 1) * QCH].rearrange("(a p) c -> p a c", p=128)
        c.emit_inc("sync", st, 16,
                   lambda e, d=dst: e.dma_start(
                       out=d, in_=ot[:, :].rearrange("p (a c) -> p a c", a=NF)))
    c.emit_wait("sync", st)


def quant_bufs(c, CC):
    return (c.sbuf("Q_w", [128, 2 * CC], F32),
            c.sbuf("Q_ac", [128, 1], F32),
            c.sbuf("Q_rd", [128, 1], F32),
            c.sbuf("Q_cp", [128, 1], F32),
            c.sbuf("Q_cn", [128, 1], F32),
            c.sbuf("Q_s1", [128, CC], BF16),
            c.sbuf("Q_s2", [128, CC], BF16),
            c.sbuf("Q_q", [128, 2 * CC], BF16),
            c.psum("Q_ps", [128, 1], F32))


def emit_quant(c, wt_src, g_dst, R, CC, onesf, pref, shared=None):
    nb = R // 128
    if shared is None:
        shared = quant_bufs(c, CC)
    wt, acc, red, cpos, cneg, s1, s2, qt, ps = shared
    wt0 = wt
    wt = lambda par: wt0[:, par * CC:(par + 1) * CC]
    qt0 = qt
    qt = lambda par: qt0[:, par * CC:(par + 1) * CC]
    s1 = s1[:, 0:CC]
    s2 = s2[:, 0:CC]

    ld, vch, gmm, sch, qd, st = (f"{pref}_ld", f"{pref}_vch", f"{pref}_gmm",
                                 f"{pref}_sch", f"{pref}_qd", f"{pref}_st")
    c.emit_inc("vector", vch, 1, lambda e: e.memset(acc[:, :], 0.0))
    for b_i in range(nb):
        par = b_i % 2
        if b_i >= 2:
            c.emit_wait("sync", vch, 2 * b_i - 1)
        c.emit_inc("sync", f"{ld}{par}", 16,
                   lambda e, b_i=b_i, par=par: e.dma_start(
                       out=wt(par), in_=wt_src[b_i * 128:(b_i + 1) * 128, :]))
        c.emit_wait("vector", f"{ld}{par}", (b_i // 2 + 1) * 16)
        c.emit_wait("vector", vch)
        c.emit_inc("vector", vch, 1,
                   lambda e, par=par: e.tensor_reduce(
                       red[:, :], wt(par), AX.X, ALU.add,
                       apply_absolute_value=True))
        c.emit_wait("vector", vch)
        c.emit_inc("vector", vch, 1,
                   lambda e: e.tensor_tensor(acc[:, :], acc[:, :], red[:, :], ALU.add))
    c.emit_wait("tensor", vch)
    c.emit_inc("tensor", gmm, 1,
               lambda e: e.matmul(ps[:, :], onesf[:, :], acc[:, :], start=True, stop=True))
    c.emit_wait("vector", gmm)
    nelem = float(R * CC)
    c.emit_inc("vector", vch, 1,
               lambda e: e.tensor_scalar(cpos[:, :], ps[:, :], 0.5 / nelem, None, ALU.mult))
    c.emit_inc("vector", vch, 1,
               lambda e: e.tensor_scalar(cneg[:, :], ps[:, :], -0.5 / nelem, None, ALU.mult))
    c.emit_wait("scalar", vch)
    for b_i in range(nb):
        par = b_i % 2
        if b_i >= 2:
            c.emit_wait("sync", sch, 2 * (b_i - 1))
        else:
            c.emit_wait("sync", vch)
        c.emit_inc("sync", f"{ld}2{par}", 16,
                   lambda e, b_i=b_i, par=par: e.dma_start(
                       out=wt(par), in_=wt_src[b_i * 128:(b_i + 1) * 128, :]))
        c.emit_wait("scalar", f"{ld}2{par}", (b_i // 2 + 1) * 16)
        c.emit_wait("scalar", qd)
        c.emit_inc("scalar", sch, 1,
                   lambda e, par=par: e.activation(
                       s1[:, :], wt(par), AF.Sign, bias=cneg[:, 0:1]))
        c.emit_inc("scalar", sch, 1,
                   lambda e, par=par: e.activation(
                       s2[:, :], wt(par), AF.Sign, bias=cpos[:, 0:1]))
        c.emit_wait("vector", sch)
        c.emit_wait("vector", f"{st}{par}", (b_i // 2) * 16)
        c.emit_inc("vector", qd, 3,
                   lambda e, par=par: e.tensor_tensor(
                       qt(par), s1[:, :], s2[:, :], ALU.add))
        c.emit_wait("sync", qd)
        c.emit_inc("sync", f"{st}{par}", 16,
                   lambda e, b_i=b_i, par=par: e.dma_start(
                       out=g_dst[b_i * 128:(b_i + 1) * 128, :],
                       in_=qt(par)))
    c.emit_wait("sync", f"{st}0")
    c.emit_wait("sync", f"{st}1")


def emit_ffn(c, xn_src, g1_src, g2_src, g3_src, x1_src, x1f_dst, pref="F"):
    NHT = FFN_H // 128
    TC = 1024
    ncch = TOK // TC
    xn = c.sbuf(f"{pref}_xn", [128, NF * TC], BF16)
    g12 = c.sbuf(f"{pref}_g12", [128, 6 * NF * 128], BF16)
    g3t = c.sbuf(f"{pref}_g3", [128, 4 * 512], BF16)
    hid = c.sbuf(f"{pref}_h", [128, NHT * TC], BF16)
    sg = c.sbuf(f"{pref}_sg", [128, 1024], BF16)
    x1t = c.sbuf(f"{pref}_x1", [128, 4 * TC], F32)
    ot = c.sbuf(f"{pref}_o", [128, 4 * TC], F32)
    ps = [c.psum(f"{pref}_ps{i}", [128, 512], F32) for i in range(8)]

    xld, gld, mmh, sgd, vsg = (f"{pref}_xld", f"{pref}_gld", f"{pref}_mmh",
                               f"{pref}_sgd", f"{pref}_vsg")
    g3ld, mmo, x1ld, ev, st = (f"{pref}_g3ld", f"{pref}_mmo", f"{pref}_x1ld",
                               f"{pref}_ev", f"{pref}_st")

    for tc in range(ncch):
        tp = 0
        if tc >= 1:
            c.emit_wait("sync", mmh, tc * NHT * 2)
        src = xn_src[:, tc * TC:(tc + 1) * TC].rearrange("(a p) c -> p a c", p=128)
        c.emit_inc("sync", f"{xld}", 16,
                   lambda e, s=src: e.dma_start(
                       out=xn[:, :].rearrange("p (a c) -> p a c", a=NF), in_=s))
        for hd in range(NHT):
            it = tc * NHT + hd
            par = it % 3
            if it >= 3:
                c.emit_wait("sync", mmh, (it - 2) * 2)
            for gsrc, off in ((g1_src, 0), (g2_src, NF * 128)):
                gs = gsrc[:, hd * 128:(hd + 1) * 128].rearrange("(a p) c -> p a c", p=128)
                c.emit_inc("sync", f"{gld}{par}", 16,
                           lambda e, gs=gs, par=par, off=off: e.dma_start(
                               out=g12[:, par * 2 * NF * 128 + off:
                                       par * 2 * NF * 128 + off + NF * 128]
                               .rearrange("p (a c) -> p a c", a=NF), in_=gs))
            c.emit_wait("tensor", f"{gld}{par}", (it // 3 + 1) * 32)
            c.emit_wait("tensor", f"{xld}", (tc + 1) * 16)
            if tc > 0 and hd == 0:
                c.emit_wait("tensor", ev)
            for n in range(2):
                bb = 2 * n
                c.emit_wait("tensor", sgd, max(0, c.cnt.get(sgd, 0) - 1))
                c.emit_wait("tensor", vsg, max(0, c.cnt.get(vsg, 0) - 1))
                for gi in range(2):
                    for e_i in range(NF):
                        fn = lambda e, e_i=e_i, par=par, n=n, gi=gi, bb=bb: e.matmul(
                            ps[bb + gi][:, :],
                            g12[:, par * 2 * NF * 128 + gi * NF * 128 + e_i * 128:
                                par * 2 * NF * 128 + gi * NF * 128 + (e_i + 1) * 128],
                            xn[:, e_i * TC + n * 512:e_i * TC + (n + 1) * 512],
                            start=(e_i == 0), stop=(e_i == NF - 1))
                        if gi == 1 and e_i == NF - 1:
                            c.emit_inc("tensor", mmh, 1, fn)
                        else:
                            c.emit("tensor", fn)
                c.emit_wait("scalar", mmh)
                c.emit_wait("scalar", vsg, max(0, c.cnt.get(vsg, 0) - 1))
                c.emit_inc("scalar", sgd, 1,
                           lambda e, bb=bb, n=n: e.activation(
                               sg[:, n * 512:(n + 1) * 512], ps[bb][:, :],
                               AF.Sigmoid, scale=0.5))
                c.emit_wait("vector", sgd)
                c.emit_inc("vector", vsg, 1,
                           lambda e, hd=hd, n=n, bb=bb: e.scalar_tensor_tensor(
                               hid[:, hd * TC + n * 512:hd * TC + (n + 1) * 512],
                               ps[bb + 1][:, :], 0.5, sg[:, n * 512:(n + 1) * 512],
                               ALU.mult, ALU.mult))

        c.emit_wait("tensor", vsg)
        for grp in range(4):
            gi_glob = tc * 4 + grp
            x1s = x1_src[grp * 512:(grp + 1) * 512, tc * TC:(tc + 1) * TC]
            c.emit_wait("sync", ev)
            c.emit_inc("sync", x1ld, 16,
                       lambda e, s=x1s: e.dma_start(
                           out=x1t[:, :].rearrange("p (a c) -> p a c", a=4),
                           in_=s.rearrange("(a p) c -> p a c", p=128)))
            for hd in range(NHT):
                par = hd % 4
                g3s = g3_src[hd * 128:(hd + 1) * 128, grp * 512:(grp + 1) * 512]
                if hd >= 4:
                    c.emit_wait("sync", mmo, gi_glob * NHT * 8 + (hd - 3) * 8)
                c.emit_inc("sync", f"{g3ld}{par}", 16,
                           lambda e, s=g3s, par=par: e.dma_start(
                               out=g3t[:, par * 512:(par + 1) * 512], in_=s))
                c.emit_wait("tensor", f"{g3ld}{par}",
                            (gi_glob * NHT // 4 + hd // 4 + 1) * 16)
                if hd == 0:
                    c.emit_wait("tensor", ev)
                for o in range(4):
                    for n in range(2):
                        fn = lambda e, o=o, n=n, par=par, hd=hd: e.matmul(
                            ps[o * 2 + n][:, :],
                            g3t[:, par * 512 + o * 128:par * 512 + (o + 1) * 128],
                            hid[:, hd * TC + n * 512:hd * TC + (n + 1) * 512],
                            start=(hd == 0), stop=(hd == NHT - 1))
                        c.emit_inc("tensor", mmo, 1, fn)
            c.emit_wait("vector", mmo)
            c.emit_wait("vector", x1ld)
            c.emit_wait("vector", st)
            for o in range(4):
                for n in range(2):
                    c.emit_inc("vector", ev, 1,
                               lambda e, o=o, n=n: e.scalar_tensor_tensor(
                                   ot[:, o * TC + n * 512:o * TC + (n + 1) * 512],
                                   ps[o * 2 + n][:, :], 0.5,
                                   x1t[:, o * TC + n * 512:o * TC + (n + 1) * 512],
                                   ALU.mult, ALU.add))
            c.emit_wait("sync", ev)
            dstv = x1f_dst[grp * 512:(grp + 1) * 512, tc * TC:(tc + 1) * TC]
            c.emit_inc("sync", st, 16,
                       lambda e, d=dstv: e.dma_start(
                           out=d.rearrange("(a p) c -> p a c", p=128),
                           in_=ot[:, :].rearrange("p (a c) -> p a c", a=4)))
    c.emit_wait("sync", st)
def emit_ln1_tm(c, x1f_src, xn1_dst, xn1tm_dst, onesf, pref="G"):
    emit_ln(c, x1f_src, xn1_dst, pref, ncols=TOK, onesf=onesf)
    tmt = c.sbuf(f"{pref}_tm", [128, 2 * E], BF16)
    tld, tst = f"{pref}_tld", f"{pref}_tst"
    for tt in range(TOK // 128):
        par = tt % 2
        if tt >= 2:
            c.emit_wait("sync", f"{tst}{par}", (tt // 2) * 16)
        c.emit_inc("sync", f"{tld}{par}", 16,
                   lambda e, tt=tt, par=par: e.dma_start(
                       out=tmt[:, par * E:(par + 1) * E],
                       in_=xn1_dst[:, tt * 128:(tt + 1) * 128], transpose=True))
        c.emit_wait("sync", f"{tld}{par}", (tt // 2 + 1) * 16)
        c.emit_inc("sync", f"{tst}{par}", 16,
                   lambda e, tt=tt, par=par: e.dma_start(
                       out=xn1tm_dst[tt * 128:(tt + 1) * 128, :],
                       in_=tmt[:, par * E:(par + 1) * E]))
    c.emit_wait("sync", f"{tst}0")
    c.emit_wait("sync", f"{tst}1")


def emit_l1attn(c, xn1_src, wq_src, bq_src, wkv_src, bkv_src, x1f_src, x2a_dst,
                onesf, pref="H"):
    wq = c.sbuf(f"{pref}_wq", [128, NF * E], BF16)
    wkv = c.sbuf(f"{pref}_wkv", [128, NF * 256], BF16)
    bq = c.sbuf(f"{pref}_bq", [128, NF], F32)
    bkv = c.sbuf(f"{pref}_bkv", [128, 2], F32)
    xn = c.sbuf(f"{pref}_xn", [128, 2 * NF * QCH], BF16)
    x1f = c.sbuf(f"{pref}_x1", [128, NF * QCH], F32)
    qtl = c.sbuf(f"{pref}_qt", [128, NF * QCH], BF16)
    ktl = c.sbuf(f"{pref}_kt", [128, QCH], BF16)
    vtl = c.sbuf(f"{pref}_vt", [128, QCH], F32)
    prod = c.sbuf(f"{pref}_pr", [128, 2 * QCH], F32)
    ssb = c.sbuf(f"{pref}_ss", [128, 2 * QCH], F32)
    esb = c.sbuf(f"{pref}_es", [128, 2 * QCH], F32)
    psb = c.sbuf(f"{pref}_ps", [128, 2 * QCH], F32)
    ot = c.sbuf(f"{pref}_o", [128, NF * QCH], F32)
    ps_q = [c.psum(f"{pref}_pq{i}", [128, QCH], F32) for i in range(4)]
    ps_s = [c.psum(f"{pref}_pss{i}", [128, QCH], F32) for i in range(2)]

    wld, xld, x1ld = f"{pref}_wld", f"{pref}_xld", f"{pref}_x1ld"
    qmm, qev, smm, vch, sch, st = (f"{pref}_qmm", f"{pref}_qev", f"{pref}_smm",
                                   f"{pref}_vch", f"{pref}_sch", f"{pref}_st")

    for e_i in range(NF):
        c.emit_inc("gpsimd", wld, 16,
                   lambda e, i=e_i: e.dma_start(
                       out=wq[:, i * E:(i + 1) * E], in_=wq_src[i * 128:(i + 1) * 128, :]))
        c.emit_inc("gpsimd", wld, 16,
                   lambda e, i=e_i: e.dma_start(
                       out=wkv[:, i * 256:(i + 1) * 256], in_=wkv_src[i * 128:(i + 1) * 128, :]))
    c.emit_inc("gpsimd", wld, 16,
               lambda e: e.dma_start(
                   out=bq[:, :].rearrange("p (a o) -> p a o", o=1),
                   in_=bq_src.rearrange("(a p) o -> p a o", p=128),
                   allow_slow_non_contiguous=True))
    c.emit_inc("gpsimd", wld, 16,
               lambda e: e.dma_start(
                   out=bkv[:, :].rearrange("p (a o) -> p a o", o=1),
                   in_=bkv_src.rearrange("(a p) o -> p a o", p=128),
                   allow_slow_non_contiguous=True))

    for ch in range(NCH):
        par = ch % 2
        if ch >= 2:
            c.emit_wait("sync", st, (ch - 1) * 16)
        src = xn1_src[:, ch * QCH:(ch + 1) * QCH].rearrange("(a p) c -> p a c", p=128)
        c.emit_inc("sync", f"{xld}{par}", 16,
                   lambda e, s=src, par=par: e.dma_start(
                       out=xn[:, par * NF * QCH:(par + 1) * NF * QCH].rearrange(
                           "p (a c) -> p a c", a=NF), in_=s))
        src1 = x1f_src[:, ch * QCH:(ch + 1) * QCH].rearrange("(a p) c -> p a c", p=128)
        c.emit_wait("sync", st, ch * 16)
        c.emit_inc("sync", x1ld, 16,
                   lambda e, s=src1: e.dma_start(
                       out=x1f[:, :].rearrange("p (a c) -> p a c", a=NF), in_=s))
        c.emit_wait("tensor", f"{xld}{par}", (ch // 2 + 1) * 16)
        if ch == 0:
            c.emit_wait("tensor", wld, (2 * NF + 2) * 16)
        for qd in range(NF):
            ti = ch * (NF + 2) + qd
            pq = ps_q[ti % 4]
            if ti >= 4:
                c.emit_wait("tensor", qev, ti - 3)
            for e_i in range(NF):
                fn = lambda e, qd=qd, e_i=e_i, pq=pq, par=par: e.matmul(
                    pq[:, :], wq[:, e_i * E + qd * 128:e_i * E + qd * 128 + 128],
                    xn[:, (par * NF + e_i) * QCH:(par * NF + e_i + 1) * QCH],
                    start=(e_i == 0), stop=(e_i == NF - 1))
                if e_i == NF - 1:
                    c.emit_inc("tensor", qmm, 1, fn)
                else:
                    c.emit("tensor", fn)
            c.emit_wait("vector", qmm)
            c.emit_inc("vector", qev, 1,
                       lambda e, qd=qd, pq=pq: e.tensor_scalar(
                           qtl[:, qd * QCH:(qd + 1) * QCH], pq[:, :],
                           bq[:, qd:qd + 1], None, ALU.add))
        for kv in range(2):
            ti = ch * (NF + 2) + NF + kv
            pq = ps_q[ti % 4]
            c.emit_wait("tensor", qev, ti - 3)
            for e_i in range(NF):
                fn = lambda e, kv=kv, e_i=e_i, pq=pq, par=par: e.matmul(
                    pq[:, :], wkv[:, e_i * 256 + kv * 128:e_i * 256 + kv * 128 + 128],
                    xn[:, (par * NF + e_i) * QCH:(par * NF + e_i + 1) * QCH],
                    start=(e_i == 0), stop=(e_i == NF - 1))
                if e_i == NF - 1:
                    c.emit_inc("tensor", qmm, 1, fn)
                else:
                    c.emit("tensor", fn)
            c.emit_wait("vector", qmm)
            if kv == 0:
                c.emit_inc("vector", qev, 1,
                           lambda e, pq=pq: e.tensor_scalar(
                               ktl[:, :], pq[:, :], bkv[:, 0:1], None, ALU.add))
            else:
                c.emit_inc("vector", qev, 1,
                           lambda e, pq=pq: e.tensor_scalar(
                               vtl[:, :], pq[:, :], bkv[:, 1:2], None, ALU.add))
        for h in range(NF):
            sl = h % 2
            pr = prod[:, sl * QCH:(sl + 1) * QCH]
            ssl = ssb[:, sl * QCH:(sl + 1) * QCH]
            esl = esb[:, sl * QCH:(sl + 1) * QCH]
            pl = psb[:, sl * QCH:(sl + 1) * QCH]
            pss = ps_s[h % 2]
            if h == 0:
                c.emit_wait("vector", qev, (ch + 1) * (NF + 2))
            c.emit_wait("vector", f"{pref}_od", max(0, c.cnt.get(f"{pref}_od", 0) - 1))
            c.emit_inc("vector", vch, 1,
                       lambda e, h=h, pr=pr: e.tensor_tensor(
                           pr, qtl[:, h * QCH:(h + 1) * QCH], ktl[:, :], ALU.mult))
            c.emit_wait("tensor", vch)
            c.emit_wait("tensor", f"{pref}_psev", max(0, c.cnt.get(f"{pref}_psev", 0) - 1))
            c.emit_inc("tensor", smm, 1,
                       lambda e, pr=pr, pss=pss: e.matmul(pss[:, :], onesf[:, :], pr,
                                                          start=True, stop=True))
            c.emit_wait("scalar", smm)
            c.emit_inc("scalar", f"{pref}_psev", 1,
                       lambda e, ssl=ssl, pss=pss: e.activation(
                           ssl, pss[:, :], AF.Copy, scale=float(1.0 / np.sqrt(128.0))))
            c.emit_wait("vector", f"{pref}_psev")
            c.emit_inc("vector", vch, 1,
                       lambda e, ssl=ssl, pl=pl: e.tensor_tensor(pl, ssl, ssl, ALU.subtract))
            c.emit_wait("scalar", vch)
            c.emit_inc("scalar", sch, 1,
                       lambda e, esl=esl, pl=pl: e.activation(esl, pl, AF.Exp))
            c.emit_wait("vector", sch)
            c.emit_inc("vector", vch, 1,
                       lambda e, esl=esl, ssl=ssl: e.reciprocal(ssl, esl))
            c.emit_wait("vector", vch)
            c.emit_inc("vector", vch, 1,
                       lambda e, esl=esl, ssl=ssl, pl=pl: e.tensor_tensor(pl, esl, ssl, ALU.mult))
            c.emit_wait("vector", vch)
            if h == 0:
                c.emit_wait("vector", x1ld, (ch + 1) * 16)
            c.emit_inc("vector", vch, 1,
                       lambda e, pr=pr, pl=pl: e.tensor_tensor(pr, pl, vtl[:, :], ALU.mult))
            c.emit_wait("vector", vch)
            if h == 0 and ch >= 1:
                c.emit_wait("vector", st, ch * 16)
            c.emit_inc("vector", f"{pref}_od", 1,
                       lambda e, h=h, pr=pr: e.tensor_tensor(
                           ot[:, h * QCH:(h + 1) * QCH], pr,
                           x1f[:, h * QCH:(h + 1) * QCH], ALU.add))
        c.emit_wait("sync", f"{pref}_od", (ch + 1) * NF)
        dst = x2a_dst[:, ch * QCH:(ch + 1) * QCH].rearrange("(a p) c -> p a c", p=128)
        c.emit_inc("sync", st, 16,
                   lambda e, d=dst: e.dma_start(
                       out=d, in_=ot[:, :].rearrange("p (a c) -> p a c", a=NF)))
    c.emit_wait("sync", st)


def build_phaseG():
    nc = bass.Bass("TRN2", target_bir_lowering=False, debug=False)
    x1f = nc.dram_tensor("X1F", [E, TOK], F32, kind="ExternalInput").ap()
    ones_in = nc.dram_tensor("ones128", [128, 128], F32, kind="ExternalInput").ap()
    xn1 = nc.dram_tensor("XN1", [E, TOK], BF16, kind="ExternalOutput").ap()
    xn1tm = nc.dram_tensor("XN1TM", [TOK, E], BF16, kind="ExternalOutput").ap()
    with ExitStack() as st:
        c = Ctx(nc, st)
        onesf = c.sbuf("onesf", [128, 128], F32)
        c.emit_inc("sync", "init", 16, lambda e: e.dma_start(out=onesf[:, :], in_=ones_in))
        c.emit_wait("tensor", "init", 16)
        emit_ln1_tm(c, x1f, xn1, xn1tm, onesf)
        c.finalize()
    return nc


def build_phaseH():
    nc = bass.Bass("TRN2", target_bir_lowering=False, debug=False)
    xn1 = nc.dram_tensor("XN1", [E, TOK], BF16, kind="ExternalInput").ap()
    wq = nc.dram_tensor("Wq1T", [E, E], F32, kind="ExternalInput").ap()
    bq = nc.dram_tensor("BQ1", [E, 1], F32, kind="ExternalInput").ap()
    wkv = nc.dram_tensor("WkvT", [E, 256], F32, kind="ExternalInput").ap()
    bkv = nc.dram_tensor("BKV", [256, 1], F32, kind="ExternalInput").ap()
    x1f = nc.dram_tensor("X1F", [E, TOK], F32, kind="ExternalInput").ap()
    ones_in = nc.dram_tensor("ones128", [128, 128], F32, kind="ExternalInput").ap()
    x2a = nc.dram_tensor("X2A", [E, TOK], F32, kind="ExternalOutput").ap()
    with ExitStack() as st:
        c = Ctx(nc, st)
        onesf = c.sbuf("onesf", [128, 128], F32)
        c.emit_inc("sync", "init", 16, lambda e: e.dma_start(out=onesf[:, :], in_=ones_in))
        c.emit_wait("tensor", "init", 16)
        emit_l1attn(c, xn1, wq, bq, wkv, bkv, x1f, x2a, onesf)
        c.finalize()
    return nc



def emit_route(c, xn1_src, gwt_src, ltc_src, agrow_src, iota_src, identf_src,
               onesf, wts_dst, cmbi_dst, bufl, bufm, hist_in, hist_all,
               pref="I"):
    nc = c.nc
    NTT = TOK // 128
    xn = c.sbuf(f"{pref}_xn", [128, 2 * NF * QCH], BF16)
    gwt = c.sbuf(f"{pref}_gw", [128, NF * 16], BF16)
    iota = c.sbuf(f"{pref}_io", [128, 16], F32)
    identf = c.sbuf(f"{pref}_idf", [128, 128], F32)
    agrow = c.sbuf(f"{pref}_ag", [128, NTT], I32)
    sm = c.sbuf(f"{pref}_sm", [128, 16], F32)
    mx = c.sbuf(f"{pref}_mx", [128, 1], F32)
    nm = c.sbuf(f"{pref}_nm", [128, 1], F32)
    topv = c.sbuf(f"{pref}_tv", [128, 8], F32)
    topi = c.sbuf(f"{pref}_ti", [128, 8], U32)
    wsum = c.sbuf(f"{pref}_ws", [128, 1], F32)
    idxall = c.sbuf(f"{pref}_ix", [128, NTT * 4], F32)
    wtsall = c.sbuf(f"{pref}_wt", [128, NTT * 4], F32)
    ohall = c.sbuf(f"{pref}_oh", [128, NTT * 4 * 16], F32)
    hall = c.sbuf(f"{pref}_h", [128, NTT * 16], F32)
    hes = c.sbuf(f"{pref}_he", [128, NTT * 16], F32)
    histr = c.sbuf(f"{pref}_hr", [1, 64], F32)
    hsb = c.sbuf(f"{pref}_hs", [32, 16], F32)
    ltc = c.sbuf(f"{pref}_ltc", [32, 4], F32)
    offr = c.sbuf(f"{pref}_of", [1, 64], F32)
    init = c.sbuf(f"{pref}_in", [16, 4], F32)
    htb = c.sbuf(f"{pref}_ht", [16, QCH], F32)
    zt = c.sbuf(f"{pref}_z", [16, QCH], F32)
    scn = c.sbuf(f"{pref}_sc", [16, QCH + 1], F32)
    t16 = c.sbuf(f"{pref}_t16", [128, 16], F32)
    sl = c.sbuf(f"{pref}_sl", [128, 1], F32)
    dstf = c.sbuf(f"{pref}_df", [128, 1], F32)
    msk = c.sbuf(f"{pref}_mk", [128, 1], F32)
    aux = c.sbuf(f"{pref}_ax", [128, 1], F32)
    dsti = c.sbuf(f"{pref}_di", [128, NTT * 4], I32)
    cmbi = c.sbuf(f"{pref}_ci", [128, NTT * 4], I32)
    ztile = c.sbuf(f"{pref}_zt", [128, 128], I32)
    ps_g = c.psum(f"{pref}_pg", [128, 16], F32)
    ps_x = c.psum(f"{pref}_px", [128, 512], F32)

    ld, xld, gmm, vch, sch = (f"{pref}_ld", f"{pref}_xld", f"{pref}_gmm",
                              f"{pref}_vch", f"{pref}_sch")
    hmm_, agd, tmm, tev = f"{pref}_hmm", f"{pref}_agd", f"{pref}_tmm", f"{pref}_tev"
    scd, std_, zd = f"{pref}_scd", f"{pref}_std", f"{pref}_zd"

    ldg = f"{pref}_ldg"
    for e_i in range(NF):
        c.emit_inc("gpsimd", ldg, 16,
                   lambda e, i=e_i: e.dma_start(
                       out=gwt[:, i * 16:(i + 1) * 16], in_=gwt_src[i * 128:(i + 1) * 128, :]))
    c.emit_inc("sync", ld, 16, lambda e: e.dma_start(out=iota[:, :], in_=iota_src))
    c.emit_inc("sync", ld, 16, lambda e: e.dma_start(out=identf[:, :], in_=identf_src))
    c.emit_inc("sync", ld, 16, lambda e: e.dma_start(out=ltc[:, :], in_=ltc_src))
    c.emit_inc("sync", ld, 16,
               lambda e: e.dma_start(
                   out=agrow[:, :].rearrange("p (a o) -> p a o", o=1),
                   in_=agrow_src.rearrange("(a p) o -> p a o", p=128),
                   allow_slow_non_contiguous=True))
    c.emit_inc("vector", vch, 1, lambda e: e.memset(zt[:, :], 0.0))
    c.emit_inc("vector", vch, 1, lambda e: e.memset(ztile[:, :], 0))

    c.emit_wait("gpsimd", vch, 2)
    c.emit_inc("gpsimd", zd, 16,
               lambda e: e.dma_start(
                   out=bufl[0:TOK * NCORE, :].rearrange("(p a) o -> p a o", p=128),
                   in_=ztile[:, :].rearrange("p (a o) -> p a o", o=1)))
    c.emit_inc("gpsimd", zd, 16,
               lambda e: e.dma_start(
                   out=bufl[TOK * NCORE:TOK * NCORE + 64, :].rearrange(
                       "(o a) b -> o a b", o=1),
                   in_=ztile[0:1, 0:64].rearrange("p (a b) -> p a b", b=1)))

    for tt in range(NTT):
        ch, par = tt // 4, (tt // 4) % 2
        if tt % 4 == 0:
            src = xn1_src[:, ch * QCH:(ch + 1) * QCH].rearrange("(a p) c -> p a c", p=128)
            if ch >= 2:
                c.emit_wait("sync", gmm, (ch - 1) * 4)
            c.emit_inc("sync", f"{xld}{par}", 16,
                       lambda e, s=src, par=par: e.dma_start(
                           out=xn[:, par * NF * QCH:(par + 1) * NF * QCH].rearrange(
                               "p (a c) -> p a c", a=NF), in_=s))
        c.emit_wait("tensor", f"{xld}{par}", (ch // 2 + 1) * 16)
        if tt == 0:
            c.emit_wait("tensor", ld, 4 * 16)
            c.emit_wait("tensor", ldg, NF * 16)
        c.emit_wait("tensor", vch)
        for e_i in range(NF):
            fn = lambda e, e_i=e_i, tt=tt, par=par: e.matmul(
                ps_g[:, :],
                xn[:, (par * NF + e_i) * QCH + (tt % 4) * 128:
                   (par * NF + e_i) * QCH + (tt % 4 + 1) * 128],
                gwt[:, e_i * 16:(e_i + 1) * 16],
                start=(e_i == 0), stop=(e_i == NF - 1))
            if e_i == NF - 1:
                c.emit_inc("tensor", gmm, 1, fn)
            else:
                c.emit("tensor", fn)
        c.emit_wait("vector", gmm)
        c.emit_inc("vector", vch, 1, lambda e: e.reduce_max(mx[:, :], ps_g[:, :], axis=AX.X))
        c.emit_wait("vector", vch)
        c.emit_inc("vector", vch, 1, lambda e: e.tensor_scalar(nm[:, :], mx[:, :], -1.0, None, ALU.mult))
        c.emit_wait("scalar", vch)
        c.emit_inc("scalar", sch, 1,
                   lambda e: e.activation(sm[:, :], ps_g[:, :], AF.Exp, bias=nm[:, 0:1]))
        c.emit_wait("vector", sch)
        c.emit_inc("vector", vch, 1, lambda e: e.reduce_sum(mx[:, :], sm[:, :], axis=AX.X))
        c.emit_wait("vector", vch)
        c.emit_inc("vector", vch, 1, lambda e: e.reciprocal(nm[:, :], mx[:, :]))
        c.emit_wait("vector", vch)
        c.emit_inc("vector", vch, 1,
                   lambda e: e.tensor_scalar(sm[:, :], sm[:, :], nm[:, 0:1], None, ALU.mult))
        c.emit_wait("vector", vch)
        c.emit_inc("vector", vch, 1, lambda e: e.max(topv[:, :], sm[:, :]))
        c.emit_wait("vector", vch)
        c.emit_inc("vector", vch, 1, lambda e: e.max_index(topi[:, :], topv[:, :], sm[:, :]))
        c.emit_wait("vector", vch)
        c.emit_inc("vector", vch, 1,
                   lambda e: e.reduce_sum(wsum[:, :], topv[:, 0:4], axis=AX.X))
        c.emit_wait("vector", vch)
        c.emit_inc("vector", vch, 1,
                   lambda e: e.tensor_scalar(wsum[:, :], wsum[:, :], 1e-20, None, ALU.add))
        c.emit_wait("vector", vch)
        c.emit_inc("vector", vch, 1, lambda e: e.reciprocal(nm[:, :], wsum[:, :]))
        c.emit_wait("vector", vch)
        c.emit_inc("vector", vch, 1,
                   lambda e, tt=tt: e.tensor_scalar(
                       wtsall[:, tt * 4:(tt + 1) * 4], topv[:, 0:4], nm[:, 0:1], None, ALU.mult))
        c.emit_inc("vector", vch, 1,
                   lambda e, tt=tt: e.tensor_copy(idxall[:, tt * 4:(tt + 1) * 4], topi[:, 0:4]))
        c.emit_wait("vector", vch)
        for k in range(4):
            c.emit_inc("vector", vch, 1,
                       lambda e, tt=tt, k=k: e.tensor_scalar(
                           ohall[:, (tt * 4 + k) * 16:(tt * 4 + k + 1) * 16],
                           iota[:, :], idxall[:, tt * 4 + k:tt * 4 + k + 1], None, ALU.is_equal))
        c.emit_wait("vector", vch)
        c.emit_inc("vector", vch, 1,
                   lambda e, tt=tt: e.tensor_tensor(
                       hall[:, tt * 16:(tt + 1) * 16],
                       ohall[:, (tt * 4) * 16:(tt * 4 + 1) * 16],
                       ohall[:, (tt * 4 + 1) * 16:(tt * 4 + 2) * 16], ALU.add))
        c.emit_wait("vector", vch)
        c.emit_inc("vector", vch, 1,
                   lambda e, tt=tt: e.tensor_tensor(
                       hall[:, tt * 16:(tt + 1) * 16], hall[:, tt * 16:(tt + 1) * 16],
                       ohall[:, (tt * 4 + 2) * 16:(tt * 4 + 3) * 16], ALU.add))
        c.emit_wait("vector", vch)
        c.emit_inc("vector", vch, 1,
                   lambda e, tt=tt: e.tensor_tensor(
                       hall[:, tt * 16:(tt + 1) * 16], hall[:, tt * 16:(tt + 1) * 16],
                       ohall[:, (tt * 4 + 3) * 16:(tt * 4 + 4) * 16], ALU.add))
        c.emit_wait("tensor", vch)
        if tt % 4 == 0:
            c.emit_wait("tensor", tev)
        fnh = lambda e, tt=tt: e.matmul(
            ps_x[0:1, 0:16], onesf[:, 0:1], hall[:, tt * 16:(tt + 1) * 16],
            start=(tt % 4 == 0), stop=(tt % 4 == 3))
        c.emit_inc("tensor", hmm_, 1, fnh)
        if tt % 4 == 3:
            c.emit_wait("vector", hmm_)
            c.emit_inc("vector", tev, 1,
                       lambda e, ch=ch: e.tensor_copy(
                           histr[0:1, ch * 16:(ch + 1) * 16], ps_x[0:1, 0:16]))

    c.emit_wait("sync", tev)
    c.emit_inc("sync", f"{pref}_hst", 16,
               lambda e: e.dma_start(
                   out=hist_in.rearrange("(o a) b -> o a b", o=1),
                   in_=histr[0:1, :].rearrange("p (a b) -> p a b", a=4)))
    c.emit_wait("gpsimd", f"{pref}_hst", 16)
    c.emit_inc("gpsimd", agd, 1,
               lambda e: e.collective_compute(
                   "AllGather", ALU.bypass, replica_groups=[list(range(8))],
                   ins=[hist_in.opt()], outs=[hist_all.opt()]))
    c.emit_wait("sync", agd, 1)
    c.emit_inc("sync", f"{pref}_hld", 16, lambda e: e.dma_start(out=hsb[:, :], in_=hist_all))

    for mc in range(NCH):
        c.emit_wait("tensor", f"{pref}_hld", 16)
        c.emit_wait("tensor", tev)
        c.emit_inc("tensor", tmm, 1,
                   lambda e, mc=mc: e.matmul(
                       ps_x[0:1, 0:16], ltc[:, mc:mc + 1], hsb[:, :], start=True, stop=True))
        c.emit_wait("vector", tmm)
        c.emit_inc("vector", tev, 1,
                   lambda e, mc=mc: e.tensor_copy(offr[0:1, mc * 16:(mc + 1) * 16],
                                                  ps_x[0:1, 0:16]))
    for mc in range(NCH):
        c.emit_wait("tensor", tev)
        c.emit_inc("tensor", tmm, 1,
                   lambda e, mc=mc: e.matmul(
                       ps_x[0:16, 0:1], offr[0:1, mc * 16:(mc + 1) * 16],
                       onesf[0:1, 0:1], start=True, stop=True))
        c.emit_wait("vector", tmm)
        c.emit_inc("vector", tev, 1,
                   lambda e, mc=mc: e.tensor_copy(init[:, mc:mc + 1], ps_x[0:16, 0:1]))

    for mc in range(NCH):
        for q4 in range(4):
            tt = mc * 4 + q4
            c.emit_wait("tensor", tev)
            c.emit_inc("tensor", tmm, 1,
                       lambda e, tt=tt: e.transpose(
                           ps_x[0:16, 0:128], hall[:, tt * 16:(tt + 1) * 16], identf[:, :]))
            c.emit_wait("vector", tmm)
            c.emit_inc("vector", tev, 1,
                       lambda e, q4=q4: e.tensor_copy(htb[:, q4 * 128:(q4 + 1) * 128],
                                                      ps_x[0:16, 0:128]))
        c.emit_wait("vector", tev)
        c.emit_inc("vector", vch, 1,
                   lambda e, mc=mc: e.tensor_copy(scn[:, 0:1], init[:, mc:mc + 1]))
        c.emit_wait("vector", vch)
        c.emit_inc("vector", scd, 1,
                   lambda e, mc=mc: e.tensor_tensor_scan(
                       scn[:, 1:QCH + 1], htb[:, :], zt[:, :], init[:, mc:mc + 1],
                       ALU.add, ALU.add))
        for q4 in range(4):
            tt = mc * 4 + q4
            c.emit_wait("tensor", scd)
            c.emit_wait("tensor", tev)
            c.emit_inc("tensor", tmm, 1,
                       lambda e, q4=q4: e.transpose(
                           ps_x[0:128, 0:16], scn[0:16, q4 * 128:(q4 + 1) * 128],
                           identf[0:16, 0:16]))
            c.emit_wait("vector", tmm)
            c.emit_inc("vector", tev, 1,
                       lambda e, tt=tt: e.tensor_copy(hes[:, tt * 16:(tt + 1) * 16],
                                                      ps_x[0:128, 0:16]))

    for tt in range(NTT):
        for k in range(4):
            kk = tt * 4 + k
            c.emit_wait("vector", tev)
            c.emit_wait("vector", vch)
            c.emit_inc("vector", vch, 1,
                       lambda e, tt=tt, kk=kk: e.tensor_tensor(
                           t16[:, :], ohall[:, kk * 16:(kk + 1) * 16],
                           hes[:, tt * 16:(tt + 1) * 16], ALU.mult))
            c.emit_wait("vector", vch)
            c.emit_inc("vector", vch, 1, lambda e: e.reduce_sum(sl[:, :], t16[:, :], axis=AX.X))
            c.emit_wait("vector", vch)
            c.emit_inc("vector", vch, 1,
                       lambda e, kk=kk: e.scalar_tensor_tensor(
                           dstf[:, :], idxall[:, kk:kk + 1], 1024.0, sl[:, :],
                           ALU.mult, ALU.add))
            c.emit_inc("vector", vch, 1,
                       lambda e: e.tensor_scalar(msk[:, :], sl[:, :], 1024.0, None, ALU.is_lt))
            c.emit_wait("vector", vch)
            trash = float(TOK * NCORE + kk)
            c.emit_inc("vector", vch, 1,
                       lambda e, trash=trash: e.tensor_scalar(
                           aux[:, :], dstf[:, :], -trash, None, ALU.add))
            c.emit_wait("vector", vch)
            c.emit_inc("vector", vch, 1,
                       lambda e: e.tensor_tensor(aux[:, :], aux[:, :], msk[:, :], ALU.mult))
            c.emit_wait("vector", vch)
            c.emit_inc("vector", vch, 1,
                       lambda e, trash=trash: e.tensor_scalar(
                           aux[:, :], aux[:, :], trash, None, ALU.add))
            c.emit_wait("vector", vch)
            c.emit_inc("vector", vch, 1,
                       lambda e, kk=kk: e.tensor_copy(dsti[:, kk:kk + 1], aux[:, :]))
            c.emit_wait("vector", vch)
            c.emit_inc("vector", vch, 1,
                       lambda e: e.tensor_tensor(aux[:, :], dstf[:, :], msk[:, :], ALU.mult))
            c.emit_wait("vector", vch)
            c.emit_inc("vector", vch, 1,
                       lambda e, kk=kk: e.tensor_copy(cmbi[:, kk:kk + 1], aux[:, :]))
            c.emit_inc("vector", vch, 1,
                       lambda e, kk=kk: e.tensor_tensor(
                           wtsall[:, kk:kk + 1], wtsall[:, kk:kk + 1], msk[:, :], ALU.mult))

    c.emit_wait("sync", vch)
    c.emit_inc("sync", f"{pref}_wst", 16,
               lambda e: e.dma_start(
                   out=wts_dst.rearrange("(a p) k -> p a k", p=128),
                   in_=wtsall[:, :].rearrange("p (a k) -> p a k", a=NTT)))
    c.emit_inc("sync", f"{pref}_wst", 16,
               lambda e: e.dma_start(
                   out=cmbi_dst.rearrange("(a p) k -> p a k", p=128),
                   in_=cmbi[:, :].rearrange("p (a k) -> p a k", a=NTT)))
    c.emit_wait("gpsimd", vch)
    c.emit_wait("gpsimd", zd, 32)
    c.emit_wait("gpsimd", ld)
    for tt in range(NTT):
        for k in range(4):
            kk = tt * 4 + k
            c.emit_inc("gpsimd", f"{pref}_sct", 16,
                       lambda e, tt=tt, kk=kk: e.indirect_dma_start(
                           bufl, IndirectOffsetOnAxis(dsti[:, kk:kk + 1], 0),
                           agrow[:, tt:tt + 1], None))
    c.emit_wait("gpsimd", f"{pref}_sct", NTT * 4 * 16)
    c.emit_inc("gpsimd", f"{pref}_rs_nowait", 1,
               lambda e: e.collective_compute(
                   "ReduceScatter", ALU.add, replica_groups=[list(range(8))],
                   ins=[bufl[0:TOK * NCORE, :].opt()], outs=[bufm.opt()]))
    c.emit_wait("gpsimd", f"{pref}_wst", 32)


def build_phaseI():
    nc = bass.Bass("TRN2", target_bir_lowering=False, debug=False)
    xn1 = nc.dram_tensor("XN1", [E, TOK], BF16, kind="ExternalInput").ap()
    gwt = nc.dram_tensor("GWT", [E, 16], F32, kind="ExternalInput").ap()
    ltc = nc.dram_tensor("LTC", [32, 4], F32, kind="ExternalInput").ap()
    agrow = nc.dram_tensor("AGROW", [TOK, 1], I32, kind="ExternalInput").ap()
    iota = nc.dram_tensor("IOTA16", [128, 16], F32, kind="ExternalInput").ap()
    identf = nc.dram_tensor("IDENTF", [128, 128], F32, kind="ExternalInput").ap()
    ones_in = nc.dram_tensor("ones128", [128, 128], F32, kind="ExternalInput").ap()
    wts = nc.dram_tensor("WTS", [TOK, 4], F32, kind="ExternalOutput").ap()
    cmbi = nc.dram_tensor("CMBI", [TOK, 4], I32, kind="ExternalOutput").ap()
    bufl = nc.dram_tensor("BUFL", [NCORE * TOK + 64, 1], I32, kind="Internal").ap()
    bufm_i = nc.dram_tensor("BUFM_I", [TOK, 1], I32, kind="Internal").ap()
    bufm = nc.dram_tensor("BUFM", [TOK, 1], I32, kind="ExternalOutput").ap()
    hist_in = nc.dram_tensor("HIST_IN", [4, 16], F32, kind="Internal").ap()
    hist_all = nc.dram_tensor("HIST_ALL", [32, 16], F32, kind="Internal").ap()
    with ExitStack() as st:
        c = Ctx(nc, st)
        onesf = c.sbuf("onesf", [128, 128], F32)
        c.emit_inc("sync", "init", 16, lambda e: e.dma_start(out=onesf[:, :], in_=ones_in))
        c.emit_wait("tensor", "init", 16)
        c.emit_wait("vector", "init", 16)
        emit_route(c, xn1, gwt, ltc, agrow, iota, identf, onesf, wts, cmbi,
                   bufl, bufm_i, hist_in, hist_all)
        c.emit_wait("gpsimd", "I_rs_nowait", 1)
        c.emit_inc("gpsimd", "fcp", 16, lambda e: e.dma_start(out=bufm, in_=bufm_i))
        c.emit_wait("gpsimd", "fcp", 16)
        c.finalize()
    return nc


def host_route_consts(cid):
    def gorder(cc, j):
        return (cc // 2) * 8 + 2 * j + (cc % 2)
    gall = np.array([gorder(cc, j) for cc in range(8) for j in range(4)])
    ltc = np.zeros((32, 4), np.float32)
    for myj in range(4):
        mine = gorder(cid, myj)
        ltc[:, myj] = (gall < mine).astype(np.float32)
    agrow = (cid * TOK + np.arange(TOK)).astype(np.int32).reshape(TOK, 1)
    iota16 = np.tile(np.arange(16, dtype=np.float32), (128, 1))
    identf = np.eye(128, dtype=np.float32)
    return ltc, agrow, iota16, identf



def emit_ag_table(c, src, ag, pad, pref):
    c.emit_inc("gpsimd", f"{pref}_agd", 1,
               lambda e: e.collective_compute(
                   "AllGather", ALU.bypass, replica_groups=[list(range(8))],
                   ins=[src.opt()], outs=[ag.opt()]))


def emit_moe_experts(c, xn1tm_src, bufm_src, wg_src, wu_src, wd_src,
                     xn1ag, xpad, xg, eoutf, eout_dst, pref="J", agpref="JA"):
    NI = MOE_I // 128
    bufm = c.sbuf(f"{pref}_bm", [128, 16], I32)
    gtile = c.sbuf(f"{pref}_gt", [128, E], BF16)
    xgt = c.sbuf(f"{pref}_xg", [128, NF * E], BF16)
    wg = c.sbuf(f"{pref}_wg", [128, 2 * NF * 128], BF16)
    wu = c.sbuf(f"{pref}_wu", [128, 2 * NF * 128], BF16)
    wd = c.sbuf(f"{pref}_wd", [128, 4 * 512], BF16)
    sg = c.sbuf(f"{pref}_sg", [128, 1024], BF16)
    mid = c.sbuf(f"{pref}_md", [128, NI * E], BF16)
    ev = c.sbuf(f"{pref}_ev", [128, 8 * 512], BF16)
    ps = [c.psum(f"{pref}_ps{i}", [128, 512], F32) for i in range(8)]

    agd, bld, gth, xst, tld = (f"{pref}_agd", f"{pref}_bld", f"{pref}_gth",
                               f"{pref}_xst", f"{pref}_tld")
    wld, mmh, sgd, vmd, dld = (f"{pref}_wld", f"{pref}_mmh", f"{pref}_sgd",
                               f"{pref}_vmd", f"{pref}_dld")
    mmo, oev, ost, tst2, est = (f"{pref}_mmo", f"{pref}_oev", f"{pref}_ost",
                                f"{pref}_tst", f"{pref}_est")

    c.emit_wait("sync", "I_rs_nowait", 1)
    c.emit_inc("sync", bld, 16,
               lambda e: e.dma_start(
                   out=bufm[:, :].rearrange("p (a o) -> p a o", o=1),
                   in_=bufm_src.rearrange("(a p) o -> p a o", p=128),
                   allow_slow_non_contiguous=True))
    c.emit_wait("gpsimd", bld, 16)
    c.emit_wait("gpsimd", f"{agpref}_agd", 1)
    for st_i in range(16):
        if st_i >= 1:
            c.emit_wait("gpsimd", xst, st_i * 16)
        c.emit_inc("gpsimd", gth, 16,
                   lambda e, st_i=st_i: e.indirect_dma_start(
                       gtile[:, :], None, xn1ag,
                       IndirectOffsetOnAxis(bufm[:, st_i:st_i + 1], 0)))
        c.emit_wait("gpsimd", gth, (st_i + 1) * 16)
        c.emit_inc("gpsimd", xst, 16,
                   lambda e, st_i=st_i: e.dma_start(
                       out=xg[st_i * 128:(st_i + 1) * 128, :], in_=gtile[:, :]))
    for ft in range(NF):
        c.emit_wait("sync", xst, 16 * 16)
        c.emit_inc("sync", tld, 16,
                   lambda e, ft=ft: e.dma_start(
                       out=xgt[:, ft * E:(ft + 1) * E],
                       in_=xg[:, ft * 128:(ft + 1) * 128], transpose=True))
    for ee in range(2):
        for i_t in range(NI):
            it = ee * NI + i_t
            par = it % 2
            if it >= 2:
                c.emit_wait("gpsimd", mmh, (it - 1) * 2)
            for src, dstb in ((wg_src, wg), (wu_src, wu)):
                ss = src[ee * E:(ee + 1) * E, i_t * 128:(i_t + 1) * 128].rearrange(
                    "(a p) c -> p a c", p=128)
                c.emit_inc("gpsimd", f"{wld}{par}", 16,
                           lambda e, ss=ss, dstb=dstb, par=par: e.dma_start(
                               out=dstb[:, par * NF * 128:(par + 1) * NF * 128]
                               .rearrange("p (a c) -> p a c", a=NF), in_=ss))
            c.emit_wait("tensor", f"{wld}{par}", (it // 2 + 1) * 32)
            if it == 0:
                c.emit_wait("tensor", tld, NF * 16)
            if i_t == 0 and ee == 1:
                c.emit_wait("tensor", oev)
            for n in range(2):
                bb = 2 * n
                c.emit_wait("tensor", sgd, max(0, c.cnt.get(sgd, 0) - 1))
                c.emit_wait("tensor", vmd, max(0, c.cnt.get(vmd, 0) - 1))
                col0 = ee * 1024 + n * 512
                for gi, wsb in ((0, wg), (1, wu)):
                    for e_i in range(NF):
                        fn = lambda e, e_i=e_i, gi=gi, wsb=wsb, par=par, col0=col0, bb=bb: e.matmul(
                            ps[bb + gi][:, :],
                            wsb[:, par * NF * 128 + e_i * 128:par * NF * 128 + (e_i + 1) * 128],
                            xgt[:, e_i * E + col0:e_i * E + col0 + 512],
                            start=(e_i == 0), stop=(e_i == NF - 1))
                        if gi == 1 and e_i == NF - 1:
                            c.emit_inc("tensor", mmh, 1, fn)
                        else:
                            c.emit("tensor", fn)
                c.emit_wait("scalar", mmh)
                c.emit_wait("scalar", vmd, max(0, c.cnt.get(vmd, 0) - 1))
                c.emit_inc("scalar", sgd, 1,
                           lambda e, bb=bb, n=n: e.activation(
                               sg[:, n * 512:(n + 1) * 512], ps[bb][:, :], AF.Silu))
                c.emit_wait("vector", sgd)
                c.emit_inc("vector", vmd, 1,
                           lambda e, i_t=i_t, col0=col0, bb=bb, n=n: e.tensor_tensor(
                               mid[:, i_t * E + col0:i_t * E + col0 + 512],
                               sg[:, n * 512:(n + 1) * 512], ps[bb + 1][:, :], ALU.mult))
        c.emit_wait("tensor", vmd)
        for og in range(4):
            for i_t in range(NI):
                it2 = (ee * 4 + og) * NI + i_t
                par = it2 % 4
                ss = wd_src[ee * MOE_I + i_t * 128:ee * MOE_I + (i_t + 1) * 128,
                            og * 512:(og + 1) * 512]
                if it2 >= 4:
                    c.emit_wait("gpsimd", mmo, (it2 - 3) * 8)
                c.emit_inc("gpsimd", f"{dld}{par}", 16,
                           lambda e, ss=ss, par=par: e.dma_start(
                               out=wd[:, par * 512:(par + 1) * 512], in_=ss))
                c.emit_wait("tensor", f"{dld}{par}", (it2 // 4 + 1) * 16)
                if i_t == 0:
                    c.emit_wait("tensor", oev)
                for o in range(4):
                    for n in range(2):
                        fn = lambda e, o=o, n=n, par=par, i_t=i_t, ee=ee: e.matmul(
                            ps[o * 2 + n][:, :],
                            wd[:, par * 512 + o * 128:par * 512 + (o + 1) * 128],
                            mid[:, i_t * E + ee * 1024 + n * 512:
                                i_t * E + ee * 1024 + (n + 1) * 512],
                            start=(i_t == 0), stop=(i_t == NI - 1))
                        c.emit_inc("tensor", mmo, 1, fn)
            c.emit_wait("vector", mmo)
            c.emit_wait("vector", ost)
            for o in range(4):
                for n in range(2):
                    c.emit_inc("vector", oev, 1,
                               lambda e, o=o, n=n: e.tensor_copy(
                                   ev[:, (o * 2 + n) * 512:(o * 2 + n + 1) * 512],
                                   ps[o * 2 + n][:, :]))
            c.emit_wait("sync", oev)
            for o in range(4):
                dstv = eoutf[(og * 4 + o) * 128:(og * 4 + o + 1) * 128,
                             ee * 1024:(ee + 1) * 1024]
                c.emit_inc("sync", ost, 16,
                           lambda e, d=dstv, o=o: e.dma_start(
                               out=d, in_=ev[:, o * 1024:(o + 1) * 1024]))
    c.emit_wait("sync", ost)
    for st_i in range(16):
        par = st_i % 2
        if st_i >= 2:
            c.emit_wait("sync", f"{est}{par}", (st_i // 2) * 16)
        c.emit_inc("sync", f"{tst2}{par}", 16,
                   lambda e, st_i=st_i, par=par: e.dma_start(
                       out=xgt[:, par * E:(par + 1) * E],
                       in_=eoutf[:, st_i * 128:(st_i + 1) * 128], transpose=True))
        c.emit_wait("sync", f"{tst2}{par}", (st_i // 2 + 1) * 16)
        c.emit_inc("sync", f"{est}{par}", 16,
                   lambda e, st_i=st_i, par=par: e.dma_start(
                       out=eout_dst[st_i * 128:(st_i + 1) * 128, :],
                       in_=xgt[:, par * E:(par + 1) * E]))
    c.emit_wait("sync", f"{est}0")
    c.emit_wait("sync", f"{est}1")


def emit_moe_combine(c, eout_src, cmbi_src, wts_src, eoutag, epad, ytm_dst,
                     pref="K", agpref="KA"):
    cmbi = c.sbuf(f"{pref}_ci", [128, 2 * 4], I32)
    wts = c.sbuf(f"{pref}_wt", [128, 2 * 4], F32)
    rows = [c.sbuf(f"{pref}_r{k}", [128, 2 * E], BF16) for k in range(4)]
    acc = c.sbuf(f"{pref}_ac", [128, E], F32)
    accb = c.sbuf(f"{pref}_ab", [128, E], BF16)

    agd, ild, gth, vch, yst = (f"{pref}_agd", f"{pref}_ild", f"{pref}_gth",
                               f"{pref}_vch", f"{pref}_yst")

    c.emit_wait("gpsimd", f"{agpref}_agd", 1)
    for tt in range(16):
        tp = tt % 2
        if tt >= 2:
            c.emit_wait("sync", vch, (tt - 1) * 4)
        c.emit_inc("sync", f"{ild}{tp}", 16,
                   lambda e, tt=tt, tp=tp: e.dma_start(
                       out=cmbi[:, tp * 4:(tp + 1) * 4],
                       in_=cmbi_src[tt * 128:(tt + 1) * 128, :]))
        c.emit_inc("sync", f"{ild}{tp}", 16,
                   lambda e, tt=tt, tp=tp: e.dma_start(
                       out=wts[:, tp * 4:(tp + 1) * 4],
                       in_=wts_src[tt * 128:(tt + 1) * 128, :]))
        c.emit_wait("gpsimd", f"{ild}{tp}", (tt // 2 + 1) * 32)
        if tt >= 2:
            c.emit_wait("gpsimd", vch, (tt - 1) * 4)
        for k in range(4):
            c.emit_inc("gpsimd", gth, 16,
                       lambda e, k=k, tp=tp: e.indirect_dma_start(
                           rows[k][:, tp * E:(tp + 1) * E], None, eoutag,
                           IndirectOffsetOnAxis(cmbi[:, tp * 4 + k:tp * 4 + k + 1], 0)))
        c.emit_wait("vector", gth, (tt + 1) * 64)
        c.emit_wait("vector", yst)
        c.emit_inc("vector", vch, 1,
                   lambda e, tp=tp: e.tensor_scalar(
                       acc[:, :], rows[0][:, tp * E:(tp + 1) * E],
                       wts[:, tp * 4:tp * 4 + 1], None, ALU.mult))
        for k in range(1, 4):
            c.emit_wait("vector", vch)
            last = (k == 3)
            c.emit_inc("vector", vch, 1,
                       lambda e, k=k, last=last, tp=tp: e.scalar_tensor_tensor(
                           accb[:, :] if last else acc[:, :],
                           rows[k][:, tp * E:(tp + 1) * E],
                           wts[:, tp * 4 + k:tp * 4 + k + 1], acc[:, :],
                           ALU.mult, ALU.add))
        c.emit_wait("sync", vch)
        c.emit_inc("sync", yst, 16,
                   lambda e, tt=tt: e.dma_start(
                       out=ytm_dst[tt * 128:(tt + 1) * 128, :], in_=accb[:, :]))
    c.emit_wait("sync", yst)


def emit_shared_final(c, xn1_src, swg_src, swu_src, swd_src, x2a_src, ytm_src,
                      out_dst, pref="L", xsh_mode=False):
    NI = MOE_IS // 128
    TC = 1024
    xn = c.sbuf(f"{pref}_xn", [128, NF * TC], BF16)
    wg = c.sbuf(f"{pref}_wg", [128, 2 * NF * 128], BF16)
    wu = c.sbuf(f"{pref}_wu", [128, 2 * NF * 128], BF16)
    wd = c.sbuf(f"{pref}_wd", [128, 4 * 512], BF16)
    sg = c.sbuf(f"{pref}_sg", [128, 1024], BF16)
    mid = c.sbuf(f"{pref}_md", [128, NI * TC], BF16)
    x2a = c.sbuf(f"{pref}_x2", [128, 4 * TC], F32)
    ytt = c.sbuf(f"{pref}_yt", [128, 4 * TC], BF16)
    ot = c.sbuf(f"{pref}_o", [128, 4 * TC], F32)
    ps = [c.psum(f"{pref}_ps{i}", [128, 512], F32) for i in range(8)]

    xld, wld, mmh, sgd, vmd = (f"{pref}_xld", f"{pref}_wld", f"{pref}_mmh",
                               f"{pref}_sgd", f"{pref}_vmd")
    dld, mmo, x2ld, ytld, ev, st = (f"{pref}_dld", f"{pref}_mmo", f"{pref}_x2ld",
                                    f"{pref}_ytld", f"{pref}_ev", f"{pref}_st")

    for tc in range(2):
        c.emit_wait("sync", vmd)
        src = xn1_src[:, tc * TC:(tc + 1) * TC].rearrange("(a p) c -> p a c", p=128)
        c.emit_inc("sync", xld, 16,
                   lambda e, s=src: e.dma_start(
                       out=xn[:, :].rearrange("p (a c) -> p a c", a=NF), in_=s))
        for i_t in range(NI):
            par = i_t % 2
            it = tc * NI + i_t
            if it >= 2:
                c.emit_wait("gpsimd", mmh, (it - 1) * 2)
            for src_w, dstb in ((swg_src, wg), (swu_src, wu)):
                ss = src_w[:, i_t * 128:(i_t + 1) * 128].rearrange("(a p) c -> p a c", p=128)
                c.emit_inc("gpsimd", f"{wld}{par}", 16,
                           lambda e, ss=ss, dstb=dstb, par=par: e.dma_start(
                               out=dstb[:, par * NF * 128:(par + 1) * NF * 128]
                               .rearrange("p (a c) -> p a c", a=NF), in_=ss))
            c.emit_wait("tensor", f"{wld}{par}", (it // 2 + 1) * 32)
            c.emit_wait("tensor", xld, (tc + 1) * 16)
            if tc > 0 and i_t == 0:
                c.emit_wait("tensor", ev)
            for n in range(2):
                bb = 2 * n
                c.emit_wait("tensor", sgd, max(0, c.cnt.get(sgd, 0) - 1))
                c.emit_wait("tensor", vmd, max(0, c.cnt.get(vmd, 0) - 1))
                for gi, wsb in ((0, wg), (1, wu)):
                    for e_i in range(NF):
                        fn = lambda e, e_i=e_i, gi=gi, wsb=wsb, par=par, n=n, bb=bb: e.matmul(
                            ps[bb + gi][:, :],
                            wsb[:, par * NF * 128 + e_i * 128:par * NF * 128 + (e_i + 1) * 128],
                            xn[:, e_i * TC + n * 512:e_i * TC + (n + 1) * 512],
                            start=(e_i == 0), stop=(e_i == NF - 1))
                        if gi == 1 and e_i == NF - 1:
                            c.emit_inc("tensor", mmh, 1, fn)
                        else:
                            c.emit("tensor", fn)
                c.emit_wait("scalar", mmh)
                c.emit_wait("scalar", vmd, max(0, c.cnt.get(vmd, 0) - 1))
                c.emit_inc("scalar", sgd, 1,
                           lambda e, bb=bb, n=n: e.activation(
                               sg[:, n * 512:(n + 1) * 512], ps[bb][:, :], AF.Silu))
                c.emit_wait("vector", sgd)
                c.emit_inc("vector", vmd, 1,
                           lambda e, i_t=i_t, n=n, bb=bb: e.tensor_tensor(
                               mid[:, i_t * TC + n * 512:i_t * TC + (n + 1) * 512],
                               sg[:, n * 512:(n + 1) * 512], ps[bb + 1][:, :], ALU.mult))
        c.emit_wait("tensor", vmd)
        for grp in range(4):
            gg = tc * 4 + grp
            c.emit_wait("sync", ev)
            if not xsh_mode:
                x2s = x2a_src[grp * 512:(grp + 1) * 512, tc * TC:(tc + 1) * TC]
                c.emit_inc("sync", x2ld, 16,
                           lambda e, s=x2s: e.dma_start(
                               out=x2a[:, :].rearrange("p (a c) -> p a c", a=4),
                               in_=s.rearrange("(a p) c -> p a c", p=128)))
                for o in range(4):
                    c.emit_inc("sync", ytld, 16,
                               lambda e, o=o, grp=grp, tc=tc: e.dma_start(
                                   out=ytt[:, o * TC:(o + 1) * TC],
                                   in_=ytm_src[tc * TC:(tc + 1) * TC,
                                               grp * 512 + o * 128:grp * 512 + (o + 1) * 128],
                                   transpose=True))
            for i_t in range(NI):
                it2 = gg * NI + i_t
                par = it2 % 4
                ss = swd_src[i_t * 128:(i_t + 1) * 128, grp * 512:(grp + 1) * 512]
                if it2 >= 4:
                    c.emit_wait("gpsimd", mmo, (it2 - 3) * 8)
                c.emit_inc("gpsimd", f"{dld}{par}", 16,
                           lambda e, ss=ss, par=par: e.dma_start(
                               out=wd[:, par * 512:(par + 1) * 512], in_=ss))
                c.emit_wait("tensor", f"{dld}{par}", (it2 // 4 + 1) * 16)
                if i_t == 0:
                    c.emit_wait("tensor", ev)
                for o in range(4):
                    for n in range(2):
                        fn = lambda e, o=o, n=n, par=par, i_t=i_t: e.matmul(
                            ps[o * 2 + n][:, :],
                            wd[:, par * 512 + o * 128:par * 512 + (o + 1) * 128],
                            mid[:, i_t * TC + n * 512:i_t * TC + (n + 1) * 512],
                            start=(i_t == 0), stop=(i_t == NI - 1))
                        c.emit_inc("tensor", mmo, 1, fn)
            c.emit_wait("vector", mmo)
            if not xsh_mode:
                c.emit_wait("vector", x2ld, (gg + 1) * 16)
                c.emit_wait("vector", ytld, (gg + 1) * 64)
            c.emit_wait("vector", st)
            for o in range(4):
                for n in range(2):
                    c.emit_wait("vector", ev)
                    if xsh_mode:
                        c.emit_inc("vector", ev, 1,
                                   lambda e, o=o, n=n: e.tensor_copy(
                                       ot[:, o * TC + n * 512:o * TC + (n + 1) * 512],
                                       ps[o * 2 + n][:, :]))
                        continue
                    c.emit_inc("vector", ev, 1,
                               lambda e, o=o, n=n: e.tensor_tensor(
                                   ot[:, o * TC + n * 512:o * TC + (n + 1) * 512],
                                   ps[o * 2 + n][:, :],
                                   x2a[:, o * TC + n * 512:o * TC + (n + 1) * 512],
                                   ALU.add))
                    c.emit_wait("vector", ev)
                    c.emit_inc("vector", ev, 1,
                               lambda e, o=o, n=n: e.tensor_tensor(
                                   ot[:, o * TC + n * 512:o * TC + (n + 1) * 512],
                                   ot[:, o * TC + n * 512:o * TC + (n + 1) * 512],
                                   ytt[:, o * TC + n * 512:o * TC + (n + 1) * 512],
                                   ALU.add))
            c.emit_wait("sync", ev)
            dstv = out_dst[grp * 512:(grp + 1) * 512, tc * TC:(tc + 1) * TC]
            c.emit_inc("sync", st, 16,
                       lambda e, d=dstv: e.dma_start(
                           out=d.rearrange("(a p) c -> p a c", p=128),
                           in_=ot[:, :].rearrange("p (a c) -> p a c", a=4)))
    c.emit_wait("sync", st)






def _core_cols(b_idx, role):
    cols = []
    for j in range(NCH):
        g = 2 * j + role
        s = b_idx * T + g * QCH
        cols.append(np.arange(s, s + QCH))
    return np.concatenate(cols)


def host_prep(inputs):
    x = np.asarray(inputs["L0_attn_x"], dtype=np.float32).reshape(B * T, E)
    per_core = []
    for cid in range(NCORE):
        b_idx, role = cid // 2, cid % 2
        cols = _core_cols(b_idx, role)
        m = {}
        m["xT_own"] = np.ascontiguousarray(x[cols].T)
        m["ones128"] = np.ones((128, 128), np.float32)
        per_core.append(m)
    return per_core


def build_phaseA():
    nc = bass.Bass("TRN2", target_bir_lowering=False, debug=False)
    xT = nc.dram_tensor("xT_own", [E, TOK], F32, kind="ExternalInput").ap()
    ones_in = nc.dram_tensor("ones128", [128, 128], F32, kind="ExternalInput").ap()
    xn0 = nc.dram_tensor("XN0", [E, TOK], BF16, kind="ExternalOutput").ap()
    with ExitStack() as st:
        c = Ctx(nc, st)
        onesf = c.sbuf("onesf", [128, 128], F32)
        c.emit_inc("sync", "init", 16, lambda e: e.dma_start(out=onesf[:, :], in_=ones_in))
        c.emit_wait("tensor", "init", 16)
        emit_ln(c, xT, xn0, "A", ncols=TOK, onesf=onesf)
        c.finalize()
    return nc


_PROGS = {}
FUSED = True


def _prog(name):
    if name not in _PROGS:
        _PROGS[name] = globals()[f"build_phase{name}"]()
    return _PROGS[name]


def kernel(**inputs):
    f32 = np.float32
    x = np.asarray(inputs["L0_attn_x"], dtype=f32).reshape(B * T, E)
    WqT = np.ascontiguousarray(np.asarray(inputs["L0_attn_Wq"], f32).T)
    KVW = np.ascontiguousarray(np.concatenate(
        [np.asarray(inputs["L0_attn_Wk"], f32).T,
         np.asarray(inputs["L0_attn_Wv"], f32).T], axis=1))
    WoT = np.ascontiguousarray(np.asarray(inputs["L0_attn_Wo"], f32).T)
    LAMS = np.concatenate([np.asarray(inputs[k], f32).ravel() for k in
                           ("L0_attn_lambda_q1", "L0_attn_lambda_k1",
                            "L0_attn_lambda_q2", "L0_attn_lambda_k2")]).reshape(1, 256)
    LNW = np.asarray(inputs["L0_attn_ln_w"], f32).reshape(128, 1)
    W1T = np.ascontiguousarray(np.asarray(inputs["L0_ffn_w1"], f32).T)
    W2T = np.ascontiguousarray(np.asarray(inputs["L0_ffn_w2"], f32).T)
    W3T = np.ascontiguousarray(np.asarray(inputs["L0_ffn_w3"], f32).T)
    Wq1T = np.ascontiguousarray(np.asarray(inputs["L1_attn_Wq"], f32).T)
    BQ1 = np.asarray(inputs["L1_attn_bq"], f32).reshape(E, 1)
    WkvT = np.ascontiguousarray(np.asarray(inputs["L1_attn_Wkv"], f32).T)
    BKV = np.asarray(inputs["L1_attn_bkv"], f32).reshape(256, 1)
    GWT = np.ascontiguousarray(np.asarray(inputs["L1_ffn_gate_w"], f32).T)
    wgate = np.asarray(inputs["L1_ffn_w_gate"], f32)
    wup = np.asarray(inputs["L1_ffn_w_up"], f32)
    wdn = np.asarray(inputs["L1_ffn_w_down"], f32)
    SWG = np.ascontiguousarray(np.asarray(inputs["L1_ffn_sw_gate"], f32))
    SWU = np.ascontiguousarray(np.asarray(inputs["L1_ffn_sw_up"], f32))
    SWD = np.ascontiguousarray(np.asarray(inputs["L1_ffn_sw_down"], f32))
    ones = np.ones((128, 128), f32)
    identb = np.eye(128, dtype=bf)
    onesb = np.ones((128, 128), bf)
    cores = list(range(NCORE))
    allcols = [_core_cols(cid // 2, cid % 2) for cid in cores]
    consts = [host_route_consts(cid) for cid in cores]

    def run(name, maps):
        return run_bass_kernel_spmd(_prog(name), maps, core_ids=cores).results

    xT = [np.ascontiguousarray(x[allcols[cid]].T) for cid in cores]
    if FUSED:
        if "FUSED" not in _PROGS:
            _PROGS["FUSED"] = build_fused()
        maps = []
        for c in cores:
            maps.append({
                "xT_own": xT[c], "ones128": ones, "WqT": WqT, "KVW": KVW,
                "LAMS": LAMS, "LNW": LNW, "MASK8": host_masks(c % 2),
                "IDENT": identb, "ONESB": onesb, "WoT": WoT,
                "W1T": W1T, "W2T": W2T, "W3T": W3T,
                "Wq1T": Wq1T, "BQ1": BQ1, "WkvT": WkvT, "BKV": BKV,
                "GWT": GWT, "LTC": consts[c][0], "AGROW": consts[c][1],
                "IOTA16": consts[c][2], "IDENTF": consts[c][3],
                "WG_MY": wgate[2 * c:2 * c + 2].reshape(2 * E, MOE_I),
                "WU_MY": wup[2 * c:2 * c + 2].reshape(2 * E, MOE_I),
                "WD_MY": wdn[2 * c:2 * c + 2].reshape(2 * MOE_I, E),
                "SWG": SWG, "SWU": SWU, "SWD": SWD,
            })
        res = run_bass_kernel_spmd(_PROGS["FUSED"], maps, core_ids=cores).results
        out = np.zeros((B * T, E), np.float32)
        for cid in cores:
            out[allcols[cid]] = np.asarray(res[cid]["OUT"]).T
        return out.reshape(B, T, E)

    rA = run("A", [{"xT_own": xT[c], "ones128": ones} for c in cores])
    rB = run("B", [{"XN0": rA[c]["XN0"], "WqT": WqT, "KVW": KVW, "ones128": ones}
                   for c in cores])
    rC = run("C", [{"QT": rB[c]["QT"], "KV_ALL": rB[c]["KV_ALL"],
                    "QMAX": rB[c]["QMAX"], "LAMS": LAMS, "LNW": LNW,
                    "MASK8": host_masks(c % 2), "IDENT": identb, "ONESB": onesb,
                    "ones128": ones} for c in cores])
    rD = run("D", [{"ATTN": rC[c]["ATTN"], "WoT": WoT, "xT_own": xT[c]}
                   for c in cores])
    rE = run("E", [{"W1T": W1T, "W2T": W2T, "W3T": W3T, "ones128": ones}
                   for c in cores])
    rF = run("F", [{"XN0": rA[c]["XN0"], "G1": rE[c]["G1"], "G2": rE[c]["G2"],
                    "G3": rE[c]["G3"], "X1": rD[c]["X1"]} for c in cores])
    rG = run("G", [{"X1F": rF[c]["X1F"], "ones128": ones} for c in cores])
    rH = run("H", [{"XN1": rG[c]["XN1"], "Wq1T": Wq1T, "BQ1": BQ1, "WkvT": WkvT,
                    "BKV": BKV, "X1F": rF[c]["X1F"], "ones128": ones}
                   for c in cores])
    rI = run("I", [{"XN1": rG[c]["XN1"], "GWT": GWT, "LTC": consts[c][0],
                    "AGROW": consts[c][1], "IOTA16": consts[c][2],
                    "IDENTF": consts[c][3], "ones128": ones} for c in cores])
    rJ = run("J", [{"XN1TM": rG[c]["XN1TM"], "BUFM": rI[c]["BUFM"],
                    "WG_MY": wgate[2 * c:2 * c + 2].reshape(2 * E, MOE_I),
                    "WU_MY": wup[2 * c:2 * c + 2].reshape(2 * E, MOE_I),
                    "WD_MY": wdn[2 * c:2 * c + 2].reshape(2 * MOE_I, E)}
                   for c in cores])
    rK = run("K", [{"EOUT": rJ[c]["EOUT"], "CMBI": rI[c]["CMBI"],
                    "WTS": rI[c]["WTS"]} for c in cores])
    rL = run("L", [{"XN1": rG[c]["XN1"], "SWG": SWG, "SWU": SWU, "SWD": SWD,
                    "X2A": rH[c]["X2A"], "YTM": rK[c]["YTM"]} for c in cores])
    out = np.zeros((B * T, E), np.float32)
    for cid in cores:
        out[allcols[cid]] = np.asarray(rL[cid]["OUT"]).T
    return out.reshape(B, T, E)


if __name__ == "__main__":
    pass


def build_phaseD():
    nc = bass.Bass("TRN2", target_bir_lowering=False, debug=False)
    attn = nc.dram_tensor("ATTN", [E, TOK], BF16, kind="ExternalInput").ap()
    wo = nc.dram_tensor("WoT", [E, E], F32, kind="ExternalInput").ap()
    xt = nc.dram_tensor("xT_own", [E, TOK], F32, kind="ExternalInput").ap()
    x1 = nc.dram_tensor("X1", [E, TOK], F32, kind="ExternalOutput").ap()
    with ExitStack() as st:
        c = Ctx(nc, st)
        emit_wo(c, attn, wo, xt, x1)
        c.finalize()
    return nc


def build_phaseE():
    nc = bass.Bass("TRN2", target_bir_lowering=False, debug=False)
    w1 = nc.dram_tensor("W1T", [E, FFN_H], F32, kind="ExternalInput").ap()
    w2 = nc.dram_tensor("W2T", [E, FFN_H], F32, kind="ExternalInput").ap()
    w3 = nc.dram_tensor("W3T", [FFN_H, E], F32, kind="ExternalInput").ap()
    ones_in = nc.dram_tensor("ones128", [128, 128], F32, kind="ExternalInput").ap()
    g1 = nc.dram_tensor("G1", [E, FFN_H], BF16, kind="ExternalOutput").ap()
    g2 = nc.dram_tensor("G2", [E, FFN_H], BF16, kind="ExternalOutput").ap()
    g3 = nc.dram_tensor("G3", [FFN_H, E], BF16, kind="ExternalOutput").ap()
    with ExitStack() as st:
        c = Ctx(nc, st)
        onesf = c.sbuf("onesf", [128, 128], F32)
        c.emit_inc("sync", "init", 16, lambda e: e.dma_start(out=onesf[:, :], in_=ones_in))
        c.emit_wait("tensor", "init", 16)
        sh = quant_bufs(c, FFN_H)
        for i, (w, g, R, CC) in enumerate((
                (w1, g1, E, FFN_H), (w2, g2, E, FFN_H), (w3, g3, FFN_H, E))):
            if i > 0:
                for eng in ("sync", "scalar", "vector", "tensor"):
                    c.emit_wait(eng, f"E{i}_st0")
                    c.emit_wait(eng, f"E{i}_st1")
            emit_quant(c, w, g, R, CC, onesf, f"E{i + 1}", sh)
        c.finalize()
    return nc


def build_phaseF():
    nc = bass.Bass("TRN2", target_bir_lowering=False, debug=False)
    xn = nc.dram_tensor("XN0", [E, TOK], BF16, kind="ExternalInput").ap()
    g1 = nc.dram_tensor("G1", [E, FFN_H], BF16, kind="ExternalInput").ap()
    g2 = nc.dram_tensor("G2", [E, FFN_H], BF16, kind="ExternalInput").ap()
    g3 = nc.dram_tensor("G3", [FFN_H, E], BF16, kind="ExternalInput").ap()
    x1 = nc.dram_tensor("X1", [E, TOK], F32, kind="ExternalInput").ap()
    x1f = nc.dram_tensor("X1F", [E, TOK], F32, kind="ExternalOutput").ap()
    with ExitStack() as st:
        c = Ctx(nc, st)
        emit_ffn(c, xn, g1, g2, g3, x1, x1f)
        c.finalize()
    return nc


def build_phaseJ():
    nc = bass.Bass("TRN2", target_bir_lowering=False, debug=False)
    xn1tm = nc.dram_tensor("XN1TM", [TOK, E], BF16, kind="ExternalInput").ap()
    bufm = nc.dram_tensor("BUFM", [TOK, 1], I32, kind="ExternalInput").ap()
    wg = nc.dram_tensor("WG_MY", [2 * E, MOE_I], F32, kind="ExternalInput").ap()
    wu = nc.dram_tensor("WU_MY", [2 * E, MOE_I], F32, kind="ExternalInput").ap()
    wd = nc.dram_tensor("WD_MY", [2 * MOE_I, E], F32, kind="ExternalInput").ap()
    xn1ag = nc.dram_tensor("XN1AG", [NCORE * TOK, E], BF16, kind="Internal").ap()
    xpad = nc.dram_tensor("XPAD", [NCORE * TOK + 1, E], BF16, kind="Internal").ap()
    xg = nc.dram_tensor("XG", [TOK, E], BF16, kind="Internal").ap()
    eoutf = nc.dram_tensor("EOUTF", [E, TOK], BF16, kind="Internal").ap()
    eout = nc.dram_tensor("EOUT", [TOK, E], BF16, kind="ExternalOutput").ap()
    xn1tm_b = nc.dram_tensor("XN1TM_B", [TOK, E], BF16, kind="Internal").ap()
    with ExitStack() as st:
        c = Ctx(nc, st)
        c.emit_inc("gpsimd", "J_bnc", 16, lambda e: e.dma_start(out=xn1tm_b, in_=xn1tm))
        c.emit_wait("gpsimd", "J_bnc", 16)
        emit_ag_table(c, xn1tm_b, xn1ag, xpad, "JA")
        emit_moe_experts(c, xn1tm_b, bufm, wg, wu, wd, xn1ag, xpad, xg, eoutf, eout)
        c.finalize()
    return nc


def build_phaseK():
    nc = bass.Bass("TRN2", target_bir_lowering=False, debug=False)
    eout = nc.dram_tensor("EOUT", [TOK, E], BF16, kind="ExternalInput").ap()
    cmbi = nc.dram_tensor("CMBI", [TOK, 4], I32, kind="ExternalInput").ap()
    wts = nc.dram_tensor("WTS", [TOK, 4], F32, kind="ExternalInput").ap()
    eoutag = nc.dram_tensor("EOUTAG", [NCORE * TOK, E], BF16, kind="Internal").ap()
    epad = nc.dram_tensor("EPAD", [NCORE * TOK + 1, E], BF16, kind="Internal").ap()
    ytm = nc.dram_tensor("YTM", [TOK, E], BF16, kind="ExternalOutput").ap()
    eout_b = nc.dram_tensor("EOUT_B", [TOK, E], BF16, kind="Internal").ap()
    with ExitStack() as st:
        c = Ctx(nc, st)
        c.emit_inc("gpsimd", "K_bnc", 16, lambda e: e.dma_start(out=eout_b, in_=eout))
        c.emit_wait("gpsimd", "K_bnc", 16)
        emit_ag_table(c, eout_b, eoutag, epad, "KA")
        emit_moe_combine(c, eout_b, cmbi, wts, eoutag, epad, ytm)
        c.finalize()
    return nc


def build_phaseL():
    nc = bass.Bass("TRN2", target_bir_lowering=False, debug=False)
    xn1 = nc.dram_tensor("XN1", [E, TOK], BF16, kind="ExternalInput").ap()
    swg = nc.dram_tensor("SWG", [E, MOE_IS], F32, kind="ExternalInput").ap()
    swu = nc.dram_tensor("SWU", [E, MOE_IS], F32, kind="ExternalInput").ap()
    swd = nc.dram_tensor("SWD", [MOE_IS, E], F32, kind="ExternalInput").ap()
    x2a = nc.dram_tensor("X2A", [E, TOK], F32, kind="ExternalInput").ap()
    ytm = nc.dram_tensor("YTM", [TOK, E], BF16, kind="ExternalInput").ap()
    out = nc.dram_tensor("OUT", [E, TOK], F32, kind="ExternalOutput").ap()
    with ExitStack() as st:
        c = Ctx(nc, st)
        emit_shared_final(c, xn1, swg, swu, swd, x2a, ytm, out)
        c.finalize()
    return nc


def build_fused():
    nc = bass.Bass("TRN2", target_bir_lowering=False, debug=False)
    EI, INT = "ExternalInput", "Internal"
    d = lambda n, shp, dt, kind: nc.dram_tensor(n, shp, dt, kind=kind).ap()
    xT = d("xT_own", [E, TOK], F32, EI)
    ones_in = d("ones128", [128, 128], F32, EI)
    wq = d("WqT", [E, E], F32, EI)
    kvw = d("KVW", [E, 1024], F32, EI)
    lams = d("LAMS", [1, 256], F32, EI)
    lnw = d("LNW", [128, 1], F32, EI)
    mask = d("MASK8", [8 * 128, QCH], BF16, EI)
    identb = d("IDENT", [128, 128], BF16, EI)
    onesb_in = d("ONESB", [128, 128], BF16, EI)
    wo = d("WoT", [E, E], F32, EI)
    w1 = d("W1T", [E, FFN_H], F32, EI)
    w2 = d("W2T", [E, FFN_H], F32, EI)
    w3 = d("W3T", [FFN_H, E], F32, EI)
    wq1 = d("Wq1T", [E, E], F32, EI)
    bq1 = d("BQ1", [E, 1], F32, EI)
    wkv1 = d("WkvT", [E, 256], F32, EI)
    bkv1 = d("BKV", [256, 1], F32, EI)
    gwt = d("GWT", [E, 16], F32, EI)
    ltc = d("LTC", [32, 4], F32, EI)
    agrow = d("AGROW", [TOK, 1], I32, EI)
    iota = d("IOTA16", [128, 16], F32, EI)
    identf = d("IDENTF", [128, 128], F32, EI)
    wgm = d("WG_MY", [2 * E, MOE_I], F32, EI)
    wum = d("WU_MY", [2 * E, MOE_I], F32, EI)
    wdm = d("WD_MY", [2 * MOE_I, E], F32, EI)
    swg = d("SWG", [E, MOE_IS], F32, EI)
    swu = d("SWU", [E, MOE_IS], F32, EI)
    swd = d("SWD", [MOE_IS, E], F32, EI)

    xn0 = d("XN0", [E, TOK], BF16, INT)
    qt = d("QT", [E, TOK], BF16, INT)
    kv_own = d("KV_OWN", [TOK, 1024], BF16, INT)
    kv_all = d("KV_ALL", [2 * TOK, 1024], BF16, INT)
    qmax = d("QMAX", [1, NH], F32, INT)
    attn = d("ATTN", [E, TOK], BF16, INT)
    x1 = d("X1", [E, TOK], F32, INT)
    g1 = d("G1", [E, FFN_H], BF16, INT)
    g2 = d("G2", [E, FFN_H], BF16, INT)
    g3 = d("G3", [FFN_H, E], BF16, INT)
    x1f = d("X1F", [E, TOK], F32, INT)
    xn1 = d("XN1", [E, TOK], BF16, INT)
    xn1tm = d("XN1TM", [TOK, E], BF16, INT)
    x2a = d("X2A", [E, TOK], F32, INT)
    wts = d("WTS", [TOK, 4], F32, INT)
    cmbi = d("CMBI", [TOK, 4], I32, INT)
    bufl = d("BUFL", [NCORE * TOK + 64, 1], I32, INT)
    bufm = d("BUFM", [TOK, 1], I32, INT)
    hist_in = d("HIST_IN", [4, 16], F32, INT)
    hist_all = d("HIST_ALL", [32, 16], F32, INT)
    xn1ag = nc.dram_tensor("XN1AG", [NCORE * TOK, E], BF16, kind=INT, addr_space="Shared").ap()
    xpad = d("XPAD", [NCORE * TOK + 1, E], BF16, INT)
    xg = d("XG", [TOK, E], BF16, INT)
    eoutf = d("EOUTF", [E, TOK], BF16, INT)
    eout = d("EOUT", [TOK, E], BF16, INT)
    eoutag = nc.dram_tensor("EOUTAG", [NCORE * TOK, E], BF16, kind=INT, addr_space="Shared").ap()
    epad = d("EPAD", [NCORE * TOK + 1, E], BF16, INT)
    ytm = d("YTM", [TOK, E], BF16, INT)
    xsh = d("XSH", [E, TOK], F32, INT)
    out = d("OUT", [E, TOK], F32, "ExternalOutput")

    with ExitStack() as gst:
        c = Ctx(nc, gst)
        c.sem_st = gst
        bar = c.get_sem("BAR")
        onesf = c.sbuf("onesf", [128, 128], F32)
        c.emit_inc("sync", "init", 16, lambda e: e.dma_start(out=onesf[:, :], in_=ones_in))
        for eng in Ctx.ENGS:
            c.emit_wait(eng, "init", 16)
        c.phase_names = []

        phases = [
            (lambda: emit_ln(c, xT, xn0, "A", ncols=TOK, onesf=onesf), 1, 1, 0),
            (lambda: emit_qkv(c, xn0, wq, kvw, onesf, qt, kv_own, kv_own,
                              kv_all, qmax), 1, 1, 0),
            (lambda: emit_attn(c, qt, kv_all, qmax, lams, lnw, mask, identb,
                               onesb_in, onesf, attn), 1, 1, 0),
            (lambda: emit_wo(c, attn, wo, xT, x1), 1, 1, 0),
            (lambda: _emit_quant3(c, w1, w2, w3, g1, g2, g3, onesf), 1, 1, 0),
            (lambda: emit_ffn(c, xn0, g1, g2, g3, x1, x1f), 1, 1, 0),
            (lambda: emit_ln1_tm(c, x1f, xn1, xn1tm, onesf), 1, 1, 0),
            (lambda: emit_ag_table(c, xn1tm, xn1ag, xpad, "JA"), 0, 0, 1),
            (lambda: emit_route(c, xn1, gwt, ltc, agrow, iota, identf, onesf,
                                wts, cmbi, bufl, bufm, hist_in, hist_all), 1, 1, 0),
            (lambda: emit_l1attn(c, xn1, wq1, bq1, wkv1, bkv1, x1f, x2a, onesf),
             1, 1, 0),
            (lambda: emit_moe_experts(c, xn1tm, bufm, wgm, wum, wdm, xn1ag,
                                      xpad, xg, eoutf, eout), 1, 1, 0),
            (lambda: emit_ag_table(c, eout, eoutag, epad, "KA"), 0, 0, 1),
            (lambda: emit_shared_final(c, xn1, swg, swu, swd, None, None, xsh,
                                       xsh_mode=True), 1, 1, 0),
            (lambda: emit_moe_combine(c, eout, cmbi, wts, eoutag, epad, ytm),
             1, 1, 0),
            (lambda: emit_final(c, x2a, ytm, xsh, out), 1, 1, 0),
        ]
        nbar = 0
        for ph, dobar, doretire, gsb in phases:
            with ExitStack() as pst:
                c.st = gst if gsb else pst
                ph()
                if dobar:
                    for name in list(c.phase_names):
                        if name.endswith("_nowait"):
                            continue
                        c.emit_wait("sync", name)
                    nbar += 1
                    for eng in Ctx.ENGS:
                        c.plan_inc("BAR", 1)
                        c.q[eng].append(lambda e, s=bar: e.nop().then_inc(s, 1))
                    for eng in Ctx.ENGS:
                        c.emit_wait(eng, "BAR", nbar * 5)
            if doretire:
                c.retire_phase()
            else:
                c.phase_names = []
        c.finalize()
    return nc


def _emit_quant3(c, w1, w2, w3, g1, g2, g3, onesf):
    sh = quant_bufs(c, FFN_H)
    for i, (w, g, R, CC) in enumerate(((w1, g1, E, FFN_H), (w2, g2, E, FFN_H),
                                       (w3, g3, FFN_H, E))):
        if i > 0:
            for eng in ("sync", "scalar", "vector", "tensor"):
                c.emit_wait(eng, f"E{i}_st0")
                c.emit_wait(eng, f"E{i}_st1")
        emit_quant(c, w, g, R, CC, onesf, f"E{i + 1}", sh)


def emit_final(c, x2a_src, ytm_src, xsh_src, out_dst, pref="M"):
    TC = 1024
    x2a = c.sbuf(f"{pref}_x2", [128, 4 * TC], F32)
    xsh = c.sbuf(f"{pref}_xs", [128, 4 * TC], F32)
    ytt = c.sbuf(f"{pref}_yt", [128, 4 * TC], BF16)
    ot = c.sbuf(f"{pref}_o", [128, 4 * TC], F32)
    ld, ytld, ev, st = f"{pref}_ld", f"{pref}_ytld", f"{pref}_ev", f"{pref}_st"
    for tc in range(2):
        for grp in range(4):
            gg = tc * 4 + grp
            c.emit_wait("sync", ev)
            for src_t, dst_t in ((x2a_src, x2a), (xsh_src, xsh)):
                ss = src_t[grp * 512:(grp + 1) * 512, tc * TC:(tc + 1) * TC]
                c.emit_inc("sync", ld, 16,
                           lambda e, ss=ss, dst_t=dst_t: e.dma_start(
                               out=dst_t[:, :].rearrange("p (a c) -> p a c", a=4),
                               in_=ss.rearrange("(a p) c -> p a c", p=128)))
            for o in range(4):
                c.emit_inc("sync", ytld, 16,
                           lambda e, o=o, grp=grp, tc=tc: e.dma_start(
                               out=ytt[:, o * TC:(o + 1) * TC],
                               in_=ytm_src[tc * TC:(tc + 1) * TC,
                                           grp * 512 + o * 128:grp * 512 + (o + 1) * 128],
                               transpose=True))
            c.emit_wait("vector", ld, (gg + 1) * 32)
            c.emit_wait("vector", ytld, (gg + 1) * 64)
            c.emit_wait("vector", st)
            for q in range(8):
                c.emit_wait("vector", ev)
                c.emit_inc("vector", ev, 1,
                           lambda e, q=q: e.tensor_tensor(
                               ot[:, q * 512:(q + 1) * 512], x2a[:, q * 512:(q + 1) * 512],
                               xsh[:, q * 512:(q + 1) * 512], ALU.add))
                c.emit_wait("vector", ev)
                c.emit_inc("vector", ev, 1,
                           lambda e, q=q: e.tensor_tensor(
                               ot[:, q * 512:(q + 1) * 512], ot[:, q * 512:(q + 1) * 512],
                               ytt[:, q * 512:(q + 1) * 512], ALU.add))
            c.emit_wait("sync", ev)
            dstv = out_dst[grp * 512:(grp + 1) * 512, tc * TC:(tc + 1) * TC]
            c.emit_inc("sync", st, 16,
                       lambda e, d=dstv: e.dma_start(
                           out=d.rearrange("(a p) c -> p a c", p=128),
                           in_=ot[:, :].rearrange("p (a c) -> p a c", a=4)))
    c.emit_wait("sync", st)


# version 13
# speedup vs baseline: 1.0105x; 1.0018x over previous
import numpy as np
import ml_dtypes
from contextlib import ExitStack

import concourse.bass as bass
import concourse.mybir as mybir
from concourse.bass import IndirectOffsetOnAxis
from concourse.bass_utils import run_bass_kernel_spmd

F32 = mybir.dt.float32
BF16 = mybir.dt.bfloat16
I32 = mybir.dt.int32
U32 = mybir.dt.uint32
AF = mybir.ActivationFunctionType
ALU = mybir.AluOpType
AX = mybir.AxisListType

B, T, E = 4, 4096, 2048
TOK = 2048
NCORE = 8
NF = 16
QCH = 512
NCH = TOK // QCH
HD = 64
NH = 32
KVD = 512
FFN_H = 5632
LAM_INIT = 0.8 - 0.6 * np.exp(-0.3 * 12)
MOE_E, MOE_K, MOE_I, MOE_CAP = 16, 4, 1408, 1024
MOE_IS = 2816

bf = ml_dtypes.bfloat16


class Ctx:
    ENGS = ("sync", "scalar", "vector", "tensor", "gpsimd")

    def __init__(self, nc, st):
        self.nc = nc
        self.st = st
        self.sem_st = st
        self.q = {e: [] for e in self.ENGS}
        self.entry = {}
        self.cnt = {}
        self.phase_names = []
        self.phys_free = []
        self._nsem = 0

    def _entry(self, name):
        if name not in self.entry:
            if self.phys_free:
                h, base = self.phys_free.pop()
            else:
                self._nsem += 1
                h = self.sem_st.enter_context(self.nc.semaphore(f"S{self._nsem}"))
                base = 0
            self.entry[name] = (h, base)
            self.cnt[name] = 0
            self.phase_names.append(name)
        return self.entry[name]

    def get_sem(self, name):
        return self._entry(name)[0]

    def plan_inc(self, name, by):
        h, base = self._entry(name)
        self.cnt[name] += by
        return h, self.cnt[name]

    def retire_phase(self):
        for n in self.phase_names:
            if n.endswith("_nowait"):
                continue
            h, base = self.entry.pop(n)
            self.phys_free.append((h, base + self.cnt.pop(n)))
        self.phase_names = []

    def emit(self, eng, fn):
        self.q[eng].append(fn)

    def emit_inc(self, eng, name, by, fn):
        h, _ = self.plan_inc(name, by)
        self.q[eng].append(lambda e, fn=fn, s=h, by=by: fn(e).then_inc(s, by))

    def emit_wait(self, eng, name, val=None):
        h, base = self._entry(name)
        v = base + (self.cnt[name] if val is None else val)
        self.q[eng].append(lambda e, s=h, v=v: e.wait_ge(s, v))

    def sbuf(self, name, shape, dtype):
        return self.st.enter_context(self.nc.sbuf_tensor(name, shape, dtype))

    def psum(self, name, shape, dtype=F32):
        return self.st.enter_context(self.nc.psum_tensor(name, shape, dtype))

    def finalize(self):
        nc = self.nc
        with nc.Block() as block:
            @block.sync
            def _(e):
                for f in self.q["sync"]:
                    f(e)

            @block.scalar
            def _(e):
                for f in self.q["scalar"]:
                    f(e)

            @block.vector
            def _(e):
                for f in self.q["vector"]:
                    f(e)

            @block.tensor
            def _(e):
                for f in self.q["tensor"]:
                    f(e)

            @block.gpsimd
            def _(e):
                for f in self.q["gpsimd"]:
                    f(e)


def emit_ln(c, xsrc, xn_dst, pref, ncols=TOK, src_f32=True, onesf=None):
    nc = c.nc
    nch = ncols // QCH
    xt = c.sbuf(f"{pref}_x", [128, 2 * NF * QCH], F32)
    sq = c.sbuf(f"{pref}_sq", [128, QCH], F32)
    at = c.sbuf(f"{pref}_at", [128, QCH], F32)
    mtile = c.sbuf(f"{pref}_m", [128, QCH], F32)
    rtile = c.sbuf(f"{pref}_r", [128, QCH], F32)
    vtile = c.sbuf(f"{pref}_v", [128, QCH], F32)
    ttile = c.sbuf(f"{pref}_t", [128, QCH], F32)
    xnout = c.sbuf(f"{pref}_o", [128, NF * QCH], BF16)
    ps_s = c.psum(f"{pref}_ps_s", [128, QCH], F32)
    ps_q = c.psum(f"{pref}_ps_q", [128, QCH], F32)

    ld = f"{pref}_ld"
    sqd = f"{pref}_sqd"
    mmS = f"{pref}_mmS"
    vch = f"{pref}_vch"
    sch = f"{pref}_sch"
    odone = f"{pref}_od"
    ostored = f"{pref}_os"

    for ch in range(nch):
        buf = ch % 2
        xv = lambda f, b=buf: xt[:, (b * NF + f) * QCH:(b * NF + f + 1) * QCH]
        if ch >= 2:
            c.emit_wait("sync", f"{ostored}{ch % 2}", (ch // 2) * 16)
        src = xsrc[:, ch * QCH:(ch + 1) * QCH].rearrange("(a p) c -> p a c", p=128)
        c.emit_inc("sync", ld, 16,
                   lambda e, s=src, b=buf: e.dma_start(
                       out=xt[:, b * NF * QCH:(b + 1) * NF * QCH].rearrange(
                           "p (a c) -> p a c", a=NF), in_=s))

        qcons = f"{pref}_qc"
        for f in range(NF):
            c.emit_wait("scalar", ld, (ch + 1) * 16)
            c.emit_wait("scalar", qcons)
            c.emit_inc("scalar", sqd, 1,
                       lambda e, f=f, xv=xv: e.activation(sq[:, :], xv(f), AF.Square))
            c.emit_wait("tensor", sqd)
            c.emit_wait("tensor", ld, (ch + 1) * 16)
            if f == 0 and ch >= 1:
                c.emit_wait("tensor", vch)
            fn_s = lambda e, f=f, xv=xv, first=(f == 0): e.matmul(
                ps_s[:, :], onesf[:, :], xv(f), start=first, stop=(f == NF - 1))
            fn_q = lambda e, f=f, first=(f == 0): e.matmul(
                ps_q[:, :], onesf[:, :], sq[:, :], start=first, stop=(f == NF - 1))
            if f == NF - 1:
                c.emit_inc("tensor", mmS, 1, fn_s)
            else:
                c.emit("tensor", fn_s)
            c.emit_inc("tensor", qcons, 1, fn_q)

        c.emit_wait("vector", mmS, ch + 1)
        c.emit_wait("vector", qcons, 16 * (ch + 1))
        c.emit_inc("vector", vch, 1,
                   lambda e: e.tensor_scalar(mtile[:, :], ps_s[:, :], 1.0 / E, None, ALU.mult))
        c.emit_inc("vector", vch, 1,
                   lambda e: e.tensor_scalar(vtile[:, :], ps_q[:, :], 1.0 / E, None, ALU.mult))
        c.emit_wait("vector", vch)
        c.emit_inc("vector", vch, 1,
                   lambda e: e.tensor_tensor(ttile[:, :], mtile[:, :], mtile[:, :], ALU.mult))
        c.emit_wait("vector", vch)
        c.emit_inc("vector", vch, 1,
                   lambda e: e.scalar_tensor_tensor(
                       vtile[:, :], vtile[:, :], 1e-5, ttile[:, :], ALU.add, ALU.subtract))
        c.emit_wait("scalar", vch)
        c.emit_inc("scalar", sch, 1,
                   lambda e: e.activation(ttile[:, :], vtile[:, :], AF.Sqrt))
        c.emit_wait("vector", sch)
        c.emit_inc("vector", vch, 1,
                   lambda e: e.reciprocal(rtile[:, :], ttile[:, :]))

        for f in range(NF):
            c.emit_wait("vector", vch)
            if f == 0 and ch >= 1:
                c.emit_wait("vector", f"{ostored}{(ch - 1) % 2}", ((ch - 1) // 2 + 1) * 16)
            c.emit_inc("vector", vch, 1,
                       lambda e, f=f, xv=xv: e.tensor_tensor(at[:, :], xv(f), mtile[:, :], ALU.subtract))
            c.emit_wait("vector", vch)
            last = (f == NF - 1)
            c.emit_inc("vector", odone if last else vch, 1,
                       lambda e, f=f: e.tensor_tensor(
                           xnout[:, f * QCH:(f + 1) * QCH], at[:, :], rtile[:, :], ALU.mult))

        c.emit_wait("sync", odone, ch + 1)
        dst = xn_dst[:, ch * QCH:(ch + 1) * QCH].rearrange("(a p) c -> p a c", p=128)
        c.emit_inc("sync", f"{ostored}{ch % 2}", 16,
                   lambda e, d=dst: e.dma_start(
                       out=d, in_=xnout[:, :].rearrange("p (a c) -> p a c", a=NF)))
    c.emit_wait("sync", f"{ostored}0")
    c.emit_wait("sync", f"{ostored}1")


def emit_qkv(c, xn_src, wq_src, kvw_src, onesf, qt_dst, kv_own, kv_all_in,
             kv_all_out, qmax_out, pref="B"):
    nc = c.nc
    xt = c.sbuf(f"{pref}_xn", [128, 2 * NF * QCH], BF16)
    wq = c.sbuf(f"{pref}_wq", [128, NF * E], BF16)
    kvw = c.sbuf(f"{pref}_kvw", [128, NF * 1024], BF16)
    qevac = c.sbuf(f"{pref}_qe", [128, 4 * QCH], BF16)
    kvevac = c.sbuf(f"{pref}_kve", [128, 2 * 1024], BF16)
    sq = c.sbuf(f"{pref}_sq", [128, QCH], F32)
    sq2 = c.sbuf(f"{pref}_sq2", [128, TOK], F32)
    qm = c.sbuf(f"{pref}_qm", [1, NH], F32)
    tmp1 = c.sbuf(f"{pref}_t1", [1, 1], F32)
    ps_q = [c.psum(f"{pref}_psq{i}", [128, QCH], F32) for i in range(4)]
    ps_k = c.psum(f"{pref}_psk", [128, 512], F32)
    ps_v = c.psum(f"{pref}_psv", [128, 512], F32)
    ps_n = ps_q[0][0:1, 0:QCH]

    wld = f"{pref}_wld"
    xld = f"{pref}_xld"
    qmm = f"{pref}_qmm"
    qev = f"{pref}_qev"
    qst = f"{pref}_qst"
    kvmm = f"{pref}_kvmm"
    kvev = f"{pref}_kvev"
    kvst = f"{pref}_kvst"
    sqd = f"{pref}_sqd"
    nmm = f"{pref}_nmm"
    vch = f"{pref}_vch"
    agd = f"{pref}_agd"

    for e_i in range(NF):
        c.emit_inc("gpsimd", wld, 16,
                   lambda e, i=e_i: e.dma_start(
                       out=wq[:, i * E:(i + 1) * E], in_=wq_src[i * 128:(i + 1) * 128, :]))
        c.emit_inc("gpsimd", wld, 16,
                   lambda e, i=e_i: e.dma_start(
                       out=kvw[:, i * 1024:(i + 1) * 1024], in_=kvw_src[i * 128:(i + 1) * 128, :]))
    c.emit_inc("vector", vch, 1, lambda e: e.memset(qm[:, :], 0.0))

    for ch in range(NCH):
        buf = ch % 2
        xv = lambda e_i, t0, n, b=buf: xt[:, (b * NF + e_i) * QCH + t0:(b * NF + e_i) * QCH + t0 + n]
        if ch >= 2:
            for par in range(4):
                c.emit_wait("sync", f"{qst}{par}", (ch - 1) * (NF // 4) * 16)
            for par in range(2):
                c.emit_wait("sync", f"{kvst}{par}", (ch - 1) * (QCH // 256) * 16)
        src = xn_src[:, ch * QCH:(ch + 1) * QCH].rearrange("(a p) c -> p a c", p=128)
        c.emit_inc("sync", xld, 16,
                   lambda e, s=src, b=buf: e.dma_start(
                       out=xt[:, b * NF * QCH:(b + 1) * NF * QCH].rearrange(
                           "p (a c) -> p a c", a=NF), in_=s))
        c.emit_wait("tensor", xld, (ch + 1) * 16)
        if ch == 0:
            c.emit_wait("tensor", wld, 32 * 16)

        for qd in range(NF):
            ti = ch * NF + qd
            pq = ps_q[ti % 4]
            if ti >= 4:
                c.emit_wait("tensor", qev, ti - 3)
            for e_i in range(NF):
                fn = lambda e, qd=qd, e_i=e_i, pq=pq, xv=xv: e.matmul(
                    pq[:, :], wq[:, e_i * E + qd * 128:e_i * E + qd * 128 + 128],
                    xv(e_i, 0, QCH), start=(e_i == 0), stop=(e_i == NF - 1))
                if e_i == NF - 1:
                    c.emit_inc("tensor", qmm, 1, fn)
                else:
                    c.emit("tensor", fn)
            c.emit_wait("scalar", qmm, ti + 1)
            qeb = qevac[:, (ti % 4) * QCH:(ti % 4 + 1) * QCH]
            if ti >= 4:
                c.emit_wait("scalar", f"{qst}{ti % 4}", (ti // 4) * 16)
            c.emit_inc("scalar", qev, 1,
                       lambda e, pq=pq, qeb=qeb: e.activation(qeb, pq[:, :], AF.Copy))
            c.emit_wait("sync", qev)
            dst = qt_dst[qd * 128:(qd + 1) * 128, ch * QCH:(ch + 1) * QCH]
            c.emit_inc("sync", f"{qst}{ti % 4}", 16,
                       lambda e, d=dst, qeb=qeb: e.dma_start(out=d, in_=qeb))

        for tt in range(QCH // 128):
            gt = ch * (QCH // 128) + tt
            if gt >= 1:
                c.emit_wait("tensor", kvev, gt)
            for e_i in range(NF):
                c.emit("tensor",
                       lambda e, e_i=e_i, tt=tt, xv=xv: e.matmul(
                           ps_k[:, :], xv(e_i, tt * 128, 128),
                           kvw[:, e_i * 1024:e_i * 1024 + 512],
                           start=(e_i == 0), stop=(e_i == NF - 1)))
                fnv = lambda e, e_i=e_i, tt=tt, xv=xv: e.matmul(
                    ps_v[:, :], xv(e_i, tt * 128, 128),
                    kvw[:, e_i * 1024 + 512:e_i * 1024 + 1024],
                    start=(e_i == 0), stop=(e_i == NF - 1))
                if e_i == NF - 1:
                    c.emit_inc("tensor", kvmm, 1, fnv)
                else:
                    c.emit("tensor", fnv)
            c.emit_wait("vector", kvmm)
            kvb = kvevac[:, (gt % 2) * 1024:(gt % 2 + 1) * 1024]
            if gt >= 2:
                c.emit_wait("vector", f"{kvst}{gt % 2}", (gt // 2) * 16)
            c.emit_inc("vector", vch, 1,
                       lambda e, kvb=kvb: e.tensor_copy(kvb[:, 0:512], ps_k[:, :]))
            c.emit_inc("vector", kvev, 1,
                       lambda e, kvb=kvb: e.tensor_copy(kvb[:, 512:1024], ps_v[:, :]))
            c.emit_wait("sync", kvev)
            dst = kv_own[gt * 128:(gt + 1) * 128, :]
            c.emit_inc("sync", f"{kvst}{gt % 2}", 16,
                       lambda e, d=dst, kvb=kvb: e.dma_start(out=d, in_=kvb))

    qnb = c.sbuf(f"{pref}_qnb", [128, 2 * TOK], BF16)
    c.emit_wait("tensor", qev)
    for qd in range(NF):
        par = qd % 2
        c.emit_wait("sync", f"{qst}{qd % 4}")
        if qd >= 2:
            c.emit_wait("sync", sqd, (qd - 1))
        c.emit_inc("sync", f"{pref}_qnl{par}", 16,
                   lambda e, qd=qd, par=par: e.dma_start(
                       out=qnb[:, par * TOK:(par + 1) * TOK],
                       in_=qt_dst[qd * 128:(qd + 1) * 128, :]))
        c.emit_wait("scalar", f"{pref}_qnl{par}", (qd // 2 + 1) * 16)
        c.emit_wait("scalar", nmm)
        c.emit_inc("scalar", sqd, 1,
                   lambda e, par=par: e.activation(sq2[:, :], qnb[:, par * TOK:(par + 1) * TOK],
                                                   AF.Square))
        c.emit_wait("tensor", sqd)
        for half in range(2):
            for cc in range(TOK // QCH):
                c.emit_wait("tensor", vch)
                c.emit_inc("tensor", nmm, 1,
                           lambda e, half=half, cc=cc: e.matmul(
                               ps_n, onesf[half * 64:half * 64 + 64, 0:1],
                               sq2[half * 64:half * 64 + 64, cc * QCH:(cc + 1) * QCH],
                               start=True, stop=True))
                h_idx = qd * 2 + half
                c.emit_wait("vector", nmm)
                c.emit_inc("vector", vch, 1,
                           lambda e: e.reduce_max(tmp1[:, :], ps_n, axis=AX.X))
                c.emit_wait("vector", vch)
                c.emit_inc("vector", vch, 1,
                           lambda e, h=h_idx: e.tensor_tensor(
                               qm[:, h:h + 1], qm[:, h:h + 1], tmp1[:, :], ALU.max))

    c.emit_wait("sync", vch)
    c.emit_inc("sync", "B_qmst", 16, lambda e: e.dma_start(out=qmax_out, in_=qm[:, :]))
    for par in range(2):
        c.emit_wait("gpsimd", f"{kvst}{par}", NCH * (QCH // 256) * 16)
    rg = [[0, 1], [2, 3], [4, 5], [6, 7]]
    c.emit_inc("gpsimd", agd, 1,
               lambda e: e.collective_compute(
                   "AllGather", ALU.bypass, replica_groups=rg,
                   ins=[kv_all_in.opt()], outs=[kv_all_out.opt()]))
    c.emit_wait("gpsimd", agd, 1)
    c.emit_wait("gpsimd", "B_qmst", 16)


def build_phaseB():
    nc = bass.Bass("TRN2", target_bir_lowering=False, debug=False)
    xn0 = nc.dram_tensor("XN0", [E, TOK], BF16, kind="ExternalInput").ap()
    wq = nc.dram_tensor("WqT", [E, E], F32, kind="ExternalInput").ap()
    kvw = nc.dram_tensor("KVW", [E, 1024], F32, kind="ExternalInput").ap()
    ones_in = nc.dram_tensor("ones128", [128, 128], F32, kind="ExternalInput").ap()
    qt = nc.dram_tensor("QT", [E, TOK], BF16, kind="ExternalOutput").ap()
    kv_own = nc.dram_tensor("KV_OWN", [TOK, 1024], BF16, kind="Internal").ap()
    kv_all_i = nc.dram_tensor("KV_ALL_I", [2 * TOK, 1024], BF16, kind="Internal").ap()
    kv_all = nc.dram_tensor("KV_ALL", [2 * TOK, 1024], BF16, kind="ExternalOutput").ap()
    qmax = nc.dram_tensor("QMAX", [1, NH], F32, kind="ExternalOutput").ap()
    with ExitStack() as st:
        c = Ctx(nc, st)
        onesf = c.sbuf("onesf", [128, 128], F32)
        c.emit_inc("sync", "init", 16, lambda e: e.dma_start(out=onesf[:, :], in_=ones_in))
        c.emit_wait("tensor", "init", 16)
        emit_qkv(c, xn0, wq, kvw, onesf, qt, kv_own, kv_own, kv_all_i, qmax)
        c.emit_inc("gpsimd", "B_cp", 16,
                   lambda e: e.dma_start(out=kv_all, in_=kv_all_i))
        c.emit_wait("gpsimd", "B_cp", 16)
        c.finalize()
    return nc


def _ag_row(t):
    g = t // 4
    return (g % 2) * TOK + (g // 2) * QCH + (t % 4) * 128


def emit_attn(c, qt_src, kv_src, qmax_src, lams_src, lnw_src, mask_src,
              ident_src, onesb_src, onesf, attn_dst, pref="C"):
    nc = c.nc
    NT = 2 * TOK // 128
    kt = c.sbuf(f"{pref}_kt", [128, 9 * 2 * TOK], BF16)
    vt = c.sbuf(f"{pref}_vt", [128, NT * 512], BF16)
    qch = c.sbuf(f"{pref}_q", [128, 2 * NF * QCH], BF16)
    att = c.sbuf(f"{pref}_a", [128, NF * QCH], BF16)
    ksq = c.sbuf(f"{pref}_ksq", [128, 2 * TOK], F32)
    masks = c.sbuf(f"{pref}_mk", [128, 8 * QCH], BF16)
    ident = c.sbuf(f"{pref}_id", [128, 128], BF16)
    onesb = c.sbuf(f"{pref}_ob", [128, 128], BF16)
    qm = c.sbuf(f"{pref}_qm", [1, NH], F32)
    km = c.sbuf(f"{pref}_km", [1, 8], F32)
    bprod = c.sbuf(f"{pref}_bp", [1, NH], F32)
    bh = c.sbuf(f"{pref}_bh", [128, NH], F32)
    lams = c.sbuf(f"{pref}_lam", [1, 256], F32)
    lscr = c.sbuf(f"{pref}_ls", [1, 64], F32)
    lam1 = c.sbuf(f"{pref}_l1", [1, 1], F32)
    lam2 = c.sbuf(f"{pref}_l2", [1, 1], F32)
    nlamb = c.sbuf(f"{pref}_nlb", [128, 1], F32)
    lnw2 = c.sbuf(f"{pref}_lnw", [128, 1], F32)
    epsb = c.sbuf(f"{pref}_eps", [128, 1], F32)
    tmp1 = c.sbuf(f"{pref}_t1", [1, 1], F32)
    expe = c.sbuf(f"{pref}_ee", [128, 2 * QCH], BF16)
    expo = c.sbuf(f"{pref}_eo", [128, 2 * QCH], BF16)
    expem = c.sbuf(f"{pref}_eem", [128, 2 * QCH], BF16)
    expom = c.sbuf(f"{pref}_eom", [128, 2 * QCH], BF16)
    av_e = c.sbuf(f"{pref}_ave", [128, QCH], F32)
    av_o = c.sbuf(f"{pref}_avo", [128, QCH], F32)
    r_e = c.sbuf(f"{pref}_re", [128, QCH], F32)
    r_o = c.sbuf(f"{pref}_ro", [128, QCH], F32)
    o_e = c.sbuf(f"{pref}_oe", [128, QCH], F32)
    o_p = c.sbuf(f"{pref}_op", [128, QCH], F32)
    sqs = c.sbuf(f"{pref}_sqs", [128, QCH], F32)
    stdt = c.sbuf(f"{pref}_std", [128, QCH], F32)
    rr = c.sbuf(f"{pref}_rr", [128, QCH], F32)

    ps_se = [c.psum(f"{pref}_se{i}", [128, QCH], F32) for i in range(2)]
    ps_so = [c.psum(f"{pref}_so{i}", [128, QCH], F32) for i in range(2)]
    ps_n = ps_se[0][0:1, 0:QCH]
    ps_b = ps_se[0][:, 0:1]
    ps_ave = c.psum(f"{pref}_pave", [128, QCH], F32)
    ps_avo = c.psum(f"{pref}_pavo", [128, QCH], F32)
    ps_sme = c.psum(f"{pref}_psme", [128, QCH], F32)
    ps_smo = c.psum(f"{pref}_psmo", [128, QCH], F32)
    ps_ms = ps_se[0]

    ld = f"{pref}_ld"
    kld = f"{pref}_kld"
    tpd = f"{pref}_tpd"
    tpe = f"{pref}_tpe"
    vch = f"{pref}_vch"
    sch = f"{pref}_sch"
    nmm = f"{pref}_nmm"
    qld = f"{pref}_qld"
    ast = f"{pref}_ast"
    smm = f"{pref}_smm"
    exd = f"{pref}_exd"
    mkd = f"{pref}_mkd"
    acc = f"{pref}_acc"
    evd = f"{pref}_evd"
    cmb = f"{pref}_cmb"

    c.emit_inc("sync", ld, 16, lambda e: e.dma_start(out=ident[:, :], in_=ident_src))
    c.emit_inc("sync", ld, 16, lambda e: e.dma_start(out=onesb[:, :], in_=onesb_src))
    c.emit_inc("sync", ld, 16, lambda e: e.dma_start(
        out=masks[:, :].rearrange("p (a c) -> p a c", a=8),
        in_=mask_src.rearrange("(a p) c -> p a c", p=128)))
    c.emit_inc("sync", ld, 16, lambda e: e.dma_start(out=qm[:, :], in_=qmax_src))
    c.emit_inc("sync", ld, 16, lambda e: e.dma_start(out=lams[:, :], in_=lams_src))
    c.emit_inc("sync", ld, 16, lambda e: e.dma_start(out=lnw2[:, :], in_=lnw_src))

    for cb in range(8):
        row = cb * QCH
        g = 2 * (cb % 4) + (cb // 4)
        for c64 in range(8):
            c.emit_inc("sync", tpd, 16,
                       lambda e, row=row, c64=c64, g=g: e.dma_start(
                           out=kt[:, c64 * 2 * TOK + g * 512:c64 * 2 * TOK + (g + 1) * 512],
                           in_=kv_src[row:row + 512, c64 * 64:c64 * 64 + 128],
                           transpose=True))
    c.emit_wait("gpsimd", tpd, 8 * 8 * 16)
    c.emit_inc("gpsimd", f"{pref}_sh", 16,
               lambda e: e.dma_start(
                   out=kt[64:128, 8 * 2 * TOK:9 * 2 * TOK],
                   in_=kt[0:64, 0:2 * TOK]))
    for t in range(NT):
        row = _ag_row(t)
        c.emit_inc("sync", f"{pref}_vld", 16,
                   lambda e, row=row, t=t: e.dma_start(
                       out=vt[:, t * 512:(t + 1) * 512],
                       in_=kv_src[row:row + 128, 512:1024]))

    c.emit_inc("vector", vch, 1, lambda e: e.memset(km[:, :], 0.0))
    c.emit_wait("scalar", tpd, 8 * 8 * 16)
    for kh in range(8):
        c.emit_wait("scalar", nmm)
        c.emit_inc("scalar", sch, 1,
                   lambda e, kh=kh: e.activation(
                       ksq[0:64, :], kt[0:64, kh * 2 * TOK:(kh + 1) * 2 * TOK], AF.Square))
        for cc in range(2 * TOK // QCH):
            c.emit_wait("tensor", sch)
            c.emit_wait("tensor", vch)
            c.emit_inc("tensor", nmm, 1,
                       lambda e, cc=cc: e.matmul(
                           ps_n, onesf[0:64, 0:1],
                           ksq[0:64, cc * QCH:(cc + 1) * QCH],
                           start=True, stop=True))
            c.emit_wait("vector", nmm)
            c.emit_inc("vector", vch, 1,
                       lambda e: e.reduce_max(tmp1[:, :], ps_n, axis=AX.X))
            c.emit_wait("vector", vch)
            c.emit_inc("vector", vch, 1,
                       lambda e, kh=kh: e.tensor_tensor(
                           km[:, kh:kh + 1], km[:, kh:kh + 1], tmp1[:, :], ALU.max))
    c.emit_wait("vector", ld, 6 * 16)
    c.emit_wait("vector", vch)
    for h in range(NH):
        c.emit_inc("vector", vch, 1,
                   lambda e, h=h: e.tensor_tensor(
                       bprod[:, h:h + 1], qm[:, h:h + 1], km[:, h // 4:h // 4 + 1], ALU.mult))
    c.emit_wait("scalar", vch)
    c.emit_inc("scalar", sch, 1, lambda e: e.activation(bprod[:, :], bprod[:, :], AF.Sqrt))
    c.emit_wait("vector", sch)
    c.emit_inc("vector", vch, 1,
               lambda e: e.tensor_scalar(bprod[:, :], bprod[:, :], -0.125, None, ALU.mult))
    for h in range(NH):
        c.emit_wait("tensor", vch)
        c.emit_wait("tensor", f"{pref}_bhe")
        c.emit_inc("tensor", f"{pref}_bhm", 1,
                   lambda e, h=h: e.matmul(
                       ps_b, onesf[0:1, :], bprod[:, h:h + 1], start=True, stop=True))
        c.emit_wait("vector", f"{pref}_bhm")
        c.emit_inc("vector", f"{pref}_bhe", 1,
                   lambda e, h=h: e.tensor_copy(bh[:, h:h + 1], ps_b))
    c.emit_inc("vector", vch, 1,
               lambda e: e.tensor_tensor(lscr[:, :], lams[:, 0:64], lams[:, 64:128], ALU.mult))
    c.emit_wait("vector", vch)
    c.emit_inc("vector", vch, 1, lambda e: e.reduce_sum(lam1[:, :], lscr[:, :], axis=AX.X))
    c.emit_wait("vector", vch)
    c.emit_inc("vector", vch, 1,
               lambda e: e.tensor_tensor(lscr[:, :], lams[:, 128:192], lams[:, 192:256], ALU.mult))
    c.emit_wait("vector", vch)
    c.emit_inc("vector", vch, 1, lambda e: e.reduce_sum(lam2[:, :], lscr[:, :], axis=AX.X))
    c.emit_wait("scalar", vch)
    c.emit_inc("scalar", sch, 1, lambda e: e.activation(lam1[:, :], lam1[:, :], AF.Exp))
    c.emit_inc("scalar", sch, 1, lambda e: e.activation(lam2[:, :], lam2[:, :], AF.Exp))
    c.emit_wait("vector", sch)
    c.emit_inc("vector", vch, 1,
               lambda e: e.scalar_tensor_tensor(
                   tmp1[:, :], lam2[:, :], -1.0, lam1[:, :], ALU.mult, ALU.add))
    c.emit_wait("vector", vch)
    c.emit_inc("vector", vch, 1,
               lambda e: e.tensor_scalar(
                   tmp1[:, :], tmp1[:, :], float(LAM_INIT), -1.0, ALU.add, ALU.mult))
    c.emit_wait("tensor", vch)
    c.emit_inc("tensor", f"{pref}_bhm", 1,
               lambda e: e.matmul(ps_b, onesf[0:1, :], tmp1[:, :], start=True, stop=True))
    c.emit_wait("vector", f"{pref}_bhm")
    c.emit_inc("vector", f"{pref}_bhe", 1, lambda e: e.tensor_copy(nlamb[:, :], ps_b))
    c.emit_inc("vector", vch, 1,
               lambda e: e.tensor_scalar(lnw2[:, :], lnw2[:, :], float(1.0 - LAM_INIT), None, ALU.mult))
    c.emit_inc("vector", vch, 1, lambda e: e.memset(epsb[:, :], 1e-5))

    npair_done = 0
    pending = [None]
    for j in range(NCH):
        par = j % 2
        span = 8 * j + 8
        if j >= 2:
            c.emit_wait("sync", cmb, (j - 1) * 16)
        src = qt_src[:, j * QCH:(j + 1) * QCH].rearrange("(a p) c -> p a c", p=128)
        c.emit_inc("sync", f"{qld}{par}", 16,
                   lambda e, s=src, par=par: e.dma_start(
                       out=qch[:, par * NF * QCH:(par + 1) * NF * QCH].rearrange(
                           "p (a c) -> p a c", a=NF), in_=s))
        for p in range(NF):
            he, kh, vh = 2 * p, p // 2, p // 4
            q_e = qch[0:64, (par * NF + p) * QCH:(par * NF + p + 1) * QCH]
            q_o = qch[64:128, (par * NF + p) * QCH:(par * NF + p + 1) * QCH]
            kte = lambda t, kh=kh: kt[0:64, kh * 2 * TOK + t * 128:kh * 2 * TOK + (t + 1) * 128]
            koff = (kh - 1) if kh >= 1 else 8
            kto = lambda t, koff=koff: kt[64:128, koff * 2 * TOK + t * 128:koff * 2 * TOK + (t + 1) * 128]
            vtile = lambda t, vh=vh: vt[:, t * 512 + vh * 128:t * 512 + (vh + 1) * 128]
            c.emit_wait("tensor", f"{qld}{par}", (j // 2 + 1) * 16)
            if j == 0 and p == 0:
                c.emit_wait("tensor", tpd, 8 * 8 * 16)
                c.emit_wait("tensor", f"{pref}_sh", 16)
                c.emit_wait("tensor", f"{pref}_vld", NT * 16)
                c.emit_wait("tensor", f"{pref}_bhe")
                c.emit_wait("scalar", f"{pref}_bhe")
                c.emit_wait("scalar", vch)

            masked = {}
            def ee_eo(t):
                tp = t % 2
                if t in masked:
                    return (expem[:, tp * QCH:(tp + 1) * QCH],
                            expom[:, tp * QCH:(tp + 1) * QCH])
                return (expe[:, tp * QCH:(tp + 1) * QCH],
                        expo[:, tp * QCH:(tp + 1) * QCH])

            def emit_av(t, pp=p):
                first, last = (t == 0), (t == span - 1)
                eet, eot = ee_eo(t)
                if t in masked:
                    c.emit_wait("tensor", mkd, masked[t])
                else:
                    c.emit_wait("tensor", exd, 2 * (t + 1) + exd_base)
                if first:
                    c.emit_wait("tensor", evd)
                fn1 = lambda e, t=t, eet=eet, vtile=vtile, first=first, last=last: e.matmul(
                    ps_ave[:, :], vtile(t), eet, start=first, stop=last)
                fn2 = lambda e, t=t, eot=eot, vtile=vtile, first=first, last=last: e.matmul(
                    ps_avo[:, :], vtile(t), eot, start=first, stop=last)
                fn3 = lambda e, eet=eet, first=first, last=last: e.matmul(
                    ps_sme[:, :], onesb[:, :], eet, start=first, stop=last)
                fn4 = lambda e, eot=eot, first=first, last=last: e.matmul(
                    ps_smo[:, :], onesb[:, :], eot, start=first, stop=last)
                c.emit("tensor", fn1)
                c.emit("tensor", fn2)
                c.emit("tensor", fn3)
                c.emit_inc("tensor", acc, 1, fn4)

            exd_base = c.cnt.get(exd, 0)
            for t in range(span):
                tp = t % 2
                s_loc = t - (span - 8)
                if t == 6 and pending[0] is not None:
                    pending[0]()
                    pending[0] = None
                c.emit_wait("tensor", exd, max(0, c.cnt.get(exd, 0) - 2))
                if t in (0, 6):
                    c.emit_wait("tensor", f"{pref}_msev")
                c.emit_inc("tensor", smm, 1,
                           lambda e, t=t, q_e=q_e, kte=kte, tp=tp: e.matmul(
                               ps_se[tp][:, :], kte(t), q_e, start=True, stop=True))
                c.emit_inc("tensor", smm, 1,
                           lambda e, t=t, q_o=q_o, kto=kto, tp=tp: e.matmul(
                               ps_so[tp][:, :], kto(t), q_o, start=True, stop=True))
                c.emit_wait("scalar", smm)
                c.emit_wait("scalar", acc)
                c.emit_wait("scalar", mkd)
                c.emit_inc("scalar", exd, 1,
                           lambda e, he=he, tp=tp: e.activation(
                               expe[:, tp * QCH:(tp + 1) * QCH], ps_se[tp][:, :],
                               AF.Exp, bias=bh[:, he:he + 1], scale=0.125))
                c.emit_inc("scalar", exd, 1,
                           lambda e, he=he, tp=tp: e.activation(
                               expo[:, tp * QCH:(tp + 1) * QCH], ps_so[tp][:, :],
                               AF.Exp, bias=bh[:, he + 1:he + 2], scale=0.125))
                if s_loc >= 0:
                    c.emit_wait("vector", exd)
                    c.emit_wait("vector", acc)
                    c.emit_inc("vector", mkd, 1,
                               lambda e, s=s_loc, tp=tp: e.tensor_tensor(
                                   expem[:, tp * QCH:(tp + 1) * QCH],
                                   expe[:, tp * QCH:(tp + 1) * QCH],
                                   masks[:, s * QCH:(s + 1) * QCH], ALU.mult))
                    c.emit_inc("vector", mkd, 1,
                               lambda e, s=s_loc, tp=tp: e.tensor_tensor(
                                   expom[:, tp * QCH:(tp + 1) * QCH],
                                   expo[:, tp * QCH:(tp + 1) * QCH],
                                   masks[:, s * QCH:(s + 1) * QCH], ALU.mult))
                    masked[t] = c.cnt[mkd]
                if t >= 1:
                    emit_av(t - 1)
            emit_av(span - 1)
            npair_done += 1
            c.emit_wait("vector", acc)
            c.emit_inc("vector", evd, 1, lambda e: e.reciprocal(r_e[:, :], ps_sme[:, :]))
            c.emit_inc("vector", evd, 1, lambda e: e.reciprocal(r_o[:, :], ps_smo[:, :]))
            c.emit_wait("scalar", acc)
            c.emit_inc("scalar", evd, 1, lambda e: e.activation(av_e[:, :], ps_ave[:, :], AF.Copy))
            c.emit_inc("scalar", evd, 1, lambda e: e.activation(av_o[:, :], ps_avo[:, :], AF.Copy))
            c.emit_wait("vector", evd)
            c.emit_inc("vector", vch, 1,
                       lambda e: e.tensor_tensor(o_e[:, :], av_e[:, :], r_e[:, :], ALU.mult))
            c.emit_inc("vector", vch, 1,
                       lambda e: e.tensor_tensor(o_p[:, :], av_o[:, :], r_o[:, :], ALU.mult))
            c.emit_wait("vector", vch)
            c.emit_inc("vector", vch, 1,
                       lambda e: e.scalar_tensor_tensor(
                           o_p[:, :], o_p[:, :], nlamb[:, 0:1], o_e[:, :], ALU.mult, ALU.add))
            c.emit_wait("scalar", vch)
            c.emit_inc("scalar", sch, 1, lambda e: e.activation(sqs[:, :], o_p[:, :], AF.Square))

            def part2(p=p, j=j):
                c.emit_wait("tensor", sch)
                c.emit_wait("tensor", exd, max(0, c.cnt.get(exd, 0) - 2))
                c.emit_inc("tensor", f"{pref}_msmm", 1,
                           lambda e: e.matmul(ps_ms[:, :], onesf[:, :], sqs[:, :],
                                              start=True, stop=True))
                c.emit_wait("scalar", f"{pref}_msmm")
                c.emit_inc("scalar", f"{pref}_msev", 1,
                           lambda e: e.activation(stdt[:, :], ps_ms[:, :], AF.Sqrt,
                                                  bias=epsb[:, 0:1], scale=1.0 / 128.0))
                c.emit_wait("vector", f"{pref}_msev")
                c.emit_inc("vector", vch, 1, lambda e: e.reciprocal(rr[:, :], stdt[:, :]))
                c.emit_wait("vector", vch)
                c.emit_inc("vector", vch, 1,
                           lambda e: e.tensor_tensor(o_p[:, :], o_p[:, :], rr[:, :], ALU.mult))
                c.emit_wait("vector", vch)
                if j >= 1 and p == 0:
                    c.emit_wait("vector", ast, j * 16)
                c.emit_inc("vector", cmb, 1,
                           lambda e, p=p: e.tensor_scalar(
                               att[:, p * QCH:(p + 1) * QCH], o_p[:, :],
                               lnw2[:, 0:1], None, ALU.mult))
            pending[0] = part2
        if pending[0] is not None:
            pending[0]()
            pending[0] = None
        c.emit_wait("sync", cmb, (j + 1) * 16)
        dst = attn_dst[:, j * QCH:(j + 1) * QCH].rearrange("(a p) c -> p a c", p=128)
        c.emit_inc("sync", ast, 16,
                   lambda e, d=dst: e.dma_start(
                       out=d, in_=att[:, :].rearrange("p (a c) -> p a c", a=NF)))
    c.emit_wait("sync", ast)


def build_phaseC():
    nc = bass.Bass("TRN2", target_bir_lowering=False, debug=False)
    qt = nc.dram_tensor("QT", [E, TOK], BF16, kind="ExternalInput").ap()
    kv = nc.dram_tensor("KV_ALL", [2 * TOK, 1024], BF16, kind="ExternalInput").ap()
    qmax = nc.dram_tensor("QMAX", [1, NH], F32, kind="ExternalInput").ap()
    lams = nc.dram_tensor("LAMS", [1, 256], F32, kind="ExternalInput").ap()
    lnw = nc.dram_tensor("LNW", [128, 1], F32, kind="ExternalInput").ap()
    mask = nc.dram_tensor("MASK8", [8 * 128, QCH], BF16, kind="ExternalInput").ap()
    ident = nc.dram_tensor("IDENT", [128, 128], BF16, kind="ExternalInput").ap()
    onesb = nc.dram_tensor("ONESB", [128, 128], BF16, kind="ExternalInput").ap()
    ones_in = nc.dram_tensor("ones128", [128, 128], F32, kind="ExternalInput").ap()
    attn = nc.dram_tensor("ATTN", [E, TOK], BF16, kind="ExternalOutput").ap()
    with ExitStack() as st:
        c = Ctx(nc, st)
        onesf = c.sbuf("onesf", [128, 128], F32)
        c.emit_inc("sync", "init", 16, lambda e: e.dma_start(out=onesf[:, :], in_=ones_in))
        c.emit_wait("tensor", "init", 16)
        emit_attn(c, qt, kv, qmax, lams, lnw, mask, ident, onesb, onesf, attn)
        c.finalize()
    return nc


def host_masks(role):
    kk = np.arange(128)[:, None]
    qq = np.arange(QCH)[None, :]
    out = np.zeros((8 * 128, QCH), np.float32)
    for s in range(8):
        out[s * 128:(s + 1) * 128] = (128 * s + kk <= QCH * role + qq)
    return out.astype(bf)


def emit_wo(c, attn_src, wo_src, x_src, x1_dst, pref="D"):
    wo = c.sbuf(f"{pref}_wo", [128, NF * E], BF16)
    an = c.sbuf(f"{pref}_a", [128, 2 * NF * QCH], BF16)
    xr = c.sbuf(f"{pref}_x", [128, 2 * NF * QCH], F32)
    ot = c.sbuf(f"{pref}_o", [128, NF * QCH], F32)
    ps = [c.psum(f"{pref}_ps{i}", [128, QCH], F32) for i in range(4)]

    wld, ald, xld = f"{pref}_wld", f"{pref}_ald", f"{pref}_xld"
    mm, ev, st = f"{pref}_mm", f"{pref}_ev", f"{pref}_st"

    for e_i in range(NF):
        c.emit_inc("gpsimd", wld, 16,
                   lambda e, i=e_i: e.dma_start(
                       out=wo[:, i * E:(i + 1) * E], in_=wo_src[i * 128:(i + 1) * 128, :]))
    for ch in range(NCH):
        par = ch % 2
        if ch >= 2:
            c.emit_wait("sync", st, (ch - 1) * 16)
        asrc = attn_src[:, ch * QCH:(ch + 1) * QCH].rearrange("(a p) c -> p a c", p=128)
        c.emit_inc("sync", f"{ald}{par}", 16,
                   lambda e, s=asrc, par=par: e.dma_start(
                       out=an[:, par * NF * QCH:(par + 1) * NF * QCH].rearrange(
                           "p (a c) -> p a c", a=NF), in_=s))
        xsrc = x_src[:, ch * QCH:(ch + 1) * QCH].rearrange("(a p) c -> p a c", p=128)
        c.emit_inc("sync", f"{xld}{par}", 16,
                   lambda e, s=xsrc, par=par: e.dma_start(
                       out=xr[:, par * NF * QCH:(par + 1) * NF * QCH].rearrange(
                           "p (a c) -> p a c", a=NF), in_=s))
        c.emit_wait("tensor", f"{ald}{par}", (ch // 2 + 1) * 16)
        if ch == 0:
            c.emit_wait("tensor", wld, NF * 16)
        for od in range(NF):
            ti = ch * NF + od
            pq = ps[ti % 4]
            if ti >= 4:
                c.emit_wait("tensor", ev, ti - 3)
            for e_i in range(NF):
                fn = lambda e, od=od, e_i=e_i, pq=pq, par=par: e.matmul(
                    pq[:, :], wo[:, e_i * E + od * 128:e_i * E + od * 128 + 128],
                    an[:, (par * NF + e_i) * QCH:(par * NF + e_i + 1) * QCH],
                    start=(e_i == 0), stop=(e_i == NF - 1))
                if e_i == NF - 1:
                    c.emit_inc("tensor", mm, 1, fn)
                else:
                    c.emit("tensor", fn)
            c.emit_wait("vector", mm)
            c.emit_wait("vector", f"{xld}{par}", (ch // 2 + 1) * 16)
            if od == 0 and ch >= 1:
                c.emit_wait("vector", st, ch * 16)
            c.emit_inc("vector", ev, 1,
                       lambda e, od=od, pq=pq, par=par: e.tensor_tensor(
                           ot[:, od * QCH:(od + 1) * QCH], pq[:, :],
                           xr[:, (par * NF + od) * QCH:(par * NF + od + 1) * QCH], ALU.add))
        c.emit_wait("sync", ev)
        dst = x1_dst[:, ch * QCH:(ch + 1) * QCH].rearrange("(a p) c -> p a c", p=128)
        c.emit_inc("sync", st, 16,
                   lambda e, d=dst: e.dma_start(
                       out=d, in_=ot[:, :].rearrange("p (a c) -> p a c", a=NF)))
    c.emit_wait("sync", st)


def quant_bufs(c, CC):
    return (c.sbuf("Q_w", [128, 2 * CC], F32),
            c.sbuf("Q_ac", [128, 3], F32),
            c.sbuf("Q_rd", [128, 1], F32),
            c.sbuf("Q_cp", [128, 3], F32),
            c.sbuf("Q_cn", [128, 3], F32),
            c.sbuf("Q_s1", [128, CC], BF16),
            c.sbuf("Q_s2", [128, CC], BF16),
            c.sbuf("Q_q", [128, 2 * CC], BF16),
            c.psum("Q_ps", [128, 1], F32))


def emit_gamma(c, wt_src, R, CC, mi, onesf, shared, pref):
    nb = R // 128
    wt0, acc, red, cpos, cneg, s1, s2, qt, ps = shared
    wt = lambda par: wt0[:, par * CC:(par + 1) * CC]
    ld, vch, gmm = f"{pref}_ld", f"{pref}_vch", f"{pref}_gmm"
    c.emit_wait("vector", f"{pref}_prev")
    c.emit_inc("vector", vch, 1, lambda e, mi=mi: e.memset(acc[:, mi:mi + 1], 0.0))
    for b_i in range(nb):
        par = b_i % 2
        if b_i >= 2:
            c.emit_wait("sync", vch, 2 * b_i - 1)
        c.emit_inc("sync", f"{ld}{par}", 16,
                   lambda e, b_i=b_i, par=par: e.dma_start(
                       out=wt(par), in_=wt_src[b_i * 128:(b_i + 1) * 128, :]))
        c.emit_wait("vector", f"{ld}{par}", (b_i // 2 + 1) * 16)
        c.emit_wait("vector", vch)
        c.emit_inc("vector", vch, 1,
                   lambda e, par=par: e.tensor_reduce(
                       red[:, :], wt(par), AX.X, ALU.add, apply_absolute_value=True))
        c.emit_wait("vector", vch)
        c.emit_inc("vector", vch, 1,
                   lambda e, mi=mi: e.tensor_tensor(
                       acc[:, mi:mi + 1], acc[:, mi:mi + 1], red[:, :], ALU.add))
    c.emit_wait("tensor", vch)
    c.emit_inc("tensor", gmm, 1,
               lambda e, mi=mi: e.matmul(ps[:, :], onesf[:, :], acc[:, mi:mi + 1],
                                         start=True, stop=True))
    nelem = float(R * CC)
    c.emit_wait("vector", gmm)
    c.emit_inc("vector", vch, 1,
               lambda e, mi=mi: e.tensor_scalar(cpos[:, mi:mi + 1], ps[:, :],
                                                0.5 / nelem, None, ALU.mult))
    c.emit_inc("vector", f"{pref}_done", 1,
               lambda e, mi=mi: e.tensor_scalar(cneg[:, mi:mi + 1], ps[:, :],
                                                -0.5 / nelem, None, ALU.mult))


def emit_qwrite(c, wt_src, g_dst, R, CC, mi, shared, pref):
    nb = R // 128
    wt0, acc, red, cpos, cneg, s1, s2, qt, ps = shared
    wt = lambda par: wt0[:, par * CC:(par + 1) * CC]
    qtv = lambda par: qt[:, par * CC:(par + 1) * CC]
    ld, sch, qd, st = f"{pref}_ld", f"{pref}_sch", f"{pref}_qd", f"{pref}_st"
    s1v, s2v = s1[:, 0:CC], s2[:, 0:CC]
    for b_i in range(nb):
        par = b_i % 2
        if b_i >= 2:
            c.emit_wait("sync", sch, 2 * (b_i - 1))
        else:
            c.emit_wait("sync", f"{pref}_prevv")
        c.emit_inc("sync", f"{ld}{par}", 16,
                   lambda e, b_i=b_i, par=par: e.dma_start(
                       out=wt(par), in_=wt_src[b_i * 128:(b_i + 1) * 128, :]))
        c.emit_wait("scalar", f"{ld}{par}", (b_i // 2 + 1) * 16)
        c.emit_wait("scalar", qd)
        c.emit_inc("scalar", sch, 1,
                   lambda e, par=par, mi=mi: e.activation(
                       s1v, wt(par), AF.Sign, bias=cneg[:, mi:mi + 1]))
        c.emit_inc("scalar", sch, 1,
                   lambda e, par=par, mi=mi: e.activation(
                       s2v, wt(par), AF.Sign, bias=cpos[:, mi:mi + 1]))
        c.emit_wait("vector", sch)
        c.emit_wait("vector", f"{st}{par}", (b_i // 2) * 16)
        c.emit_inc("vector", qd, 3,
                   lambda e, par=par: e.tensor_tensor(qtv(par), s1v, s2v, ALU.add))
        c.emit_wait("sync", qd)
        c.emit_inc("sync", f"{st}{par}", 16,
                   lambda e, b_i=b_i, par=par: e.dma_start(
                       out=g_dst[b_i * 128:(b_i + 1) * 128, :], in_=qtv(par)))
    c.emit_wait("sync", f"{st}0")
    c.emit_wait("sync", f"{st}1")


def emit_quant(c, wt_src, g_dst, R, CC, onesf, pref, shared=None):
    nb = R // 128
    if shared is None:
        shared = quant_bufs(c, CC)
    wt, acc, red, cpos, cneg, s1, s2, qt, ps = shared
    wt0 = wt
    wt = lambda par: wt0[:, par * CC:(par + 1) * CC]
    qt0 = qt
    qt = lambda par: qt0[:, par * CC:(par + 1) * CC]
    s1 = s1[:, 0:CC]
    s2 = s2[:, 0:CC]

    ld, vch, gmm, sch, qd, st = (f"{pref}_ld", f"{pref}_vch", f"{pref}_gmm",
                                 f"{pref}_sch", f"{pref}_qd", f"{pref}_st")
    c.emit_inc("vector", vch, 1, lambda e: e.memset(acc[:, :], 0.0))
    for b_i in range(nb):
        par = b_i % 2
        if b_i >= 2:
            c.emit_wait("sync", vch, 2 * b_i - 1)
        c.emit_inc("sync", f"{ld}{par}", 16,
                   lambda e, b_i=b_i, par=par: e.dma_start(
                       out=wt(par), in_=wt_src[b_i * 128:(b_i + 1) * 128, :]))
        c.emit_wait("vector", f"{ld}{par}", (b_i // 2 + 1) * 16)
        c.emit_wait("vector", vch)
        c.emit_inc("vector", vch, 1,
                   lambda e, par=par: e.tensor_reduce(
                       red[:, :], wt(par), AX.X, ALU.add,
                       apply_absolute_value=True))
        c.emit_wait("vector", vch)
        c.emit_inc("vector", vch, 1,
                   lambda e: e.tensor_tensor(acc[:, :], acc[:, :], red[:, :], ALU.add))
    c.emit_wait("tensor", vch)
    c.emit_inc("tensor", gmm, 1,
               lambda e: e.matmul(ps[:, :], onesf[:, :], acc[:, :], start=True, stop=True))
    c.emit_wait("vector", gmm)
    nelem = float(R * CC)
    c.emit_inc("vector", vch, 1,
               lambda e: e.tensor_scalar(cpos[:, :], ps[:, :], 0.5 / nelem, None, ALU.mult))
    c.emit_inc("vector", vch, 1,
               lambda e: e.tensor_scalar(cneg[:, :], ps[:, :], -0.5 / nelem, None, ALU.mult))
    c.emit_wait("scalar", vch)
    for b_i in range(nb):
        par = b_i % 2
        if b_i >= 2:
            c.emit_wait("sync", sch, 2 * (b_i - 1))
        else:
            c.emit_wait("sync", vch)
        c.emit_inc("sync", f"{ld}2{par}", 16,
                   lambda e, b_i=b_i, par=par: e.dma_start(
                       out=wt(par), in_=wt_src[b_i * 128:(b_i + 1) * 128, :]))
        c.emit_wait("scalar", f"{ld}2{par}", (b_i // 2 + 1) * 16)
        c.emit_wait("scalar", qd)
        c.emit_inc("scalar", sch, 1,
                   lambda e, par=par: e.activation(
                       s1[:, :], wt(par), AF.Sign, bias=cneg[:, 0:1]))
        c.emit_inc("scalar", sch, 1,
                   lambda e, par=par: e.activation(
                       s2[:, :], wt(par), AF.Sign, bias=cpos[:, 0:1]))
        c.emit_wait("vector", sch)
        c.emit_wait("vector", f"{st}{par}", (b_i // 2) * 16)
        c.emit_inc("vector", qd, 3,
                   lambda e, par=par: e.tensor_tensor(
                       qt(par), s1[:, :], s2[:, :], ALU.add))
        c.emit_wait("sync", qd)
        c.emit_inc("sync", f"{st}{par}", 16,
                   lambda e, b_i=b_i, par=par: e.dma_start(
                       out=g_dst[b_i * 128:(b_i + 1) * 128, :],
                       in_=qt(par)))
    c.emit_wait("sync", f"{st}0")
    c.emit_wait("sync", f"{st}1")


def emit_ffn(c, xn_src, g1_src, g2_src, g3_src, x1_src, x1f_dst, pref="F"):
    NHT = FFN_H // 128
    TC = 1024
    ncch = TOK // TC
    xn = c.sbuf(f"{pref}_xn", [128, NF * TC], BF16)
    g12 = c.sbuf(f"{pref}_g12", [128, 6 * NF * 128], BF16)
    g3t = c.sbuf(f"{pref}_g3", [128, 4 * 512], BF16)
    hid = c.sbuf(f"{pref}_h", [128, NHT * TC], BF16)
    sg = c.sbuf(f"{pref}_sg", [128, 1024], BF16)
    x1t = c.sbuf(f"{pref}_x1", [128, 4 * TC], F32)
    ot = c.sbuf(f"{pref}_o", [128, 4 * TC], F32)
    ps = [c.psum(f"{pref}_ps{i}", [128, 512], F32) for i in range(8)]

    xld, gld, mmh, sgd, vsg = (f"{pref}_xld", f"{pref}_gld", f"{pref}_mmh",
                               f"{pref}_sgd", f"{pref}_vsg")
    g3ld, mmo, x1ld, ev, st = (f"{pref}_g3ld", f"{pref}_mmo", f"{pref}_x1ld",
                               f"{pref}_ev", f"{pref}_st")

    for tc in range(ncch):
        tp = 0
        if tc >= 1:
            c.emit_wait("sync", mmh, tc * NHT * 2)
        src = xn_src[:, tc * TC:(tc + 1) * TC].rearrange("(a p) c -> p a c", p=128)
        c.emit_inc("sync", f"{xld}", 16,
                   lambda e, s=src: e.dma_start(
                       out=xn[:, :].rearrange("p (a c) -> p a c", a=NF), in_=s))
        for hd in range(NHT):
            it = tc * NHT + hd
            par = it % 3
            if it >= 3:
                c.emit_wait("sync", mmh, (it - 2) * 2)
            for gsrc, off in ((g1_src, 0), (g2_src, NF * 128)):
                gs = gsrc[:, hd * 128:(hd + 1) * 128].rearrange("(a p) c -> p a c", p=128)
                c.emit_inc("sync", f"{gld}{par}", 16,
                           lambda e, gs=gs, par=par, off=off: e.dma_start(
                               out=g12[:, par * 2 * NF * 128 + off:
                                       par * 2 * NF * 128 + off + NF * 128]
                               .rearrange("p (a c) -> p a c", a=NF), in_=gs))
            c.emit_wait("tensor", f"{gld}{par}", (it // 3 + 1) * 32)
            c.emit_wait("tensor", f"{xld}", (tc + 1) * 16)
            if tc > 0 and hd == 0:
                c.emit_wait("tensor", ev)
            for n in range(2):
                bb = 2 * n
                c.emit_wait("tensor", sgd, max(0, c.cnt.get(sgd, 0) - 1))
                c.emit_wait("tensor", vsg, max(0, c.cnt.get(vsg, 0) - 1))
                for gi in range(2):
                    for e_i in range(NF):
                        fn = lambda e, e_i=e_i, par=par, n=n, gi=gi, bb=bb: e.matmul(
                            ps[bb + gi][:, :],
                            g12[:, par * 2 * NF * 128 + gi * NF * 128 + e_i * 128:
                                par * 2 * NF * 128 + gi * NF * 128 + (e_i + 1) * 128],
                            xn[:, e_i * TC + n * 512:e_i * TC + (n + 1) * 512],
                            start=(e_i == 0), stop=(e_i == NF - 1))
                        if gi == 1 and e_i == NF - 1:
                            c.emit_inc("tensor", mmh, 1, fn)
                        else:
                            c.emit("tensor", fn)
                c.emit_wait("scalar", mmh)
                c.emit_wait("scalar", vsg, max(0, c.cnt.get(vsg, 0) - 1))
                c.emit_inc("scalar", sgd, 1,
                           lambda e, bb=bb, n=n: e.activation(
                               sg[:, n * 512:(n + 1) * 512], ps[bb][:, :],
                               AF.Sigmoid, scale=0.5))
                c.emit_wait("vector", sgd)
                c.emit_inc("vector", vsg, 1,
                           lambda e, hd=hd, n=n, bb=bb: e.scalar_tensor_tensor(
                               hid[:, hd * TC + n * 512:hd * TC + (n + 1) * 512],
                               ps[bb + 1][:, :], 0.5, sg[:, n * 512:(n + 1) * 512],
                               ALU.mult, ALU.mult))

        c.emit_wait("tensor", vsg)
        for grp in range(4):
            gi_glob = tc * 4 + grp
            x1s = x1_src[grp * 512:(grp + 1) * 512, tc * TC:(tc + 1) * TC]
            c.emit_wait("sync", ev)
            c.emit_inc("sync", x1ld, 16,
                       lambda e, s=x1s: e.dma_start(
                           out=x1t[:, :].rearrange("p (a c) -> p a c", a=4),
                           in_=s.rearrange("(a p) c -> p a c", p=128)))
            for hd in range(NHT):
                par = hd % 4
                g3s = g3_src[hd * 128:(hd + 1) * 128, grp * 512:(grp + 1) * 512]
                if hd >= 4:
                    c.emit_wait("sync", mmo, gi_glob * NHT * 8 + (hd - 3) * 8)
                c.emit_inc("sync", f"{g3ld}{par}", 16,
                           lambda e, s=g3s, par=par: e.dma_start(
                               out=g3t[:, par * 512:(par + 1) * 512], in_=s))
                c.emit_wait("tensor", f"{g3ld}{par}",
                            (gi_glob * NHT // 4 + hd // 4 + 1) * 16)
                if hd == 0:
                    c.emit_wait("tensor", ev)
                for o in range(4):
                    for n in range(2):
                        fn = lambda e, o=o, n=n, par=par, hd=hd: e.matmul(
                            ps[o * 2 + n][:, :],
                            g3t[:, par * 512 + o * 128:par * 512 + (o + 1) * 128],
                            hid[:, hd * TC + n * 512:hd * TC + (n + 1) * 512],
                            start=(hd == 0), stop=(hd == NHT - 1))
                        c.emit_inc("tensor", mmo, 1, fn)
            c.emit_wait("vector", mmo)
            c.emit_wait("vector", x1ld)
            c.emit_wait("vector", st)
            for o in range(4):
                for n in range(2):
                    c.emit_inc("vector", ev, 1,
                               lambda e, o=o, n=n: e.scalar_tensor_tensor(
                                   ot[:, o * TC + n * 512:o * TC + (n + 1) * 512],
                                   ps[o * 2 + n][:, :], 0.5,
                                   x1t[:, o * TC + n * 512:o * TC + (n + 1) * 512],
                                   ALU.mult, ALU.add))
            c.emit_wait("sync", ev)
            dstv = x1f_dst[grp * 512:(grp + 1) * 512, tc * TC:(tc + 1) * TC]
            c.emit_inc("sync", st, 16,
                       lambda e, d=dstv: e.dma_start(
                           out=d.rearrange("(a p) c -> p a c", p=128),
                           in_=ot[:, :].rearrange("p (a c) -> p a c", a=4)))
    c.emit_wait("sync", st)
def emit_ln1_tm(c, x1f_src, xn1_dst, xn1tm_dst, onesf, pref="G"):
    emit_ln(c, x1f_src, xn1_dst, pref, ncols=TOK, onesf=onesf)
    tmt = c.sbuf(f"{pref}_tm", [128, 2 * E], BF16)
    tld, tst = f"{pref}_tld", f"{pref}_tst"
    for tt in range(TOK // 128):
        par = tt % 2
        if tt >= 2:
            c.emit_wait("sync", f"{tst}{par}", (tt // 2) * 16)
        c.emit_inc("sync", f"{tld}{par}", 16,
                   lambda e, tt=tt, par=par: e.dma_start(
                       out=tmt[:, par * E:(par + 1) * E],
                       in_=xn1_dst[:, tt * 128:(tt + 1) * 128], transpose=True))
        c.emit_wait("sync", f"{tld}{par}", (tt // 2 + 1) * 16)
        c.emit_inc("sync", f"{tst}{par}", 16,
                   lambda e, tt=tt, par=par: e.dma_start(
                       out=xn1tm_dst[tt * 128:(tt + 1) * 128, :],
                       in_=tmt[:, par * E:(par + 1) * E]))
    c.emit_wait("sync", f"{tst}0")
    c.emit_wait("sync", f"{tst}1")


def emit_l1attn(c, xn1_src, wq_src, bq_src, wkv_src, bkv_src, x1f_src, x2a_dst,
                onesf, pref="H"):
    wq = c.sbuf(f"{pref}_wq", [128, NF * E], BF16)
    wkv = c.sbuf(f"{pref}_wkv", [128, NF * 256], BF16)
    bq = c.sbuf(f"{pref}_bq", [128, NF], F32)
    bkv = c.sbuf(f"{pref}_bkv", [128, 2], F32)
    xn = c.sbuf(f"{pref}_xn", [128, 2 * NF * QCH], BF16)
    x1f = c.sbuf(f"{pref}_x1", [128, NF * QCH], F32)
    qtl = c.sbuf(f"{pref}_qt", [128, NF * QCH], BF16)
    ktl = c.sbuf(f"{pref}_kt", [128, QCH], BF16)
    vtl = c.sbuf(f"{pref}_vt", [128, QCH], F32)
    prod = c.sbuf(f"{pref}_pr", [128, 2 * QCH], F32)
    ssb = c.sbuf(f"{pref}_ss", [128, 2 * QCH], F32)
    esb = c.sbuf(f"{pref}_es", [128, 2 * QCH], F32)
    psb = c.sbuf(f"{pref}_ps", [128, 2 * QCH], F32)
    ot = c.sbuf(f"{pref}_o", [128, NF * QCH], F32)
    ps_q = [c.psum(f"{pref}_pq{i}", [128, QCH], F32) for i in range(4)]
    ps_s = [c.psum(f"{pref}_pss{i}", [128, QCH], F32) for i in range(2)]

    wld, xld, x1ld = f"{pref}_wld", f"{pref}_xld", f"{pref}_x1ld"
    qmm, qev, smm, vch, sch, st = (f"{pref}_qmm", f"{pref}_qev", f"{pref}_smm",
                                   f"{pref}_vch", f"{pref}_sch", f"{pref}_st")

    for e_i in range(NF):
        c.emit_inc("gpsimd", wld, 16,
                   lambda e, i=e_i: e.dma_start(
                       out=wq[:, i * E:(i + 1) * E], in_=wq_src[i * 128:(i + 1) * 128, :]))
        c.emit_inc("gpsimd", wld, 16,
                   lambda e, i=e_i: e.dma_start(
                       out=wkv[:, i * 256:(i + 1) * 256], in_=wkv_src[i * 128:(i + 1) * 128, :]))
    c.emit_inc("gpsimd", wld, 16,
               lambda e: e.dma_start(
                   out=bq[:, :].rearrange("p (a o) -> p a o", o=1),
                   in_=bq_src.rearrange("(a p) o -> p a o", p=128),
                   allow_slow_non_contiguous=True))
    c.emit_inc("gpsimd", wld, 16,
               lambda e: e.dma_start(
                   out=bkv[:, :].rearrange("p (a o) -> p a o", o=1),
                   in_=bkv_src.rearrange("(a p) o -> p a o", p=128),
                   allow_slow_non_contiguous=True))

    for ch in range(NCH):
        par = ch % 2
        if ch >= 2:
            c.emit_wait("sync", st, (ch - 1) * 16)
        src = xn1_src[:, ch * QCH:(ch + 1) * QCH].rearrange("(a p) c -> p a c", p=128)
        c.emit_inc("sync", f"{xld}{par}", 16,
                   lambda e, s=src, par=par: e.dma_start(
                       out=xn[:, par * NF * QCH:(par + 1) * NF * QCH].rearrange(
                           "p (a c) -> p a c", a=NF), in_=s))
        src1 = x1f_src[:, ch * QCH:(ch + 1) * QCH].rearrange("(a p) c -> p a c", p=128)
        c.emit_wait("sync", st, ch * 16)
        c.emit_inc("sync", x1ld, 16,
                   lambda e, s=src1: e.dma_start(
                       out=x1f[:, :].rearrange("p (a c) -> p a c", a=NF), in_=s))
        c.emit_wait("tensor", f"{xld}{par}", (ch // 2 + 1) * 16)
        if ch == 0:
            c.emit_wait("tensor", wld, (2 * NF + 2) * 16)
        for qd in range(NF):
            ti = ch * (NF + 2) + qd
            pq = ps_q[ti % 4]
            if ti >= 4:
                c.emit_wait("tensor", qev, ti - 3)
            for e_i in range(NF):
                fn = lambda e, qd=qd, e_i=e_i, pq=pq, par=par: e.matmul(
                    pq[:, :], wq[:, e_i * E + qd * 128:e_i * E + qd * 128 + 128],
                    xn[:, (par * NF + e_i) * QCH:(par * NF + e_i + 1) * QCH],
                    start=(e_i == 0), stop=(e_i == NF - 1))
                if e_i == NF - 1:
                    c.emit_inc("tensor", qmm, 1, fn)
                else:
                    c.emit("tensor", fn)
            c.emit_wait("vector", qmm)
            c.emit_inc("vector", qev, 1,
                       lambda e, qd=qd, pq=pq: e.tensor_scalar(
                           qtl[:, qd * QCH:(qd + 1) * QCH], pq[:, :],
                           bq[:, qd:qd + 1], None, ALU.add))
        for kv in range(2):
            ti = ch * (NF + 2) + NF + kv
            pq = ps_q[ti % 4]
            c.emit_wait("tensor", qev, ti - 3)
            for e_i in range(NF):
                fn = lambda e, kv=kv, e_i=e_i, pq=pq, par=par: e.matmul(
                    pq[:, :], wkv[:, e_i * 256 + kv * 128:e_i * 256 + kv * 128 + 128],
                    xn[:, (par * NF + e_i) * QCH:(par * NF + e_i + 1) * QCH],
                    start=(e_i == 0), stop=(e_i == NF - 1))
                if e_i == NF - 1:
                    c.emit_inc("tensor", qmm, 1, fn)
                else:
                    c.emit("tensor", fn)
            c.emit_wait("vector", qmm)
            if kv == 0:
                c.emit_inc("vector", qev, 1,
                           lambda e, pq=pq: e.tensor_scalar(
                               ktl[:, :], pq[:, :], bkv[:, 0:1], None, ALU.add))
            else:
                c.emit_inc("vector", qev, 1,
                           lambda e, pq=pq: e.tensor_scalar(
                               vtl[:, :], pq[:, :], bkv[:, 1:2], None, ALU.add))
        for h in range(NF):
            sl = h % 2
            pr = prod[:, sl * QCH:(sl + 1) * QCH]
            ssl = ssb[:, sl * QCH:(sl + 1) * QCH]
            esl = esb[:, sl * QCH:(sl + 1) * QCH]
            pl = psb[:, sl * QCH:(sl + 1) * QCH]
            pss = ps_s[h % 2]
            if h == 0:
                c.emit_wait("vector", qev, (ch + 1) * (NF + 2))
            c.emit_wait("vector", f"{pref}_od", max(0, c.cnt.get(f"{pref}_od", 0) - 1))
            c.emit_inc("vector", vch, 1,
                       lambda e, h=h, pr=pr: e.tensor_tensor(
                           pr, qtl[:, h * QCH:(h + 1) * QCH], ktl[:, :], ALU.mult))
            c.emit_wait("tensor", vch)
            c.emit_wait("tensor", f"{pref}_psev", max(0, c.cnt.get(f"{pref}_psev", 0) - 1))
            c.emit_inc("tensor", smm, 1,
                       lambda e, pr=pr, pss=pss: e.matmul(pss[:, :], onesf[:, :], pr,
                                                          start=True, stop=True))
            c.emit_wait("scalar", smm)
            c.emit_inc("scalar", f"{pref}_psev", 1,
                       lambda e, ssl=ssl, pss=pss: e.activation(
                           ssl, pss[:, :], AF.Copy, scale=float(1.0 / np.sqrt(128.0))))
            c.emit_wait("vector", f"{pref}_psev")
            c.emit_inc("vector", vch, 1,
                       lambda e, ssl=ssl, pl=pl: e.tensor_tensor(pl, ssl, ssl, ALU.subtract))
            c.emit_wait("scalar", vch)
            c.emit_inc("scalar", sch, 1,
                       lambda e, esl=esl, pl=pl: e.activation(esl, pl, AF.Exp))
            c.emit_wait("vector", sch)
            c.emit_inc("vector", vch, 1,
                       lambda e, esl=esl, ssl=ssl: e.reciprocal(ssl, esl))
            c.emit_wait("vector", vch)
            c.emit_inc("vector", vch, 1,
                       lambda e, esl=esl, ssl=ssl, pl=pl: e.tensor_tensor(pl, esl, ssl, ALU.mult))
            c.emit_wait("vector", vch)
            if h == 0:
                c.emit_wait("vector", x1ld, (ch + 1) * 16)
            c.emit_inc("vector", vch, 1,
                       lambda e, pr=pr, pl=pl: e.tensor_tensor(pr, pl, vtl[:, :], ALU.mult))
            c.emit_wait("vector", vch)
            if h == 0 and ch >= 1:
                c.emit_wait("vector", st, ch * 16)
            c.emit_inc("vector", f"{pref}_od", 1,
                       lambda e, h=h, pr=pr: e.tensor_tensor(
                           ot[:, h * QCH:(h + 1) * QCH], pr,
                           x1f[:, h * QCH:(h + 1) * QCH], ALU.add))
        c.emit_wait("sync", f"{pref}_od", (ch + 1) * NF)
        dst = x2a_dst[:, ch * QCH:(ch + 1) * QCH].rearrange("(a p) c -> p a c", p=128)
        c.emit_inc("sync", st, 16,
                   lambda e, d=dst: e.dma_start(
                       out=d, in_=ot[:, :].rearrange("p (a c) -> p a c", a=NF)))
    c.emit_wait("sync", st)


def build_phaseG():
    nc = bass.Bass("TRN2", target_bir_lowering=False, debug=False)
    x1f = nc.dram_tensor("X1F", [E, TOK], F32, kind="ExternalInput").ap()
    ones_in = nc.dram_tensor("ones128", [128, 128], F32, kind="ExternalInput").ap()
    xn1 = nc.dram_tensor("XN1", [E, TOK], BF16, kind="ExternalOutput").ap()
    xn1tm = nc.dram_tensor("XN1TM", [TOK, E], BF16, kind="ExternalOutput").ap()
    with ExitStack() as st:
        c = Ctx(nc, st)
        onesf = c.sbuf("onesf", [128, 128], F32)
        c.emit_inc("sync", "init", 16, lambda e: e.dma_start(out=onesf[:, :], in_=ones_in))
        c.emit_wait("tensor", "init", 16)
        emit_ln1_tm(c, x1f, xn1, xn1tm, onesf)
        c.finalize()
    return nc


def build_phaseH():
    nc = bass.Bass("TRN2", target_bir_lowering=False, debug=False)
    xn1 = nc.dram_tensor("XN1", [E, TOK], BF16, kind="ExternalInput").ap()
    wq = nc.dram_tensor("Wq1T", [E, E], F32, kind="ExternalInput").ap()
    bq = nc.dram_tensor("BQ1", [E, 1], F32, kind="ExternalInput").ap()
    wkv = nc.dram_tensor("WkvT", [E, 256], F32, kind="ExternalInput").ap()
    bkv = nc.dram_tensor("BKV", [256, 1], F32, kind="ExternalInput").ap()
    x1f = nc.dram_tensor("X1F", [E, TOK], F32, kind="ExternalInput").ap()
    ones_in = nc.dram_tensor("ones128", [128, 128], F32, kind="ExternalInput").ap()
    x2a = nc.dram_tensor("X2A", [E, TOK], F32, kind="ExternalOutput").ap()
    with ExitStack() as st:
        c = Ctx(nc, st)
        onesf = c.sbuf("onesf", [128, 128], F32)
        c.emit_inc("sync", "init", 16, lambda e: e.dma_start(out=onesf[:, :], in_=ones_in))
        c.emit_wait("tensor", "init", 16)
        emit_l1attn(c, xn1, wq, bq, wkv, bkv, x1f, x2a, onesf)
        c.finalize()
    return nc



def emit_route(c, xn1_src, gwt_src, ltc_src, agrow_src, iota_src, identf_src,
               onesf, wts_dst, cmbi_dst, bufl, bufm, hist_in, hist_all,
               pref="I"):
    nc = c.nc
    NTT = TOK // 128
    xn = c.sbuf(f"{pref}_xn", [128, 2 * NF * QCH], BF16)
    gwt = c.sbuf(f"{pref}_gw", [128, NF * 16], BF16)
    iota = c.sbuf(f"{pref}_io", [128, 16], F32)
    identf = c.sbuf(f"{pref}_idf", [128, 128], F32)
    agrow = c.sbuf(f"{pref}_ag", [128, NTT], I32)
    sm = c.sbuf(f"{pref}_sm", [128, 16], F32)
    mx = c.sbuf(f"{pref}_mx", [128, 1], F32)
    nm = c.sbuf(f"{pref}_nm", [128, 1], F32)
    topv = c.sbuf(f"{pref}_tv", [128, 8], F32)
    topi = c.sbuf(f"{pref}_ti", [128, 8], U32)
    wsum = c.sbuf(f"{pref}_ws", [128, 1], F32)
    idxall = c.sbuf(f"{pref}_ix", [128, NTT * 4], F32)
    wtsall = c.sbuf(f"{pref}_wt", [128, NTT * 4], F32)
    ohall = c.sbuf(f"{pref}_oh", [128, NTT * 4 * 16], F32)
    hall = c.sbuf(f"{pref}_h", [128, NTT * 16], F32)
    hes = c.sbuf(f"{pref}_he", [128, NTT * 16], F32)
    histr = c.sbuf(f"{pref}_hr", [1, 64], F32)
    hsb = c.sbuf(f"{pref}_hs", [32, 16], F32)
    ltc = c.sbuf(f"{pref}_ltc", [32, 4], F32)
    offr = c.sbuf(f"{pref}_of", [1, 64], F32)
    init = c.sbuf(f"{pref}_in", [16, 4], F32)
    htb = c.sbuf(f"{pref}_ht", [16, QCH], F32)
    zt = c.sbuf(f"{pref}_z", [16, QCH], F32)
    scn = c.sbuf(f"{pref}_sc", [16, QCH + 1], F32)
    t16 = c.sbuf(f"{pref}_t16", [128, 16], F32)
    sl = c.sbuf(f"{pref}_sl", [128, 1], F32)
    dstf = c.sbuf(f"{pref}_df", [128, 1], F32)
    msk = c.sbuf(f"{pref}_mk", [128, 1], F32)
    aux = c.sbuf(f"{pref}_ax", [128, 1], F32)
    dsti = c.sbuf(f"{pref}_di", [128, NTT * 4], I32)
    cmbi = c.sbuf(f"{pref}_ci", [128, NTT * 4], I32)
    ztile = c.sbuf(f"{pref}_zt", [128, 128], I32)
    ps_g = c.psum(f"{pref}_pg", [128, 16], F32)
    ps_x = c.psum(f"{pref}_px", [128, 512], F32)

    ld, xld, gmm, vch, sch = (f"{pref}_ld", f"{pref}_xld", f"{pref}_gmm",
                              f"{pref}_vch", f"{pref}_sch")
    hmm_, agd, tmm, tev = f"{pref}_hmm", f"{pref}_agd", f"{pref}_tmm", f"{pref}_tev"
    scd, std_, zd = f"{pref}_scd", f"{pref}_std", f"{pref}_zd"

    ldg = f"{pref}_ldg"
    for e_i in range(NF):
        c.emit_inc("gpsimd", ldg, 16,
                   lambda e, i=e_i: e.dma_start(
                       out=gwt[:, i * 16:(i + 1) * 16], in_=gwt_src[i * 128:(i + 1) * 128, :]))
    c.emit_inc("sync", ld, 16, lambda e: e.dma_start(out=iota[:, :], in_=iota_src))
    c.emit_inc("sync", ld, 16, lambda e: e.dma_start(out=identf[:, :], in_=identf_src))
    c.emit_inc("sync", ld, 16, lambda e: e.dma_start(out=ltc[:, :], in_=ltc_src))
    c.emit_inc("sync", ld, 16,
               lambda e: e.dma_start(
                   out=agrow[:, :].rearrange("p (a o) -> p a o", o=1),
                   in_=agrow_src.rearrange("(a p) o -> p a o", p=128),
                   allow_slow_non_contiguous=True))
    c.emit_inc("vector", vch, 1, lambda e: e.memset(zt[:, :], 0.0))
    c.emit_inc("vector", vch, 1, lambda e: e.memset(ztile[:, :], 0))

    c.emit_wait("gpsimd", vch, 2)
    c.emit_inc("gpsimd", zd, 16,
               lambda e: e.dma_start(
                   out=bufl[0:TOK * NCORE, :].rearrange("(p a) o -> p a o", p=128),
                   in_=ztile[:, :].rearrange("p (a o) -> p a o", o=1)))
    c.emit_inc("gpsimd", zd, 16,
               lambda e: e.dma_start(
                   out=bufl[TOK * NCORE:TOK * NCORE + 64, :].rearrange(
                       "(o a) b -> o a b", o=1),
                   in_=ztile[0:1, 0:64].rearrange("p (a b) -> p a b", b=1)))

    for tt in range(NTT):
        ch, par = tt // 4, (tt // 4) % 2
        if tt % 4 == 0:
            src = xn1_src[:, ch * QCH:(ch + 1) * QCH].rearrange("(a p) c -> p a c", p=128)
            if ch >= 2:
                c.emit_wait("sync", gmm, (ch - 1) * 4)
            c.emit_inc("sync", f"{xld}{par}", 16,
                       lambda e, s=src, par=par: e.dma_start(
                           out=xn[:, par * NF * QCH:(par + 1) * NF * QCH].rearrange(
                               "p (a c) -> p a c", a=NF), in_=s))
        c.emit_wait("tensor", f"{xld}{par}", (ch // 2 + 1) * 16)
        if tt == 0:
            c.emit_wait("tensor", ld, 4 * 16)
            c.emit_wait("tensor", ldg, NF * 16)
        c.emit_wait("tensor", vch)
        for e_i in range(NF):
            fn = lambda e, e_i=e_i, tt=tt, par=par: e.matmul(
                ps_g[:, :],
                xn[:, (par * NF + e_i) * QCH + (tt % 4) * 128:
                   (par * NF + e_i) * QCH + (tt % 4 + 1) * 128],
                gwt[:, e_i * 16:(e_i + 1) * 16],
                start=(e_i == 0), stop=(e_i == NF - 1))
            if e_i == NF - 1:
                c.emit_inc("tensor", gmm, 1, fn)
            else:
                c.emit("tensor", fn)
        c.emit_wait("vector", gmm)
        c.emit_inc("vector", vch, 1, lambda e: e.reduce_max(mx[:, :], ps_g[:, :], axis=AX.X))
        c.emit_wait("vector", vch)
        c.emit_inc("vector", vch, 1, lambda e: e.tensor_scalar(nm[:, :], mx[:, :], -1.0, None, ALU.mult))
        c.emit_wait("scalar", vch)
        c.emit_inc("scalar", sch, 1,
                   lambda e: e.activation(sm[:, :], ps_g[:, :], AF.Exp, bias=nm[:, 0:1]))
        c.emit_wait("vector", sch)
        c.emit_inc("vector", vch, 1, lambda e: e.reduce_sum(mx[:, :], sm[:, :], axis=AX.X))
        c.emit_wait("vector", vch)
        c.emit_inc("vector", vch, 1, lambda e: e.reciprocal(nm[:, :], mx[:, :]))
        c.emit_wait("vector", vch)
        c.emit_inc("vector", vch, 1,
                   lambda e: e.tensor_scalar(sm[:, :], sm[:, :], nm[:, 0:1], None, ALU.mult))
        c.emit_wait("vector", vch)
        c.emit_inc("vector", vch, 1, lambda e: e.max(topv[:, :], sm[:, :]))
        c.emit_wait("vector", vch)
        c.emit_inc("vector", vch, 1, lambda e: e.max_index(topi[:, :], topv[:, :], sm[:, :]))
        c.emit_wait("vector", vch)
        c.emit_inc("vector", vch, 1,
                   lambda e: e.reduce_sum(wsum[:, :], topv[:, 0:4], axis=AX.X))
        c.emit_wait("vector", vch)
        c.emit_inc("vector", vch, 1,
                   lambda e: e.tensor_scalar(wsum[:, :], wsum[:, :], 1e-20, None, ALU.add))
        c.emit_wait("vector", vch)
        c.emit_inc("vector", vch, 1, lambda e: e.reciprocal(nm[:, :], wsum[:, :]))
        c.emit_wait("vector", vch)
        c.emit_inc("vector", vch, 1,
                   lambda e, tt=tt: e.tensor_scalar(
                       wtsall[:, tt * 4:(tt + 1) * 4], topv[:, 0:4], nm[:, 0:1], None, ALU.mult))
        c.emit_inc("vector", vch, 1,
                   lambda e, tt=tt: e.tensor_copy(idxall[:, tt * 4:(tt + 1) * 4], topi[:, 0:4]))
        c.emit_wait("vector", vch)
        for k in range(4):
            c.emit_inc("vector", vch, 1,
                       lambda e, tt=tt, k=k: e.tensor_scalar(
                           ohall[:, (tt * 4 + k) * 16:(tt * 4 + k + 1) * 16],
                           iota[:, :], idxall[:, tt * 4 + k:tt * 4 + k + 1], None, ALU.is_equal))
        c.emit_wait("vector", vch)
        c.emit_inc("vector", vch, 1,
                   lambda e, tt=tt: e.tensor_tensor(
                       hall[:, tt * 16:(tt + 1) * 16],
                       ohall[:, (tt * 4) * 16:(tt * 4 + 1) * 16],
                       ohall[:, (tt * 4 + 1) * 16:(tt * 4 + 2) * 16], ALU.add))
        c.emit_wait("vector", vch)
        c.emit_inc("vector", vch, 1,
                   lambda e, tt=tt: e.tensor_tensor(
                       hall[:, tt * 16:(tt + 1) * 16], hall[:, tt * 16:(tt + 1) * 16],
                       ohall[:, (tt * 4 + 2) * 16:(tt * 4 + 3) * 16], ALU.add))
        c.emit_wait("vector", vch)
        c.emit_inc("vector", vch, 1,
                   lambda e, tt=tt: e.tensor_tensor(
                       hall[:, tt * 16:(tt + 1) * 16], hall[:, tt * 16:(tt + 1) * 16],
                       ohall[:, (tt * 4 + 3) * 16:(tt * 4 + 4) * 16], ALU.add))
        c.emit_wait("tensor", vch)
        if tt % 4 == 0:
            c.emit_wait("tensor", tev)
        fnh = lambda e, tt=tt: e.matmul(
            ps_x[0:1, 0:16], onesf[:, 0:1], hall[:, tt * 16:(tt + 1) * 16],
            start=(tt % 4 == 0), stop=(tt % 4 == 3))
        c.emit_inc("tensor", hmm_, 1, fnh)
        if tt % 4 == 3:
            c.emit_wait("vector", hmm_)
            c.emit_inc("vector", tev, 1,
                       lambda e, ch=ch: e.tensor_copy(
                           histr[0:1, ch * 16:(ch + 1) * 16], ps_x[0:1, 0:16]))

    c.emit_wait("sync", tev)
    c.emit_inc("sync", f"{pref}_hst", 16,
               lambda e: e.dma_start(
                   out=hist_in.rearrange("(o a) b -> o a b", o=1),
                   in_=histr[0:1, :].rearrange("p (a b) -> p a b", a=4)))
    c.emit_wait("gpsimd", f"{pref}_hst", 16)
    c.emit_inc("gpsimd", agd, 1,
               lambda e: e.collective_compute(
                   "AllGather", ALU.bypass, replica_groups=[list(range(8))],
                   ins=[hist_in.opt()], outs=[hist_all.opt()]))
    c.emit_wait("sync", agd, 1)
    c.emit_inc("sync", f"{pref}_hld", 16, lambda e: e.dma_start(out=hsb[:, :], in_=hist_all))

    for mc in range(NCH):
        c.emit_wait("tensor", f"{pref}_hld", 16)
        c.emit_wait("tensor", tev)
        c.emit_inc("tensor", tmm, 1,
                   lambda e, mc=mc: e.matmul(
                       ps_x[0:1, 0:16], ltc[:, mc:mc + 1], hsb[:, :], start=True, stop=True))
        c.emit_wait("vector", tmm)
        c.emit_inc("vector", tev, 1,
                   lambda e, mc=mc: e.tensor_copy(offr[0:1, mc * 16:(mc + 1) * 16],
                                                  ps_x[0:1, 0:16]))
    for mc in range(NCH):
        c.emit_wait("tensor", tev)
        c.emit_inc("tensor", tmm, 1,
                   lambda e, mc=mc: e.matmul(
                       ps_x[0:16, 0:1], offr[0:1, mc * 16:(mc + 1) * 16],
                       onesf[0:1, 0:1], start=True, stop=True))
        c.emit_wait("vector", tmm)
        c.emit_inc("vector", tev, 1,
                   lambda e, mc=mc: e.tensor_copy(init[:, mc:mc + 1], ps_x[0:16, 0:1]))

    for mc in range(NCH):
        for q4 in range(4):
            tt = mc * 4 + q4
            c.emit_wait("tensor", tev)
            c.emit_inc("tensor", tmm, 1,
                       lambda e, tt=tt: e.transpose(
                           ps_x[0:16, 0:128], hall[:, tt * 16:(tt + 1) * 16], identf[:, :]))
            c.emit_wait("vector", tmm)
            c.emit_inc("vector", tev, 1,
                       lambda e, q4=q4: e.tensor_copy(htb[:, q4 * 128:(q4 + 1) * 128],
                                                      ps_x[0:16, 0:128]))
        c.emit_wait("vector", tev)
        c.emit_inc("vector", vch, 1,
                   lambda e, mc=mc: e.tensor_copy(scn[:, 0:1], init[:, mc:mc + 1]))
        c.emit_wait("vector", vch)
        c.emit_inc("vector", scd, 1,
                   lambda e, mc=mc: e.tensor_tensor_scan(
                       scn[:, 1:QCH + 1], htb[:, :], zt[:, :], init[:, mc:mc + 1],
                       ALU.add, ALU.add))
        for q4 in range(4):
            tt = mc * 4 + q4
            c.emit_wait("tensor", scd)
            c.emit_wait("tensor", tev)
            c.emit_inc("tensor", tmm, 1,
                       lambda e, q4=q4: e.transpose(
                           ps_x[0:128, 0:16], scn[0:16, q4 * 128:(q4 + 1) * 128],
                           identf[0:16, 0:16]))
            c.emit_wait("vector", tmm)
            c.emit_inc("vector", tev, 1,
                       lambda e, tt=tt: e.tensor_copy(hes[:, tt * 16:(tt + 1) * 16],
                                                      ps_x[0:128, 0:16]))

    for tt in range(NTT):
        for k in range(4):
            kk = tt * 4 + k
            c.emit_wait("vector", tev)
            c.emit_wait("vector", vch)
            c.emit_inc("vector", vch, 1,
                       lambda e, tt=tt, kk=kk: e.tensor_tensor(
                           t16[:, :], ohall[:, kk * 16:(kk + 1) * 16],
                           hes[:, tt * 16:(tt + 1) * 16], ALU.mult))
            c.emit_wait("vector", vch)
            c.emit_inc("vector", vch, 1, lambda e: e.reduce_sum(sl[:, :], t16[:, :], axis=AX.X))
            c.emit_wait("vector", vch)
            c.emit_inc("vector", vch, 1,
                       lambda e, kk=kk: e.scalar_tensor_tensor(
                           dstf[:, :], idxall[:, kk:kk + 1], 1024.0, sl[:, :],
                           ALU.mult, ALU.add))
            c.emit_inc("vector", vch, 1,
                       lambda e: e.tensor_scalar(msk[:, :], sl[:, :], 1024.0, None, ALU.is_lt))
            c.emit_wait("vector", vch)
            trash = float(TOK * NCORE + kk)
            c.emit_inc("vector", vch, 1,
                       lambda e, trash=trash: e.tensor_scalar(
                           aux[:, :], dstf[:, :], -trash, None, ALU.add))
            c.emit_wait("vector", vch)
            c.emit_inc("vector", vch, 1,
                       lambda e: e.tensor_tensor(aux[:, :], aux[:, :], msk[:, :], ALU.mult))
            c.emit_wait("vector", vch)
            c.emit_inc("vector", vch, 1,
                       lambda e, trash=trash: e.tensor_scalar(
                           aux[:, :], aux[:, :], trash, None, ALU.add))
            c.emit_wait("vector", vch)
            c.emit_inc("vector", vch, 1,
                       lambda e, kk=kk: e.tensor_copy(dsti[:, kk:kk + 1], aux[:, :]))
            c.emit_wait("vector", vch)
            c.emit_inc("vector", vch, 1,
                       lambda e: e.tensor_tensor(aux[:, :], dstf[:, :], msk[:, :], ALU.mult))
            c.emit_wait("vector", vch)
            c.emit_inc("vector", vch, 1,
                       lambda e, kk=kk: e.tensor_copy(cmbi[:, kk:kk + 1], aux[:, :]))
            c.emit_inc("vector", vch, 1,
                       lambda e, kk=kk: e.tensor_tensor(
                           wtsall[:, kk:kk + 1], wtsall[:, kk:kk + 1], msk[:, :], ALU.mult))

    c.emit_wait("sync", vch)
    c.emit_inc("sync", f"{pref}_wst", 16,
               lambda e: e.dma_start(
                   out=wts_dst.rearrange("(a p) k -> p a k", p=128),
                   in_=wtsall[:, :].rearrange("p (a k) -> p a k", a=NTT)))
    c.emit_inc("sync", f"{pref}_wst", 16,
               lambda e: e.dma_start(
                   out=cmbi_dst.rearrange("(a p) k -> p a k", p=128),
                   in_=cmbi[:, :].rearrange("p (a k) -> p a k", a=NTT)))
    c.emit_wait("gpsimd", vch)
    c.emit_wait("gpsimd", zd, 32)
    c.emit_wait("gpsimd", ld)
    for tt in range(NTT):
        for k in range(4):
            kk = tt * 4 + k
            c.emit_inc("gpsimd", f"{pref}_sct", 16,
                       lambda e, tt=tt, kk=kk: e.indirect_dma_start(
                           bufl, IndirectOffsetOnAxis(dsti[:, kk:kk + 1], 0),
                           agrow[:, tt:tt + 1], None))
    c.emit_wait("gpsimd", f"{pref}_sct", NTT * 4 * 16)
    c.emit_inc("gpsimd", f"{pref}_rs_nowait", 1,
               lambda e: e.collective_compute(
                   "ReduceScatter", ALU.add, replica_groups=[list(range(8))],
                   ins=[bufl[0:TOK * NCORE, :].opt()], outs=[bufm.opt()]))
    c.emit_wait("gpsimd", f"{pref}_wst", 32)


def build_phaseI():
    nc = bass.Bass("TRN2", target_bir_lowering=False, debug=False)
    xn1 = nc.dram_tensor("XN1", [E, TOK], BF16, kind="ExternalInput").ap()
    gwt = nc.dram_tensor("GWT", [E, 16], F32, kind="ExternalInput").ap()
    ltc = nc.dram_tensor("LTC", [32, 4], F32, kind="ExternalInput").ap()
    agrow = nc.dram_tensor("AGROW", [TOK, 1], I32, kind="ExternalInput").ap()
    iota = nc.dram_tensor("IOTA16", [128, 16], F32, kind="ExternalInput").ap()
    identf = nc.dram_tensor("IDENTF", [128, 128], F32, kind="ExternalInput").ap()
    ones_in = nc.dram_tensor("ones128", [128, 128], F32, kind="ExternalInput").ap()
    wts = nc.dram_tensor("WTS", [TOK, 4], F32, kind="ExternalOutput").ap()
    cmbi = nc.dram_tensor("CMBI", [TOK, 4], I32, kind="ExternalOutput").ap()
    bufl = nc.dram_tensor("BUFL", [NCORE * TOK + 64, 1], I32, kind="Internal").ap()
    bufm_i = nc.dram_tensor("BUFM_I", [TOK, 1], I32, kind="Internal").ap()
    bufm = nc.dram_tensor("BUFM", [TOK, 1], I32, kind="ExternalOutput").ap()
    hist_in = nc.dram_tensor("HIST_IN", [4, 16], F32, kind="Internal").ap()
    hist_all = nc.dram_tensor("HIST_ALL", [32, 16], F32, kind="Internal").ap()
    with ExitStack() as st:
        c = Ctx(nc, st)
        onesf = c.sbuf("onesf", [128, 128], F32)
        c.emit_inc("sync", "init", 16, lambda e: e.dma_start(out=onesf[:, :], in_=ones_in))
        c.emit_wait("tensor", "init", 16)
        c.emit_wait("vector", "init", 16)
        emit_route(c, xn1, gwt, ltc, agrow, iota, identf, onesf, wts, cmbi,
                   bufl, bufm_i, hist_in, hist_all)
        c.emit_wait("gpsimd", "I_rs_nowait", 1)
        c.emit_inc("gpsimd", "fcp", 16, lambda e: e.dma_start(out=bufm, in_=bufm_i))
        c.emit_wait("gpsimd", "fcp", 16)
        c.finalize()
    return nc


def host_route_consts(cid):
    def gorder(cc, j):
        return (cc // 2) * 8 + 2 * j + (cc % 2)
    gall = np.array([gorder(cc, j) for cc in range(8) for j in range(4)])
    ltc = np.zeros((32, 4), np.float32)
    for myj in range(4):
        mine = gorder(cid, myj)
        ltc[:, myj] = (gall < mine).astype(np.float32)
    agrow = (cid * TOK + np.arange(TOK)).astype(np.int32).reshape(TOK, 1)
    iota16 = np.tile(np.arange(16, dtype=np.float32), (128, 1))
    identf = np.eye(128, dtype=np.float32)
    return ltc, agrow, iota16, identf



def emit_ag_table(c, src, ag, pad, pref):
    c.emit_inc("gpsimd", f"{pref}_agd", 1,
               lambda e: e.collective_compute(
                   "AllGather", ALU.bypass, replica_groups=[list(range(8))],
                   ins=[src.opt()], outs=[ag.opt()]))


def emit_moe_experts(c, xn1tm_src, bufm_src, wg_src, wu_src, wd_src,
                     xn1ag, xpad, xg, eoutf, eout_dst, pref="J", agpref="JA"):
    NI = MOE_I // 128
    bufm = c.sbuf(f"{pref}_bm", [128, 16], I32)
    gtile = c.sbuf(f"{pref}_gt", [128, E], BF16)
    xgt = c.sbuf(f"{pref}_xg", [128, NF * E], BF16)
    wg = c.sbuf(f"{pref}_wg", [128, 2 * NF * 128], BF16)
    wu = c.sbuf(f"{pref}_wu", [128, 2 * NF * 128], BF16)
    wd = c.sbuf(f"{pref}_wd", [128, 4 * 512], BF16)
    sg = c.sbuf(f"{pref}_sg", [128, 1024], BF16)
    mid = c.sbuf(f"{pref}_md", [128, NI * E], BF16)
    ev = c.sbuf(f"{pref}_ev", [128, 8 * 512], BF16)
    ps = [c.psum(f"{pref}_ps{i}", [128, 512], F32) for i in range(8)]

    agd, bld, gth, xst, tld = (f"{pref}_agd", f"{pref}_bld", f"{pref}_gth",
                               f"{pref}_xst", f"{pref}_tld")
    wld, mmh, sgd, vmd, dld = (f"{pref}_wld", f"{pref}_mmh", f"{pref}_sgd",
                               f"{pref}_vmd", f"{pref}_dld")
    mmo, oev, ost, tst2, est = (f"{pref}_mmo", f"{pref}_oev", f"{pref}_ost",
                                f"{pref}_tst", f"{pref}_est")

    c.emit_wait("sync", "I_rs_nowait", 1)
    c.emit_inc("sync", bld, 16,
               lambda e: e.dma_start(
                   out=bufm[:, :].rearrange("p (a o) -> p a o", o=1),
                   in_=bufm_src.rearrange("(a p) o -> p a o", p=128),
                   allow_slow_non_contiguous=True))
    c.emit_wait("gpsimd", bld, 16)
    c.emit_wait("gpsimd", f"{agpref}_agd", 1)
    for st_i in range(16):
        if st_i >= 1:
            c.emit_wait("gpsimd", xst, st_i * 16)
        c.emit_inc("gpsimd", gth, 16,
                   lambda e, st_i=st_i: e.indirect_dma_start(
                       gtile[:, :], None, xn1ag,
                       IndirectOffsetOnAxis(bufm[:, st_i:st_i + 1], 0)))
        c.emit_wait("gpsimd", gth, (st_i + 1) * 16)
        c.emit_inc("gpsimd", xst, 16,
                   lambda e, st_i=st_i: e.dma_start(
                       out=xg[st_i * 128:(st_i + 1) * 128, :], in_=gtile[:, :]))
    for ft in range(NF):
        c.emit_wait("sync", xst, 16 * 16)
        c.emit_inc("sync", tld, 16,
                   lambda e, ft=ft: e.dma_start(
                       out=xgt[:, ft * E:(ft + 1) * E],
                       in_=xg[:, ft * 128:(ft + 1) * 128], transpose=True))
    for ee in range(2):
        for i_t in range(NI):
            it = ee * NI + i_t
            par = it % 2
            if it >= 2:
                c.emit_wait("gpsimd", mmh, (it - 1) * 2)
            for src, dstb in ((wg_src, wg), (wu_src, wu)):
                ss = src[ee * E:(ee + 1) * E, i_t * 128:(i_t + 1) * 128].rearrange(
                    "(a p) c -> p a c", p=128)
                c.emit_inc("gpsimd", f"{wld}{par}", 16,
                           lambda e, ss=ss, dstb=dstb, par=par: e.dma_start(
                               out=dstb[:, par * NF * 128:(par + 1) * NF * 128]
                               .rearrange("p (a c) -> p a c", a=NF), in_=ss))
            c.emit_wait("tensor", f"{wld}{par}", (it // 2 + 1) * 32)
            if it == 0:
                c.emit_wait("tensor", tld, NF * 16)
            if i_t == 0 and ee == 1:
                c.emit_wait("tensor", oev)
            for n in range(2):
                bb = 2 * n
                c.emit_wait("tensor", sgd, max(0, c.cnt.get(sgd, 0) - 1))
                c.emit_wait("tensor", vmd, max(0, c.cnt.get(vmd, 0) - 1))
                col0 = ee * 1024 + n * 512
                for gi, wsb in ((0, wg), (1, wu)):
                    for e_i in range(NF):
                        fn = lambda e, e_i=e_i, gi=gi, wsb=wsb, par=par, col0=col0, bb=bb: e.matmul(
                            ps[bb + gi][:, :],
                            wsb[:, par * NF * 128 + e_i * 128:par * NF * 128 + (e_i + 1) * 128],
                            xgt[:, e_i * E + col0:e_i * E + col0 + 512],
                            start=(e_i == 0), stop=(e_i == NF - 1))
                        if gi == 1 and e_i == NF - 1:
                            c.emit_inc("tensor", mmh, 1, fn)
                        else:
                            c.emit("tensor", fn)
                c.emit_wait("scalar", mmh)
                c.emit_wait("scalar", vmd, max(0, c.cnt.get(vmd, 0) - 1))
                c.emit_inc("scalar", sgd, 1,
                           lambda e, bb=bb, n=n: e.activation(
                               sg[:, n * 512:(n + 1) * 512], ps[bb][:, :], AF.Silu))
                c.emit_wait("vector", sgd)
                c.emit_inc("vector", vmd, 1,
                           lambda e, i_t=i_t, col0=col0, bb=bb, n=n: e.tensor_tensor(
                               mid[:, i_t * E + col0:i_t * E + col0 + 512],
                               sg[:, n * 512:(n + 1) * 512], ps[bb + 1][:, :], ALU.mult))
        c.emit_wait("tensor", vmd)
        for og in range(4):
            for i_t in range(NI):
                it2 = (ee * 4 + og) * NI + i_t
                par = it2 % 4
                ss = wd_src[ee * MOE_I + i_t * 128:ee * MOE_I + (i_t + 1) * 128,
                            og * 512:(og + 1) * 512]
                if it2 >= 4:
                    c.emit_wait("gpsimd", mmo, (it2 - 3) * 8)
                c.emit_inc("gpsimd", f"{dld}{par}", 16,
                           lambda e, ss=ss, par=par: e.dma_start(
                               out=wd[:, par * 512:(par + 1) * 512], in_=ss))
                c.emit_wait("tensor", f"{dld}{par}", (it2 // 4 + 1) * 16)
                if i_t == 0:
                    c.emit_wait("tensor", oev)
                for o in range(4):
                    for n in range(2):
                        fn = lambda e, o=o, n=n, par=par, i_t=i_t, ee=ee: e.matmul(
                            ps[o * 2 + n][:, :],
                            wd[:, par * 512 + o * 128:par * 512 + (o + 1) * 128],
                            mid[:, i_t * E + ee * 1024 + n * 512:
                                i_t * E + ee * 1024 + (n + 1) * 512],
                            start=(i_t == 0), stop=(i_t == NI - 1))
                        c.emit_inc("tensor", mmo, 1, fn)
            c.emit_wait("vector", mmo)
            c.emit_wait("vector", ost)
            for o in range(4):
                for n in range(2):
                    c.emit_inc("vector", oev, 1,
                               lambda e, o=o, n=n: e.tensor_copy(
                                   ev[:, (o * 2 + n) * 512:(o * 2 + n + 1) * 512],
                                   ps[o * 2 + n][:, :]))
            c.emit_wait("sync", oev)
            for o in range(4):
                dstv = eoutf[(og * 4 + o) * 128:(og * 4 + o + 1) * 128,
                             ee * 1024:(ee + 1) * 1024]
                c.emit_inc("sync", ost, 16,
                           lambda e, d=dstv, o=o: e.dma_start(
                               out=d, in_=ev[:, o * 1024:(o + 1) * 1024]))
    c.emit_wait("sync", ost)
    for st_i in range(16):
        par = st_i % 2
        if st_i >= 2:
            c.emit_wait("sync", f"{est}{par}", (st_i // 2) * 16)
        c.emit_inc("sync", f"{tst2}{par}", 16,
                   lambda e, st_i=st_i, par=par: e.dma_start(
                       out=xgt[:, par * E:(par + 1) * E],
                       in_=eoutf[:, st_i * 128:(st_i + 1) * 128], transpose=True))
        c.emit_wait("sync", f"{tst2}{par}", (st_i // 2 + 1) * 16)
        c.emit_inc("sync", f"{est}{par}", 16,
                   lambda e, st_i=st_i, par=par: e.dma_start(
                       out=eout_dst[st_i * 128:(st_i + 1) * 128, :],
                       in_=xgt[:, par * E:(par + 1) * E]))
    c.emit_wait("sync", f"{est}0")
    c.emit_wait("sync", f"{est}1")


def emit_moe_combine(c, eout_src, cmbi_src, wts_src, eoutag, epad, ytm_dst,
                     pref="K", agpref="KA"):
    cmbi = c.sbuf(f"{pref}_ci", [128, 2 * 4], I32)
    wts = c.sbuf(f"{pref}_wt", [128, 2 * 4], F32)
    rows = [c.sbuf(f"{pref}_r{k}", [128, 2 * E], BF16) for k in range(4)]
    acc = c.sbuf(f"{pref}_ac", [128, E], F32)
    accb = c.sbuf(f"{pref}_ab", [128, E], BF16)

    agd, ild, gth, vch, yst = (f"{pref}_agd", f"{pref}_ild", f"{pref}_gth",
                               f"{pref}_vch", f"{pref}_yst")

    c.emit_wait("gpsimd", f"{agpref}_agd", 1)
    for tt in range(16):
        tp = tt % 2
        if tt >= 2:
            c.emit_wait("sync", vch, (tt - 1) * 4)
        c.emit_inc("sync", f"{ild}{tp}", 16,
                   lambda e, tt=tt, tp=tp: e.dma_start(
                       out=cmbi[:, tp * 4:(tp + 1) * 4],
                       in_=cmbi_src[tt * 128:(tt + 1) * 128, :]))
        c.emit_inc("sync", f"{ild}{tp}", 16,
                   lambda e, tt=tt, tp=tp: e.dma_start(
                       out=wts[:, tp * 4:(tp + 1) * 4],
                       in_=wts_src[tt * 128:(tt + 1) * 128, :]))
        c.emit_wait("gpsimd", f"{ild}{tp}", (tt // 2 + 1) * 32)
        if tt >= 2:
            c.emit_wait("gpsimd", vch, (tt - 1) * 4)
        for k in range(4):
            c.emit_inc("gpsimd", gth, 16,
                       lambda e, k=k, tp=tp: e.indirect_dma_start(
                           rows[k][:, tp * E:(tp + 1) * E], None, eoutag,
                           IndirectOffsetOnAxis(cmbi[:, tp * 4 + k:tp * 4 + k + 1], 0)))
        c.emit_wait("vector", gth, (tt + 1) * 64)
        c.emit_wait("vector", yst)
        c.emit_inc("vector", vch, 1,
                   lambda e, tp=tp: e.tensor_scalar(
                       acc[:, :], rows[0][:, tp * E:(tp + 1) * E],
                       wts[:, tp * 4:tp * 4 + 1], None, ALU.mult))
        for k in range(1, 4):
            c.emit_wait("vector", vch)
            last = (k == 3)
            c.emit_inc("vector", vch, 1,
                       lambda e, k=k, last=last, tp=tp: e.scalar_tensor_tensor(
                           accb[:, :] if last else acc[:, :],
                           rows[k][:, tp * E:(tp + 1) * E],
                           wts[:, tp * 4 + k:tp * 4 + k + 1], acc[:, :],
                           ALU.mult, ALU.add))
        c.emit_wait("sync", vch)
        c.emit_inc("sync", yst, 16,
                   lambda e, tt=tt: e.dma_start(
                       out=ytm_dst[tt * 128:(tt + 1) * 128, :], in_=accb[:, :]))
    c.emit_wait("sync", yst)


def emit_shared_final(c, xn1_src, swg_src, swu_src, swd_src, x2a_src, ytm_src,
                      out_dst, pref="L", xsh_mode=False):
    NI = MOE_IS // 128
    TC = 1024
    xn = c.sbuf(f"{pref}_xn", [128, NF * TC], BF16)
    wg = c.sbuf(f"{pref}_wg", [128, 2 * NF * 128], BF16)
    wu = c.sbuf(f"{pref}_wu", [128, 2 * NF * 128], BF16)
    wd = c.sbuf(f"{pref}_wd", [128, 4 * 512], BF16)
    sg = c.sbuf(f"{pref}_sg", [128, 1024], BF16)
    mid = c.sbuf(f"{pref}_md", [128, NI * TC], BF16)
    x2a = c.sbuf(f"{pref}_x2", [128, 4 * TC], F32)
    ytt = c.sbuf(f"{pref}_yt", [128, 4 * TC], BF16)
    ot = c.sbuf(f"{pref}_o", [128, 4 * TC], F32)
    ps = [c.psum(f"{pref}_ps{i}", [128, 512], F32) for i in range(8)]

    xld, wld, mmh, sgd, vmd = (f"{pref}_xld", f"{pref}_wld", f"{pref}_mmh",
                               f"{pref}_sgd", f"{pref}_vmd")
    dld, mmo, x2ld, ytld, ev, st = (f"{pref}_dld", f"{pref}_mmo", f"{pref}_x2ld",
                                    f"{pref}_ytld", f"{pref}_ev", f"{pref}_st")

    for tc in range(2):
        c.emit_wait("sync", vmd)
        src = xn1_src[:, tc * TC:(tc + 1) * TC].rearrange("(a p) c -> p a c", p=128)
        c.emit_inc("sync", xld, 16,
                   lambda e, s=src: e.dma_start(
                       out=xn[:, :].rearrange("p (a c) -> p a c", a=NF), in_=s))
        for i_t in range(NI):
            par = i_t % 2
            it = tc * NI + i_t
            if it >= 2:
                c.emit_wait("gpsimd", mmh, (it - 1) * 2)
            for src_w, dstb in ((swg_src, wg), (swu_src, wu)):
                ss = src_w[:, i_t * 128:(i_t + 1) * 128].rearrange("(a p) c -> p a c", p=128)
                c.emit_inc("gpsimd", f"{wld}{par}", 16,
                           lambda e, ss=ss, dstb=dstb, par=par: e.dma_start(
                               out=dstb[:, par * NF * 128:(par + 1) * NF * 128]
                               .rearrange("p (a c) -> p a c", a=NF), in_=ss))
            c.emit_wait("tensor", f"{wld}{par}", (it // 2 + 1) * 32)
            c.emit_wait("tensor", xld, (tc + 1) * 16)
            if tc > 0 and i_t == 0:
                c.emit_wait("tensor", ev)
            for n in range(2):
                bb = 2 * n
                c.emit_wait("tensor", sgd, max(0, c.cnt.get(sgd, 0) - 1))
                c.emit_wait("tensor", vmd, max(0, c.cnt.get(vmd, 0) - 1))
                for gi, wsb in ((0, wg), (1, wu)):
                    for e_i in range(NF):
                        fn = lambda e, e_i=e_i, gi=gi, wsb=wsb, par=par, n=n, bb=bb: e.matmul(
                            ps[bb + gi][:, :],
                            wsb[:, par * NF * 128 + e_i * 128:par * NF * 128 + (e_i + 1) * 128],
                            xn[:, e_i * TC + n * 512:e_i * TC + (n + 1) * 512],
                            start=(e_i == 0), stop=(e_i == NF - 1))
                        if gi == 1 and e_i == NF - 1:
                            c.emit_inc("tensor", mmh, 1, fn)
                        else:
                            c.emit("tensor", fn)
                c.emit_wait("scalar", mmh)
                c.emit_wait("scalar", vmd, max(0, c.cnt.get(vmd, 0) - 1))
                c.emit_inc("scalar", sgd, 1,
                           lambda e, bb=bb, n=n: e.activation(
                               sg[:, n * 512:(n + 1) * 512], ps[bb][:, :], AF.Silu))
                c.emit_wait("vector", sgd)
                c.emit_inc("vector", vmd, 1,
                           lambda e, i_t=i_t, n=n, bb=bb: e.tensor_tensor(
                               mid[:, i_t * TC + n * 512:i_t * TC + (n + 1) * 512],
                               sg[:, n * 512:(n + 1) * 512], ps[bb + 1][:, :], ALU.mult))
        c.emit_wait("tensor", vmd)
        for grp in range(4):
            gg = tc * 4 + grp
            c.emit_wait("sync", ev)
            if not xsh_mode:
                x2s = x2a_src[grp * 512:(grp + 1) * 512, tc * TC:(tc + 1) * TC]
                c.emit_inc("sync", x2ld, 16,
                           lambda e, s=x2s: e.dma_start(
                               out=x2a[:, :].rearrange("p (a c) -> p a c", a=4),
                               in_=s.rearrange("(a p) c -> p a c", p=128)))
                for o in range(4):
                    c.emit_inc("sync", ytld, 16,
                               lambda e, o=o, grp=grp, tc=tc: e.dma_start(
                                   out=ytt[:, o * TC:(o + 1) * TC],
                                   in_=ytm_src[tc * TC:(tc + 1) * TC,
                                               grp * 512 + o * 128:grp * 512 + (o + 1) * 128],
                                   transpose=True))
            for i_t in range(NI):
                it2 = gg * NI + i_t
                par = it2 % 4
                ss = swd_src[i_t * 128:(i_t + 1) * 128, grp * 512:(grp + 1) * 512]
                if it2 >= 4:
                    c.emit_wait("gpsimd", mmo, (it2 - 3) * 8)
                c.emit_inc("gpsimd", f"{dld}{par}", 16,
                           lambda e, ss=ss, par=par: e.dma_start(
                               out=wd[:, par * 512:(par + 1) * 512], in_=ss))
                c.emit_wait("tensor", f"{dld}{par}", (it2 // 4 + 1) * 16)
                if i_t == 0:
                    c.emit_wait("tensor", ev)
                for o in range(4):
                    for n in range(2):
                        fn = lambda e, o=o, n=n, par=par, i_t=i_t: e.matmul(
                            ps[o * 2 + n][:, :],
                            wd[:, par * 512 + o * 128:par * 512 + (o + 1) * 128],
                            mid[:, i_t * TC + n * 512:i_t * TC + (n + 1) * 512],
                            start=(i_t == 0), stop=(i_t == NI - 1))
                        c.emit_inc("tensor", mmo, 1, fn)
            c.emit_wait("vector", mmo)
            if not xsh_mode:
                c.emit_wait("vector", x2ld, (gg + 1) * 16)
                c.emit_wait("vector", ytld, (gg + 1) * 64)
            c.emit_wait("vector", st)
            for o in range(4):
                for n in range(2):
                    c.emit_wait("vector", ev)
                    if xsh_mode:
                        c.emit_inc("vector", ev, 1,
                                   lambda e, o=o, n=n: e.tensor_copy(
                                       ot[:, o * TC + n * 512:o * TC + (n + 1) * 512],
                                       ps[o * 2 + n][:, :]))
                        continue
                    c.emit_inc("vector", ev, 1,
                               lambda e, o=o, n=n: e.tensor_tensor(
                                   ot[:, o * TC + n * 512:o * TC + (n + 1) * 512],
                                   ps[o * 2 + n][:, :],
                                   x2a[:, o * TC + n * 512:o * TC + (n + 1) * 512],
                                   ALU.add))
                    c.emit_wait("vector", ev)
                    c.emit_inc("vector", ev, 1,
                               lambda e, o=o, n=n: e.tensor_tensor(
                                   ot[:, o * TC + n * 512:o * TC + (n + 1) * 512],
                                   ot[:, o * TC + n * 512:o * TC + (n + 1) * 512],
                                   ytt[:, o * TC + n * 512:o * TC + (n + 1) * 512],
                                   ALU.add))
            c.emit_wait("sync", ev)
            dstv = out_dst[grp * 512:(grp + 1) * 512, tc * TC:(tc + 1) * TC]
            c.emit_inc("sync", st, 16,
                       lambda e, d=dstv: e.dma_start(
                           out=d.rearrange("(a p) c -> p a c", p=128),
                           in_=ot[:, :].rearrange("p (a c) -> p a c", a=4)))
    c.emit_wait("sync", st)






def _core_cols(b_idx, role):
    cols = []
    for j in range(NCH):
        g = 2 * j + role
        s = b_idx * T + g * QCH
        cols.append(np.arange(s, s + QCH))
    return np.concatenate(cols)


def host_prep(inputs):
    x = np.asarray(inputs["L0_attn_x"], dtype=np.float32).reshape(B * T, E)
    per_core = []
    for cid in range(NCORE):
        b_idx, role = cid // 2, cid % 2
        cols = _core_cols(b_idx, role)
        m = {}
        m["xT_own"] = np.ascontiguousarray(x[cols].T)
        m["ones128"] = np.ones((128, 128), np.float32)
        per_core.append(m)
    return per_core


def build_phaseA():
    nc = bass.Bass("TRN2", target_bir_lowering=False, debug=False)
    xT = nc.dram_tensor("xT_own", [E, TOK], F32, kind="ExternalInput").ap()
    ones_in = nc.dram_tensor("ones128", [128, 128], F32, kind="ExternalInput").ap()
    xn0 = nc.dram_tensor("XN0", [E, TOK], BF16, kind="ExternalOutput").ap()
    with ExitStack() as st:
        c = Ctx(nc, st)
        onesf = c.sbuf("onesf", [128, 128], F32)
        c.emit_inc("sync", "init", 16, lambda e: e.dma_start(out=onesf[:, :], in_=ones_in))
        c.emit_wait("tensor", "init", 16)
        emit_ln(c, xT, xn0, "A", ncols=TOK, onesf=onesf)
        c.finalize()
    return nc


_PROGS = {}
FUSED = True


def _prog(name):
    if name not in _PROGS:
        _PROGS[name] = globals()[f"build_phase{name}"]()
    return _PROGS[name]


def kernel(**inputs):
    f32 = np.float32
    x = np.asarray(inputs["L0_attn_x"], dtype=f32).reshape(B * T, E)
    WqT = np.ascontiguousarray(np.asarray(inputs["L0_attn_Wq"], f32).T)
    KVW = np.ascontiguousarray(np.concatenate(
        [np.asarray(inputs["L0_attn_Wk"], f32).T,
         np.asarray(inputs["L0_attn_Wv"], f32).T], axis=1))
    WoT = np.ascontiguousarray(np.asarray(inputs["L0_attn_Wo"], f32).T)
    LAMS = np.concatenate([np.asarray(inputs[k], f32).ravel() for k in
                           ("L0_attn_lambda_q1", "L0_attn_lambda_k1",
                            "L0_attn_lambda_q2", "L0_attn_lambda_k2")]).reshape(1, 256)
    LNW = np.asarray(inputs["L0_attn_ln_w"], f32).reshape(128, 1)
    W1T = np.ascontiguousarray(np.asarray(inputs["L0_ffn_w1"], f32).T)
    W2T = np.ascontiguousarray(np.asarray(inputs["L0_ffn_w2"], f32).T)
    W3T = np.ascontiguousarray(np.asarray(inputs["L0_ffn_w3"], f32).T)
    Wq1T = np.ascontiguousarray(np.asarray(inputs["L1_attn_Wq"], f32).T)
    BQ1 = np.asarray(inputs["L1_attn_bq"], f32).reshape(E, 1)
    WkvT = np.ascontiguousarray(np.asarray(inputs["L1_attn_Wkv"], f32).T)
    BKV = np.asarray(inputs["L1_attn_bkv"], f32).reshape(256, 1)
    GWT = np.ascontiguousarray(np.asarray(inputs["L1_ffn_gate_w"], f32).T)
    wgate = np.asarray(inputs["L1_ffn_w_gate"], f32)
    wup = np.asarray(inputs["L1_ffn_w_up"], f32)
    wdn = np.asarray(inputs["L1_ffn_w_down"], f32)
    SWG = np.ascontiguousarray(np.asarray(inputs["L1_ffn_sw_gate"], f32))
    SWU = np.ascontiguousarray(np.asarray(inputs["L1_ffn_sw_up"], f32))
    SWD = np.ascontiguousarray(np.asarray(inputs["L1_ffn_sw_down"], f32))
    ones = np.ones((128, 128), f32)
    identb = np.eye(128, dtype=bf)
    onesb = np.ones((128, 128), bf)
    cores = list(range(NCORE))
    allcols = [_core_cols(cid // 2, cid % 2) for cid in cores]
    consts = [host_route_consts(cid) for cid in cores]

    def run(name, maps):
        return run_bass_kernel_spmd(_prog(name), maps, core_ids=cores).results

    xT = [np.ascontiguousarray(x[allcols[cid]].T) for cid in cores]
    if FUSED:
        if "FUSED" not in _PROGS:
            _PROGS["FUSED"] = build_fused()
        maps = []
        for c in cores:
            maps.append({
                "xT_own": xT[c], "ones128": ones, "WqT": WqT, "KVW": KVW,
                "LAMS": LAMS, "LNW": LNW, "MASK8": host_masks(c % 2),
                "IDENT": identb, "ONESB": onesb, "WoT": WoT,
                "W1T": W1T, "W2T": W2T, "W3T": W3T,
                "Wq1T": Wq1T, "BQ1": BQ1, "WkvT": WkvT, "BKV": BKV,
                "GWT": GWT, "LTC": consts[c][0], "AGROW": consts[c][1],
                "IOTA16": consts[c][2], "IDENTF": consts[c][3],
                "WG_MY": wgate[2 * c:2 * c + 2].reshape(2 * E, MOE_I),
                "WU_MY": wup[2 * c:2 * c + 2].reshape(2 * E, MOE_I),
                "WD_MY": wdn[2 * c:2 * c + 2].reshape(2 * MOE_I, E),
                "SWG": SWG, "SWU": SWU, "SWD": SWD,
            })
        res = run_bass_kernel_spmd(_PROGS["FUSED"], maps, core_ids=cores).results
        out = np.zeros((B * T, E), np.float32)
        for cid in cores:
            out[allcols[cid]] = np.asarray(res[cid]["OUT"]).T
        return out.reshape(B, T, E)

    rA = run("A", [{"xT_own": xT[c], "ones128": ones} for c in cores])
    rB = run("B", [{"XN0": rA[c]["XN0"], "WqT": WqT, "KVW": KVW, "ones128": ones}
                   for c in cores])
    rC = run("C", [{"QT": rB[c]["QT"], "KV_ALL": rB[c]["KV_ALL"],
                    "QMAX": rB[c]["QMAX"], "LAMS": LAMS, "LNW": LNW,
                    "MASK8": host_masks(c % 2), "IDENT": identb, "ONESB": onesb,
                    "ones128": ones} for c in cores])
    rD = run("D", [{"ATTN": rC[c]["ATTN"], "WoT": WoT, "xT_own": xT[c]}
                   for c in cores])
    rE = run("E", [{"W1T": W1T, "W2T": W2T, "W3T": W3T, "ones128": ones}
                   for c in cores])
    rF = run("F", [{"XN0": rA[c]["XN0"], "G1": rE[c]["G1"], "G2": rE[c]["G2"],
                    "G3": rE[c]["G3"], "X1": rD[c]["X1"]} for c in cores])
    rG = run("G", [{"X1F": rF[c]["X1F"], "ones128": ones} for c in cores])
    rH = run("H", [{"XN1": rG[c]["XN1"], "Wq1T": Wq1T, "BQ1": BQ1, "WkvT": WkvT,
                    "BKV": BKV, "X1F": rF[c]["X1F"], "ones128": ones}
                   for c in cores])
    rI = run("I", [{"XN1": rG[c]["XN1"], "GWT": GWT, "LTC": consts[c][0],
                    "AGROW": consts[c][1], "IOTA16": consts[c][2],
                    "IDENTF": consts[c][3], "ones128": ones} for c in cores])
    rJ = run("J", [{"XN1TM": rG[c]["XN1TM"], "BUFM": rI[c]["BUFM"],
                    "WG_MY": wgate[2 * c:2 * c + 2].reshape(2 * E, MOE_I),
                    "WU_MY": wup[2 * c:2 * c + 2].reshape(2 * E, MOE_I),
                    "WD_MY": wdn[2 * c:2 * c + 2].reshape(2 * MOE_I, E)}
                   for c in cores])
    rK = run("K", [{"EOUT": rJ[c]["EOUT"], "CMBI": rI[c]["CMBI"],
                    "WTS": rI[c]["WTS"]} for c in cores])
    rL = run("L", [{"XN1": rG[c]["XN1"], "SWG": SWG, "SWU": SWU, "SWD": SWD,
                    "X2A": rH[c]["X2A"], "YTM": rK[c]["YTM"]} for c in cores])
    out = np.zeros((B * T, E), np.float32)
    for cid in cores:
        out[allcols[cid]] = np.asarray(rL[cid]["OUT"]).T
    return out.reshape(B, T, E)


if __name__ == "__main__":
    pass


def build_phaseD():
    nc = bass.Bass("TRN2", target_bir_lowering=False, debug=False)
    attn = nc.dram_tensor("ATTN", [E, TOK], BF16, kind="ExternalInput").ap()
    wo = nc.dram_tensor("WoT", [E, E], F32, kind="ExternalInput").ap()
    xt = nc.dram_tensor("xT_own", [E, TOK], F32, kind="ExternalInput").ap()
    x1 = nc.dram_tensor("X1", [E, TOK], F32, kind="ExternalOutput").ap()
    with ExitStack() as st:
        c = Ctx(nc, st)
        emit_wo(c, attn, wo, xt, x1)
        c.finalize()
    return nc


def build_phaseE():
    nc = bass.Bass("TRN2", target_bir_lowering=False, debug=False)
    w1 = nc.dram_tensor("W1T", [E, FFN_H], F32, kind="ExternalInput").ap()
    w2 = nc.dram_tensor("W2T", [E, FFN_H], F32, kind="ExternalInput").ap()
    w3 = nc.dram_tensor("W3T", [FFN_H, E], F32, kind="ExternalInput").ap()
    ones_in = nc.dram_tensor("ones128", [128, 128], F32, kind="ExternalInput").ap()
    g1 = nc.dram_tensor("G1", [E, FFN_H], BF16, kind="ExternalOutput").ap()
    g2 = nc.dram_tensor("G2", [E, FFN_H], BF16, kind="ExternalOutput").ap()
    g3 = nc.dram_tensor("G3", [FFN_H, E], BF16, kind="ExternalOutput").ap()
    with ExitStack() as st:
        c = Ctx(nc, st)
        onesf = c.sbuf("onesf", [128, 128], F32)
        c.emit_inc("sync", "init", 16, lambda e: e.dma_start(out=onesf[:, :], in_=ones_in))
        c.emit_wait("tensor", "init", 16)
        sh = quant_bufs(c, FFN_H)
        for i, (w, g, R, CC) in enumerate((
                (w1, g1, E, FFN_H), (w2, g2, E, FFN_H), (w3, g3, FFN_H, E))):
            if i > 0:
                for eng in ("sync", "scalar", "vector", "tensor"):
                    c.emit_wait(eng, f"E{i}_st0")
                    c.emit_wait(eng, f"E{i}_st1")
            emit_quant(c, w, g, R, CC, onesf, f"E{i + 1}", sh)
        c.finalize()
    return nc


def build_phaseF():
    nc = bass.Bass("TRN2", target_bir_lowering=False, debug=False)
    xn = nc.dram_tensor("XN0", [E, TOK], BF16, kind="ExternalInput").ap()
    g1 = nc.dram_tensor("G1", [E, FFN_H], BF16, kind="ExternalInput").ap()
    g2 = nc.dram_tensor("G2", [E, FFN_H], BF16, kind="ExternalInput").ap()
    g3 = nc.dram_tensor("G3", [FFN_H, E], BF16, kind="ExternalInput").ap()
    x1 = nc.dram_tensor("X1", [E, TOK], F32, kind="ExternalInput").ap()
    x1f = nc.dram_tensor("X1F", [E, TOK], F32, kind="ExternalOutput").ap()
    with ExitStack() as st:
        c = Ctx(nc, st)
        emit_ffn(c, xn, g1, g2, g3, x1, x1f)
        c.finalize()
    return nc


def build_phaseJ():
    nc = bass.Bass("TRN2", target_bir_lowering=False, debug=False)
    xn1tm = nc.dram_tensor("XN1TM", [TOK, E], BF16, kind="ExternalInput").ap()
    bufm = nc.dram_tensor("BUFM", [TOK, 1], I32, kind="ExternalInput").ap()
    wg = nc.dram_tensor("WG_MY", [2 * E, MOE_I], F32, kind="ExternalInput").ap()
    wu = nc.dram_tensor("WU_MY", [2 * E, MOE_I], F32, kind="ExternalInput").ap()
    wd = nc.dram_tensor("WD_MY", [2 * MOE_I, E], F32, kind="ExternalInput").ap()
    xn1ag = nc.dram_tensor("XN1AG", [NCORE * TOK, E], BF16, kind="Internal").ap()
    xpad = nc.dram_tensor("XPAD", [NCORE * TOK + 1, E], BF16, kind="Internal").ap()
    xg = nc.dram_tensor("XG", [TOK, E], BF16, kind="Internal").ap()
    eoutf = nc.dram_tensor("EOUTF", [E, TOK], BF16, kind="Internal").ap()
    eout = nc.dram_tensor("EOUT", [TOK, E], BF16, kind="ExternalOutput").ap()
    xn1tm_b = nc.dram_tensor("XN1TM_B", [TOK, E], BF16, kind="Internal").ap()
    with ExitStack() as st:
        c = Ctx(nc, st)
        c.emit_inc("gpsimd", "J_bnc", 16, lambda e: e.dma_start(out=xn1tm_b, in_=xn1tm))
        c.emit_wait("gpsimd", "J_bnc", 16)
        emit_ag_table(c, xn1tm_b, xn1ag, xpad, "JA")
        emit_moe_experts(c, xn1tm_b, bufm, wg, wu, wd, xn1ag, xpad, xg, eoutf, eout)
        c.finalize()
    return nc


def build_phaseK():
    nc = bass.Bass("TRN2", target_bir_lowering=False, debug=False)
    eout = nc.dram_tensor("EOUT", [TOK, E], BF16, kind="ExternalInput").ap()
    cmbi = nc.dram_tensor("CMBI", [TOK, 4], I32, kind="ExternalInput").ap()
    wts = nc.dram_tensor("WTS", [TOK, 4], F32, kind="ExternalInput").ap()
    eoutag = nc.dram_tensor("EOUTAG", [NCORE * TOK, E], BF16, kind="Internal").ap()
    epad = nc.dram_tensor("EPAD", [NCORE * TOK + 1, E], BF16, kind="Internal").ap()
    ytm = nc.dram_tensor("YTM", [TOK, E], BF16, kind="ExternalOutput").ap()
    eout_b = nc.dram_tensor("EOUT_B", [TOK, E], BF16, kind="Internal").ap()
    with ExitStack() as st:
        c = Ctx(nc, st)
        c.emit_inc("gpsimd", "K_bnc", 16, lambda e: e.dma_start(out=eout_b, in_=eout))
        c.emit_wait("gpsimd", "K_bnc", 16)
        emit_ag_table(c, eout_b, eoutag, epad, "KA")
        emit_moe_combine(c, eout_b, cmbi, wts, eoutag, epad, ytm)
        c.finalize()
    return nc


def build_phaseL():
    nc = bass.Bass("TRN2", target_bir_lowering=False, debug=False)
    xn1 = nc.dram_tensor("XN1", [E, TOK], BF16, kind="ExternalInput").ap()
    swg = nc.dram_tensor("SWG", [E, MOE_IS], F32, kind="ExternalInput").ap()
    swu = nc.dram_tensor("SWU", [E, MOE_IS], F32, kind="ExternalInput").ap()
    swd = nc.dram_tensor("SWD", [MOE_IS, E], F32, kind="ExternalInput").ap()
    x2a = nc.dram_tensor("X2A", [E, TOK], F32, kind="ExternalInput").ap()
    ytm = nc.dram_tensor("YTM", [TOK, E], BF16, kind="ExternalInput").ap()
    out = nc.dram_tensor("OUT", [E, TOK], F32, kind="ExternalOutput").ap()
    with ExitStack() as st:
        c = Ctx(nc, st)
        emit_shared_final(c, xn1, swg, swu, swd, x2a, ytm, out)
        c.finalize()
    return nc


def build_fused():
    nc = bass.Bass("TRN2", target_bir_lowering=False, debug=False)
    EI, INT = "ExternalInput", "Internal"
    d = lambda n, shp, dt, kind: nc.dram_tensor(n, shp, dt, kind=kind).ap()
    xT = d("xT_own", [E, TOK], F32, EI)
    ones_in = d("ones128", [128, 128], F32, EI)
    wq = d("WqT", [E, E], F32, EI)
    kvw = d("KVW", [E, 1024], F32, EI)
    lams = d("LAMS", [1, 256], F32, EI)
    lnw = d("LNW", [128, 1], F32, EI)
    mask = d("MASK8", [8 * 128, QCH], BF16, EI)
    identb = d("IDENT", [128, 128], BF16, EI)
    onesb_in = d("ONESB", [128, 128], BF16, EI)
    wo = d("WoT", [E, E], F32, EI)
    w1 = d("W1T", [E, FFN_H], F32, EI)
    w2 = d("W2T", [E, FFN_H], F32, EI)
    w3 = d("W3T", [FFN_H, E], F32, EI)
    wq1 = d("Wq1T", [E, E], F32, EI)
    bq1 = d("BQ1", [E, 1], F32, EI)
    wkv1 = d("WkvT", [E, 256], F32, EI)
    bkv1 = d("BKV", [256, 1], F32, EI)
    gwt = d("GWT", [E, 16], F32, EI)
    ltc = d("LTC", [32, 4], F32, EI)
    agrow = d("AGROW", [TOK, 1], I32, EI)
    iota = d("IOTA16", [128, 16], F32, EI)
    identf = d("IDENTF", [128, 128], F32, EI)
    wgm = d("WG_MY", [2 * E, MOE_I], F32, EI)
    wum = d("WU_MY", [2 * E, MOE_I], F32, EI)
    wdm = d("WD_MY", [2 * MOE_I, E], F32, EI)
    swg = d("SWG", [E, MOE_IS], F32, EI)
    swu = d("SWU", [E, MOE_IS], F32, EI)
    swd = d("SWD", [MOE_IS, E], F32, EI)

    xn0 = d("XN0", [E, TOK], BF16, INT)
    qt = d("QT", [E, TOK], BF16, INT)
    kv_own = d("KV_OWN", [TOK, 1024], BF16, INT)
    kv_all = d("KV_ALL", [2 * TOK, 1024], BF16, INT)
    qmax = d("QMAX", [1, NH], F32, INT)
    attn = d("ATTN", [E, TOK], BF16, INT)
    x1 = d("X1", [E, TOK], F32, INT)
    g1 = d("G1", [E, FFN_H], BF16, INT)
    g2 = d("G2", [E, FFN_H], BF16, INT)
    g3 = d("G3", [FFN_H, E], BF16, INT)
    x1f = d("X1F", [E, TOK], F32, INT)
    xn1 = d("XN1", [E, TOK], BF16, INT)
    xn1tm = d("XN1TM", [TOK, E], BF16, INT)
    x2a = d("X2A", [E, TOK], F32, INT)
    wts = d("WTS", [TOK, 4], F32, INT)
    cmbi = d("CMBI", [TOK, 4], I32, INT)
    bufl = d("BUFL", [NCORE * TOK + 64, 1], I32, INT)
    bufm = d("BUFM", [TOK, 1], I32, INT)
    hist_in = d("HIST_IN", [4, 16], F32, INT)
    hist_all = d("HIST_ALL", [32, 16], F32, INT)
    xn1ag = nc.dram_tensor("XN1AG", [NCORE * TOK, E], BF16, kind=INT, addr_space="Shared").ap()
    xpad = d("XPAD", [NCORE * TOK + 1, E], BF16, INT)
    xg = d("XG", [TOK, E], BF16, INT)
    eoutf = d("EOUTF", [E, TOK], BF16, INT)
    eout = d("EOUT", [TOK, E], BF16, INT)
    eoutag = nc.dram_tensor("EOUTAG", [NCORE * TOK, E], BF16, kind=INT, addr_space="Shared").ap()
    epad = d("EPAD", [NCORE * TOK + 1, E], BF16, INT)
    ytm = d("YTM", [TOK, E], BF16, INT)
    xsh = d("XSH", [E, TOK], F32, INT)
    out = d("OUT", [E, TOK], F32, "ExternalOutput")

    with ExitStack() as gst:
        c = Ctx(nc, gst)
        c.sem_st = gst
        bar = c.get_sem("BAR")
        onesf = c.sbuf("onesf", [128, 128], F32)
        c.emit_inc("sync", "init", 16, lambda e: e.dma_start(out=onesf[:, :], in_=ones_in))
        for eng in Ctx.ENGS:
            c.emit_wait(eng, "init", 16)
        c.phase_names = []

        phases = [
            (lambda: emit_ln(c, xT, xn0, "A", ncols=TOK, onesf=onesf), 1, 1, 0),
            (lambda: emit_qkv(c, xn0, wq, kvw, onesf, qt, kv_own, kv_own,
                              kv_all, qmax), 1, 1, 0),
            (lambda: emit_attn(c, qt, kv_all, qmax, lams, lnw, mask, identb,
                               onesb_in, onesf, attn), 1, 1, 0),
            (lambda: emit_wo(c, attn, wo, xT, x1), 1, 1, 0),
            (lambda: _emit_quant3(c, w1, w2, w3, g1, g2, g3, onesf), 1, 1, 0),
            (lambda: emit_ffn(c, xn0, g1, g2, g3, x1, x1f), 1, 1, 0),
            (lambda: emit_ln1_tm(c, x1f, xn1, xn1tm, onesf), 1, 1, 0),
            (lambda: emit_ag_table(c, xn1tm, xn1ag, xpad, "JA"), 0, 0, 1),
            (lambda: emit_route(c, xn1, gwt, ltc, agrow, iota, identf, onesf,
                                wts, cmbi, bufl, bufm, hist_in, hist_all), 1, 1, 0),
            (lambda: emit_l1attn(c, xn1, wq1, bq1, wkv1, bkv1, x1f, x2a, onesf),
             1, 1, 0),
            (lambda: emit_moe_experts(c, xn1tm, bufm, wgm, wum, wdm, xn1ag,
                                      xpad, xg, eoutf, eout), 1, 1, 0),
            (lambda: emit_ag_table(c, eout, eoutag, epad, "KA"), 0, 0, 1),
            (lambda: emit_shared_final(c, xn1, swg, swu, swd, None, None, xsh,
                                       xsh_mode=True), 1, 1, 0),
            (lambda: emit_moe_combine(c, eout, cmbi, wts, eoutag, epad, ytm),
             1, 1, 0),
            (lambda: emit_final(c, x2a, ytm, xsh, out), 1, 1, 0),
        ]
        nbar = 0
        for ph, dobar, doretire, gsb in phases:
            with ExitStack() as pst:
                c.st = gst if gsb else pst
                ph()
                if dobar:
                    for name in list(c.phase_names):
                        if name.endswith("_nowait"):
                            continue
                        c.emit_wait("sync", name)
                    nbar += 1
                    for eng in Ctx.ENGS:
                        c.plan_inc("BAR", 1)
                        c.q[eng].append(lambda e, s=bar: e.nop().then_inc(s, 1))
                    for eng in Ctx.ENGS:
                        c.emit_wait(eng, "BAR", nbar * 5)
            if doretire:
                c.retire_phase()
            else:
                c.phase_names = []
        c.finalize()
    return nc


def _emit_quant3(c, w1, w2, w3, g1, g2, g3, onesf):
    sh = quant_bufs(c, FFN_H)
    mats = ((w1, g1, E, FFN_H), (w2, g2, E, FFN_H), (w3, g3, FFN_H, E))
    for i, (w, g, R, CC) in enumerate(mats):
        pref = f"Ea{i + 1}"
        if i > 0:
            c.emit_wait("sync", f"Ea{i}_vch")
        c.get_sem(f"{pref}_prev")
        if i > 0:
            c.plan_inc(f"{pref}_prev", 0)
            cprev = c.cnt[f"Ea{i}_vch"]
            c.q["vector"].append(lambda e, s=c.get_sem(f"Ea{i}_vch"), v=cprev: e.wait_ge(s, v))
        emit_gamma(c, w, R, CC, i, onesf, sh, pref)
    for i, (w, g, R, CC) in enumerate(mats):
        pref = f"Eb{i + 1}"
        c.get_sem(f"{pref}_prevv")
        if i == 0:
            prevn, prevc = "Ea3_vch", c.cnt["Ea3_vch"]
        else:
            prevn, prevc = f"Eb{i}_sch", c.cnt[f"Eb{i}_sch"]
        c.q["sync"].append(lambda e, s=c.get_sem(prevn), v=prevc: e.wait_ge(s, v))
        c.q["scalar"].append(lambda e, s=c.get_sem(f"Ea{i + 1}_done"),
                             v=c.cnt[f"Ea{i + 1}_done"]: e.wait_ge(s, v))
        emit_qwrite(c, w, g, R, CC, i, sh, pref)


def emit_final(c, x2a_src, ytm_src, xsh_src, out_dst, pref="M"):
    TC = 1024
    x2a = c.sbuf(f"{pref}_x2", [128, 4 * TC], F32)
    xsh = c.sbuf(f"{pref}_xs", [128, 4 * TC], F32)
    ytt = c.sbuf(f"{pref}_yt", [128, 4 * TC], BF16)
    ot = c.sbuf(f"{pref}_o", [128, 4 * TC], F32)
    ld, ytld, ev, st = f"{pref}_ld", f"{pref}_ytld", f"{pref}_ev", f"{pref}_st"
    for tc in range(2):
        for grp in range(4):
            gg = tc * 4 + grp
            c.emit_wait("sync", ev)
            for src_t, dst_t in ((x2a_src, x2a), (xsh_src, xsh)):
                ss = src_t[grp * 512:(grp + 1) * 512, tc * TC:(tc + 1) * TC]
                c.emit_inc("sync", ld, 16,
                           lambda e, ss=ss, dst_t=dst_t: e.dma_start(
                               out=dst_t[:, :].rearrange("p (a c) -> p a c", a=4),
                               in_=ss.rearrange("(a p) c -> p a c", p=128)))
            for o in range(4):
                c.emit_inc("sync", ytld, 16,
                           lambda e, o=o, grp=grp, tc=tc: e.dma_start(
                               out=ytt[:, o * TC:(o + 1) * TC],
                               in_=ytm_src[tc * TC:(tc + 1) * TC,
                                           grp * 512 + o * 128:grp * 512 + (o + 1) * 128],
                               transpose=True))
            c.emit_wait("vector", ld, (gg + 1) * 32)
            c.emit_wait("vector", ytld, (gg + 1) * 64)
            c.emit_wait("vector", st)
            for q in range(8):
                c.emit_wait("vector", ev)
                c.emit_inc("vector", ev, 1,
                           lambda e, q=q: e.tensor_tensor(
                               ot[:, q * 512:(q + 1) * 512], x2a[:, q * 512:(q + 1) * 512],
                               xsh[:, q * 512:(q + 1) * 512], ALU.add))
                c.emit_wait("vector", ev)
                c.emit_inc("vector", ev, 1,
                           lambda e, q=q: e.tensor_tensor(
                               ot[:, q * 512:(q + 1) * 512], ot[:, q * 512:(q + 1) * 512],
                               ytt[:, q * 512:(q + 1) * 512], ALU.add))
            c.emit_wait("sync", ev)
            dstv = out_dst[grp * 512:(grp + 1) * 512, tc * TC:(tc + 1) * TC]
            c.emit_inc("sync", st, 16,
                       lambda e, d=dstv: e.dma_start(
                           out=d.rearrange("(a p) c -> p a c", p=128),
                           in_=ot[:, :].rearrange("p (a c) -> p a c", a=4)))
    c.emit_wait("sync", st)
